# version 05
# baseline (speedup 1.0000x reference)
import os
import numpy as np
import ml_dtypes
import concourse.bass as bass
import concourse.mybir as mybir
from concourse.bass_utils import run_bass_kernel_spmd

F32 = mybir.dt.float32
BF16 = mybir.dt.bfloat16
AF = mybir.ActivationFunctionType
ALU = mybir.AluOpType
AX = mybir.AxisListType

P = 128
T = 2048
DM = 1024
NTB = T // P
ENGINES = ("sync", "tensor", "vector", "scalar", "gpsimd")


class Sem:

    def __init__(self, handle, dma=False):
        self.h = handle
        self.n = 0
        self.dma = dma
        self.events = []

    def inc(self, k=1, gated=False):
        self.n += k
        if self.dma:
            self.events.append((self.n, gated))
        return self.n

    def resolve(self, v):
        if v <= 0:
            return 0
        if not self.dma:
            return v
        idx = None
        for i, (val, _) in enumerate(self.events):
            if val >= v:
                idx = i
                break
        assert idx is not None, (v, self.events[-5:])
        j = idx + 1
        while j < len(self.events) and not self.events[j][1]:
            j += 1
        return self.events[j - 1][0]


class Sched:

    def __init__(self, self_serial=None):
        self.q = {e: [] for e in ENGINES}
        self.self_serial = self_serial or {}

    def op(self, eng, fn, waits=(), inc=None, inc_by=1):
        ss = self.self_serial.get(eng)
        if ss is not None and inc is not None and inc is not ss:
            ss = None
        if ss is not None:
            pre = ss.n
            planned = ss.inc(1)
            def run(e, fn=fn, waits=tuple(waits), ss=ss, pre=pre):
                for sm, v in waits:
                    rv = sm.resolve(v)
                    if rv > 0:
                        e.wait_ge(sm.h, rv)
                ins = fn(e)
                if pre > 0:
                    ins._wait_ge(ss.h, pre)
                ins.then_inc(ss.h, 1)
            self.q[eng].append(run)
            return planned
        planned = inc.inc(inc_by, gated=bool(waits)) if inc is not None else None

        def run(e, fn=fn, waits=tuple(waits), inc=inc, inc_by=inc_by):
            first = True
            ws = []
            for sm, v in waits:
                rv = sm.resolve(v)
                if rv > 0:
                    ws.append((sm, rv))
            for sm, rv in ws[1:]:
                e.wait_ge(sm.h, rv)
            ins = fn(e)
            if ws:
                ins._wait_ge(ws[0][0].h, ws[0][1])
            if inc is not None:
                ins.then_inc(inc.h, inc_by)
        self.q[eng].append(run)
        return planned

    def wait(self, eng, sem, val):
        self.q[eng].append(lambda e, sm=sem, v=val: e.wait_ge(sm.h, sm.resolve(v)))

    def build(self, block):
        for eng in ENGINES:
            qe = self.q[eng]
            if not qe:
                continue

            def body(engobj, qe=qe):
                for fn in qe:
                    fn(engobj)
            getattr(block, eng)(body)


class Arena:
    def __init__(self, tensor, words):
        self.t = tensor
        self.words = words
        self.top = 0
        self.marks = []
        self.peak = 0

    def _take(self, w):
        w = (w + 15) & ~15
        assert self.top + w <= self.words, f"arena overflow {self.top}+{w}>{self.words}"
        off = self.top
        self.top += w
        self.peak = max(self.peak, self.top)
        return off, w

    def alloc(self, shape, dtype=F32):
        assert shape[0] == P
        elems = 1
        for x in shape[1:]:
            elems *= x
        off, _ = self._take(elems if dtype == F32 else (elems + 1) // 2)
        ap = self.t[:, off:off + (elems if dtype == F32 else (elems + 1) // 2)]
        if dtype != F32:
            ap = ap.bitcast(dtype)
        if len(shape) == 2:
            return ap[:, 0:shape[1]]
        if len(shape) == 3:
            return ap[:, 0:shape[1] * shape[2]].rearrange("p (a b) -> p a b", b=shape[2])
        assert len(shape) == 4
        n = shape[1] * shape[2] * shape[3]
        return ap[:, 0:n].rearrange("p (a b c) -> p a b c", b=shape[2], c=shape[3])

    def row(self, n, dtype=F32):
        off, _ = self._take(n if dtype == F32 else (n + 1) // 2)
        ap = self.t[0:1, off:off + (n if dtype == F32 else (n + 1) // 2)]
        if dtype != F32:
            ap = ap.bitcast(dtype)
        return ap[:, 0:n]

    def mark(self):
        self.marks.append(self.top)

    def release(self):
        self.top = self.marks.pop()


def _consts():
    lt = np.triu(np.ones((P, P), dtype=np.float32))
    eh = np.zeros((32, 32 * P), dtype=np.float32)
    for h in range(32):
        eh[h, h * P:(h + 1) * P] = 1.0
    return {"c_ident": np.eye(P, dtype=np.float32),
            "c_identb": np.eye(P, dtype=ml_dtypes.bfloat16),
            "c_onesb": np.ones((1, P), dtype=ml_dtypes.bfloat16),
            "c_onesf": np.ones((1, P), dtype=np.float32),
            "c_lt": lt, "c_eh": eh}


def build_l0(ctx):
    s = ctx["s"]
    ar = ctx["ar"]
    d = ctx["dram"]
    sems = ctx["sems"]
    sem_ld = sems["ld"]; sem_pe = sems["pe"]; sem_act = sems["act"]
    sem_ve = sems["ve"]; sem_st = sems["st"]
    sem_lx = (sems["lxa"], sems["lxb"])
    sem_sxn = sems["sxn"]; sem_sx2 = sems["sx2"]; sem_sscan = sems["sscan"]
    sem_xq = sems["xq"]; sem_w1s = sems["w1s"]; sem_w2s = sems["w2s"]
    sem_fc = sems["fc"]; sem_fl = (sems["fla"], sems["flb"])
    ident = ctx["ident"]; identb = ctx["identb"]; onesb = ctx["onesb"]
    dnA = ctx["dnA"]; dnB = ctx["dnB"]; dnC = ctx["dnC"]
    hp0 = ctx["hp0"]; hp1 = ctx["hp1"]
    cst = ctx["cst"]
    if "cw_sb" in ctx:
        cw_sb = ctx["cw_sb"]; accb = ctx["accb"]; b1c = ctx["b1c"]
        w2b = ctx["w2b"]; b2r = ctx["b2r"]; xnq = ctx["xnq"]
    else:
        cw_sb = ar.alloc([P, NTB, 8])
        accb = ar.alloc([P, NTB, DM], BF16)
        b1c = ar.alloc([P, 8, 32])
        w2b = ar.alloc([P, 32, 1024], BF16)
        b2r_t2 = ar.alloc([P, 2048], BF16)
        b2r = b2r_t2[0:1, :].rearrange("o (a b) -> o a b", b=1024)
        xnq = ar.alloc([P, 8, 512], BF16)

    ar.mark()
    gw = ar.alloc([P, 8, 8])
    Bm = ar.alloc([P, 8, 512], BF16)
    dt4 = ar.alloc([P, 128])
    A4 = ar.alloc([P, 128])
    b1r = ar.alloc([P, 128])
    dtc = ar.alloc([P, 4])
    ac = ar.alloc([P, 4])
    upj = ar.alloc([P, 4, T], BF16)
    gamma_b = ar.alloc([P, DM])
    beta_b = ar.alloc([P, DM])
    normw_b = ar.alloc([P, DM])
    emit_slabs = lambda k, **kw: None
    alloc_slab_bufs = lambda gates=(): None
    if "L1_ffn_w_gate" in d:
        sem_wgh = sems["wgh"]
        sem_wqh = sems["wqh"]
        slq = {}
        slabs = []
        for i in range(22):
            slabs.append(("L1_ffn_w_gate", i, 0, 1024, "wgt_d"))
        for i in range(22):
            slabs.append(("L1_ffn_w_up", i, 0, 1024, "wut_d"))
        for i in range(8):
            for piece in range(3):
                w0 = piece * 1024
                slabs.append(("L1_ffn_w_down", i, w0, min(2816, w0 + 1024) - w0, "wdt_d"))
        NSLAB = len(slabs)
        sl_loaded = {}
        sl_ve_done = {}
        sl_state = {"sj": 0, "tr_prev": 0, "st_prev": 0, "lim": NSLAB}

        def alloc_slab_bufs(gates=()):
            slq["wsl"] = [ar.alloc([P, 1024]) for _ in range(2)]
            slq["wqo"] = ar.alloc([P, 1024], BF16)
            _bm = ar.alloc([P, 64])
            slq["wqb"] = {"t1": ar.alloc([P, 1024], BF16), "t2": slq["wqo"],
                          "q": ar.alloc([P, 1024]), "bm": _bm,
                          "sc": ar.alloc([P, 64]), "rs": _bm}
            slq["wqs"] = ar.alloc([P, 8, 128], BF16)
            slq["gates"] = list(gates)
            emit_slab_load(sl_state["sj"])
            emit_slab_load(sl_state["sj"] + 1)

        def emit_slab_load(j):
            if j >= NSLAB or j >= sl_state["lim"]:
                return
            nm, i, w0, w, _dst = slabs[j]
            gate = list(slq["gates"]) + ([(sem_ve, sl_ve_done[j - 2])] if (j - 2) in sl_ve_done else [])
            dstb = slq["wsl"][j % 2][:, 0:w]
            sl_loaded[j] = s.op("gpsimd", lambda e, nm=nm, i=i, w0=w0, w=w, dstb=dstb: e.dma_start(
                dstb, d[nm][i * P:(i + 1) * P, w0:w0 + w]),
                waits=gate, inc=sem_wgh, inc_by=16)

        def emit_one_slab():
            j = sl_state["sj"]
            if j >= NSLAB or j >= sl_state["lim"]:
                return
            sl_state["sj"] += 1
            nm, i, w0, w, dst = slabs[j]
            wqo = slq["wqo"]
            wqs = slq["wqs"]
            v_q = emit_quant(s, sems, {**slq["wqb"], "absx": slq["wsl"][j % 2]},
                             slq["wsl"][j % 2][:, 0:w], wqo[:, 0:w], w,
                             src_ready=[(sem_wgh, sl_loaded[j])],
                             out_gate=[(sem_wqh, sl_state["tr_prev"])] if sl_state["tr_prev"] else [])
            sl_ve_done[j] = v_q
            nch = w // 128
            for cc in range(nch):
                s.op("sync", lambda e, cc=cc, wqo=wqo, wqs=wqs: e.dma_start_transpose(
                    wqs[:, cc, :], wqo[:, cc * P:(cc + 1) * P]),
                    waits=([(sem_ve, v_q), (sem_wqh, sl_state["st_prev"])] if cc == 0 else []),
                    inc=sem_wqh, inc_by=16)
            sl_state["tr_prev"] = sem_wqh.n
            s.op("sync", lambda e, dst=dst, i=i, w0=w0, nch=nch, wqs=wqs: e.dma_start(
                d[dst].rearrange("(b p) n -> p b n", p=P)[:, w0 // P:w0 // P + nch, i * P:(i + 1) * P],
                wqs[:, 0:nch, :]),
                waits=[(sem_wqh, sl_state["tr_prev"])], inc=sem_wqh, inc_by=16)
            sl_state["st_prev"] = sem_wqh.n
            emit_slab_load(j + 2)

        def emit_slabs(k, lim=None):
            if lim is not None:
                sl_state["lim"] = lim
            for _ in range(k):
                emit_one_slab()

    alloc_slab_bufs()

    s.op("sync", lambda e: e.dma_start(gw[:], d["L0_ffn_gate_w"].rearrange("(c p) e -> p c e", p=P)),
         inc=sem_ld, inc_by=16)
    v_gw = sem_ld.n
    s.op("sync", lambda e: e.dma_start(gamma_b, d["L0_ffn_gamma"].unsqueeze(0).broadcast_to((P, DM))),
         inc=sem_ld, inc_by=16)
    s.op("sync", lambda e: e.dma_start(beta_b, d["L0_ffn_beta"].unsqueeze(0).broadcast_to((P, DM))),
         inc=sem_ld, inc_by=16)
    s.op("sync", lambda e: e.dma_start(normw_b, d["L0_attn_norm_w"].unsqueeze(0).broadcast_to((P, DM))),
         inc=sem_ld, inc_by=16)
    v_rows = sem_ld.n
    s.op("sync", lambda e: e.dma_start(dt4[0:4, :], d["L0_attn_dt"].rearrange("(c p) -> c p", p=P)),
         inc=sem_ld, inc_by=16)
    s.op("sync", lambda e: e.dma_start(A4[0:4, :], d["L0_attn_A"].rearrange("(c p) -> c p", p=P)),
         inc=sem_ld, inc_by=16)
    v_dtA = sem_ld.n
    s.op("gpsimd", lambda e: e.dma_start(Bm[:], d["L0_attn_Bmat"].rearrange("(c p) s -> p c s", p=P)),
         inc=sem_w2s, inc_by=16)
    v_Bm = sem_w2s.n
    w1_done = [None] * 8
    w2_done = [None] * 8
    b2_done = [None] * 8

    def emit_w1(e_, gate_waits):
        for k in range(8):
            wts = list(gate_waits) if k == 0 else []
            s.op("gpsimd", lambda en, e_=e_, k=k: en.dma_start(
                w1b[:, k, :], d["L0_ffn_w1"][e_, k * P:(k + 1) * P, :]),
                waits=wts, inc=sem_w1s, inc_by=16)
        w1_done[e_] = sem_w1s.n

    def emit_w2b2(e_, gate_pe):
        s.op("gpsimd", lambda en, e_=e_: en.dma_start(
            b2r[0:1, e_ % 2, :], d["L0_ffn_b2"][e_:e_ + 1, :]),
            waits=[(sem_pe, gate_pe)] if gate_pe is not None else [], inc=sem_w2s, inc_by=16)
        b2_done[e_] = sem_w2s.n
        for g in range(4):
            s.op("gpsimd", lambda en, e_=e_, g=g: en.dma_start(
                w2b[:, g * 8:(g + 1) * 8, :],
                d["L0_ffn_w2"][e_, g * 1024:(g + 1) * 1024, :].rearrange("(j p) dd -> p j dd", p=P)),
                inc=sem_w2s, inc_by=16)
        w2_done[e_] = sem_w2s.n

    emit_w2b2(0, None)

    v_b1p = 0
    for e_ in range(8):
        v_l = s.op("sync", lambda en, e_=e_: en.dma_start(
            b1r[0:32, :], d["L0_ffn_b1"][e_].rearrange("(n p) -> n p", p=P)),
            waits=([] if e_ == 0 else [(sem_ve, v_b1p)]), inc=sem_ld, inc_by=16)
        v_t = s.op("tensor", lambda en: en.matmul(hp0[:, 0:32], b1r[0:32, :], ident[0:32, 0:32],
                                                  start=True, stop=True),
                   waits=[(sem_ld, v_l)] + ([(sem_ve, v_b1p)] if e_ else []), inc=sem_pe)
        v_b1p = s.op("vector", lambda en, e_=e_: en.tensor_copy(b1c[:, e_, :], hp0[:, 0:32]),
                     waits=[(sem_pe, v_t)], inc=sem_ve)
    v_b1all = v_b1p
    PH = os.environ.get("KPHASES", "all")

    def trivial_final_u():
        ar.mark()
        tmpf = ar.alloc([P, DM])
        vcp = 0
        vst = 0
        for tb in range(NTB):
            vl = s.op("sync", lambda e, tb=tb: e.dma_start(tmpf[:], d["u"][tb * P:(tb + 1) * P, :]),
                      waits=[(sem_st, vst)] if vst else [], inc=sem_fc, inc_by=16)
            vst = s.op("sync", lambda e, tb=tb: e.dma_start(d["x1"][tb * P:(tb + 1) * P, :], tmpf[:]),
                      waits=[(sem_fc, vl)], inc=sem_st, inc_by=16)
        ar.release()

    def trivial_final():
        ar.mark()
        tmpb = ar.alloc([P, DM], BF16)
        tmpf = ar.alloc([P, DM])
        vcp = 0
        vst = 0
        for tb in range(NTB):
            vl = s.op("sync", lambda e, tb=tb: e.dma_start(tmpb[:], d["xn_d"][tb * P:(tb + 1) * P, :]),
                      waits=[(sem_ve, vcp)] if vcp else [], inc=sem_fc, inc_by=16)
            vcp = s.op("vector", lambda e: e.tensor_copy(tmpf[:], tmpb[:]),
                       waits=[(sem_fc, vl)] + ([(sem_st, vst)] if vst else []), inc=sem_ve)
            vst = s.op("sync", lambda e, tb=tb: e.dma_start(d["x1"][tb * P:(tb + 1) * P, :], tmpf[:]),
                       waits=[(sem_ve, vcp)], inc=sem_st, inc_by=16)
        ar.release()

    if PH == "pro":
        ar.release()
        trivial_final_u()
        return
    ar.mark()
    xr = ar.alloc([P, 2, DM])
    xc = ar.alloc([P, DM])
    xsq = ar.alloc([P, DM])
    xn_t = ar.alloc([P, DM])
    xn2_t = ar.alloc([P, DM])
    xg = ar.alloc([P, DM])
    xn2f = ar.alloc([P, 8, 128])
    xn2b = ar.alloc([P, 8, 128], BF16)
    attb = ar.alloc([P, DM], BF16)
    attf = ar.alloc([P, 8, 128], BF16)
    xnb = ar.alloc([P, DM], BF16)
    lg = ar.alloc([P, 8])
    st_ = ar.alloc([P, 16])
    zrt = ar.alloc([P, 8])
    msk = ar.alloc([P, 8])

    x_rd = {}
    x_consumed = {}
    last_hp0_act = 0
    prev_xn2dma = 0
    prev_xndma = 0
    LNTB = int(os.environ.get("KLNTB", NTB))
    SKIP = set(os.environ.get("KLNSKIP", "").split(","))
    DEPTH = int(os.environ.get("KLNDEPTH", 99))

    def emit_xload(tb):
        waits = []
        if tb >= 2:
            waits.append((sem_ve, x_consumed[tb - 2]))
        x_rd[tb] = s.op("sync", lambda e, tb=tb: e.dma_start(
            xr[:, tb % 2, :], d["u"][tb * P:(tb + 1) * P, :]),
            waits=waits, inc=sem_lx[tb % 2], inc_by=16)

    emit_xload(0)
    if LNTB > 1:
        emit_xload(1)
    for tb in range(LNTB):
        xin = xr[:, tb % 2, :]
        s.op("vector", lambda e, xin=xin: e.reduce_sum(st_[:, 0:1], xin, axis=AX.X),
             waits=[(sem_lx[tb % 2], x_rd[tb])])
        s.op("vector", lambda e: e.tensor_scalar(st_[:, 0:1], st_[:, 0:1], 1.0 / DM, None, ALU.mult))
        v_xc = s.op("vector", lambda e, xin=xin: e.tensor_scalar(xc[:], xin, st_[:, 0:1], None, ALU.subtract),
                    inc=sem_ve)
        x_consumed[tb] = v_xc
        if tb + 2 < LNTB:
            emit_xload(tb + 2)
        s.op("scalar", lambda e: e.activation(xsq[:], xc[:], AF.Square, accum_out=st_[:, 1:2]),
             waits=[(sem_ve, v_xc)])
        s.op("scalar", lambda e: e.activation(st_[:, 1:2], st_[:, 1:2], AF.Ln, bias=cst[:, 0:1], scale=1.0 / DM))
        v_rs = s.op("scalar", lambda e: e.activation(st_[:, 1:2], st_[:, 1:2], AF.Exp, scale=-0.5),
                    inc=sem_act)
        v_xn = s.op("vector", lambda e: e.tensor_scalar(xn_t[:], xc[:], st_[:, 1:2], None, ALU.mult),
                    waits=[(sem_act, v_rs)], inc=sem_ve)
        if "xnd" in SKIP:
            v_xnb = None
        if "xnd" not in SKIP:
            v_xnb = s.op("vector", lambda e: e.tensor_copy(xnb[:], xn_t[:]),
                         waits=[(sem_sxn, prev_xndma)] if prev_xndma else [], inc=sem_ve)
            prev_xndma = s.op("sync", lambda e, tb=tb: e.dma_start(d["xn_d"][tb * P:(tb + 1) * P, :], xnb[:]),
                              waits=[(sem_ve, v_xnb)], inc=sem_sxn, inc_by=16)
        if DEPTH < 2:
            continue
        s.op("vector", lambda e: e.reduce_sum(st_[:, 2:3], xn_t[:], axis=AX.X))
        s.op("vector", lambda e: e.tensor_scalar(st_[:, 2:3], st_[:, 2:3], 1.0 / DM, None, ALU.mult))
        v_xc2 = s.op("vector", lambda e: e.tensor_scalar(xc[:], xn_t[:], st_[:, 2:3], None, ALU.subtract),
                     inc=sem_ve)
        s.op("scalar", lambda e: e.activation(xsq[:], xc[:], AF.Square, accum_out=st_[:, 3:4]),
             waits=[(sem_ve, v_xc2)])
        s.op("scalar", lambda e: e.activation(st_[:, 3:4], st_[:, 3:4], AF.Ln, bias=cst[:, 0:1], scale=1.0 / DM))
        v_rs2 = s.op("scalar", lambda e: e.activation(st_[:, 3:4], st_[:, 3:4], AF.Exp, scale=-0.5),
                     inc=sem_act)
        s.op("vector", lambda e: e.scalar_tensor_tensor(xg[:], xc[:], st_[:, 3:4], gamma_b, ALU.mult, ALU.mult),
             waits=[(sem_act, v_rs2), (sem_ld, v_rows)])
        v_xn2 = s.op("vector", lambda e: e.tensor_tensor(xn2_t[:], xg[:], beta_b, ALU.add), inc=sem_ve)
        if DEPTH < 3:
            continue
        s.op("scalar", lambda e: e.activation(xsq[:], xn_t[:], AF.Square, accum_out=st_[:, 4:5]),
             waits=[(sem_ve, v_xn)])
        s.op("scalar", lambda e: e.activation(st_[:, 4:5], st_[:, 4:5], AF.Ln, bias=cst[:, 1:2], scale=1.0 / DM))
        v_rs3 = s.op("scalar", lambda e: e.activation(st_[:, 4:5], st_[:, 4:5], AF.Exp, scale=-0.5),
                     inc=sem_act)
        v_att = s.op("vector", lambda e: e.scalar_tensor_tensor(
            attb[:], xn_t[:], st_[:, 4:5], normw_b, ALU.mult, ALU.mult),
            waits=[(sem_act, v_rs3)], inc=sem_ve)
        if DEPTH < 4:
            continue
        hp1b = hp1[:, 0:P].bitcast(BF16)[:, 0:P]
        for c in range(8):
            w1_ = [(sem_ve, v_xn2 if c == 0 else sem_ve.n)]
            v_t1 = s.op("tensor", lambda e, c=c: e.transpose(hp0[:, 0:P], xn2_t[:, c * P:(c + 1) * P], ident[:]),
                        waits=w1_, inc=sem_pe)
            v_c1 = s.op("vector", lambda e, c=c: e.tensor_copy(xn2f[:, c, :], hp0[:, 0:P]),
                        waits=[(sem_pe, v_t1)], inc=sem_ve)
            wc1b = []
            if c == 0 and prev_xn2dma:
                wc1b.append((sem_sx2, prev_xn2dma))
            v_c1b = s.op("vector", lambda e, c=c: e.tensor_copy(xn2b[:, c, :], xn2f[:, c, :]),
                         waits=wc1b, inc=sem_ve)
            v_t2 = s.op("tensor", lambda e, c=c, hp1b=hp1b: e.transpose(
                hp1b, attb[:, c * P:(c + 1) * P], identb[:]),
                waits=[(sem_ve, v_att if c == 0 else sem_ve.n)], inc=sem_pe)
            s.op("vector", lambda e, c=c, hp1b=hp1b: e.tensor_copy(attf[:, c, :], hp1b),
                 waits=[(sem_pe, v_t2)], inc=sem_ve)
        v_xn2b_done = sem_ve.n
        v_trdone = sem_ve.n
        if DEPTH < 5:
            continue
        prev_xn2dma = s.op("sync", lambda e, tb=tb: e.dma_start(
            d["xn2fm_d"].rearrange("(c p) t -> p c t", p=P)[:, :, tb * P:(tb + 1) * P], xn2b[:]),
            waits=[(sem_ve, v_xn2b_done)], inc=sem_sx2, inc_by=16)
        if DEPTH < 6:
            continue
        for c in range(8):
            def mml(e, c=c):
                return e.matmul(dnA[:, 0:8], xn2f[:, c, :], gw[:, c, :],
                                start=(c == 0), stop=(c == 7))
            if c == 0:
                s.op("tensor", mml, waits=[(sem_ve, v_trdone), (sem_ld, v_gw)])
            elif c == 7:
                v_lgmm = s.op("tensor", mml, inc=sem_pe)
            else:
                s.op("tensor", mml)
        v_lgc = s.op("vector", lambda e: e.tensor_copy(lg[:], dnA[:, 0:8]),
                     waits=[(sem_pe, v_lgmm)], inc=sem_ve)
        s.op("vector", lambda e: e.reduce_max(st_[:, 5:6], lg[:], axis=AX.X))
        s.op("vector", lambda e: e.tensor_scalar(st_[:, 6:7], st_[:, 5:6], -1.0, None, ALU.mult))
        s.op("vector", lambda e: e.tensor_scalar(msk[:], lg[:], st_[:, 5:6], None, ALU.is_ge))
        s.op("vector", lambda e: e.scalar_tensor_tensor(zrt[:], msk[:], -1e30, lg[:], ALU.mult, ALU.add))
        s.op("vector", lambda e: e.reduce_max(st_[:, 7:8], zrt[:], axis=AX.X))
        v_mk = s.op("vector", lambda e: e.tensor_scalar(msk[:], lg[:], st_[:, 7:8], None, ALU.is_ge),
                    inc=sem_ve)
        v_ez = s.op("scalar", lambda e: e.activation(zrt[:], lg[:], AF.Exp, bias=st_[:, 6:7]),
                    waits=[(sem_ve, v_mk)], inc=sem_act)
        s.op("vector", lambda e: e.tensor_tensor(zrt[:], zrt[:], msk[:], ALU.mult),
             waits=[(sem_act, v_ez)])
        s.op("vector", lambda e: e.reduce_sum(st_[:, 8:9], zrt[:], axis=AX.X))
        s.op("vector", lambda e: e.reciprocal(st_[:, 9:10], st_[:, 8:9]))
        s.op("vector", lambda e, tb=tb: e.tensor_scalar(cw_sb[:, tb, :], zrt[:], st_[:, 9:10], None, ALU.mult),
             inc=sem_ve)
        if DEPTH < 7:
            continue
        for st4 in range(4):
            psU = (dnB if st4 % 2 == 0 else dnC)[:, 0:P]
            for k in range(8):
                def mmu(e, st4=st4, k=k, psU=psU):
                    return e.matmul(psU, Bm[:, k, st4 * P:(st4 + 1) * P], attf[:, k, :],
                                    start=(k == 0), stop=(k == 7))
                if k == 0:
                    wts = [(sem_ve, v_trdone)]
                    if tb == 0 and st4 == 0:
                        wts.append((sem_w2s, v_Bm))
                    if st4 >= 2 or tb > 0:
                        wts.append((sem_ve, sem_ve.n))
                    s.op("tensor", mmu, waits=wts)
                elif k == 7:
                    v_um = s.op("tensor", mmu, inc=sem_pe)
                else:
                    s.op("tensor", mmu)
            s.op("vector", lambda e, st4=st4, tb=tb, psU=psU: e.tensor_copy(
                upj[:, st4, tb * P:(tb + 1) * P], psU),
                waits=[(sem_pe, v_um)], inc=sem_ve)
        emit_slabs(3, lim=52)
    v_ln_pe = sem_pe.n
    ar.release()
    if PH == "ln":
        ar.release()
        trivial_final()
        return

    ar.mark()
    binc = ar.alloc([P, T])
    xsc_f = ar.alloc([P, T])
    xsc_b = ar.alloc([P, T], BF16)
    s.op("scalar", lambda e: e.activation(dt4[0:4, :], dt4[0:4, :], AF.Sigmoid),
         waits=[(sem_ld, v_dtA)])
    v_ab = s.op("scalar", lambda e: e.activation(A4[0:4, :], A4[0:4, :], AF.Abs), inc=sem_act)
    v_m = s.op("vector", lambda e: e.tensor_tensor(A4[0:4, :], A4[0:4, :], dt4[0:4, :], ALU.mult),
               waits=[(sem_act, v_ab)], inc=sem_ve)
    v_ex = s.op("scalar", lambda e: e.activation(A4[0:4, :], A4[0:4, :], AF.Exp, scale=-1.0),
                waits=[(sem_ve, v_m)], inc=sem_act)
    v_tt1 = s.op("tensor", lambda e: e.matmul(hp0[:, 0:4], dt4[0:4, :], ident[0:4, 0:4],
                                              start=True, stop=True),
                 waits=[(sem_act, v_ex), (sem_ve, sem_ve.n)], inc=sem_pe)
    v_tt2 = s.op("tensor", lambda e: e.matmul(hp1[:, 0:4], A4[0:4, :], ident[0:4, 0:4],
                                              start=True, stop=True), inc=sem_pe)
    v_dc = s.op("vector", lambda e: e.tensor_copy(dtc[:], hp0[:, 0:4]), waits=[(sem_pe, v_tt1)], inc=sem_ve)
    v_ac = s.op("vector", lambda e: e.tensor_copy(ac[:], hp1[:, 0:4]), waits=[(sem_pe, v_tt2)], inc=sem_ve)
    v_sdma = 0
    for c in range(4):
        wts = [(sem_ve, max(v_dc, v_ac))] if c == 0 else [(sem_sscan, v_sdma)]
        s.op("vector", lambda e, c=c: e.tensor_scalar(binc[:], upj[:, c, :], dtc[:, c:c + 1], None, ALU.mult),
             waits=wts)
        s.op("vector", lambda e, c=c: e.tensor_tensor_scan(
            xsc_f[:], ac[:, c:c + 1].broadcast_to((P, T)), binc[:], 0.0, ALU.mult, ALU.add))
        v_cb = s.op("vector", lambda e: e.tensor_copy(xsc_b[:], xsc_f[:]), inc=sem_ve)
        v_sdma = s.op("sync", lambda e, c=c: e.dma_start(d["xscan_d"][c * P:(c + 1) * P, :], xsc_b[:]),
                      waits=[(sem_ve, v_cb)], inc=sem_sscan, inc_by=16)
    emit_slabs(4, lim=52)
    v_scan_ve = sem_ve.n
    ar.release()
    ar.release()
    if PH == "scan":
        trivial_final()
        return

    ar.mark()
    w1b = ar.alloc([P, 8, 4096], BF16)
    hb = ar.alloc([P, 32, 512], BF16)
    QT = 512

    v_z = s.op("vector", lambda e: e.memset(accb[:], 0.0), inc=sem_ve)
    w1_gates = [(sem_ve, v_scan_ve), (sem_sscan, v_sdma)]
    if "L1_ffn_w_gate" in d:
        w1_gates.append((sems["wqh"], sl_state["st_prev"]))
    emit_w1(0, w1_gates)

    xnq_done = {}

    def emit_xnq(e_, q, gate_pe):
        xnq_done[(e_, q)] = s.op("sync", lambda en, q=q: en.dma_start(
            xnq[:],
            d["xn2fm_d"].rearrange("(c p) t -> p c t", p=P)[:, :, q * QT:(q + 1) * QT]),
            waits=[(sem_pe, gate_pe)] if gate_pe is not None else [], inc=sem_xq, inc_by=16)

    xnq_done[(0, 0)] = s.op("sync", lambda en: en.dma_start(
        xnq[:], d["xn2fm_d"].rearrange("(c p) t -> p c t", p=P)[:, :, 0:QT]),
        waits=[(sem_sx2, sem_sx2.n)], inc=sem_xq, inc_by=16)

    dn_evac = []
    ngrp = 0
    prev_g30 = None
    prev_g31 = None
    pe_after_h = {}
    pe_after_dn = {}
    for e_ in range(8):
        for q in range(4):
            gelu_vals = []
            for n in range(32):
                ps = hp0 if n % 2 == 0 else hp1
                waits = []
                if n == 0:
                    waits.append((sem_xq, xnq_done[(e_, q)]))
                    waits.append((sem_w1s, w1_done[e_]))
                    if e_ == 0 and q == 0:
                        waits.append((sem_ve, max(v_b1all, v_scan_ve)))
                        waits.append((sem_pe, v_ln_pe))
                    else:
                        waits.append((sem_act, prev_g30))
                if n == 1 and prev_g31 is not None and not (e_ == 0 and q == 0):
                    waits.append((sem_act, prev_g31))
                if n >= 2:
                    waits.append((sem_act, gelu_vals[n - 2]))
                for k in range(8):
                    def mmh(e, k=k, n=n, ps=ps):
                        return e.matmul(ps[:], w1b[:, k, n * P:(n + 1) * P], xnq[:, k, :],
                                        start=(k == 0), stop=(k == 7))
                    if k == 0:
                        s.op("tensor", mmh, waits=waits)
                    elif k == 7:
                        v_hm = s.op("tensor", mmh, inc=sem_pe)
                    else:
                        s.op("tensor", mmh)
                gelu_vals.append(s.op("scalar", lambda e, n=n, ps=ps, e_=e_: e.activation(
                    hb[:, n, :], ps[:], AF.Gelu, bias=b1c[:, e_, n:n + 1]),
                    waits=[(sem_pe, v_hm)], inc=sem_act))
            prev_g30, prev_g31 = gelu_vals[30], gelu_vals[31]
            pe_after_h[(e_, q)] = sem_pe.n
            nxt = (e_, q + 1) if q < 3 else (e_ + 1, 0)
            if nxt[0] < 8:
                emit_xnq(nxt[0], nxt[1], pe_after_h[(e_, q)])
            for tb in range(4):
                tbg = q * 4 + tb
                ps = (dnA, dnB, dnC)[ngrp % 3]
                waits = [(sem_act, gelu_vals[-1])]
                if ngrp == 0:
                    waits.append((sem_w2s, w2_done[0]))
                    waits.append((sem_ve, v_scan_ve))
                elif ngrp >= 3:
                    waits.append((sem_ve, dn_evac[ngrp - 3]))
                first = True
                for n in range(32):
                    for j in range(2):
                        def mmd(e, n=n, j=j, tb=tb, ps=ps):
                            return e.matmul(ps[:, j * 512:(j + 1) * 512],
                                            hb[:, n, tb * P:(tb + 1) * P],
                                            w2b[:, n, j * 512:(j + 1) * 512],
                                            start=(n == 0), stop=False, skip_group_check=True)
                        if first:
                            s.op("tensor", mmd, waits=waits)
                            first = False
                        else:
                            s.op("tensor", mmd)
                for j in range(2):
                    def mmb(e, j=j, ps=ps, e_=e_):
                        return e.matmul(ps[:, j * 512:(j + 1) * 512], onesb[:],
                                        b2r[0:1, e_ % 2, j * 512:(j + 1) * 512],
                                        start=False, stop=True, skip_group_check=True)
                    if j == 1:
                        v_dm = s.op("tensor", mmb, inc=sem_pe)
                    else:
                        s.op("tensor", mmb)
                dn_evac.append(s.op("vector", lambda e, tbg=tbg, ps=ps, e_=e_: e.scalar_tensor_tensor(
                    accb[:, tbg, :], ps[:], cw_sb[:, tbg, e_].unsqueeze(-1), accb[:, tbg, :],
                    ALU.mult, ALU.add),
                    waits=[(sem_pe, v_dm), (sem_ve, v_z)], inc=sem_ve))
                ngrp += 1
            pe_after_dn[(e_, q)] = sem_pe.n
        if e_ + 1 < 8:
            emit_w1(e_ + 1, [(sem_pe, pe_after_h[(e_, 3)])])
            emit_w2b2(e_ + 1, pe_after_dn[(e_, 3)])
    v_moe_ve = sem_ve.n
    ar.release()
    if PH == "moe":
        s.wait("vector", sem_ve, v_moe_ve)
        trivial_final()
        return

    ar.mark()
    C_sb = ar.alloc([P, 4, 1024], BF16)
    xscan = ar.alloc([P, 4, T], BF16)
    D_b = ar.alloc([P, DM])
    xr2 = ar.alloc([P, 2, DM])
    xnr = ar.alloc([P, 2, DM], BF16)
    sD = ar.alloc([P, DM])
    gsum = ar.alloc([P, DM])
    gout = ar.alloc([P, DM])
    t3 = ar.alloc([P, DM])
    xout = ar.alloc([P, DM])
    emit_slabs(0, lim=10 ** 9)
    alloc_slab_bufs(gates=[(sem_pe, sem_pe.n), (sem_act, sem_act.n), (sem_ve, sem_ve.n)])
    v_C = s.op("gpsimd", lambda e: e.dma_start(C_sb[:], d["L0_attn_C"].rearrange("(c p) o -> p c o", p=P)),
               waits=[(sem_pe, pe_after_dn[(7, 3)])], inc=sem_w2s, inc_by=16)
    v_xs = s.op("sync", lambda e: e.dma_start(xscan[:], d["xscan_d"].rearrange("(c p) t -> p c t", p=P)),
                waits=[(sem_pe, pe_after_dn[(7, 3)]), (sem_sscan, sem_sscan.n)], inc=sem_fc, inc_by=16)
    v_D = s.op("sync", lambda e: e.dma_start(D_b, d["L0_attn_D"].unsqueeze(0).broadcast_to((P, DM))),
                inc=sem_fc, inc_by=16)

    f_x = {}
    f_xn = {}
    f_consumed = {}
    f_sD = {}

    def emit_fload(tb):
        sm = sem_fl[tb % 2]
        waits = [(sem_pe, pe_after_dn[(7, 3)])] if tb == 0 else []
        if tb >= 2:
            waits.append((sem_ve, f_consumed[tb - 2]))
        s.op("sync", lambda e, tb=tb: e.dma_start(xr2[:, tb % 2, :], d["u"][tb * P:(tb + 1) * P, :]),
             waits=waits, inc=sm, inc_by=16)
        f_x[tb] = (sm, sm.n)
        s.op("sync", lambda e, tb=tb: e.dma_start(xnr[:, tb % 2, :], d["xn_d"][tb * P:(tb + 1) * P, :]),
             waits=[(sem_sxn, sem_sxn.n)] if tb == 0 else [], inc=sm, inc_by=16)
        f_xn[tb] = (sm, sm.n)

    emit_fload(0)
    emit_fload(1)
    for tb in range(NTB):
        ps = (dnA, dnB, dnC)[tb % 3]
        waits = []
        if tb == 0:
            waits = [(sem_fc, v_D), (sem_w2s, v_C), (sem_ve, v_moe_ve)]
        elif tb >= 3:
            waits = [(sem_ve, f_sD[tb - 3])]
        first = True
        for sc in range(4):
            for j in range(2):
                def mms(e, sc=sc, j=j, tb=tb, ps=ps):
                    return e.matmul(ps[:, j * 512:(j + 1) * 512],
                                    xscan[:, sc, tb * P:(tb + 1) * P],
                                    C_sb[:, sc, j * 512:(j + 1) * 512],
                                    start=(sc == 0), stop=(sc == 3), skip_group_check=True)
                if first:
                    s.op("tensor", mms, waits=waits)
                    first = False
                elif sc == 3 and j == 1:
                    v_sm = s.op("tensor", mms, inc=sem_pe)
                else:
                    s.op("tensor", mms)
        f_sD[tb] = s.op("vector", lambda e, ps=ps: e.tensor_tensor(sD[:], ps[:], D_b, ALU.add),
                        waits=[(sem_pe, v_sm), (sem_fc, v_D)], inc=sem_ve)
        v_gs = s.op("vector", lambda e, tb=tb: e.tensor_tensor(
            gsum[:], accb[:, tb, :], xnr[:, tb % 2, :], ALU.add),
            waits=[f_xn[tb]], inc=sem_ve)
        v_g = s.op("scalar", lambda e: e.activation(gout[:], gsum[:], AF.Silu),
                   waits=[(sem_ve, v_gs)], inc=sem_act)
        s.op("vector", lambda e: e.tensor_tensor(t3[:], sD[:], gout[:], ALU.mult),
             waits=[(sem_act, v_g)])
        wxo = [f_x[tb]]
        if tb > 0:
            wxo.append((sem_st, sem_st.n))
        v_xo = s.op("vector", lambda e, tb=tb: e.tensor_tensor(xout[:], xr2[:, tb % 2, :], t3[:], ALU.add),
                    waits=wxo, inc=sem_ve)
        f_consumed[tb] = v_xo
        s.op("sync", lambda e, tb=tb: e.dma_start(d["x1"][tb * P:(tb + 1) * P, :], xout[:]),
             waits=[(sem_ve, v_xo)], inc=sem_st, inc_by=16)
        if tb + 2 < NTB:
            emit_fload(tb + 2)
        emit_slabs(1)
    emit_slabs(100)
    ar.release()





def emit_quant(s, sems, bufs, src_ap, out_ap, W, src_ready=(), eng_big="vector",
               out_gate=()):
    sem_ve = sems["ve"]; sem_gp = sems["gp"]; sem_act = sems["act"]
    nb = W // 16
    absx, t1, t2, q = bufs["absx"], bufs["t1"], bufs["t2"], bufs["q"]
    bm, sc, rs = bufs["bm"], bufs["sc"], bufs["rs"]
    v_sg = s.op("scalar", lambda e: e.sign(q[:, 0:W], src_ap),
                waits=list(src_ready) + [(sem_ve, sems["ve"].n)])
    v_a = s.op("vector", lambda e: e.tensor_tensor(absx[:, 0:W], src_ap, q[:, 0:W], ALU.mult),
               waits=list(src_ready) + [(sem_gp, sem_gp.n), (sem_act, v_sg)])
    s.op("vector", lambda e: e.tensor_reduce(bm[:, 0:nb].unsqueeze(-1),
                                             absx[:, 0:W].rearrange("p (a b) -> p a b", b=16),
                                             axis=AX.X, op=ALU.max))
    s.op("vector", lambda e: e.tensor_scalar(sc[:, 0:nb], bm[:, 0:nb], 1.0 / 6.0, 1e-12,
                                             ALU.mult, ALU.max))
    s.op("vector", lambda e: e.reciprocal(rs[:, 0:nb], sc[:, 0:nb]))
    rs_b = rs[:, 0:nb].unsqueeze(-1).broadcast_to((P, nb, 16))
    sc_b = sc[:, 0:nb].unsqueeze(-1).broadcast_to((P, nb, 16))
    v_v = s.op("vector", lambda e: e.tensor_tensor(
        absx[:, 0:W].rearrange("p (a b) -> p a b", b=16),
        absx[:, 0:W].rearrange("p (a b) -> p a b", b=16), rs_b, ALU.mult))
    vv = absx[:, 0:W]
    v_ss = s.op("vector", lambda e: e.tensor_tensor(
        q[:, 0:W].rearrange("p (a b) -> p a b", b=16),
        q[:, 0:W].rearrange("p (a b) -> p a b", b=16), sc_b, ALU.mult),
        waits=[(sem_act, v_sg)])
    s.op("vector", lambda e: e.tensor_scalar(t1[:, 0:W], vv, 0.25, None, ALU.is_gt))
    s.op("vector", lambda e: e.scalar_tensor_tensor(t1[:, 0:W], vv, 0.75, t1[:, 0:W],
                                                    ALU.is_gt, ALU.add))
    s.op("vector", lambda e: e.scalar_tensor_tensor(t1[:, 0:W], vv, 1.25, t1[:, 0:W],
                                                    ALU.is_gt, ALU.add))
    g4 = s.op("vector", lambda e: e.scalar_tensor_tensor(t1[:, 0:W], vv, 1.75, t1[:, 0:W],
                                                         ALU.is_gt, ALU.add))
    s.op("vector", lambda e: e.tensor_scalar(t2[:, 0:W], vv, 2.5, None, ALU.is_gt))
    s.op("vector", lambda e: e.scalar_tensor_tensor(t2[:, 0:W], vv, 3.5, t2[:, 0:W],
                                                    ALU.is_gt, ALU.add))
    s.op("vector", lambda e: e.scalar_tensor_tensor(t2[:, 0:W], vv, 5.0, t2[:, 0:W],
                                                    ALU.is_gt, ALU.add))
    s.op("vector", lambda e: e.scalar_tensor_tensor(t2[:, 0:W], vv, 5.0, t2[:, 0:W],
                                                    ALU.is_gt, ALU.add))
    s.op("vector", lambda e: e.scalar_tensor_tensor(t2[:, 0:W], t1[:, 0:W], 0.5, t2[:, 0:W],
                                                    ALU.mult, ALU.add))
    v_out = s.op("vector", lambda e: e.tensor_tensor(out_ap, t2[:, 0:W], q[:, 0:W], ALU.mult),
                 waits=list(out_gate))
    return v_out


def build_l1(ctx):
    s = ctx["s"]
    ar = ctx["ar"]
    d = ctx["dram"]
    sems = ctx["sems"]
    cst = ctx["cst"]
    sem_ld = sems["ld"]; sem_pe = sems["pe"]; sem_act = sems["act"]
    sem_ve = sems["ve"]; sem_st = sems["st"]; sem_gp = sems["gp"]
    sem_lx = (sems["lxa"], sems["lxb"])
    sem_sxf = sems["sx2"]
    sem_sxq = sems["sscan"]
    sem_sbz = sems["sxn"]
    sem_cc = (sems["fla"], sems["flb"])
    sem_ss1 = sems["fc"]
    sem_wg = sems["w1s"]
    sem_wq = sems["wqh"] if "wqh" in sems else sems["w2s"]
    sem_xq = sems["xq"]
    ident = ctx["ident"]; identb = ctx["identb"]
    dnA = ctx["dnA"]; dnB = ctx["dnB"]; dnC = ctx["dnC"]
    hp0 = ctx["hp0"]; hp1 = ctx["hp1"]
    DI = 2048
    NH = 32
    HD = 64
    DS = 128

    ar.mark()
    dtraw = ar.alloc([P, NTB, NH])
    Wdt = ar.alloc([P, 8, NH])
    w4 = ar.alloc([P, 18, 4])
    cb_c = ar.alloc([P, 18])
    A_bc = ar.alloc([P, NH])
    D_bc = ar.alloc([P, NH])
    dtb_bc = ar.alloc([P, NH])
    normw_bc = ar.alloc([P, DI])
    eh_flat = ar.alloc([P, 4096])
    eh_sb = eh_flat[0:32, :]
    lt_sb = ar.alloc([P, P])
    onesf_t = ar.alloc([P, 128])
    onesf = onesf_t[0:1, :]
    state = ar.alloc([P, DI])
    state_bf = ar.alloc([P, DI], BF16)
    BmF = ar.alloc([P, T], BF16)
    CmF = ar.alloc([P, T], BF16)
    Wout_sb = ar.alloc([P, 16, 1024], BF16)

    gate0 = ctx.get("l1_gate", [])
    s.op("sync", lambda e: e.dma_start(lt_sb[:], d["c_lt"]), waits=gate0, inc=sem_ld, inc_by=16)
    s.op("sync", lambda e: e.dma_start(onesf_t[0:1, :], d["c_onesf"]), inc=sem_ld, inc_by=16)
    s.op("sync", lambda e: e.dma_start(eh_flat[0:32, :], d["c_eh"]), inc=sem_ld, inc_by=16)
    s.op("sync", lambda e: e.dma_start(
        Wdt[:], d["L1_attn_W_in"][:, 4352:4384].rearrange("(c p) e -> p c e", p=P)),
        inc=sem_ld, inc_by=16)
    s.op("sync", lambda e: e.dma_start(
        normw_bc[:], d["L1_attn_norm_w"].unsqueeze(0).broadcast_to((P, DI))),
        inc=sem_ld, inc_by=16)
    s.op("sync", lambda e: e.dma_start(
        dtb_bc[:], d["L1_attn_dt_bias"].unsqueeze(0).broadcast_to((P, NH))),
        inc=sem_ld, inc_by=16)
    ar.mark()
    arow = ar.alloc([P, NH])
    s.op("sync", lambda e: e.dma_start(arow[0:1, :], d["L1_attn_A_log"].unsqueeze(0)),
         inc=sem_ld, inc_by=16)
    drow = ar.alloc([P, NH])
    s.op("sync", lambda e: e.dma_start(drow[0:1, :], d["L1_attn_D"].unsqueeze(0)),
         inc=sem_ld, inc_by=16)
    cwst = ar.alloc([P, 512])
    s.op("sync", lambda e: e.dma_start(
        cwst[0:18, :], d["L1_attn_conv_w"].rearrange("(c p) j -> c (p j)", p=P)),
        inc=sem_ld, inc_by=16)
    s.op("sync", lambda e: e.dma_start(
        cwst[32:50, 0:P], d["L1_attn_conv_b"].rearrange("(c p) -> c p", p=P)),
        inc=sem_ld, inc_by=16)
    v_par = sem_ld.n

    v_ae = s.op("scalar", lambda e: e.activation(arow[0:1, :], arow[0:1, :], AF.Exp),
                waits=[(sem_ld, v_par)])
    v_an = s.op("vector", lambda e: e.tensor_scalar(arow[0:1, :], arow[0:1, :], -1.0, None, ALU.mult),
                waits=[(sem_act, v_ae)])
    v_tA = s.op("tensor", lambda e: e.matmul(hp0[:, 0:NH], onesf, arow[0:1, :], start=True, stop=True),
                waits=[(sem_ve, v_an), (sem_ld, v_par)], inc=sem_pe)
    v_cA = s.op("vector", lambda e: e.tensor_copy(A_bc[:], hp0[:, 0:NH]), waits=[(sem_pe, v_tA)])
    v_tD = s.op("tensor", lambda e: e.matmul(hp1[:, 0:NH], onesf, drow[0:1, :], start=True, stop=True),
                waits=[(sem_ve, v_cA)], inc=sem_pe)
    v_cD = s.op("vector", lambda e: e.tensor_copy(D_bc[:], hp1[:, 0:NH]), waits=[(sem_pe, v_tD)])
    for j in range(4):
        v_tw = s.op("tensor", lambda e, j=j: e.matmul(
            hp0[:, 0:18], cwst[0:18, :].rearrange("c (p j) -> c p j", j=4)[:, :, j],
            ident[0:18, 0:18], start=True, stop=True),
            waits=[(sem_ve, v_cD if j == 0 else sem_ve.n)], inc=sem_pe)
        s.op("vector", lambda e, j=j: e.tensor_copy(w4[:, :, j], hp0[:, 0:18]),
             waits=[(sem_pe, v_tw)])
    v_tb = s.op("tensor", lambda e: e.matmul(hp0[:, 0:18], cwst[32:50, 0:P], ident[32:50, 32:50],
                                             start=True, stop=True),
                waits=[(sem_ve, sem_ve.n)], inc=sem_pe)
    v_cbc = s.op("vector", lambda e: e.tensor_copy(cb_c[:], hp0[:, 0:18]), waits=[(sem_pe, v_tb)])
    if "dbg_d" in d and "p" in os.environ.get("KDBGSEL", "psmbB"):
        s.op("sync", lambda e: e.dma_start(d["dbg_d"][:, 0:72], w4[:].rearrange("p a b -> p (a b)")),
             waits=[(sem_ve, v_cbc)], inc=sems["fc"], inc_by=16)
        s.op("sync", lambda e: e.dma_start(d["dbg_d"][:, 128:146], cb_c[:]), inc=sems["fc"], inc_by=16)
    s.op("gpsimd", lambda e: e.dma_start(
        Wout_sb[:], d["L1_attn_W_out"].rearrange("(c p) o -> p c o", p=P)),
        waits=gate0, inc=sem_wg, inc_by=16)
    v_wout = sem_wg.n
    v_zero = s.op("vector", lambda e: e.memset(state[:], 0.0))
    v_one = s.op("vector", lambda e: e.memset(cst[:, 2:3], 1.0))
    ar.release()

    ar.mark()
    xr = ar.alloc([P, 2, DM])
    xc = ar.alloc([P, DM])
    xsq = ar.alloc([P, DM])
    xn_t = ar.alloc([P, DM])
    st_ = ar.alloc([P, 16])
    xnb = ar.alloc([P, DM], BF16)
    xqb = ar.alloc([P, DM], BF16)
    xnf32 = ar.alloc([P, 8, 128])
    xnf8 = ar.alloc([P, 8, 128], BF16)
    xqf8 = ar.alloc([P, 8, 128], BF16)
    qb = {"absx": ar.alloc([P, DM]), "t1": ar.alloc([P, DM]), "t2": ar.alloc([P, DM]),
          "q": ar.alloc([P, DM]), "bm": ar.alloc([P, DM // 16]), "sc": ar.alloc([P, DM // 16]),
          "rs": ar.alloc([P, DM // 16])}

    x_rd = {}
    x_cons = {}
    prev_fdma = 0
    prev_qdma = 0

    def a_xload(tb):
        waits = []
        if tb <= 1:
            waits.append((sem_ve, v_cbc))
        if tb >= 2:
            waits.append((sem_ve, x_cons[tb - 2]))
        x_rd[tb] = s.op("sync", lambda e, tb=tb: e.dma_start(
            xr[:, tb % 2, :], d["x1_l1"][tb * P:(tb + 1) * P, :]),
            waits=waits, inc=sem_lx[tb % 2], inc_by=16)

    a_xload(0)
    a_xload(1)
    for tb in range(NTB):
        xin = xr[:, tb % 2, :]
        s.op("vector", lambda e, xin=xin: e.reduce_sum(st_[:, 0:1], xin, axis=AX.X),
             waits=[(sem_lx[tb % 2], x_rd[tb])])
        s.op("vector", lambda e: e.tensor_scalar(st_[:, 0:1], st_[:, 0:1], 1.0 / DM, None, ALU.mult))
        v_xc = s.op("vector", lambda e, xin=xin: e.tensor_scalar(xc[:], xin, st_[:, 0:1], None,
                                                                 ALU.subtract))
        x_cons[tb] = v_xc
        if tb + 2 < NTB:
            a_xload(tb + 2)
        s.op("scalar", lambda e: e.activation(xsq[:], xc[:], AF.Square, accum_out=st_[:, 1:2]),
             waits=[(sem_ve, v_xc)])
        s.op("scalar", lambda e: e.activation(st_[:, 1:2], st_[:, 1:2], AF.Ln, bias=cst[:, 0:1],
                                              scale=1.0 / DM))
        v_rs = s.op("scalar", lambda e: e.activation(st_[:, 1:2], st_[:, 1:2], AF.Exp, scale=-0.5))
        v_xn = s.op("vector", lambda e: e.tensor_scalar(xn_t[:], xc[:], st_[:, 1:2], None, ALU.mult),
                    waits=[(sem_act, v_rs)])
        v_xnb = s.op("vector", lambda e: e.tensor_copy(xnb[:], xn_t[:]))
        v_xq = emit_quant(s, sems, qb, xn_t[:], xqb[:], DM, src_ready=[(sem_ve, v_xn)])
        hp1b = hp1[:, 0:P].bitcast(BF16)[:, 0:P]
        hp0b = hp0[:, 128:256].bitcast(BF16)[:, 0:P]
        for c in range(8):
            v_t1 = s.op("tensor", lambda e, c=c: e.transpose(hp0[:, 0:P], xn_t[:, c * P:(c + 1) * P],
                                                             ident[:]),
                        waits=[(sem_ve, v_xnb if c == 0 else sem_ve.n)], inc=sem_pe)
            v_c1 = s.op("vector", lambda e, c=c: e.tensor_copy(xnf32[:, c, :], hp0[:, 0:P]),
                        waits=[(sem_pe, v_t1)])
            v_c1b = s.op("vector", lambda e, c=c: e.tensor_copy(xnf8[:, c, :], xnf32[:, c, :]),
                         waits=[(sem_sxf, prev_fdma)] if (c == 0 and prev_fdma) else [])
            v_t2 = s.op("tensor", lambda e, c=c, hp1b=hp1b: e.transpose(
                hp1b, xqb[:, c * P:(c + 1) * P], identb[:]),
                waits=[(sem_ve, v_xq if c == 0 else sem_ve.n)], inc=sem_pe)
            v_c2 = s.op("vector", lambda e, c=c, hp1b=hp1b: e.tensor_copy(xqf8[:, c, :], hp1b),
                        waits=[(sem_pe, v_t2)] + ([(sem_sxq, prev_qdma)] if (c == 0 and prev_qdma) else []))
        v_trd = sem_ve.n
        prev_fdma = s.op("sync", lambda e, tb=tb: e.dma_start(
            d["xnfm_d"].rearrange("(c p) t -> p c t", p=P)[:, :, tb * P:(tb + 1) * P], xnf8[:]),
            waits=[(sem_ve, v_trd)], inc=sem_sxf, inc_by=16)
        prev_qdma = s.op("sync", lambda e, tb=tb: e.dma_start(
            d["xqfm_d"].rearrange("(c p) t -> p c t", p=P)[:, :, tb * P:(tb + 1) * P], xqf8[:]),
            waits=[(sem_ve, v_trd)], inc=sem_sxq, inc_by=16)
        for c in range(8):
            def mmd(e, c=c, tb=tb):
                return e.matmul(dnA[:, 0:32], xnf32[:, c, :], Wdt[:, c, :],
                                start=(c == 0), stop=(c == 7))
            if c == 0:
                s.op("tensor", mmd, waits=[(sem_ve, v_trd), (sem_ld, v_par)])
            elif c == 7:
                v_dmm = s.op("tensor", mmd, inc=sem_pe)
            else:
                s.op("tensor", mmd)
        s.op("vector", lambda e, tb=tb: e.tensor_copy(dtraw[:, tb, :], dnA[:, 0:32]),
             waits=[(sem_pe, v_dmm)])
    v_phA_ve = sem_ve.n
    v_phA_pe = sem_pe.n
    ar.release()

    ar.mark()
    win = [ar.alloc([P, 4384], BF16) for _ in range(8)]
    xbc = ar.alloc([P, T])
    cacc = ar.alloc([P, T])
    btile = ar.alloc([P, T], BF16)
    bst = ar.alloc([P, 16, 128], BF16)
    for k in range(8):
        s.op("gpsimd", lambda e, k=k: e.dma_start(win[k][:], d["L1_attn_W_in"][k * P:(k + 1) * P, :]),
             waits=[(sem_ve, v_phA_ve), (sem_act, sem_act.n), (sem_gp, sem_gp.n),
                    (sem_lx[0], sem_lx[0].n), (sem_lx[1], sem_lx[1].n)] if k == 0 else [],
             inc=sem_wg, inc_by=16)
    v_win = sem_wg.n

    def phaseB_tile(n, kind):
        halves = []
        for half in range(2):
            ps = (dnB if half == 0 else dnC)
            for k in range(8):
                for j in range(2):
                    def mmw(e, n=n, k=k, half=half, j=j, ps=ps):
                        return e.matmul(ps[:, j * 512:(j + 1) * 512],
                                        win[k][:, n * P:(n + 1) * P],
                                        d_xnfm[:, k, half * 1024 + j * 512:half * 1024 + (j + 1) * 512],
                                        start=(k == 0), stop=(k == 7), skip_group_check=True)
                    wts = []
                    if k == 0 and j == 0:
                        wts = [(sem_wg, v_win), (sem_ld, v_xnfm_ld)]
                        wts.append((sem_ve, sem_ve.n))
                    if k == 0 and j == 0:
                        s.op("tensor", mmw, waits=wts)
                    elif k == 7 and j == 1:
                        halves.append(s.op("tensor", mmw, inc=sem_pe))
                    else:
                        s.op("tensor", mmw)
        v_mm = halves[-1]
        if kind == "z":
            v_e1 = s.op("vector", lambda e: e.tensor_copy(btile[:, 0:1024], dnB[:, 0:1024]),
                        waits=[(sem_pe, halves[0])])
            v_e2 = s.op("vector", lambda e: e.tensor_copy(btile[:, 1024:2048], dnC[:, 0:1024]),
                        waits=[(sem_pe, v_mm)])
            src = btile
            v_rdy = v_e2
        else:
            v_e1 = s.op("vector", lambda e: e.tensor_copy(xbc[:, 0:1024], dnB[:, 0:1024]),
                        waits=[(sem_pe, halves[0])])
            v_e2 = s.op("vector", lambda e: e.tensor_copy(xbc[:, 1024:2048], dnC[:, 0:1024]),
                        waits=[(sem_pe, v_mm)])
            ci = n - 16
            v_cv = s.op("vector", lambda e, ci=ci: e.tensor_scalar(
                cacc[:], xbc[:], w4[:, ci, 3:4], None, ALU.mult),
                waits=[(sem_ld, v_par), (sem_act, sem_act.n)])
            v_cva = None
            for jj in range(3):
                sh = 3 - jj
                v_cva = s.op("vector", lambda e, ci=ci, jj=jj, sh=sh: e.scalar_tensor_tensor(
                    cacc[:, sh:T], xbc[:, 0:T - sh], w4[:, ci, jj:jj + 1], cacc[:, sh:T],
                    ALU.mult, ALU.add))
            v_si = s.op("scalar", lambda e, ci=ci: e.activation(
                xbc[:], cacc[:], AF.Silu, bias=cb_c[:, ci:ci + 1]),
                waits=[(sem_ve, v_cva)])
            if kind == "bm":
                v_rdy = s.op("vector", lambda e: e.tensor_copy(BmF[:], xbc[:]),
                             waits=[(sem_act, v_si)])
                src = None
            elif kind == "cm":
                v_rdy = s.op("vector", lambda e: e.tensor_copy(CmF[:], xbc[:]),
                             waits=[(sem_act, v_si)])
                src = None
            else:
                v_rdy = s.op("vector", lambda e: e.tensor_copy(btile[:], xbc[:]),
                             waits=[(sem_act, v_si)])
                src = btile
                if n == 16 and "dbg_xh" in d and "b" in os.environ.get("KDBGSEL", "psmbB"):
                    s.op("sync", lambda e: e.dma_start(d["dbg_xh"], btile[:]),
                         waits=[(sem_ve, v_rdy)], inc=sems["fc"], inc_by=16)
        if src is not None:
            for tbk in range(16):
                v_tt = s.op("tensor", lambda e, tbk=tbk: e.transpose(
                    hp0[:, 0:P].bitcast(BF16)[:, 0:P], src[:, tbk * P:(tbk + 1) * P], identb[:]),
                    waits=[(sem_ve, v_rdy if tbk == 0 else sem_ve.n)], inc=sem_pe)
                s.op("vector", lambda e, tbk=tbk: e.tensor_copy(
                    bst[:, tbk, :], hp0[:, 0:P].bitcast(BF16)[:, 0:P]),
                    waits=[(sem_pe, v_tt)] + ([(sem_sbz, sem_sbz.n)] if tbk == 0 else []))
            dst = d_ztm if kind == "z" else d_xhtm
            col = n if kind == "z" else (n - 16)
            s.op("sync", lambda e, dst=dst, col=col: e.dma_start(
                dst.rearrange("(b p) ch -> p b ch", p=P)[:, :, col * P:(col + 1) * P], bst[:]),
                waits=[(sem_ve, sem_ve.n)], inc=sem_sbz, inc_by=16)
            if n == 16 and "dbg_bst" in d and "b" in os.environ.get("KDBGSEL", "psmbB"):
                s.op("sync", lambda e: e.dma_start(d["dbg_bst"],
                     bst[:].rearrange("p a b -> p (a b)")), inc=sems["fc"], inc_by=16)

    d_xnfm = None

    xnfm_sb = ar.alloc([P, 8, T], BF16)
    v_xnfm_ld = s.op("sync", lambda e: e.dma_start(
        xnfm_sb[:], d["xnfm_d"].rearrange("(c p) t -> p c t", p=P)),
        waits=[(sem_sxf, sem_sxf.n)], inc=sem_ld, inc_by=16)
    d_xnfm = xnfm_sb
    d_ztm = d["ztm_d"]
    d_xhtm = d["xhtm_d"]

    for n in range(16):
        phaseB_tile(n, "z")
    for n in range(16, 32):
        phaseB_tile(n, "xh")
    phaseB_tile(32, "bm")
    phaseB_tile(33, "cm")
    if "dbg_bm" in d and "B" in os.environ.get("KDBGSEL", "psmbB"):
        s.op("sync", lambda e: e.dma_start(d["dbg_bm"], BmF[:]), waits=[(sem_ve, sem_ve.n)],
             inc=sems["fc"], inc_by=16)
        s.op("sync", lambda e: e.dma_start(d["dbg_cm"], CmF[:]), inc=sems["fc"], inc_by=16)
    v_phB_ve = sem_ve.n
    v_phB_pe = sem_pe.n
    ar.release()

    ar.mark()
    xh_r = ar.alloc([P, 2, DI], BF16)
    z_r = ar.alloc([P, 2, DI], BF16)
    dacs = ar.alloc([P, NH])
    dacsT = ar.alloc([P, 128])
    dtc_sb = ar.alloc([P, NH])
    dtx = ar.alloc([P, NH])
    sp1 = ar.alloc([P, NH])
    sp2 = ar.alloc([P, NH])
    dA_sb = ar.alloc([P, NH])
    bevac = ar.alloc([P, NH])
    expdacs = ar.alloc([P, NH])
    dT1 = ar.alloc([P, 128])
    dtT = ar.alloc([P, 128])
    dd_tm = ar.alloc([P, NH])
    cdbc = ar.alloc([P, NH])
    cdrow = ar.alloc([P, NH])
    Gmask = ar.alloc([P, P])
    mex = [ar.alloc([P, P]) for _ in range(2)]
    mexp = [ar.alloc([P, P]) for _ in range(2)]
    mh = [ar.alloc([P, P], BF16) for _ in range(2)]
    yoff = ar.alloc([P, DI])
    y_sb = ar.alloc([P, DI])
    zsil = ar.alloc([P, DI])
    ysq = ar.alloc([P, DI])
    ynb = ar.alloc([P, DI], BF16)
    ynfm = ar.alloc([P, 16, 128], BF16)
    BmT = ar.alloc([P, P], BF16)
    xw_sb = ar.alloc([P, DI], BF16)
    s1b = ar.alloc([P, DM], BF16)
    st2 = ar.alloc([P, 4])
    mex_done = {}

    ch_ld = {}
    ch_cons = {}

    def c_load(c):
        waits = [(sem_sbz, sem_sbz.n), (sem_ve, v_phB_ve)] if c <= 1 else []
        if c >= 2:
            waits = [(sem_ve, ch_cons[c - 2])]
        sm = sem_cc[c % 2]
        s.op("sync", lambda e, c=c: e.dma_start(
            xh_r[:, c % 2, :], d_xhtm[c * P:(c + 1) * P, :]), waits=waits, inc=sm, inc_by=16)
        s.op("sync", lambda e, c=c: e.dma_start(
            z_r[:, c % 2, :], d_ztm[c * P:(c + 1) * P, :]), inc=sm, inc_by=16)
        ch_ld[c] = (sm, sm.n)

    c_load(0)
    c_load(1)
    v_ss1_prev = 0
    for c in range(NTB):
        xh_c = xh_r[:, c % 2, :]
        z_c = z_r[:, c % 2, :]
        xh3 = xh_c.rearrange("p (h q) -> p h q", q=HD)
        v_1 = s.op("vector", lambda e, c=c: e.tensor_tensor(dtx[:], dtraw[:, c, :], dtb_bc, ALU.add),
                   waits=[(sem_ld, v_par)] + ([(sem_ve, v_phB_ve), (sem_pe, v_phB_pe)] if c == 0 else []))
        s.op("scalar", lambda e: e.activation(sp1[:], dtx[:], AF.Abs), waits=[(sem_ve, v_1)])
        s.op("scalar", lambda e: e.activation(sp1[:], sp1[:], AF.Exp, scale=-1.0))
        v_sp = s.op("scalar", lambda e: e.activation(sp1[:], sp1[:], AF.Ln, bias=cst[:, 2:3]),
                    waits=[(sem_ve, v_one)])
        s.op("vector", lambda e: e.tensor_scalar(sp2[:], dtx[:], 0.0, None, ALU.max))
        v_dt = s.op("vector", lambda e: e.tensor_tensor(dtc_sb[:], sp1[:], sp2[:], ALU.add),
                    waits=[(sem_act, v_sp)])
        v_dA = s.op("vector", lambda e: e.tensor_tensor(dA_sb[:], dtc_sb[:], A_bc, ALU.mult))
        v_cs = s.op("tensor", lambda e: e.matmul(hp0[:, 0:NH], lt_sb[:], dA_sb[:],
                                                 start=True, stop=True),
                    waits=[(sem_ve, v_dA)], inc=sem_pe)
        v_dacs = s.op("vector", lambda e: e.tensor_copy(dacs[:], hp0[:, 0:NH]),
                      waits=[(sem_pe, v_cs)])
        v_tT = s.op("tensor", lambda e: e.transpose(hp1[0:32, 0:P], dacs[:], ident[:]),
                    waits=[(sem_ve, v_dacs)], inc=sem_pe)
        v_dacsT = s.op("vector", lambda e: e.tensor_copy(dacsT[0:32, :], hp1[0:32, 0:P]),
                       waits=[(sem_pe, v_tT)])
        v_ed = s.op("scalar", lambda e: e.activation(expdacs[:], dacs[:], AF.Exp),
                    waits=[(sem_ve, v_dacs)])
        v_ln = s.op("scalar", lambda e: e.activation(bevac[:], dtc_sb[:], AF.Ln),
                    waits=[(sem_ve, v_dt)])
        v_be = s.op("vector", lambda e: e.tensor_tensor(bevac[:], bevac[:], dacs[:], ALU.subtract),
                    waits=[(sem_act, v_ln)])
        v_d1 = s.op("vector", lambda e: e.tensor_scalar(dT1[0:32, :], dacsT[0:32, :],
                                                        dacsT[0:32, 127:128], None, ALU.subtract),
                    waits=[(sem_ve, v_dacsT)])
        v_d2 = s.op("scalar", lambda e: e.activation(dT1[0:32, :], dT1[0:32, :], AF.Exp, scale=-1.0),
                    waits=[(sem_ve, v_d1)])
        v_t3 = s.op("tensor", lambda e: e.transpose(hp1[0:32, 64:192], dtc_sb[:], ident[:]),
                    waits=[(sem_ve, v_be)], inc=sem_pe)
        v_dtT = s.op("vector", lambda e: e.tensor_copy(dtT[0:32, :], hp1[0:32, 64:192]),
                     waits=[(sem_pe, v_t3)])
        v_dd = s.op("vector", lambda e: e.tensor_tensor(dT1[0:32, :], dT1[0:32, :], dtT[0:32, :],
                                                        ALU.mult),
                    waits=[(sem_act, v_d2)])
        v_t4 = s.op("tensor", lambda e: e.transpose(hp1[:, 96:128], dT1[0:32, :], ident[0:32, 0:32]),
                    waits=[(sem_ve, v_dd)], inc=sem_pe)
        v_ddtm = s.op("vector", lambda e: e.tensor_copy(dd_tm[:], hp1[0:P, 96:128]),
                      waits=[(sem_pe, v_t4)])
        v_t5 = s.op("tensor", lambda e: e.matmul(hp1[0:1, 128:160], ident[:, 127:128], dacs[:],
                                                 start=True, stop=True), inc=sem_pe)
        v_cd1 = s.op("scalar", lambda e: e.activation(cdrow[0:1, :], hp1[0:1, 128:160], AF.Exp),
                     waits=[(sem_pe, v_t5)])
        v_t6 = s.op("tensor", lambda e: e.matmul(hp1[:, 160:192], onesf, cdrow[0:1, :],
                                                 start=True, stop=True),
                    waits=[(sem_act, v_cd1)], inc=sem_pe)
        v_cdbc = s.op("vector", lambda e: e.tensor_copy(cdbc[:], hp1[:, 160:192]),
                      waits=[(sem_pe, v_t6)])
        if c == 0 and "dbg_d" in d and "m" in os.environ.get("KDBGSEL", "psmbB"):
            s.op("sync", lambda e: e.dma_start(d["dbg_d"][:, 160:192], dacs[:]),
                 waits=[(sem_ve, v_dacs)], inc=sems["fc"], inc_by=16)
            s.op("sync", lambda e: e.dma_start(d["dbg_d"][:, 192:224], dd_tm[:]),
                 waits=[(sem_ve, v_ddtm)], inc=sems["fc"], inc_by=16)
            s.op("sync", lambda e: e.dma_start(d["dbg_d"][:, 224:256], cdbc[:]),
                 waits=[(sem_ve, v_cdbc)], inc=sems["fc"], inc_by=16)
            s.op("sync", lambda e: e.dma_start(d["dbg_d"][:, 96:128], dtc_sb[:]),
                 waits=[(sem_ve, v_dt)], inc=sems["fc"], inc_by=16)
        v_g = s.op("tensor", lambda e, c=c: e.matmul(dnC[:, 0:P], BmF[:, c * P:(c + 1) * P],
                                                     CmF[:, c * P:(c + 1) * P],
                                                     start=True, stop=True),
                   waits=[(sem_ve, sem_ve.n)], inc=sem_pe)
        v_gm = s.op("vector", lambda e: e.tensor_tensor(Gmask[:], dnC[:, 0:P], lt_sb[:], ALU.mult),
                    waits=[(sem_pe, v_g)])
        v_sbf = s.op("vector", lambda e: e.tensor_copy(state_bf[:], state[:]),
                     waits=[(sem_ve, v_zero)] if c == 0 else [])
        if c == 1 and "dbg_st" in d and "s" in os.environ.get("KDBGSEL", "psmbB"):
            s.op("sync", lambda e: e.dma_start(d["dbg_st"], state_bf[:]),
                 waits=[(sem_ve, v_sbf)], inc=sems["fc"], inc_by=16)
        offmm = []
        for jj in range(4):
            ps = (dnA if jj < 2 else dnB)
            def mmo(e, jj=jj, ps=ps, c=c):
                return e.matmul(ps[:, (jj % 2) * 512:(jj % 2 + 1) * 512],
                                CmF[:, c * P:(c + 1) * P],
                                state_bf[:, (jj) * 512:(jj + 1) * 512],
                                start=True, stop=True, skip_group_check=True)
            wts = [(sem_ve, v_sbf)]
            if c > 0:
                wts.append((sem_ve, sem_ve.n))
            offmm.append(s.op("tensor", mmo, waits=wts if jj == 0 else [],
                              inc=sem_pe if jj == 3 else None))
        v_off = offmm[-1]
        ed3a = expdacs[:, 0:16].unsqueeze(-1).broadcast_to((P, 16, HD))
        ed3b = expdacs[:, 16:32].unsqueeze(-1).broadcast_to((P, 16, HD))
        v_ev1 = s.op("vector", lambda e: e.tensor_tensor(
            yoff[:, 0:1024].rearrange("p (h q) -> p h q", q=HD), dnA[:, 0:1024].rearrange("p (h q) -> p h q", q=HD),
            ed3a, ALU.mult), waits=[(sem_pe, v_off), (sem_act, sem_act.n)])
        v_ev2 = s.op("vector", lambda e: e.tensor_tensor(
            yoff[:, 1024:2048].rearrange("p (h q) -> p h q", q=HD), dnB[:, 0:1024].rearrange("p (h q) -> p h q", q=HD),
            ed3b, ALU.mult))
        ydmm = []
        for h in range(NH):
            pr = (hp0[:, 256:384] if h % 2 == 0 else hp1[:, 256:384])
            v_bc = s.op("tensor", lambda e, h=h, pr=pr: e.matmul(
                pr[:, 0:P], eh_sb[:, h * P:(h + 1) * P], dacsT[0:32, :],
                start=True, stop=True),
                waits=[(sem_ve, sem_ve.n)] if h < 2 else [(sem_ve, mex_done[h - 2])],
                inc=sem_pe)
            v_mx = s.op("vector", lambda e, h=h, pr=pr: e.tensor_scalar(
                mex[h % 2][:], pr[:, 0:P], bevac[:, h:h + 1], 80.0, ALU.add, ALU.min),
                waits=[(sem_pe, v_bc)])
            mex_done[h] = v_mx
            v_me = s.op("scalar", lambda e, h=h: e.activation(mexp[h % 2][:], mex[h % 2][:], AF.Exp),
                        waits=[(sem_ve, v_mx)])
            v_mh = s.op("vector", lambda e, h=h: e.tensor_tensor(mh[h % 2][:], mexp[h % 2][:],
                                                                 Gmask[:], ALU.mult),
                        waits=[(sem_act, v_me), (sem_ve, v_gm)])
            ps = (dnA if h < 16 else dnB)
            def mmy(e, h=h, ps=ps, xh3=xh3):
                return e.matmul(ps[:, (h % 16) // 8 * 512 + (h % 8) * 64:
                                   (h % 16) // 8 * 512 + (h % 8) * 64 + 64],
                                mh[h % 2][:], xh3[:, h, :],
                                start=(h % 8 == 0), stop=(h % 8 == 7), skip_group_check=True)
            wts = [(sem_ve, v_mh)]
            if h % 8 == 0:
                wts.append((sem_ve, v_ev2))
                wts.append(ch_ld[c])
            ydmm.append(s.op("tensor", mmy, waits=wts, inc=sem_pe if h == NH - 1 else None))
        v_yd = ydmm[-1]
        v_y1 = s.op("vector", lambda e: e.tensor_tensor(y_sb[:, 0:1024], dnA[:, 0:1024],
                                                        yoff[:, 0:1024], ALU.add),
                    waits=[(sem_pe, v_yd)])
        v_y2 = s.op("vector", lambda e: e.tensor_tensor(y_sb[:, 1024:2048], dnB[:, 0:1024],
                                                        yoff[:, 1024:2048], ALU.add))
        d3 = D_bc.unsqueeze(-1).broadcast_to((P, NH, HD))
        v_y3 = s.op("vector", lambda e, xh3=xh3: e.tensor_tensor(
            yoff[:].rearrange("p (h q) -> p h q", q=HD), xh3, d3, ALU.mult))
        v_y4 = s.op("vector", lambda e: e.tensor_tensor(y_sb[:], y_sb[:], yoff[:], ALU.add))
        v_zs = s.op("scalar", lambda e, z_c=z_c: e.activation(zsil[:], z_c, AF.Silu),
                    waits=[ch_ld[c]])
        v_y5 = s.op("vector", lambda e: e.tensor_tensor(y_sb[:], y_sb[:], zsil[:], ALU.mult),
                    waits=[(sem_act, v_zs)])
        s.op("scalar", lambda e: e.activation(ysq[:], y_sb[:], AF.Square, accum_out=st2[:, 0:1]),
             waits=[(sem_ve, v_y5)])
        s.op("scalar", lambda e: e.activation(st2[:, 0:1], st2[:, 0:1], AF.Ln, bias=cst[:, 0:1],
                                              scale=1.0 / DI))
        v_rst = s.op("scalar", lambda e: e.activation(st2[:, 0:1], st2[:, 0:1], AF.Exp, scale=-0.5))
        v_yn = s.op("vector", lambda e: e.scalar_tensor_tensor(ynb[:], y_sb[:], st2[:, 0:1],
                                                               normw_bc, ALU.mult, ALU.mult),
                    waits=[(sem_act, v_rst)])
        for k in range(16):
            v_tt = s.op("tensor", lambda e, k=k: e.transpose(
                hp0[:, 384:512].bitcast(BF16)[:, 0:P], ynb[:, k * P:(k + 1) * P], identb[:]),
                waits=[(sem_ve, v_yn if k == 0 else sem_ve.n)], inc=sem_pe)
            s.op("vector", lambda e, k=k: e.tensor_copy(ynfm[:, k, :],
                                                        hp0[:, 384:512].bitcast(BF16)[:, 0:P]),
                 waits=[(sem_pe, v_tt)])
        v_tr_done = sem_ve.n
        womm = None
        for k in range(16):
            for j in range(2):
                def mmw2(e, k=k, j=j):
                    return e.matmul(dnC[:, j * 512:(j + 1) * 512], ynfm[:, k, :],
                                    Wout_sb[:, k, j * 512:(j + 1) * 512],
                                    start=(k == 0), stop=(k == 15), skip_group_check=True)
                wts = []
                if k == 0 and j == 0:
                    wts = [(sem_ve, v_tr_done), (sem_wg, v_wout)]
                if k == 15 and j == 1:
                    womm = s.op("tensor", mmw2, waits=wts, inc=sem_pe)
                else:
                    s.op("tensor", mmw2, waits=wts)
        v_s1 = s.op("vector", lambda e: e.tensor_copy(s1b[:], dnC[:, 0:1024]),
                    waits=[(sem_pe, womm), (sem_ss1, v_ss1_prev)])
        v_ss1_prev = s.op("sync", lambda e, c=c: e.dma_start(
            d["s1_d"][c * P:(c + 1) * P, :], s1b[:]),
            waits=[(sem_ve, v_s1)], inc=sem_ss1, inc_by=16)
        v_bt = s.op("tensor", lambda e, c=c: e.transpose(
            hp0[:, 256:384].bitcast(BF16)[:, 0:P], BmF[:, c * P:(c + 1) * P], identb[:]),
            waits=[(sem_ve, sem_ve.n)], inc=sem_pe)
        v_bmt = s.op("vector", lambda e: e.tensor_copy(BmT[:], hp0[:, 256:384].bitcast(BF16)[:, 0:P]),
                     waits=[(sem_pe, v_bt)])
        dd3 = dd_tm.unsqueeze(-1).broadcast_to((P, NH, HD))
        v_xw = s.op("vector", lambda e, xh3=xh3: e.tensor_tensor(
            xw_sb[:].rearrange("p (h q) -> p h q", q=HD), xh3, dd3, ALU.mult),
            waits=[(sem_ve, v_ddtm)])
        smm = []
        for jj in range(4):
            ps = (dnA if jj < 2 else dnB)
            def mms2(e, jj=jj, ps=ps):
                return e.matmul(ps[:, (jj % 2) * 512:(jj % 2 + 1) * 512], BmT[:],
                                xw_sb[:, jj * 512:(jj + 1) * 512],
                                start=True, stop=True, skip_group_check=True)
            wts = []
            if jj == 0:
                wts = [(sem_ve, max(v_xw, v_bmt, v_y4))]
            smm.append(s.op("tensor", mms2, waits=wts, inc=sem_pe if jj == 3 else None))
        v_smm = smm[-1]
        cd3 = cdbc.unsqueeze(-1).broadcast_to((P, NH, HD))
        v_su1 = s.op("vector", lambda e: e.tensor_tensor(
            state[:].rearrange("p (h q) -> p h q", q=HD),
            state[:].rearrange("p (h q) -> p h q", q=HD), cd3, ALU.mult),
            waits=[(sem_ve, v_cdbc), (sem_pe, v_smm)])
        s.op("vector", lambda e: e.tensor_tensor(state[:, 0:1024], state[:, 0:1024],
                                                 dnA[:, 0:1024], ALU.add))
        v_sup = s.op("vector", lambda e: e.tensor_tensor(state[:, 1024:2048], state[:, 1024:2048],
                                                         dnB[:, 0:1024], ALU.add))
        ch_cons[c] = v_sup
        if c + 2 < NTB:
            c_load(c + 2)
    v_phC_ve = sem_ve.n
    v_phC_pe = sem_pe.n
    ar.release()
    ar.release()

    ar.mark()
    wgT = ar.alloc([P, 8, 2816], BF16)
    wuT = ar.alloc([P, 8, 2816], BF16)
    wdT = ar.alloc([P, 22, 1024], BF16)
    xq8 = ar.alloc([P, 2, 8, 128], BF16)
    sg = ar.alloc([P, 2, 512])
    htm = ar.alloc([P, 2816])
    hq_tm = ar.alloc([P, 2, 2816], BF16)
    hqf = ar.alloc([P, 22, 128], BF16)
    dq_absx = ar.alloc([P, 1408])
    dq_t1 = ar.alloc([P, 1408], BF16)
    dq_q = ar.alloc([P, 1408])
    dq_bm = ar.alloc([P, 88])
    dq_sc = ar.alloc([P, 88])
    dq_rs = ar.alloc([P, 88])
    x1r = ar.alloc([P, 2, DM])
    s1r = ar.alloc([P, 2, DM], BF16)
    gsb = ar.alloc([P, DM])
    xo = ar.alloc([P, DM])

    wld = {}

    def d_wload(tt):
        waits = [(sem_wq, sem_wq.n), (sem_ve, v_phC_ve), (sem_pe, v_phC_pe),
                 (sem_act, sem_act.n), (sem_ss1, v_ss1_prev)]
        s.op("sync", lambda e: e.dma_start(
            wgT[:], d["wgt_d"].rearrange("(c p) n -> p c n", p=P)), waits=waits, inc=sem_wq, inc_by=16)
        s.op("sync", lambda e: e.dma_start(
            wuT[:], d["wut_d"].rearrange("(c p) n -> p c n", p=P)), inc=sem_wq, inc_by=16)
        wld["gu"] = sem_wq.n
        s.op("sync", lambda e: e.dma_start(
            wdT[:], d["wdt_d"].rearrange("(c p) n -> p c n", p=P)), inc=sem_wq, inc_by=16)
        wld["d"] = sem_wq.n

    d_pe_after = {}
    d_wload(0)
    xq_ld = {}
    x1_ld = {}

    xs_ld = {}

    def d_xqload_xq(tbk):
        if tbk <= 1:
            waits = [(sem_sxq, sem_sxq.n), (sem_ss1, sem_ss1.n)]
        else:
            waits = [(sem_pe, fwd_pe[tbk - 2])]
        s.op("sync", lambda e, tbk=tbk: e.dma_start(
            xq8[:, tbk % 2, :, :],
            d["xqfm_d"].rearrange("(c p) t -> p c t", p=P)[:, :, tbk * P:(tbk + 1) * P]),
            waits=waits, inc=sem_xq, inc_by=16)
        xq_ld[tbk] = sem_xq.n

    def d_xqload_xs(tbk):
        waits = [] if tbk <= 1 else [(sem_ve, d_cons[tbk - 2])]
        s.op("sync", lambda e, tbk=tbk: e.dma_start(
            x1r[:, tbk % 2, :], d["x1_l1"][tbk * P:(tbk + 1) * P, :]),
            waits=waits, inc=sem_xq, inc_by=16)
        s.op("sync", lambda e, tbk=tbk: e.dma_start(
            s1r[:, tbk % 2, :], d["s1_d"][tbk * P:(tbk + 1) * P, :]), inc=sem_xq, inc_by=16)
        xs_ld[tbk] = sem_xq.n

    def d_xqload(tbk):
        d_xqload_xq(tbk)
        d_xqload_xs(tbk)

    d_tbk_pe = {}
    d_cons = {}
    d_xqload(0)
    d_xqload(1)
    prev_x2_st = [0]
    CH = [(0, 512), (512, 512), (1024, 512), (1536, 512), (2048, 512), (2304, 512)]
    CH = [(0, 512), (512, 512), (1024, 512), (1536, 512), (2048, 512), (2560, 256)]
    evac_done = {}
    quant_done = {}
    fwd_pe = {}

    def d_fwd(tbk):
        buf = tbk % 2
        for ci, (c0, cw) in enumerate(CH):
            gps = dnA[:, (ci % 2) * 512:(ci % 2) * 512 + cw]
            ups = dnB[:, (ci % 2) * 512:(ci % 2) * 512 + cw]
            for (mat, ps) in ((wgT, gps), (wuT, ups)):
                for k in range(8):
                    def mmg(e, mat=mat, ps=ps, k=k, c0=c0, cw=cw, buf=buf):
                        return e.matmul(ps, xq8[:, buf, k, :], mat[:, k, c0:c0 + cw],
                                        start=(k == 0), stop=(k == 7), skip_group_check=True)
                    wts = []
                    if k == 0:
                        if ci >= 2:
                            wts.append((sem_ve, evac_done[(tbk, ci - 2)]))
                        elif tbk > 0:
                            wts.append((sem_ve, evac_done[(tbk - 1, len(CH) - 2 + ci)]))
                        if ci == 0 and mat is wgT:
                            wts += [(sem_xq, xq_ld[tbk]), (sem_wq, wld["gu"])]
                    if k == 7 and mat is wuT:
                        v_mm = s.op("tensor", mmg, waits=wts, inc=sem_pe)
                    else:
                        s.op("tensor", mmg, waits=wts)
            v_sg = s.op("scalar", lambda e, ci=ci, gps=gps, cw=cw: e.activation(
                sg[:, ci % 2, 0:cw], gps, AF.Silu), waits=[(sem_pe, v_mm)])
            evac_done[(tbk, ci)] = s.op("vector", lambda e, ci=ci, ups=ups, c0=c0, cw=cw, buf=buf:
                e.tensor_tensor(htm[:, c0:c0 + cw], sg[:, ci % 2, 0:cw], ups, ALU.mult),
                waits=[(sem_act, v_sg)])
        fwd_pe[tbk] = sem_pe.n
        d_tbk_pe[tbk] = sem_pe.n
        for piece in range(2):
            p0 = piece * 1408
            dqb = {"absx": dq_absx, "t1": dq_t1,
                   "t2": hq_tm[:, buf, p0:p0 + 1408], "q": dq_q,
                   "bm": dq_bm, "sc": dq_sc, "rs": dq_rs}
            v_hq = emit_quant(s, sems, dqb, htm[:, p0:p0 + 1408],
                              hq_tm[:, buf, p0:p0 + 1408], 1408,
                              src_ready=[(sem_ve, sem_ve.n)])
        quant_done[tbk] = v_hq

    def d_bwd(tbk):
        buf = tbk % 2
        for n in range(22):
            v_tq = s.op("tensor", lambda e, n=n, buf=buf: e.transpose(
                hp0[:, 128:256].bitcast(BF16)[:, 0:P], hq_tm[:, buf, n * P:(n + 1) * P], identb[:]),
                waits=[(sem_ve, quant_done[tbk] if n == 0 else sem_ve.n)], inc=sem_pe)
            s.op("vector", lambda e, n=n: e.tensor_copy(hqf[:, n, :],
                                                        hp0[:, 128:256].bitcast(BF16)[:, 0:P]),
                 waits=[(sem_pe, v_tq)])
        v_hqf = sem_ve.n
        dmm = None
        for n in range(22):
            for j in range(2):
                def mmd2(e, n=n, j=j):
                    return e.matmul(dnC[:, j * 512:(j + 1) * 512], hqf[:, n, :],
                                    wdT[:, n, j * 512:(j + 1) * 512],
                                    start=(n == 0), stop=(n == 21), skip_group_check=True)
                wts = []
                if n == 0 and j == 0:
                    wts = [(sem_ve, v_hqf), (sem_act, sem_act.n), (sem_wq, wld["d"])]
                if n == 21 and j == 1:
                    dmm = s.op("tensor", mmd2, waits=wts, inc=sem_pe)
                else:
                    s.op("tensor", mmd2, waits=wts)
        v_gs = s.op("scalar", lambda e: e.activation(gsb[:], dnC[:, 0:1024], AF.Silu),
                    waits=[(sem_pe, dmm)])
        v_t3 = s.op("vector", lambda e, buf=buf: e.tensor_tensor(gsb[:], s1r[:, buf, :], gsb[:],
                                                                 ALU.mult),
                    waits=[(sem_act, v_gs), (sem_xq, xs_ld[tbk])])
        v_xo = s.op("vector", lambda e, buf=buf: e.tensor_tensor(xo[:], x1r[:, buf, :], gsb[:],
                                                                 ALU.add),
                    waits=[(sem_st, prev_x2_st[0])] if prev_x2_st[0] else [])
        d_cons[tbk] = v_xo
        prev_x2_st[0] = s.op("sync", lambda e, tbk=tbk: e.dma_start(
            d["x2"][tbk * P:(tbk + 1) * P, :], xo[:]),
            waits=[(sem_ve, v_xo)], inc=sem_st, inc_by=16)

    d_fwd(0)
    for tbk in range(1, NTB):
        if tbk + 1 < NTB:
            d_xqload_xq(tbk + 1)
        d_fwd(tbk)
        d_bwd(tbk - 1)
        if tbk + 1 < NTB:
            d_xqload_xs(tbk + 1)
    d_bwd(NTB - 1)
    ar.release()


def build_program(which="l0"):
    nc = bass.Bass("TRN2", target_bir_lowering=False, detect_race_conditions=False)
    d = {}
    d["u"] = nc.dram_tensor("u", [T, DM], F32, kind="ExternalInput").ap()
    l0_specs = [
            ("L0_attn_dt", [512]), ("L0_attn_A", [512]), ("L0_attn_Bmat", [1024, 512]),
            ("L0_attn_C", [512, 1024]), ("L0_attn_D", [1024]), ("L0_attn_norm_w", [1024]),
            ("L0_ffn_gate_w", [1024, 8]), ("L0_ffn_w1", [8, 1024, 4096]), ("L0_ffn_b1", [8, 4096]),
            ("L0_ffn_w2", [8, 4096, 1024]), ("L0_ffn_b2", [8, 1024]),
            ("L0_ffn_gamma", [1024]), ("L0_ffn_beta", [1024])]
    l1_specs = [
            ("L1_attn_W_in", [1024, 4384]), ("L1_attn_conv_w", [2304, 4]),
            ("L1_attn_conv_b", [2304]), ("L1_attn_dt_bias", [32]), ("L1_attn_A_log", [32]),
            ("L1_attn_D", [32]), ("L1_attn_norm_w", [2048]), ("L1_attn_W_out", [2048, 1024]),
            ("L1_ffn_w_gate", [2816, 1024]), ("L1_ffn_w_up", [2816, 1024]),
            ("L1_ffn_w_down", [1024, 2816])]
    specs = {"l0": l0_specs, "l1": l1_specs, "both": l0_specs + l1_specs}[which]
    for name, shape in specs:
        d[name] = nc.dram_tensor(name, shape, F32, kind="ExternalInput").ap()
    d["c_ident"] = nc.dram_tensor("c_ident", [P, P], F32, kind="ExternalInput").ap()
    d["c_onesf"] = nc.dram_tensor("c_onesf", [1, P], F32, kind="ExternalInput").ap()
    d["c_lt"] = nc.dram_tensor("c_lt", [P, P], F32, kind="ExternalInput").ap()
    d["c_eh"] = nc.dram_tensor("c_eh", [32, 32 * P], F32, kind="ExternalInput").ap()
    d["c_identb"] = nc.dram_tensor("c_identb", [P, P], BF16, kind="ExternalInput").ap()
    d["c_onesb"] = nc.dram_tensor("c_onesb", [1, P], BF16, kind="ExternalInput").ap()
    _ik = "ExternalOutput" if os.environ.get("KDBG") else "Internal"
    d["xn2fm_d"] = nc.dram_tensor("xn2fm_d", [DM, T], BF16, kind=_ik).ap()
    d["xn_d"] = nc.dram_tensor("xn_d", [T, DM], BF16, kind=_ik).ap()
    d["xscan_d"] = nc.dram_tensor("xscan_d", [512, T], BF16, kind=_ik).ap()
    if which == "l0":
        d["x1"] = nc.dram_tensor("x1", [T, DM], F32, kind="ExternalOutput").ap()
    elif which == "both":
        d["x1"] = nc.dram_tensor("x1", [T, DM], F32, kind="Internal").ap()
        d["x1_l1"] = d["x1"]
        d["xnfm_d"] = nc.dram_tensor("xnfm_d", [DM, T], BF16, kind="Internal").ap()
        d["xqfm_d"] = nc.dram_tensor("xqfm_d", [DM, T], BF16, kind="Internal").ap()
        d["ztm_d"] = nc.dram_tensor("ztm_d", [T, 2048], BF16, kind="Internal").ap()
        d["xhtm_d"] = nc.dram_tensor("xhtm_d", [T, 2048], BF16, kind="Internal").ap()
        d["s1_d"] = nc.dram_tensor("s1_d", [T, DM], BF16, kind="Internal").ap()
        d["wgt_d"] = nc.dram_tensor("wgt_d", [1024, 2816], BF16, kind="Internal").ap()
        d["wut_d"] = nc.dram_tensor("wut_d", [1024, 2816], BF16, kind="Internal").ap()
        d["wdt_d"] = nc.dram_tensor("wdt_d", [2816, 1024], BF16, kind="Internal").ap()
        d["x2"] = nc.dram_tensor("x2", [T, DM], F32, kind="ExternalOutput").ap()
    else:
        d["x1_l1"] = nc.dram_tensor("x1_l1", [T, DM], F32, kind="ExternalInput").ap()
        d["xnfm_d"] = nc.dram_tensor("xnfm_d", [DM, T], BF16, kind="Internal").ap()
        d["xqfm_d"] = nc.dram_tensor("xqfm_d", [DM, T], BF16, kind="Internal").ap()
        d["ztm_d"] = nc.dram_tensor("ztm_d", [T, 2048], BF16, kind="Internal").ap()
        d["xhtm_d"] = nc.dram_tensor("xhtm_d", [T, 2048], BF16, kind="Internal").ap()
        d["s1_d"] = nc.dram_tensor("s1_d", [T, DM], BF16, kind="Internal").ap()
        d["wgt_d"] = nc.dram_tensor("wgt_d", [1024, 2816], BF16, kind="Internal").ap()
        d["wut_d"] = nc.dram_tensor("wut_d", [1024, 2816], BF16, kind="Internal").ap()
        d["wdt_d"] = nc.dram_tensor("wdt_d", [2816, 1024], BF16, kind="Internal").ap()
        d["x2"] = nc.dram_tensor("x2", [T, DM], F32, kind="ExternalOutput").ap()
        if os.environ.get("KDBG2"):
            d["dbg_d"] = nc.dram_tensor("dbg_d", [P, 256], F32, kind="ExternalOutput").ap()
            d["dbg_bm"] = nc.dram_tensor("dbg_bm", [P, T], BF16, kind="ExternalOutput").ap()
            d["dbg_cm"] = nc.dram_tensor("dbg_cm", [P, T], BF16, kind="ExternalOutput").ap()
            d["dbg_xh"] = nc.dram_tensor("dbg_xh", [P, T], BF16, kind="ExternalOutput").ap()
            d["dbg_bst"] = nc.dram_tensor("dbg_bst", [P, T], BF16, kind="ExternalOutput").ap()
            d["dbg_st"] = nc.dram_tensor("dbg_st", [P, T], BF16, kind="ExternalOutput").ap()

    from contextlib import ExitStack
    with ExitStack() as stack:
        en = stack.enter_context
        ident_t = en(nc.sbuf_tensor("ident_sb", [P, P], F32))
        identb_t = en(nc.sbuf_tensor("identb_sb", [P, P], BF16))
        onesb_t = en(nc.sbuf_tensor("onesb_sb", [1, P], BF16))
        if which == "l0":
            cw_t = en(nc.sbuf_tensor("cw_sb", [P, NTB, 8], F32))
            accb_t = en(nc.sbuf_tensor("accb", [P, NTB, DM], BF16))
            b1c_t = en(nc.sbuf_tensor("b1c", [P, 8, 32], F32))
            w2b_t = en(nc.sbuf_tensor("w2b", [P, 32, 1024], BF16))
            b2r_t = en(nc.sbuf_tensor("b2r", [1, 2, 1024], BF16))
            xnq_t = en(nc.sbuf_tensor("xnq", [P, 8, 512], BF16))
            AR_WORDS = 24800
        else:
            AR_WORDS = 52800
        ar_t = en(nc.sbuf_tensor("arena", [P, AR_WORDS], F32))
        cst_t = en(nc.sbuf_tensor("cst", [P, 4], F32))
        dnA = en(nc.psum_tensor("dnA", [P, 1024], F32))
        dnB = en(nc.psum_tensor("dnB", [P, 1024], F32))
        dnC = en(nc.psum_tensor("dnC", [P, 1024], F32))
        hp0 = en(nc.psum_tensor("hp0", [P, 512], F32))
        hp1 = en(nc.psum_tensor("hp1", [P, 512], F32))
        dma_sems = ("ld", "lxa", "lxb", "sxn", "sx2", "sscan", "xq", "w1s", "w2s",
                    "fc", "fla", "flb", "st", "wqh", "wgh")
        sems = {k: Sem(en(nc.semaphore(k)), dma=True) for k in dma_sems}
        sems.update({k: Sem(en(nc.semaphore(k))) for k in ("pe", "act", "ve", "gp")})
        block = en(nc.Block())

        s = Sched(self_serial={"vector": sems["ve"], "scalar": sems["act"],
                               "gpsimd": sems["gp"]})
        ctx = dict(nc=nc, s=s, dram=d, ar=Arena(ar_t[:], AR_WORDS),
                   ident=ident_t[:], identb=identb_t[:], onesb=onesb_t[:],
                   dnA=dnA[:], dnB=dnB[:], dnC=dnC[:], hp0=hp0[:], hp1=hp1[:],
                   sems=sems, cst=cst_t[:])
        if which == "l0":
            ctx.update(cw_sb=cw_t[:], accb=accb_t[:], b1c=b1c_t[:],
                       w2b=w2b_t[:], b2r=b2r_t[:], xnq=xnq_t[:])
        v_c0 = s.op("vector", lambda e: e.memset(cst_t[:, 0:1], 1e-5), inc=sems["ve"])
        v_c1 = s.op("vector", lambda e: e.memset(cst_t[:, 1:2], 1e-6), inc=sems["ve"])
        s.wait("scalar", sems["ve"], v_c1)
        s.op("sync", lambda e: e.dma_start(ctx["ident"], d["c_ident"]), inc=sems["ld"], inc_by=16)
        s.op("sync", lambda e: e.dma_start(ctx["identb"], d["c_identb"]), inc=sems["ld"], inc_by=16)
        s.op("sync", lambda e: e.dma_start(ctx["onesb"], d["c_onesb"]), inc=sems["ld"], inc_by=16)
        s.wait("tensor", sems["ld"], 48)

        if which == "l0":
            build_l0(ctx)
        elif which == "l1":
            build_l1(ctx)
        else:
            ctx["ar"].mark()
            build_l0(ctx)
            ctx["ar"].release()
            ctx["l1_gate"] = [(sems["ve"], sems["ve"].n), (sems["pe"], sems["pe"].n),
                              (sems["st"], sems["st"].n)]
            if sems["wqh"].n:
                ctx["l1_gate"].append((sems["wqh"], sems["wqh"].n))
            if sems["wgh"].n:
                ctx["l1_gate"].append((sems["wgh"], sems["wgh"].n))
            build_l1(ctx)
        for k in dma_sems:
            if sems[k].n:
                s.wait("sync", sems[k], sems[k].n)
        s.build(block)
    return nc


def kernel(**inputs):
    consts = _consts()
    wdict = {k: np.ascontiguousarray(v) for k, v in inputs.items() if k != "L0_attn_u"}
    nc = build_program("both")
    in_maps = []
    for b in range(8):
        m = {"u": np.ascontiguousarray(inputs["L0_attn_u"][b])}
        m.update(wdict)
        m.update(consts)
        in_maps.append(m)
    res = run_bass_kernel_spmd(nc, in_maps, core_ids=list(range(8)))
    return np.stack([np.asarray(r["x2"], dtype=np.float32) for r in res.results], axis=0)


# version 06
# speedup vs baseline: 1.1395x; 1.1395x over previous
import os
import numpy as np
import ml_dtypes
import concourse.bass as bass
import concourse.mybir as mybir
from concourse.bass_utils import run_bass_kernel_spmd

F32 = mybir.dt.float32
BF16 = mybir.dt.bfloat16
AF = mybir.ActivationFunctionType
ALU = mybir.AluOpType
AX = mybir.AxisListType

P = 128
T = 2048
DM = 1024
NTB = T // P
ENGINES = ("sync", "tensor", "vector", "scalar", "gpsimd")


class Sem:

    def __init__(self, handle, dma=False):
        self.h = handle
        self.n = 0
        self.dma = dma
        self.events = []

    def inc(self, k=1, gated=False):
        self.n += k
        if self.dma:
            self.events.append((self.n, gated))
        return self.n

    def resolve(self, v):
        if v <= 0:
            return 0
        if not self.dma:
            return v
        idx = None
        for i, (val, _) in enumerate(self.events):
            if val >= v:
                idx = i
                break
        assert idx is not None, (v, self.events[-5:])
        j = idx + 1
        while j < len(self.events) and not self.events[j][1]:
            j += 1
        return self.events[j - 1][0]


class Sched:

    def __init__(self, self_serial=None):
        self.q = {e: [] for e in ENGINES}
        self.self_serial = self_serial or {}

    def op(self, eng, fn, waits=(), inc=None, inc_by=1):
        ss = self.self_serial.get(eng)
        if ss is not None and inc is not None and inc is not ss:
            ss = None
        if ss is not None:
            pre = ss.n
            planned = ss.inc(1)
            def run(e, fn=fn, waits=tuple(waits), ss=ss, pre=pre):
                for sm, v in waits:
                    rv = sm.resolve(v)
                    if rv > 0:
                        e.wait_ge(sm.h, rv)
                ins = fn(e)
                if pre > 0:
                    ins._wait_ge(ss.h, pre)
                ins.then_inc(ss.h, 1)
            self.q[eng].append(run)
            return planned
        planned = inc.inc(inc_by, gated=bool(waits)) if inc is not None else None

        def run(e, fn=fn, waits=tuple(waits), inc=inc, inc_by=inc_by):
            first = True
            ws = []
            for sm, v in waits:
                rv = sm.resolve(v)
                if rv > 0:
                    ws.append((sm, rv))
            for sm, rv in ws[1:]:
                e.wait_ge(sm.h, rv)
            ins = fn(e)
            if ws:
                ins._wait_ge(ws[0][0].h, ws[0][1])
            if inc is not None:
                ins.then_inc(inc.h, inc_by)
        self.q[eng].append(run)
        return planned

    def wait(self, eng, sem, val):
        self.q[eng].append(lambda e, sm=sem, v=val: e.wait_ge(sm.h, sm.resolve(v)))

    def build(self, block):
        for eng in ENGINES:
            qe = self.q[eng]
            if not qe:
                continue

            def body(engobj, qe=qe):
                for fn in qe:
                    fn(engobj)
            getattr(block, eng)(body)


class Arena:
    def __init__(self, tensor, words):
        self.t = tensor
        self.words = words
        self.top = 0
        self.marks = []
        self.peak = 0

    def _take(self, w):
        w = (w + 15) & ~15
        assert self.top + w <= self.words, f"arena overflow {self.top}+{w}>{self.words}"
        off = self.top
        self.top += w
        self.peak = max(self.peak, self.top)
        return off, w

    def alloc(self, shape, dtype=F32):
        assert shape[0] == P
        elems = 1
        for x in shape[1:]:
            elems *= x
        off, _ = self._take(elems if dtype == F32 else (elems + 1) // 2)
        ap = self.t[:, off:off + (elems if dtype == F32 else (elems + 1) // 2)]
        if dtype != F32:
            ap = ap.bitcast(dtype)
        if len(shape) == 2:
            return ap[:, 0:shape[1]]
        if len(shape) == 3:
            return ap[:, 0:shape[1] * shape[2]].rearrange("p (a b) -> p a b", b=shape[2])
        assert len(shape) == 4
        n = shape[1] * shape[2] * shape[3]
        return ap[:, 0:n].rearrange("p (a b c) -> p a b c", b=shape[2], c=shape[3])

    def row(self, n, dtype=F32):
        off, _ = self._take(n if dtype == F32 else (n + 1) // 2)
        ap = self.t[0:1, off:off + (n if dtype == F32 else (n + 1) // 2)]
        if dtype != F32:
            ap = ap.bitcast(dtype)
        return ap[:, 0:n]

    def mark(self):
        self.marks.append(self.top)

    def release(self):
        self.top = self.marks.pop()


def _consts():
    lt = np.triu(np.ones((P, P), dtype=np.float32))
    eh = np.zeros((32, 32 * P), dtype=np.float32)
    for h in range(32):
        eh[h, h * P:(h + 1) * P] = 1.0
    return {"c_ident": np.eye(P, dtype=np.float32),
            "c_identb": np.eye(P, dtype=ml_dtypes.bfloat16),
            "c_onesb": np.ones((1, P), dtype=ml_dtypes.bfloat16),
            "c_onesf": np.ones((1, P), dtype=np.float32),
            "c_lt": lt, "c_eh": eh}


def build_l0(ctx):
    s = ctx["s"]
    ar = ctx["ar"]
    d = ctx["dram"]
    sems = ctx["sems"]
    sem_ld = sems["ld"]; sem_pe = sems["pe"]; sem_act = sems["act"]
    sem_ve = sems["ve"]; sem_st = sems["st"]
    sem_lx = (sems["lxa"], sems["lxb"])
    sem_sxn = sems["sxn"]; sem_sx2 = sems["sx2"]; sem_sscan = sems["sscan"]
    sem_xq = sems["xq"]; sem_w1s = sems["w1s"]; sem_w2s = sems["w2s"]
    sem_fc = sems["fc"]; sem_fl = (sems["fla"], sems["flb"])
    ident = ctx["ident"]; identb = ctx["identb"]; onesb = ctx["onesb"]
    dnA = ctx["dnA"]; dnB = ctx["dnB"]; dnC = ctx["dnC"]
    hp0 = ctx["hp0"]; hp1 = ctx["hp1"]
    cst = ctx["cst"]
    if "cw_sb" in ctx:
        cw_sb = ctx["cw_sb"]; accb = ctx["accb"]; b1c = ctx["b1c"]
        w2b = ctx["w2b"]; b2r = ctx["b2r"]; xnq = ctx["xnq"]
    else:
        cw_sb = ar.alloc([P, NTB, 8])
        accb = ar.alloc([P, NTB, DM], BF16)
        b1c = ar.alloc([P, 8, 32])
        w2b = ar.alloc([P, 32, 1024], BF16)
        b2r_t2 = ar.alloc([P, 2048], BF16)
        b2r = b2r_t2[0:1, :].rearrange("o (a b) -> o a b", b=1024)
        xnq = ar.alloc([P, 8, 512], BF16)

    ar.mark()
    gw = ar.alloc([P, 8, 8])
    Bm = ar.alloc([P, 8, 512], BF16)
    dt4 = ar.alloc([P, 128])
    A4 = ar.alloc([P, 128])
    b1r = ar.alloc([P, 128])
    dtc = ar.alloc([P, 4])
    ac = ar.alloc([P, 4])
    upj = ar.alloc([P, 4, T], BF16)
    gamma_b = ar.alloc([P, DM])
    beta_b = ar.alloc([P, DM])
    normw_b = ar.alloc([P, DM])
    emit_slabs = lambda k, **kw: None
    alloc_slab_bufs = lambda gates=(): None
    if "L1_ffn_w_gate" in d:
        sem_wgh = sems["wgh"]
        sem_wqh = sems["wqh"]
        slq = {}
        slabs = []
        for i in range(22):
            slabs.append(("L1_ffn_w_gate", i, 0, 1024, "wgt_d"))
        for i in range(22):
            slabs.append(("L1_ffn_w_up", i, 0, 1024, "wut_d"))
        for i in range(8):
            for piece in range(3):
                w0 = piece * 1024
                slabs.append(("L1_ffn_w_down", i, w0, min(2816, w0 + 1024) - w0, "wdt_d"))
        NSLAB = len(slabs)
        sl_loaded = {}
        sl_ve_done = {}
        sl_state = {"sj": 0, "tr_prev": 0, "st_prev": 0, "lim": NSLAB}

        def alloc_slab_bufs(gates=()):
            slq["wsl"] = [ar.alloc([P, 1024]) for _ in range(2)]
            slq["wqo"] = ar.alloc([P, 1024], BF16)
            _bm = ar.alloc([P, 64])
            slq["wqb"] = {"t1": ar.alloc([P, 1024], BF16), "t2": ar.alloc([P, 1024], BF16),
                          "q": ar.alloc([P, 1024]), "bm": _bm,
                          "sc": ar.alloc([P, 64]), "rs": _bm}
            slq["wqs"] = [ar.alloc([P, 8, 128], BF16) for _ in range(2)]
            slq["gates"] = list(gates)
            emit_slab_load(sl_state["sj"])
            emit_slab_load(sl_state["sj"] + 1)

        def emit_slab_load(j):
            if j >= NSLAB or j >= sl_state["lim"]:
                return
            nm, i, w0, w, _dst = slabs[j]
            gate = list(slq["gates"]) + ([(sem_ve, sl_ve_done[j - 2])] if (j - 2) in sl_ve_done else [])
            dstb = slq["wsl"][j % 2][:, 0:w]
            sl_loaded[j] = s.op("gpsimd", lambda e, nm=nm, i=i, w0=w0, w=w, dstb=dstb: e.dma_start(
                dstb, d[nm][i * P:(i + 1) * P, w0:w0 + w]),
                waits=gate, inc=sem_wgh, inc_by=16)

        vq_of = {}
        tr_of = {}
        st_of = {}
        info_of = {}

        def issue_transposes(j):
            nm, i, w0, w, dst = info_of[j]
            wqo = slq["wqo"]
            wqs = slq["wqs"][j % 2]
            for cc in range(w // 128):
                wts = []
                if cc == 0:
                    wts = [(sem_ve, vq_of[j])]
                    if (j - 2) in st_of:
                        wts.append((sem_wqh, st_of[j - 2]))
                s.op("scalar", lambda e, cc=cc, wqo=wqo, wqs=wqs: e.dma_start_transpose(
                    wqs[:, cc, :], wqo[:, cc * P:(cc + 1) * P]),
                    waits=wts, inc=sem_wqh, inc_by=16)
            tr_of[j] = sem_wqh.n

        def issue_store(j):
            nm, i, w0, w, dst = info_of.pop(j)
            nch = w // 128
            wqs = slq["wqs"][j % 2]
            s.op("scalar", lambda e, dst=dst, i=i, w0=w0, nch=nch, wqs=wqs: e.dma_start(
                d[dst].rearrange("(b p) n -> p b n", p=P)[:, w0 // P:w0 // P + nch, i * P:(i + 1) * P],
                wqs[:, 0:nch, :]),
                waits=[(sem_wqh, tr_of[j])], inc=sem_wqh, inc_by=16)
            st_of[j] = sem_wqh.n
            sl_state["st_prev"] = sem_wqh.n

        def emit_one_slab():
            j = sl_state["sj"]
            if j >= NSLAB or j >= sl_state["lim"]:
                return
            sl_state["sj"] += 1
            if (j - 2) in info_of and (j - 2) in tr_of:
                issue_store(j - 2)
            if (j - 1) in vq_of and (j - 1) not in tr_of:
                issue_transposes(j - 1)
            info_of[j] = slabs[j]
            nm, i, w0, w, dst = slabs[j]
            vq_of[j] = emit_quant(s, sems, {**slq["wqb"], "absx": slq["wsl"][j % 2]},
                                  slq["wsl"][j % 2][:, 0:w], slq["wqo"][:, 0:w], w,
                                  src_ready=[(sem_wgh, sl_loaded[j])],
                                  out_gate=[(sem_wqh, tr_of[j - 1])] if (j - 1) in tr_of else [])
            sl_ve_done[j] = vq_of[j]
            emit_slab_load(j + 2)

        def flush_slabs():
            for j in sorted(set(vq_of) - set(tr_of)):
                issue_transposes(j)
            for j in sorted(info_of):
                issue_store(j)

        def emit_slabs(k, lim=None):
            if lim is not None:
                sl_state["lim"] = lim
            for _ in range(k):
                emit_one_slab()

    alloc_slab_bufs()

    s.op("sync", lambda e: e.dma_start(gw[:], d["L0_ffn_gate_w"].rearrange("(c p) e -> p c e", p=P)),
         inc=sem_ld, inc_by=16)
    v_gw = sem_ld.n
    s.op("sync", lambda e: e.dma_start(gamma_b, d["L0_ffn_gamma"].unsqueeze(0).broadcast_to((P, DM))),
         inc=sem_ld, inc_by=16)
    s.op("sync", lambda e: e.dma_start(beta_b, d["L0_ffn_beta"].unsqueeze(0).broadcast_to((P, DM))),
         inc=sem_ld, inc_by=16)
    s.op("sync", lambda e: e.dma_start(normw_b, d["L0_attn_norm_w"].unsqueeze(0).broadcast_to((P, DM))),
         inc=sem_ld, inc_by=16)
    v_rows = sem_ld.n
    s.op("sync", lambda e: e.dma_start(dt4[0:4, :], d["L0_attn_dt"].rearrange("(c p) -> c p", p=P)),
         inc=sem_ld, inc_by=16)
    s.op("sync", lambda e: e.dma_start(A4[0:4, :], d["L0_attn_A"].rearrange("(c p) -> c p", p=P)),
         inc=sem_ld, inc_by=16)
    v_dtA = sem_ld.n
    s.op("gpsimd", lambda e: e.dma_start(Bm[:], d["L0_attn_Bmat"].rearrange("(c p) s -> p c s", p=P)),
         inc=sem_w2s, inc_by=16)
    v_Bm = sem_w2s.n
    w1_done = [None] * 8
    w2_done = [None] * 8
    b2_done = [None] * 8

    def emit_w1(e_, gate_waits):
        for k in range(8):
            wts = list(gate_waits) if k == 0 else []
            s.op("gpsimd", lambda en, e_=e_, k=k: en.dma_start(
                w1b[:, k, :], d["L0_ffn_w1"][e_, k * P:(k + 1) * P, :]),
                waits=wts, inc=sem_w1s, inc_by=16)
        w1_done[e_] = sem_w1s.n

    def emit_w2b2(e_, gate_pe):
        s.op("gpsimd", lambda en, e_=e_: en.dma_start(
            b2r[0:1, e_ % 2, :], d["L0_ffn_b2"][e_:e_ + 1, :]),
            waits=[(sem_pe, gate_pe)] if gate_pe is not None else [], inc=sem_w2s, inc_by=16)
        b2_done[e_] = sem_w2s.n
        for g in range(4):
            s.op("gpsimd", lambda en, e_=e_, g=g: en.dma_start(
                w2b[:, g * 8:(g + 1) * 8, :],
                d["L0_ffn_w2"][e_, g * 1024:(g + 1) * 1024, :].rearrange("(j p) dd -> p j dd", p=P)),
                inc=sem_w2s, inc_by=16)
        w2_done[e_] = sem_w2s.n

    emit_w2b2(0, None)

    v_b1p = 0
    for e_ in range(8):
        v_l = s.op("sync", lambda en, e_=e_: en.dma_start(
            b1r[0:32, :], d["L0_ffn_b1"][e_].rearrange("(n p) -> n p", p=P)),
            waits=([] if e_ == 0 else [(sem_ve, v_b1p)]), inc=sem_ld, inc_by=16)
        v_t = s.op("tensor", lambda en: en.matmul(hp0[:, 0:32], b1r[0:32, :], ident[0:32, 0:32],
                                                  start=True, stop=True),
                   waits=[(sem_ld, v_l)] + ([(sem_ve, v_b1p)] if e_ else []), inc=sem_pe)
        v_b1p = s.op("vector", lambda en, e_=e_: en.tensor_copy(b1c[:, e_, :], hp0[:, 0:32]),
                     waits=[(sem_pe, v_t)], inc=sem_ve)
    v_b1all = v_b1p
    PH = os.environ.get("KPHASES", "all")

    def trivial_final_u():
        ar.mark()
        tmpf = ar.alloc([P, DM])
        vcp = 0
        vst = 0
        for tb in range(NTB):
            vl = s.op("sync", lambda e, tb=tb: e.dma_start(tmpf[:], d["u"][tb * P:(tb + 1) * P, :]),
                      waits=[(sem_st, vst)] if vst else [], inc=sem_fc, inc_by=16)
            vst = s.op("sync", lambda e, tb=tb: e.dma_start(d["x1"][tb * P:(tb + 1) * P, :], tmpf[:]),
                      waits=[(sem_fc, vl)], inc=sem_st, inc_by=16)
        ar.release()

    def trivial_final():
        ar.mark()
        tmpb = ar.alloc([P, DM], BF16)
        tmpf = ar.alloc([P, DM])
        vcp = 0
        vst = 0
        for tb in range(NTB):
            vl = s.op("sync", lambda e, tb=tb: e.dma_start(tmpb[:], d["xn_d"][tb * P:(tb + 1) * P, :]),
                      waits=[(sem_ve, vcp)] if vcp else [], inc=sem_fc, inc_by=16)
            vcp = s.op("vector", lambda e: e.tensor_copy(tmpf[:], tmpb[:]),
                       waits=[(sem_fc, vl)] + ([(sem_st, vst)] if vst else []), inc=sem_ve)
            vst = s.op("sync", lambda e, tb=tb: e.dma_start(d["x1"][tb * P:(tb + 1) * P, :], tmpf[:]),
                       waits=[(sem_ve, vcp)], inc=sem_st, inc_by=16)
        ar.release()

    if PH == "pro":
        ar.release()
        trivial_final_u()
        return
    ar.mark()
    xr = ar.alloc([P, 2, DM])
    xc = ar.alloc([P, DM])
    xn_t = ar.alloc([P, DM])
    xn2_t = ar.alloc([P, DM])
    xg = ar.alloc([P, DM])
    xsq = xg
    xn2f = ar.alloc([P, 8, 128])
    xn2b = ar.alloc([P, 8, 128], BF16)
    attb = ar.alloc([P, DM], BF16)
    attf = ar.alloc([P, 8, 128], BF16)
    xnb = ar.alloc([P, DM], BF16)
    lg = ar.alloc([P, 8])
    st_ = ar.alloc([P, 16])
    zrt = ar.alloc([P, 8])
    msk = ar.alloc([P, 8])

    x_rd = {}
    x_consumed = {}
    last_hp0_act = 0
    prev_xn2dma = 0
    prev_xndma = 0
    LNTB = int(os.environ.get("KLNTB", NTB))
    SKIP = set(os.environ.get("KLNSKIP", "").split(","))
    DEPTH = int(os.environ.get("KLNDEPTH", 99))

    def emit_xload(tb):
        waits = []
        if tb >= 2:
            waits.append((sem_ve, x_consumed[tb - 2]))
        x_rd[tb] = s.op("sync", lambda e, tb=tb: e.dma_start(
            xr[:, tb % 2, :], d["u"][tb * P:(tb + 1) * P, :]),
            waits=waits, inc=sem_lx[tb % 2], inc_by=16)

    emit_xload(0)
    if LNTB > 1:
        emit_xload(1)
    for tb in range(LNTB):
        xin = xr[:, tb % 2, :]
        s.op("vector", lambda e, xin=xin: e.reduce_sum(st_[:, 0:1], xin, axis=AX.X),
             waits=[(sem_lx[tb % 2], x_rd[tb])])
        s.op("vector", lambda e: e.tensor_scalar(st_[:, 0:1], st_[:, 0:1], 1.0 / DM, None, ALU.mult))
        v_xc = s.op("vector", lambda e, xin=xin: e.tensor_scalar(xc[:], xin, st_[:, 0:1], None, ALU.subtract),
                    inc=sem_ve)
        x_consumed[tb] = v_xc
        if tb + 2 < LNTB:
            emit_xload(tb + 2)
        s.op("scalar", lambda e: e.activation(xsq[:], xc[:], AF.Square, accum_out=st_[:, 1:2]),
             waits=[(sem_ve, v_xc)])
        s.op("scalar", lambda e: e.activation(st_[:, 1:2], st_[:, 1:2], AF.Ln, bias=cst[:, 0:1], scale=1.0 / DM))
        v_rs = s.op("scalar", lambda e: e.activation(st_[:, 1:2], st_[:, 1:2], AF.Exp, scale=-0.5),
                    inc=sem_act)
        v_xn = s.op("vector", lambda e: e.tensor_scalar(xn_t[:], xc[:], st_[:, 1:2], None, ALU.mult),
                    waits=[(sem_act, v_rs)], inc=sem_ve)
        if "xnd" in SKIP:
            v_xnb = None
        if "xnd" not in SKIP:
            v_xnb = s.op("vector", lambda e: e.tensor_copy(xnb[:], xn_t[:]),
                         waits=[(sem_sxn, prev_xndma)] if prev_xndma else [], inc=sem_ve)
            prev_xndma = s.op("sync", lambda e, tb=tb: e.dma_start(d["xn_d"][tb * P:(tb + 1) * P, :], xnb[:]),
                              waits=[(sem_ve, v_xnb)], inc=sem_sxn, inc_by=16)
        if DEPTH < 2:
            continue
        s.op("vector", lambda e: e.reduce_sum(st_[:, 2:3], xn_t[:], axis=AX.X))
        s.op("vector", lambda e: e.tensor_scalar(st_[:, 2:3], st_[:, 2:3], 1.0 / DM, None, ALU.mult))
        v_xc2 = s.op("vector", lambda e: e.tensor_scalar(xc[:], xn_t[:], st_[:, 2:3], None, ALU.subtract),
                     inc=sem_ve)
        s.op("scalar", lambda e: e.activation(xsq[:], xc[:], AF.Square, accum_out=st_[:, 3:4]),
             waits=[(sem_ve, v_xc2)])
        s.op("scalar", lambda e: e.activation(st_[:, 3:4], st_[:, 3:4], AF.Ln, bias=cst[:, 0:1], scale=1.0 / DM))
        v_rs2 = s.op("scalar", lambda e: e.activation(st_[:, 3:4], st_[:, 3:4], AF.Exp, scale=-0.5),
                     inc=sem_act)
        s.op("vector", lambda e: e.scalar_tensor_tensor(xg[:], xc[:], st_[:, 3:4], gamma_b, ALU.mult, ALU.mult),
             waits=[(sem_act, v_rs2), (sem_ld, v_rows)])
        v_xn2 = s.op("vector", lambda e: e.tensor_tensor(xn2_t[:], xg[:], beta_b, ALU.add), inc=sem_ve)
        if DEPTH < 3:
            continue
        s.op("scalar", lambda e: e.activation(xsq[:], xn_t[:], AF.Square, accum_out=st_[:, 4:5]),
             waits=[(sem_ve, v_xn)])
        s.op("scalar", lambda e: e.activation(st_[:, 4:5], st_[:, 4:5], AF.Ln, bias=cst[:, 1:2], scale=1.0 / DM))
        v_rs3 = s.op("scalar", lambda e: e.activation(st_[:, 4:5], st_[:, 4:5], AF.Exp, scale=-0.5),
                     inc=sem_act)
        v_att = s.op("vector", lambda e: e.scalar_tensor_tensor(
            attb[:], xn_t[:], st_[:, 4:5], normw_b, ALU.mult, ALU.mult),
            waits=[(sem_act, v_rs3)], inc=sem_ve)
        if DEPTH < 4:
            continue
        hp1b = hp1[:, 0:P].bitcast(BF16)[:, 0:P]
        for c in range(8):
            w1_ = [(sem_ve, v_xn2 if c == 0 else sem_ve.n)]
            v_t1 = s.op("tensor", lambda e, c=c: e.transpose(hp0[:, 0:P], xn2_t[:, c * P:(c + 1) * P], ident[:]),
                        waits=w1_, inc=sem_pe)
            v_c1 = s.op("vector", lambda e, c=c: e.tensor_copy(xn2f[:, c, :], hp0[:, 0:P]),
                        waits=[(sem_pe, v_t1)], inc=sem_ve)
            wc1b = []
            if c == 0 and prev_xn2dma:
                wc1b.append((sem_sx2, prev_xn2dma))
            v_c1b = s.op("vector", lambda e, c=c: e.tensor_copy(xn2b[:, c, :], xn2f[:, c, :]),
                         waits=wc1b, inc=sem_ve)
            v_t2 = s.op("tensor", lambda e, c=c, hp1b=hp1b: e.transpose(
                hp1b, attb[:, c * P:(c + 1) * P], identb[:]),
                waits=[(sem_ve, v_att if c == 0 else sem_ve.n)], inc=sem_pe)
            s.op("vector", lambda e, c=c, hp1b=hp1b: e.tensor_copy(attf[:, c, :], hp1b),
                 waits=[(sem_pe, v_t2)], inc=sem_ve)
        v_xn2b_done = sem_ve.n
        v_trdone = sem_ve.n
        if DEPTH < 5:
            continue
        prev_xn2dma = s.op("sync", lambda e, tb=tb: e.dma_start(
            d["xn2fm_d"].rearrange("(c p) t -> p c t", p=P)[:, :, tb * P:(tb + 1) * P], xn2b[:]),
            waits=[(sem_ve, v_xn2b_done)], inc=sem_sx2, inc_by=16)
        if DEPTH < 6:
            continue
        for c in range(8):
            def mml(e, c=c):
                return e.matmul(dnA[:, 0:8], xn2f[:, c, :], gw[:, c, :],
                                start=(c == 0), stop=(c == 7))
            if c == 0:
                s.op("tensor", mml, waits=[(sem_ve, v_trdone), (sem_ld, v_gw)])
            elif c == 7:
                v_lgmm = s.op("tensor", mml, inc=sem_pe)
            else:
                s.op("tensor", mml)
        v_lgc = s.op("vector", lambda e: e.tensor_copy(lg[:], dnA[:, 0:8]),
                     waits=[(sem_pe, v_lgmm)], inc=sem_ve)
        s.op("vector", lambda e: e.reduce_max(st_[:, 5:6], lg[:], axis=AX.X))
        s.op("vector", lambda e: e.tensor_scalar(st_[:, 6:7], st_[:, 5:6], -1.0, None, ALU.mult))
        s.op("vector", lambda e: e.tensor_scalar(msk[:], lg[:], st_[:, 5:6], None, ALU.is_ge))
        s.op("vector", lambda e: e.scalar_tensor_tensor(zrt[:], msk[:], -1e30, lg[:], ALU.mult, ALU.add))
        s.op("vector", lambda e: e.reduce_max(st_[:, 7:8], zrt[:], axis=AX.X))
        v_mk = s.op("vector", lambda e: e.tensor_scalar(msk[:], lg[:], st_[:, 7:8], None, ALU.is_ge),
                    inc=sem_ve)
        v_ez = s.op("scalar", lambda e: e.activation(zrt[:], lg[:], AF.Exp, bias=st_[:, 6:7]),
                    waits=[(sem_ve, v_mk)], inc=sem_act)
        s.op("vector", lambda e: e.tensor_tensor(zrt[:], zrt[:], msk[:], ALU.mult),
             waits=[(sem_act, v_ez)])
        s.op("vector", lambda e: e.reduce_sum(st_[:, 8:9], zrt[:], axis=AX.X))
        s.op("vector", lambda e: e.reciprocal(st_[:, 9:10], st_[:, 8:9]))
        s.op("vector", lambda e, tb=tb: e.tensor_scalar(cw_sb[:, tb, :], zrt[:], st_[:, 9:10], None, ALU.mult),
             inc=sem_ve)
        if DEPTH < 7:
            continue
        for st4 in range(4):
            psU = (dnB if st4 % 2 == 0 else dnC)[:, 0:P]
            for k in range(8):
                def mmu(e, st4=st4, k=k, psU=psU):
                    return e.matmul(psU, Bm[:, k, st4 * P:(st4 + 1) * P], attf[:, k, :],
                                    start=(k == 0), stop=(k == 7))
                if k == 0:
                    wts = [(sem_ve, v_trdone)]
                    if tb == 0 and st4 == 0:
                        wts.append((sem_w2s, v_Bm))
                    if st4 >= 2 or tb > 0:
                        wts.append((sem_ve, sem_ve.n))
                    s.op("tensor", mmu, waits=wts)
                elif k == 7:
                    v_um = s.op("tensor", mmu, inc=sem_pe)
                else:
                    s.op("tensor", mmu)
            s.op("vector", lambda e, st4=st4, tb=tb, psU=psU: e.tensor_copy(
                upj[:, st4, tb * P:(tb + 1) * P], psU),
                waits=[(sem_pe, v_um)], inc=sem_ve)
        emit_slabs(3, lim=52)
    v_ln_pe = sem_pe.n
    ar.release()
    if PH == "ln":
        ar.release()
        trivial_final()
        return

    ar.mark()
    binc = ar.alloc([P, T])
    xsc_f = ar.alloc([P, T])
    xsc_b = ar.alloc([P, T], BF16)
    s.op("scalar", lambda e: e.activation(dt4[0:4, :], dt4[0:4, :], AF.Sigmoid),
         waits=[(sem_ld, v_dtA)])
    v_ab = s.op("scalar", lambda e: e.activation(A4[0:4, :], A4[0:4, :], AF.Abs), inc=sem_act)
    v_m = s.op("vector", lambda e: e.tensor_tensor(A4[0:4, :], A4[0:4, :], dt4[0:4, :], ALU.mult),
               waits=[(sem_act, v_ab)], inc=sem_ve)
    v_ex = s.op("scalar", lambda e: e.activation(A4[0:4, :], A4[0:4, :], AF.Exp, scale=-1.0),
                waits=[(sem_ve, v_m)], inc=sem_act)
    v_tt1 = s.op("tensor", lambda e: e.matmul(hp0[:, 0:4], dt4[0:4, :], ident[0:4, 0:4],
                                              start=True, stop=True),
                 waits=[(sem_act, v_ex), (sem_ve, sem_ve.n)], inc=sem_pe)
    v_tt2 = s.op("tensor", lambda e: e.matmul(hp1[:, 0:4], A4[0:4, :], ident[0:4, 0:4],
                                              start=True, stop=True), inc=sem_pe)
    v_dc = s.op("vector", lambda e: e.tensor_copy(dtc[:], hp0[:, 0:4]), waits=[(sem_pe, v_tt1)], inc=sem_ve)
    v_ac = s.op("vector", lambda e: e.tensor_copy(ac[:], hp1[:, 0:4]), waits=[(sem_pe, v_tt2)], inc=sem_ve)
    v_sdma = 0
    for c in range(4):
        wts = [(sem_ve, max(v_dc, v_ac))] if c == 0 else [(sem_sscan, v_sdma)]
        s.op("vector", lambda e, c=c: e.tensor_scalar(binc[:], upj[:, c, :], dtc[:, c:c + 1], None, ALU.mult),
             waits=wts)
        s.op("vector", lambda e, c=c: e.tensor_tensor_scan(
            xsc_f[:], ac[:, c:c + 1].broadcast_to((P, T)), binc[:], 0.0, ALU.mult, ALU.add))
        v_cb = s.op("vector", lambda e: e.tensor_copy(xsc_b[:], xsc_f[:]), inc=sem_ve)
        v_sdma = s.op("sync", lambda e, c=c: e.dma_start(d["xscan_d"][c * P:(c + 1) * P, :], xsc_b[:]),
                      waits=[(sem_ve, v_cb)], inc=sem_sscan, inc_by=16)
    emit_slabs(4, lim=52)
    if "L1_ffn_w_gate" in d:
        flush_slabs()
    v_scan_ve = sem_ve.n
    ar.release()
    ar.release()
    if PH == "scan":
        trivial_final()
        return

    ar.mark()
    w1b = ar.alloc([P, 8, 4096], BF16)
    hb = ar.alloc([P, 32, 512], BF16)
    QT = 512

    v_z = s.op("vector", lambda e: e.memset(accb[:], 0.0), inc=sem_ve)
    w1_gates = [(sem_ve, v_scan_ve), (sem_sscan, v_sdma)]
    if "L1_ffn_w_gate" in d:
        w1_gates.append((sems["wqh"], sl_state["st_prev"]))
    emit_w1(0, w1_gates)

    xnq_done = {}

    def emit_xnq(e_, q, gate_pe):
        xnq_done[(e_, q)] = s.op("sync", lambda en, q=q: en.dma_start(
            xnq[:],
            d["xn2fm_d"].rearrange("(c p) t -> p c t", p=P)[:, :, q * QT:(q + 1) * QT]),
            waits=[(sem_pe, gate_pe)] if gate_pe is not None else [], inc=sem_xq, inc_by=16)

    xnq_done[(0, 0)] = s.op("sync", lambda en: en.dma_start(
        xnq[:], d["xn2fm_d"].rearrange("(c p) t -> p c t", p=P)[:, :, 0:QT]),
        waits=[(sem_sx2, sem_sx2.n)], inc=sem_xq, inc_by=16)

    dn_evac = []
    ngrp = 0
    prev_g30 = None
    prev_g31 = None
    pe_after_h = {}
    pe_after_dn = {}
    for e_ in range(8):
        for q in range(4):
            gelu_vals = []
            for n in range(32):
                ps = hp0 if n % 2 == 0 else hp1
                waits = []
                if n == 0:
                    waits.append((sem_xq, xnq_done[(e_, q)]))
                    waits.append((sem_w1s, w1_done[e_]))
                    if e_ == 0 and q == 0:
                        waits.append((sem_ve, max(v_b1all, v_scan_ve)))
                        waits.append((sem_pe, v_ln_pe))
                    else:
                        waits.append((sem_act, prev_g30))
                if n == 1 and prev_g31 is not None and not (e_ == 0 and q == 0):
                    waits.append((sem_act, prev_g31))
                if n >= 2:
                    waits.append((sem_act, gelu_vals[n - 2]))
                for k in range(8):
                    def mmh(e, k=k, n=n, ps=ps):
                        return e.matmul(ps[:], w1b[:, k, n * P:(n + 1) * P], xnq[:, k, :],
                                        start=(k == 0), stop=(k == 7))
                    if k == 0:
                        s.op("tensor", mmh, waits=waits)
                    elif k == 7:
                        v_hm = s.op("tensor", mmh, inc=sem_pe)
                    else:
                        s.op("tensor", mmh)
                gelu_vals.append(s.op("scalar", lambda e, n=n, ps=ps, e_=e_: e.activation(
                    hb[:, n, :], ps[:], AF.Gelu, bias=b1c[:, e_, n:n + 1]),
                    waits=[(sem_pe, v_hm)], inc=sem_act))
            prev_g30, prev_g31 = gelu_vals[30], gelu_vals[31]
            pe_after_h[(e_, q)] = sem_pe.n
            nxt = (e_, q + 1) if q < 3 else (e_ + 1, 0)
            if nxt[0] < 8:
                emit_xnq(nxt[0], nxt[1], pe_after_h[(e_, q)])
            for tb in range(4):
                tbg = q * 4 + tb
                ps = (dnA, dnB, dnC)[ngrp % 3]
                waits = [(sem_act, gelu_vals[-1])]
                if ngrp == 0:
                    waits.append((sem_w2s, w2_done[0]))
                    waits.append((sem_ve, v_scan_ve))
                elif ngrp >= 3:
                    waits.append((sem_ve, dn_evac[ngrp - 3]))
                first = True
                for n in range(32):
                    for j in range(2):
                        def mmd(e, n=n, j=j, tb=tb, ps=ps):
                            return e.matmul(ps[:, j * 512:(j + 1) * 512],
                                            hb[:, n, tb * P:(tb + 1) * P],
                                            w2b[:, n, j * 512:(j + 1) * 512],
                                            start=(n == 0), stop=False, skip_group_check=True)
                        if first:
                            s.op("tensor", mmd, waits=waits)
                            first = False
                        else:
                            s.op("tensor", mmd)
                for j in range(2):
                    def mmb(e, j=j, ps=ps, e_=e_):
                        return e.matmul(ps[:, j * 512:(j + 1) * 512], onesb[:],
                                        b2r[0:1, e_ % 2, j * 512:(j + 1) * 512],
                                        start=False, stop=True, skip_group_check=True)
                    if j == 1:
                        v_dm = s.op("tensor", mmb, inc=sem_pe)
                    else:
                        s.op("tensor", mmb)
                dn_evac.append(s.op("vector", lambda e, tbg=tbg, ps=ps, e_=e_: e.scalar_tensor_tensor(
                    accb[:, tbg, :], ps[:], cw_sb[:, tbg, e_].unsqueeze(-1), accb[:, tbg, :],
                    ALU.mult, ALU.add),
                    waits=[(sem_pe, v_dm), (sem_ve, v_z)], inc=sem_ve))
                ngrp += 1
            pe_after_dn[(e_, q)] = sem_pe.n
        if e_ + 1 < 8:
            emit_w1(e_ + 1, [(sem_pe, pe_after_h[(e_, 3)])])
            emit_w2b2(e_ + 1, pe_after_dn[(e_, 3)])
    v_moe_ve = sem_ve.n
    ar.release()
    if PH == "moe":
        s.wait("vector", sem_ve, v_moe_ve)
        trivial_final()
        return

    ar.mark()
    C_sb = ar.alloc([P, 4, 1024], BF16)
    xscan = ar.alloc([P, 4, T], BF16)
    D_b = ar.alloc([P, DM])
    xr2 = ar.alloc([P, 2, DM])
    xnr = ar.alloc([P, 2, DM], BF16)
    sD = ar.alloc([P, DM])
    gsum = ar.alloc([P, DM])
    gout = ar.alloc([P, DM])
    xout = ar.alloc([P, DM])
    emit_slabs(0, lim=10 ** 9)
    alloc_slab_bufs(gates=[(sem_pe, sem_pe.n), (sem_act, sem_act.n), (sem_ve, sem_ve.n)])
    v_C = s.op("gpsimd", lambda e: e.dma_start(C_sb[:], d["L0_attn_C"].rearrange("(c p) o -> p c o", p=P)),
               waits=[(sem_pe, pe_after_dn[(7, 3)])], inc=sem_w2s, inc_by=16)
    v_xs = s.op("sync", lambda e: e.dma_start(xscan[:], d["xscan_d"].rearrange("(c p) t -> p c t", p=P)),
                waits=[(sem_pe, pe_after_dn[(7, 3)]), (sem_sscan, sem_sscan.n)], inc=sem_fc, inc_by=16)
    v_D = s.op("sync", lambda e: e.dma_start(D_b, d["L0_attn_D"].unsqueeze(0).broadcast_to((P, DM))),
                inc=sem_fc, inc_by=16)

    f_x = {}
    f_xn = {}
    f_consumed = {}
    f_sD = {}

    def emit_fload(tb):
        sm = sem_fl[tb % 2]
        waits = [(sem_pe, pe_after_dn[(7, 3)])] if tb == 0 else []
        if tb >= 2:
            waits.append((sem_ve, f_consumed[tb - 2]))
        s.op("sync", lambda e, tb=tb: e.dma_start(xr2[:, tb % 2, :], d["u"][tb * P:(tb + 1) * P, :]),
             waits=waits, inc=sm, inc_by=16)
        f_x[tb] = (sm, sm.n)
        s.op("sync", lambda e, tb=tb: e.dma_start(xnr[:, tb % 2, :], d["xn_d"][tb * P:(tb + 1) * P, :]),
             waits=[(sem_sxn, sem_sxn.n)] if tb == 0 else [], inc=sm, inc_by=16)
        f_xn[tb] = (sm, sm.n)

    emit_fload(0)
    emit_fload(1)
    for tb in range(NTB):
        ps = (dnA, dnB, dnC)[tb % 3]
        waits = []
        if tb == 0:
            waits = [(sem_fc, v_D), (sem_w2s, v_C), (sem_ve, v_moe_ve)]
        elif tb >= 3:
            waits = [(sem_ve, f_sD[tb - 3])]
        first = True
        for sc in range(4):
            for j in range(2):
                def mms(e, sc=sc, j=j, tb=tb, ps=ps):
                    return e.matmul(ps[:, j * 512:(j + 1) * 512],
                                    xscan[:, sc, tb * P:(tb + 1) * P],
                                    C_sb[:, sc, j * 512:(j + 1) * 512],
                                    start=(sc == 0), stop=(sc == 3), skip_group_check=True)
                if first:
                    s.op("tensor", mms, waits=waits)
                    first = False
                elif sc == 3 and j == 1:
                    v_sm = s.op("tensor", mms, inc=sem_pe)
                else:
                    s.op("tensor", mms)
        f_sD[tb] = s.op("vector", lambda e, ps=ps: e.tensor_tensor(sD[:], ps[:], D_b, ALU.add),
                        waits=[(sem_pe, v_sm), (sem_fc, v_D)], inc=sem_ve)
        v_gs = s.op("vector", lambda e, tb=tb: e.tensor_tensor(
            gsum[:], accb[:, tb, :], xnr[:, tb % 2, :], ALU.add),
            waits=[f_xn[tb]], inc=sem_ve)
        v_g = s.op("scalar", lambda e: e.activation(gout[:], gsum[:], AF.Silu),
                   waits=[(sem_ve, v_gs)], inc=sem_act)
        s.op("vector", lambda e: e.tensor_tensor(gout[:], sD[:], gout[:], ALU.mult),
             waits=[(sem_act, v_g)])
        wxo = [f_x[tb]]
        if tb > 0:
            wxo.append((sem_st, sem_st.n))
        v_xo = s.op("vector", lambda e, tb=tb: e.tensor_tensor(xout[:], xr2[:, tb % 2, :], gout[:], ALU.add),
                    waits=wxo, inc=sem_ve)
        f_consumed[tb] = v_xo
        s.op("sync", lambda e, tb=tb: e.dma_start(d["x1"][tb * P:(tb + 1) * P, :], xout[:]),
             waits=[(sem_ve, v_xo)], inc=sem_st, inc_by=16)
        if tb + 2 < NTB:
            emit_fload(tb + 2)
        emit_slabs(1)
    emit_slabs(100)
    if "L1_ffn_w_gate" in d:
        flush_slabs()
    ar.release()





def emit_quant(s, sems, bufs, src_ap, out_ap, W, src_ready=(), eng_big="vector",
               out_gate=()):
    sem_ve = sems["ve"]; sem_gp = sems["gp"]; sem_act = sems["act"]
    nb = W // 16
    absx, t1, t2, q = bufs["absx"], bufs["t1"], bufs["t2"], bufs["q"]
    bm, sc, rs = bufs["bm"], bufs["sc"], bufs["rs"]
    v_sg = s.op("scalar", lambda e: e.sign(q[:, 0:W], src_ap),
                waits=list(src_ready) + [(sem_ve, sems["ve"].n)])
    v_a = s.op("vector", lambda e: e.tensor_tensor(absx[:, 0:W], src_ap, q[:, 0:W], ALU.mult),
               waits=list(src_ready) + [(sem_gp, sem_gp.n), (sem_act, v_sg)])
    s.op("vector", lambda e: e.tensor_reduce(bm[:, 0:nb].unsqueeze(-1),
                                             absx[:, 0:W].rearrange("p (a b) -> p a b", b=16),
                                             axis=AX.X, op=ALU.max))
    s.op("vector", lambda e: e.tensor_scalar(sc[:, 0:nb], bm[:, 0:nb], 1.0 / 6.0, 1e-12,
                                             ALU.mult, ALU.max))
    s.op("vector", lambda e: e.reciprocal(rs[:, 0:nb], sc[:, 0:nb]))
    rs_b = rs[:, 0:nb].unsqueeze(-1).broadcast_to((P, nb, 16))
    sc_b = sc[:, 0:nb].unsqueeze(-1).broadcast_to((P, nb, 16))
    v_v = s.op("vector", lambda e: e.tensor_tensor(
        absx[:, 0:W].rearrange("p (a b) -> p a b", b=16),
        absx[:, 0:W].rearrange("p (a b) -> p a b", b=16), rs_b, ALU.mult))
    vv = absx[:, 0:W]
    v_ss = s.op("vector", lambda e: e.tensor_tensor(
        q[:, 0:W].rearrange("p (a b) -> p a b", b=16),
        q[:, 0:W].rearrange("p (a b) -> p a b", b=16), sc_b, ALU.mult),
        waits=[(sem_act, v_sg)])
    s.op("vector", lambda e: e.tensor_scalar(t1[:, 0:W], vv, 0.25, None, ALU.is_gt))
    s.op("vector", lambda e: e.scalar_tensor_tensor(t1[:, 0:W], vv, 0.75, t1[:, 0:W],
                                                    ALU.is_gt, ALU.add))
    s.op("vector", lambda e: e.scalar_tensor_tensor(t1[:, 0:W], vv, 1.25, t1[:, 0:W],
                                                    ALU.is_gt, ALU.add))
    g4 = s.op("vector", lambda e: e.scalar_tensor_tensor(t1[:, 0:W], vv, 1.75, t1[:, 0:W],
                                                         ALU.is_gt, ALU.add))
    s.op("vector", lambda e: e.tensor_scalar(t2[:, 0:W], vv, 2.5, None, ALU.is_gt))
    s.op("vector", lambda e: e.scalar_tensor_tensor(t2[:, 0:W], vv, 3.5, t2[:, 0:W],
                                                    ALU.is_gt, ALU.add))
    s.op("vector", lambda e: e.scalar_tensor_tensor(t2[:, 0:W], vv, 5.0, t2[:, 0:W],
                                                    ALU.is_gt, ALU.add))
    s.op("vector", lambda e: e.scalar_tensor_tensor(t2[:, 0:W], vv, 5.0, t2[:, 0:W],
                                                    ALU.is_gt, ALU.add))
    s.op("vector", lambda e: e.scalar_tensor_tensor(t2[:, 0:W], t1[:, 0:W], 0.5, t2[:, 0:W],
                                                    ALU.mult, ALU.add))
    v_out = s.op("vector", lambda e: e.tensor_tensor(out_ap, t2[:, 0:W], q[:, 0:W], ALU.mult),
                 waits=list(out_gate))
    return v_out


def build_l1(ctx):
    s = ctx["s"]
    ar = ctx["ar"]
    d = ctx["dram"]
    sems = ctx["sems"]
    cst = ctx["cst"]
    sem_ld = sems["ld"]; sem_pe = sems["pe"]; sem_act = sems["act"]
    sem_ve = sems["ve"]; sem_st = sems["st"]; sem_gp = sems["gp"]
    sem_lx = (sems["lxa"], sems["lxb"])
    sem_sxf = sems["sx2"]
    sem_sxq = sems["sscan"]
    sem_sbz = sems["sxn"]
    sem_cc = (sems["fla"], sems["flb"])
    sem_ss1 = sems["fc"]
    sem_wg = sems["w1s"]
    sem_wq = sems["wqh"] if "wqh" in sems else sems["w2s"]
    sem_xq = sems["xq"]
    ident = ctx["ident"]; identb = ctx["identb"]
    dnA = ctx["dnA"]; dnB = ctx["dnB"]; dnC = ctx["dnC"]
    hp0 = ctx["hp0"]; hp1 = ctx["hp1"]
    DI = 2048
    NH = 32
    HD = 64
    DS = 128

    ar.mark()
    dtraw = ar.alloc([P, NTB, NH])
    Wdt = ar.alloc([P, 8, NH])
    w4 = ar.alloc([P, 18, 4])
    cb_c = ar.alloc([P, 18])
    A_bc = ar.alloc([P, NH])
    D_bc = ar.alloc([P, NH])
    dtb_bc = ar.alloc([P, NH])
    normw_bc = ar.alloc([P, DI])
    eh_flat = ar.alloc([P, 4096])
    eh_sb = eh_flat[0:32, :]
    lt_sb = ar.alloc([P, P])
    onesf_t = ar.alloc([P, 128])
    onesf = onesf_t[0:1, :]
    state = ar.alloc([P, DI])
    state_bf = ar.alloc([P, DI], BF16)
    BmF = ar.alloc([P, T], BF16)
    CmF = ar.alloc([P, T], BF16)
    Wout_sb = ar.alloc([P, 16, 1024], BF16)

    gate0 = ctx.get("l1_gate", [])
    s.op("sync", lambda e: e.dma_start(lt_sb[:], d["c_lt"]), waits=gate0, inc=sem_ld, inc_by=16)
    s.op("sync", lambda e: e.dma_start(onesf_t[0:1, :], d["c_onesf"]), inc=sem_ld, inc_by=16)
    s.op("sync", lambda e: e.dma_start(eh_flat[0:32, :], d["c_eh"]), inc=sem_ld, inc_by=16)
    s.op("sync", lambda e: e.dma_start(
        Wdt[:], d["L1_attn_W_in"][:, 4352:4384].rearrange("(c p) e -> p c e", p=P)),
        inc=sem_ld, inc_by=16)
    s.op("sync", lambda e: e.dma_start(
        normw_bc[:], d["L1_attn_norm_w"].unsqueeze(0).broadcast_to((P, DI))),
        inc=sem_ld, inc_by=16)
    s.op("sync", lambda e: e.dma_start(
        dtb_bc[:], d["L1_attn_dt_bias"].unsqueeze(0).broadcast_to((P, NH))),
        inc=sem_ld, inc_by=16)
    ar.mark()
    arow = ar.alloc([P, NH])
    s.op("sync", lambda e: e.dma_start(arow[0:1, :], d["L1_attn_A_log"].unsqueeze(0)),
         inc=sem_ld, inc_by=16)
    drow = ar.alloc([P, NH])
    s.op("sync", lambda e: e.dma_start(drow[0:1, :], d["L1_attn_D"].unsqueeze(0)),
         inc=sem_ld, inc_by=16)
    cwst = ar.alloc([P, 512])
    s.op("sync", lambda e: e.dma_start(
        cwst[0:18, :], d["L1_attn_conv_w"].rearrange("(c p) j -> c (p j)", p=P)),
        inc=sem_ld, inc_by=16)
    s.op("sync", lambda e: e.dma_start(
        cwst[32:50, 0:P], d["L1_attn_conv_b"].rearrange("(c p) -> c p", p=P)),
        inc=sem_ld, inc_by=16)
    v_par = sem_ld.n

    v_ae = s.op("scalar", lambda e: e.activation(arow[0:1, :], arow[0:1, :], AF.Exp),
                waits=[(sem_ld, v_par)])
    v_an = s.op("vector", lambda e: e.tensor_scalar(arow[0:1, :], arow[0:1, :], -1.0, None, ALU.mult),
                waits=[(sem_act, v_ae)])
    v_tA = s.op("tensor", lambda e: e.matmul(hp0[:, 0:NH], onesf, arow[0:1, :], start=True, stop=True),
                waits=[(sem_ve, v_an), (sem_ld, v_par)], inc=sem_pe)
    v_cA = s.op("vector", lambda e: e.tensor_copy(A_bc[:], hp0[:, 0:NH]), waits=[(sem_pe, v_tA)])
    v_tD = s.op("tensor", lambda e: e.matmul(hp1[:, 0:NH], onesf, drow[0:1, :], start=True, stop=True),
                waits=[(sem_ve, v_cA)], inc=sem_pe)
    v_cD = s.op("vector", lambda e: e.tensor_copy(D_bc[:], hp1[:, 0:NH]), waits=[(sem_pe, v_tD)])
    for j in range(4):
        v_tw = s.op("tensor", lambda e, j=j: e.matmul(
            hp0[:, 0:18], cwst[0:18, :].rearrange("c (p j) -> c p j", j=4)[:, :, j],
            ident[0:18, 0:18], start=True, stop=True),
            waits=[(sem_ve, v_cD if j == 0 else sem_ve.n)], inc=sem_pe)
        s.op("vector", lambda e, j=j: e.tensor_copy(w4[:, :, j], hp0[:, 0:18]),
             waits=[(sem_pe, v_tw)])
    v_tb = s.op("tensor", lambda e: e.matmul(hp0[:, 0:18], cwst[32:50, 0:P], ident[32:50, 32:50],
                                             start=True, stop=True),
                waits=[(sem_ve, sem_ve.n)], inc=sem_pe)
    v_cbc = s.op("vector", lambda e: e.tensor_copy(cb_c[:], hp0[:, 0:18]), waits=[(sem_pe, v_tb)])
    if "dbg_d" in d and "p" in os.environ.get("KDBGSEL", "psmbB"):
        s.op("sync", lambda e: e.dma_start(d["dbg_d"][:, 0:72], w4[:].rearrange("p a b -> p (a b)")),
             waits=[(sem_ve, v_cbc)], inc=sems["fc"], inc_by=16)
        s.op("sync", lambda e: e.dma_start(d["dbg_d"][:, 128:146], cb_c[:]), inc=sems["fc"], inc_by=16)
    s.op("gpsimd", lambda e: e.dma_start(
        Wout_sb[:], d["L1_attn_W_out"].rearrange("(c p) o -> p c o", p=P)),
        waits=gate0, inc=sem_wg, inc_by=16)
    v_wout = sem_wg.n
    v_zero = s.op("vector", lambda e: e.memset(state[:], 0.0))
    v_one = s.op("vector", lambda e: e.memset(cst[:, 2:3], 1.0))
    ar.release()

    ar.mark()
    xr = ar.alloc([P, 2, DM])
    xc = ar.alloc([P, DM])
    xsq = ar.alloc([P, DM])
    xn_t = ar.alloc([P, DM])
    st_ = ar.alloc([P, 16])
    xnb = ar.alloc([P, DM], BF16)
    xqb = ar.alloc([P, DM], BF16)
    xnf32 = ar.alloc([P, 8, 128])
    xnf8 = ar.alloc([P, 8, 128], BF16)
    xqf8 = ar.alloc([P, 8, 128], BF16)
    qb = {"absx": ar.alloc([P, DM]), "t1": ar.alloc([P, DM]), "t2": ar.alloc([P, DM]),
          "q": ar.alloc([P, DM]), "bm": ar.alloc([P, DM // 16]), "sc": ar.alloc([P, DM // 16]),
          "rs": ar.alloc([P, DM // 16])}

    x_rd = {}
    x_cons = {}
    prev_fdma = 0
    prev_qdma = 0

    def a_xload(tb):
        waits = []
        if tb <= 1:
            waits.append((sem_ve, v_cbc))
        if tb >= 2:
            waits.append((sem_ve, x_cons[tb - 2]))
        x_rd[tb] = s.op("sync", lambda e, tb=tb: e.dma_start(
            xr[:, tb % 2, :], d["x1_l1"][tb * P:(tb + 1) * P, :]),
            waits=waits, inc=sem_lx[tb % 2], inc_by=16)

    a_xload(0)
    a_xload(1)
    for tb in range(NTB):
        xin = xr[:, tb % 2, :]
        s.op("vector", lambda e, xin=xin: e.reduce_sum(st_[:, 0:1], xin, axis=AX.X),
             waits=[(sem_lx[tb % 2], x_rd[tb])])
        s.op("vector", lambda e: e.tensor_scalar(st_[:, 0:1], st_[:, 0:1], 1.0 / DM, None, ALU.mult))
        v_xc = s.op("vector", lambda e, xin=xin: e.tensor_scalar(xc[:], xin, st_[:, 0:1], None,
                                                                 ALU.subtract))
        x_cons[tb] = v_xc
        if tb + 2 < NTB:
            a_xload(tb + 2)
        s.op("scalar", lambda e: e.activation(xsq[:], xc[:], AF.Square, accum_out=st_[:, 1:2]),
             waits=[(sem_ve, v_xc)])
        s.op("scalar", lambda e: e.activation(st_[:, 1:2], st_[:, 1:2], AF.Ln, bias=cst[:, 0:1],
                                              scale=1.0 / DM))
        v_rs = s.op("scalar", lambda e: e.activation(st_[:, 1:2], st_[:, 1:2], AF.Exp, scale=-0.5))
        v_xn = s.op("vector", lambda e: e.tensor_scalar(xn_t[:], xc[:], st_[:, 1:2], None, ALU.mult),
                    waits=[(sem_act, v_rs)])
        v_xnb = s.op("vector", lambda e: e.tensor_copy(xnb[:], xn_t[:]))
        v_xq = emit_quant(s, sems, qb, xn_t[:], xqb[:], DM, src_ready=[(sem_ve, v_xn)])
        hp1b = hp1[:, 0:P].bitcast(BF16)[:, 0:P]
        hp0b = hp0[:, 128:256].bitcast(BF16)[:, 0:P]
        for c in range(8):
            v_t1 = s.op("tensor", lambda e, c=c: e.transpose(hp0[:, 0:P], xn_t[:, c * P:(c + 1) * P],
                                                             ident[:]),
                        waits=[(sem_ve, v_xnb if c == 0 else sem_ve.n)], inc=sem_pe)
            v_c1 = s.op("vector", lambda e, c=c: e.tensor_copy(xnf32[:, c, :], hp0[:, 0:P]),
                        waits=[(sem_pe, v_t1)])
            v_c1b = s.op("vector", lambda e, c=c: e.tensor_copy(xnf8[:, c, :], xnf32[:, c, :]),
                         waits=[(sem_sxf, prev_fdma)] if (c == 0 and prev_fdma) else [])
            v_t2 = s.op("tensor", lambda e, c=c, hp1b=hp1b: e.transpose(
                hp1b, xqb[:, c * P:(c + 1) * P], identb[:]),
                waits=[(sem_ve, v_xq if c == 0 else sem_ve.n)], inc=sem_pe)
            v_c2 = s.op("vector", lambda e, c=c, hp1b=hp1b: e.tensor_copy(xqf8[:, c, :], hp1b),
                        waits=[(sem_pe, v_t2)] + ([(sem_sxq, prev_qdma)] if (c == 0 and prev_qdma) else []))
        v_trd = sem_ve.n
        prev_fdma = s.op("sync", lambda e, tb=tb: e.dma_start(
            d["xnfm_d"].rearrange("(c p) t -> p c t", p=P)[:, :, tb * P:(tb + 1) * P], xnf8[:]),
            waits=[(sem_ve, v_trd)], inc=sem_sxf, inc_by=16)
        prev_qdma = s.op("sync", lambda e, tb=tb: e.dma_start(
            d["xqfm_d"].rearrange("(c p) t -> p c t", p=P)[:, :, tb * P:(tb + 1) * P], xqf8[:]),
            waits=[(sem_ve, v_trd)], inc=sem_sxq, inc_by=16)
        for c in range(8):
            def mmd(e, c=c, tb=tb):
                return e.matmul(dnA[:, 0:32], xnf32[:, c, :], Wdt[:, c, :],
                                start=(c == 0), stop=(c == 7))
            if c == 0:
                s.op("tensor", mmd, waits=[(sem_ve, v_trd), (sem_ld, v_par)])
            elif c == 7:
                v_dmm = s.op("tensor", mmd, inc=sem_pe)
            else:
                s.op("tensor", mmd)
        s.op("vector", lambda e, tb=tb: e.tensor_copy(dtraw[:, tb, :], dnA[:, 0:32]),
             waits=[(sem_pe, v_dmm)])
    v_phA_ve = sem_ve.n
    v_phA_pe = sem_pe.n
    ar.release()

    ar.mark()
    win = [ar.alloc([P, 4384], BF16) for _ in range(8)]
    xbc = ar.alloc([P, T])
    cacc = ar.alloc([P, T])
    btile = ar.alloc([P, T], BF16)
    bst = ar.alloc([P, 16, 128], BF16)
    for k in range(8):
        s.op("gpsimd", lambda e, k=k: e.dma_start(win[k][:], d["L1_attn_W_in"][k * P:(k + 1) * P, :]),
             waits=[(sem_ve, v_phA_ve), (sem_act, sem_act.n), (sem_gp, sem_gp.n),
                    (sem_lx[0], sem_lx[0].n), (sem_lx[1], sem_lx[1].n)] if k == 0 else [],
             inc=sem_wg, inc_by=16)
    v_win = sem_wg.n

    def phaseB_tile(n, kind):
        halves = []
        for half in range(2):
            ps = (dnB if half == 0 else dnC)
            for k in range(8):
                for j in range(2):
                    def mmw(e, n=n, k=k, half=half, j=j, ps=ps):
                        return e.matmul(ps[:, j * 512:(j + 1) * 512],
                                        win[k][:, n * P:(n + 1) * P],
                                        d_xnfm[:, k, half * 1024 + j * 512:half * 1024 + (j + 1) * 512],
                                        start=(k == 0), stop=(k == 7), skip_group_check=True)
                    wts = []
                    if k == 0 and j == 0:
                        wts = [(sem_wg, v_win), (sem_ld, v_xnfm_ld)]
                        wts.append((sem_ve, sem_ve.n))
                    if k == 0 and j == 0:
                        s.op("tensor", mmw, waits=wts)
                    elif k == 7 and j == 1:
                        halves.append(s.op("tensor", mmw, inc=sem_pe))
                    else:
                        s.op("tensor", mmw)
        v_mm = halves[-1]
        if kind == "z":
            v_e1 = s.op("vector", lambda e: e.tensor_copy(btile[:, 0:1024], dnB[:, 0:1024]),
                        waits=[(sem_pe, halves[0])])
            v_e2 = s.op("vector", lambda e: e.tensor_copy(btile[:, 1024:2048], dnC[:, 0:1024]),
                        waits=[(sem_pe, v_mm)])
            src = btile
            v_rdy = v_e2
        else:
            v_e1 = s.op("vector", lambda e: e.tensor_copy(xbc[:, 0:1024], dnB[:, 0:1024]),
                        waits=[(sem_pe, halves[0])])
            v_e2 = s.op("vector", lambda e: e.tensor_copy(xbc[:, 1024:2048], dnC[:, 0:1024]),
                        waits=[(sem_pe, v_mm)])
            ci = n - 16
            v_cv = s.op("vector", lambda e, ci=ci: e.tensor_scalar(
                cacc[:], xbc[:], w4[:, ci, 3:4], None, ALU.mult),
                waits=[(sem_ld, v_par), (sem_act, sem_act.n)])
            v_cva = None
            for jj in range(3):
                sh = 3 - jj
                v_cva = s.op("vector", lambda e, ci=ci, jj=jj, sh=sh: e.scalar_tensor_tensor(
                    cacc[:, sh:T], xbc[:, 0:T - sh], w4[:, ci, jj:jj + 1], cacc[:, sh:T],
                    ALU.mult, ALU.add))
            v_si = s.op("scalar", lambda e, ci=ci: e.activation(
                xbc[:], cacc[:], AF.Silu, bias=cb_c[:, ci:ci + 1]),
                waits=[(sem_ve, v_cva)])
            if kind == "bm":
                v_rdy = s.op("vector", lambda e: e.tensor_copy(BmF[:], xbc[:]),
                             waits=[(sem_act, v_si)])
                src = None
            elif kind == "cm":
                v_rdy = s.op("vector", lambda e: e.tensor_copy(CmF[:], xbc[:]),
                             waits=[(sem_act, v_si)])
                src = None
            else:
                v_rdy = s.op("vector", lambda e: e.tensor_copy(btile[:], xbc[:]),
                             waits=[(sem_act, v_si)])
                src = btile
                if n == 16 and "dbg_xh" in d and "b" in os.environ.get("KDBGSEL", "psmbB"):
                    s.op("sync", lambda e: e.dma_start(d["dbg_xh"], btile[:]),
                         waits=[(sem_ve, v_rdy)], inc=sems["fc"], inc_by=16)
        if src is not None:
            for tbk in range(16):
                v_tt = s.op("tensor", lambda e, tbk=tbk: e.transpose(
                    hp0[:, 0:P].bitcast(BF16)[:, 0:P], src[:, tbk * P:(tbk + 1) * P], identb[:]),
                    waits=[(sem_ve, v_rdy if tbk == 0 else sem_ve.n)], inc=sem_pe)
                s.op("vector", lambda e, tbk=tbk: e.tensor_copy(
                    bst[:, tbk, :], hp0[:, 0:P].bitcast(BF16)[:, 0:P]),
                    waits=[(sem_pe, v_tt)] + ([(sem_sbz, sem_sbz.n)] if tbk == 0 else []))
            dst = d_ztm if kind == "z" else d_xhtm
            col = n if kind == "z" else (n - 16)
            s.op("sync", lambda e, dst=dst, col=col: e.dma_start(
                dst.rearrange("(b p) ch -> p b ch", p=P)[:, :, col * P:(col + 1) * P], bst[:]),
                waits=[(sem_ve, sem_ve.n)], inc=sem_sbz, inc_by=16)
            if n == 16 and "dbg_bst" in d and "b" in os.environ.get("KDBGSEL", "psmbB"):
                s.op("sync", lambda e: e.dma_start(d["dbg_bst"],
                     bst[:].rearrange("p a b -> p (a b)")), inc=sems["fc"], inc_by=16)

    d_xnfm = None

    xnfm_sb = ar.alloc([P, 8, T], BF16)
    v_xnfm_ld = s.op("sync", lambda e: e.dma_start(
        xnfm_sb[:], d["xnfm_d"].rearrange("(c p) t -> p c t", p=P)),
        waits=[(sem_sxf, sem_sxf.n)], inc=sem_ld, inc_by=16)
    d_xnfm = xnfm_sb
    d_ztm = d["ztm_d"]
    d_xhtm = d["xhtm_d"]

    for n in range(16):
        phaseB_tile(n, "z")
    for n in range(16, 32):
        phaseB_tile(n, "xh")
    phaseB_tile(32, "bm")
    phaseB_tile(33, "cm")
    if "dbg_bm" in d and "B" in os.environ.get("KDBGSEL", "psmbB"):
        s.op("sync", lambda e: e.dma_start(d["dbg_bm"], BmF[:]), waits=[(sem_ve, sem_ve.n)],
             inc=sems["fc"], inc_by=16)
        s.op("sync", lambda e: e.dma_start(d["dbg_cm"], CmF[:]), inc=sems["fc"], inc_by=16)
    v_phB_ve = sem_ve.n
    v_phB_pe = sem_pe.n
    ar.release()

    ar.mark()
    xh_r = ar.alloc([P, 2, DI], BF16)
    z_r = ar.alloc([P, 2, DI], BF16)
    dacs = ar.alloc([P, NH])
    dacsT = ar.alloc([P, 128])
    dtc_sb = ar.alloc([P, NH])
    dtx = ar.alloc([P, NH])
    sp1 = ar.alloc([P, NH])
    sp2 = ar.alloc([P, NH])
    dA_sb = ar.alloc([P, NH])
    bevac = ar.alloc([P, NH])
    expdacs = ar.alloc([P, NH])
    dT1 = ar.alloc([P, 128])
    dtT = ar.alloc([P, 128])
    dd_tm = ar.alloc([P, NH])
    cdbc = ar.alloc([P, NH])
    cdrow = ar.alloc([P, NH])
    Gmask = ar.alloc([P, P])
    mex = [ar.alloc([P, P]) for _ in range(2)]
    mexp = [ar.alloc([P, P]) for _ in range(2)]
    mh = [ar.alloc([P, P], BF16) for _ in range(2)]
    yoff = ar.alloc([P, DI])
    y_sb = ar.alloc([P, DI])
    zsil = ar.alloc([P, DI])
    ysq = ar.alloc([P, DI])
    ynb = ar.alloc([P, DI], BF16)
    ynfm = ar.alloc([P, 16, 128], BF16)
    BmT = ar.alloc([P, P], BF16)
    xw_sb = ar.alloc([P, DI], BF16)
    s1b = ar.alloc([P, DM], BF16)
    st2 = ar.alloc([P, 4])
    mex_done = {}

    ch_ld = {}
    ch_cons = {}

    def c_load(c):
        waits = [(sem_sbz, sem_sbz.n), (sem_ve, v_phB_ve)] if c <= 1 else []
        if c >= 2:
            waits = [(sem_ve, ch_cons[c - 2])]
        sm = sem_cc[c % 2]
        s.op("sync", lambda e, c=c: e.dma_start(
            xh_r[:, c % 2, :], d_xhtm[c * P:(c + 1) * P, :]), waits=waits, inc=sm, inc_by=16)
        s.op("sync", lambda e, c=c: e.dma_start(
            z_r[:, c % 2, :], d_ztm[c * P:(c + 1) * P, :]), inc=sm, inc_by=16)
        ch_ld[c] = (sm, sm.n)

    c_load(0)
    c_load(1)
    v_ss1_prev = 0
    for c in range(NTB):
        xh_c = xh_r[:, c % 2, :]
        z_c = z_r[:, c % 2, :]
        xh3 = xh_c.rearrange("p (h q) -> p h q", q=HD)
        v_1 = s.op("vector", lambda e, c=c: e.tensor_tensor(dtx[:], dtraw[:, c, :], dtb_bc, ALU.add),
                   waits=[(sem_ld, v_par)] + ([(sem_ve, v_phB_ve), (sem_pe, v_phB_pe)] if c == 0 else []))
        s.op("scalar", lambda e: e.activation(sp1[:], dtx[:], AF.Abs), waits=[(sem_ve, v_1)])
        s.op("scalar", lambda e: e.activation(sp1[:], sp1[:], AF.Exp, scale=-1.0))
        v_sp = s.op("scalar", lambda e: e.activation(sp1[:], sp1[:], AF.Ln, bias=cst[:, 2:3]),
                    waits=[(sem_ve, v_one)])
        s.op("vector", lambda e: e.tensor_scalar(sp2[:], dtx[:], 0.0, None, ALU.max))
        v_dt = s.op("vector", lambda e: e.tensor_tensor(dtc_sb[:], sp1[:], sp2[:], ALU.add),
                    waits=[(sem_act, v_sp)])
        v_dA = s.op("vector", lambda e: e.tensor_tensor(dA_sb[:], dtc_sb[:], A_bc, ALU.mult))
        v_cs = s.op("tensor", lambda e: e.matmul(hp0[:, 0:NH], lt_sb[:], dA_sb[:],
                                                 start=True, stop=True),
                    waits=[(sem_ve, v_dA)], inc=sem_pe)
        v_dacs = s.op("vector", lambda e: e.tensor_copy(dacs[:], hp0[:, 0:NH]),
                      waits=[(sem_pe, v_cs)])
        v_tT = s.op("tensor", lambda e: e.transpose(hp1[0:32, 0:P], dacs[:], ident[:]),
                    waits=[(sem_ve, v_dacs)], inc=sem_pe)
        v_dacsT = s.op("vector", lambda e: e.tensor_copy(dacsT[0:32, :], hp1[0:32, 0:P]),
                       waits=[(sem_pe, v_tT)])
        v_ed = s.op("scalar", lambda e: e.activation(expdacs[:], dacs[:], AF.Exp),
                    waits=[(sem_ve, v_dacs)])
        v_ln = s.op("scalar", lambda e: e.activation(bevac[:], dtc_sb[:], AF.Ln),
                    waits=[(sem_ve, v_dt)])
        v_be = s.op("vector", lambda e: e.tensor_tensor(bevac[:], bevac[:], dacs[:], ALU.subtract),
                    waits=[(sem_act, v_ln)])
        v_d1 = s.op("vector", lambda e: e.tensor_scalar(dT1[0:32, :], dacsT[0:32, :],
                                                        dacsT[0:32, 127:128], None, ALU.subtract),
                    waits=[(sem_ve, v_dacsT)])
        v_d2 = s.op("scalar", lambda e: e.activation(dT1[0:32, :], dT1[0:32, :], AF.Exp, scale=-1.0),
                    waits=[(sem_ve, v_d1)])
        v_t3 = s.op("tensor", lambda e: e.transpose(hp1[0:32, 64:192], dtc_sb[:], ident[:]),
                    waits=[(sem_ve, v_be)], inc=sem_pe)
        v_dtT = s.op("vector", lambda e: e.tensor_copy(dtT[0:32, :], hp1[0:32, 64:192]),
                     waits=[(sem_pe, v_t3)])
        v_dd = s.op("vector", lambda e: e.tensor_tensor(dT1[0:32, :], dT1[0:32, :], dtT[0:32, :],
                                                        ALU.mult),
                    waits=[(sem_act, v_d2)])
        v_t4 = s.op("tensor", lambda e: e.transpose(hp1[:, 96:128], dT1[0:32, :], ident[0:32, 0:32]),
                    waits=[(sem_ve, v_dd)], inc=sem_pe)
        v_ddtm = s.op("vector", lambda e: e.tensor_copy(dd_tm[:], hp1[0:P, 96:128]),
                      waits=[(sem_pe, v_t4)])
        v_t5 = s.op("tensor", lambda e: e.matmul(hp1[0:1, 128:160], ident[:, 127:128], dacs[:],
                                                 start=True, stop=True), inc=sem_pe)
        v_cd1 = s.op("scalar", lambda e: e.activation(cdrow[0:1, :], hp1[0:1, 128:160], AF.Exp),
                     waits=[(sem_pe, v_t5)])
        v_t6 = s.op("tensor", lambda e: e.matmul(hp1[:, 160:192], onesf, cdrow[0:1, :],
                                                 start=True, stop=True),
                    waits=[(sem_act, v_cd1)], inc=sem_pe)
        v_cdbc = s.op("vector", lambda e: e.tensor_copy(cdbc[:], hp1[:, 160:192]),
                      waits=[(sem_pe, v_t6)])
        if c == 0 and "dbg_d" in d and "m" in os.environ.get("KDBGSEL", "psmbB"):
            s.op("sync", lambda e: e.dma_start(d["dbg_d"][:, 160:192], dacs[:]),
                 waits=[(sem_ve, v_dacs)], inc=sems["fc"], inc_by=16)
            s.op("sync", lambda e: e.dma_start(d["dbg_d"][:, 192:224], dd_tm[:]),
                 waits=[(sem_ve, v_ddtm)], inc=sems["fc"], inc_by=16)
            s.op("sync", lambda e: e.dma_start(d["dbg_d"][:, 224:256], cdbc[:]),
                 waits=[(sem_ve, v_cdbc)], inc=sems["fc"], inc_by=16)
            s.op("sync", lambda e: e.dma_start(d["dbg_d"][:, 96:128], dtc_sb[:]),
                 waits=[(sem_ve, v_dt)], inc=sems["fc"], inc_by=16)
        v_g = s.op("tensor", lambda e, c=c: e.matmul(dnC[:, 0:P], BmF[:, c * P:(c + 1) * P],
                                                     CmF[:, c * P:(c + 1) * P],
                                                     start=True, stop=True),
                   waits=[(sem_ve, sem_ve.n)], inc=sem_pe)
        v_gm = s.op("vector", lambda e: e.tensor_tensor(Gmask[:], dnC[:, 0:P], lt_sb[:], ALU.mult),
                    waits=[(sem_pe, v_g)])
        v_sbf = s.op("vector", lambda e: e.tensor_copy(state_bf[:], state[:]),
                     waits=[(sem_ve, v_zero)] if c == 0 else [])
        if c == 1 and "dbg_st" in d and "s" in os.environ.get("KDBGSEL", "psmbB"):
            s.op("sync", lambda e: e.dma_start(d["dbg_st"], state_bf[:]),
                 waits=[(sem_ve, v_sbf)], inc=sems["fc"], inc_by=16)
        offmm = []
        for jj in range(4):
            ps = (dnA if jj < 2 else dnB)
            def mmo(e, jj=jj, ps=ps, c=c):
                return e.matmul(ps[:, (jj % 2) * 512:(jj % 2 + 1) * 512],
                                CmF[:, c * P:(c + 1) * P],
                                state_bf[:, (jj) * 512:(jj + 1) * 512],
                                start=True, stop=True, skip_group_check=True)
            wts = [(sem_ve, v_sbf)]
            if c > 0:
                wts.append((sem_ve, sem_ve.n))
            offmm.append(s.op("tensor", mmo, waits=wts if jj == 0 else [],
                              inc=sem_pe if jj == 3 else None))
        v_off = offmm[-1]
        ed3a = expdacs[:, 0:16].unsqueeze(-1).broadcast_to((P, 16, HD))
        ed3b = expdacs[:, 16:32].unsqueeze(-1).broadcast_to((P, 16, HD))
        v_ev1 = s.op("vector", lambda e: e.tensor_tensor(
            yoff[:, 0:1024].rearrange("p (h q) -> p h q", q=HD), dnA[:, 0:1024].rearrange("p (h q) -> p h q", q=HD),
            ed3a, ALU.mult), waits=[(sem_pe, v_off), (sem_act, sem_act.n)])
        v_ev2 = s.op("vector", lambda e: e.tensor_tensor(
            yoff[:, 1024:2048].rearrange("p (h q) -> p h q", q=HD), dnB[:, 0:1024].rearrange("p (h q) -> p h q", q=HD),
            ed3b, ALU.mult))
        ydmm = []
        for h in range(NH):
            pr = (hp0[:, 256:384] if h % 2 == 0 else hp1[:, 256:384])
            v_bc = s.op("tensor", lambda e, h=h, pr=pr: e.matmul(
                pr[:, 0:P], eh_sb[:, h * P:(h + 1) * P], dacsT[0:32, :],
                start=True, stop=True),
                waits=[(sem_ve, sem_ve.n)] if h < 2 else [(sem_ve, mex_done[h - 2])],
                inc=sem_pe)
            v_mx = s.op("vector", lambda e, h=h, pr=pr: e.tensor_scalar(
                mex[h % 2][:], pr[:, 0:P], bevac[:, h:h + 1], 80.0, ALU.add, ALU.min),
                waits=[(sem_pe, v_bc)])
            mex_done[h] = v_mx
            v_me = s.op("scalar", lambda e, h=h: e.activation(mexp[h % 2][:], mex[h % 2][:], AF.Exp),
                        waits=[(sem_ve, v_mx)])
            v_mh = s.op("vector", lambda e, h=h: e.tensor_tensor(mh[h % 2][:], mexp[h % 2][:],
                                                                 Gmask[:], ALU.mult),
                        waits=[(sem_act, v_me), (sem_ve, v_gm)])
            ps = (dnA if h < 16 else dnB)
            def mmy(e, h=h, ps=ps, xh3=xh3):
                return e.matmul(ps[:, (h % 16) // 8 * 512 + (h % 8) * 64:
                                   (h % 16) // 8 * 512 + (h % 8) * 64 + 64],
                                mh[h % 2][:], xh3[:, h, :],
                                start=(h % 8 == 0), stop=(h % 8 == 7), skip_group_check=True)
            wts = [(sem_ve, v_mh)]
            if h % 8 == 0:
                wts.append((sem_ve, v_ev2))
                wts.append(ch_ld[c])
            ydmm.append(s.op("tensor", mmy, waits=wts, inc=sem_pe if h == NH - 1 else None))
        v_yd = ydmm[-1]
        v_y1 = s.op("vector", lambda e: e.tensor_tensor(y_sb[:, 0:1024], dnA[:, 0:1024],
                                                        yoff[:, 0:1024], ALU.add),
                    waits=[(sem_pe, v_yd)])
        v_y2 = s.op("vector", lambda e: e.tensor_tensor(y_sb[:, 1024:2048], dnB[:, 0:1024],
                                                        yoff[:, 1024:2048], ALU.add))
        d3 = D_bc.unsqueeze(-1).broadcast_to((P, NH, HD))
        v_y3 = s.op("vector", lambda e, xh3=xh3: e.tensor_tensor(
            yoff[:].rearrange("p (h q) -> p h q", q=HD), xh3, d3, ALU.mult))
        v_y4 = s.op("vector", lambda e: e.tensor_tensor(y_sb[:], y_sb[:], yoff[:], ALU.add))
        v_zs = s.op("scalar", lambda e, z_c=z_c: e.activation(zsil[:], z_c, AF.Silu),
                    waits=[ch_ld[c]])
        v_y5 = s.op("vector", lambda e: e.tensor_tensor(y_sb[:], y_sb[:], zsil[:], ALU.mult),
                    waits=[(sem_act, v_zs)])
        s.op("scalar", lambda e: e.activation(ysq[:], y_sb[:], AF.Square, accum_out=st2[:, 0:1]),
             waits=[(sem_ve, v_y5)])
        s.op("scalar", lambda e: e.activation(st2[:, 0:1], st2[:, 0:1], AF.Ln, bias=cst[:, 0:1],
                                              scale=1.0 / DI))
        v_rst = s.op("scalar", lambda e: e.activation(st2[:, 0:1], st2[:, 0:1], AF.Exp, scale=-0.5))
        v_yn = s.op("vector", lambda e: e.scalar_tensor_tensor(ynb[:], y_sb[:], st2[:, 0:1],
                                                               normw_bc, ALU.mult, ALU.mult),
                    waits=[(sem_act, v_rst)])
        for k in range(16):
            v_tt = s.op("tensor", lambda e, k=k: e.transpose(
                hp0[:, 384:512].bitcast(BF16)[:, 0:P], ynb[:, k * P:(k + 1) * P], identb[:]),
                waits=[(sem_ve, v_yn if k == 0 else sem_ve.n)], inc=sem_pe)
            s.op("vector", lambda e, k=k: e.tensor_copy(ynfm[:, k, :],
                                                        hp0[:, 384:512].bitcast(BF16)[:, 0:P]),
                 waits=[(sem_pe, v_tt)])
        v_tr_done = sem_ve.n
        womm = None
        for k in range(16):
            for j in range(2):
                def mmw2(e, k=k, j=j):
                    return e.matmul(dnC[:, j * 512:(j + 1) * 512], ynfm[:, k, :],
                                    Wout_sb[:, k, j * 512:(j + 1) * 512],
                                    start=(k == 0), stop=(k == 15), skip_group_check=True)
                wts = []
                if k == 0 and j == 0:
                    wts = [(sem_ve, v_tr_done), (sem_wg, v_wout)]
                if k == 15 and j == 1:
                    womm = s.op("tensor", mmw2, waits=wts, inc=sem_pe)
                else:
                    s.op("tensor", mmw2, waits=wts)
        v_s1 = s.op("vector", lambda e: e.tensor_copy(s1b[:], dnC[:, 0:1024]),
                    waits=[(sem_pe, womm), (sem_ss1, v_ss1_prev)])
        v_ss1_prev = s.op("sync", lambda e, c=c: e.dma_start(
            d["s1_d"][c * P:(c + 1) * P, :], s1b[:]),
            waits=[(sem_ve, v_s1)], inc=sem_ss1, inc_by=16)
        v_bt = s.op("tensor", lambda e, c=c: e.transpose(
            hp0[:, 256:384].bitcast(BF16)[:, 0:P], BmF[:, c * P:(c + 1) * P], identb[:]),
            waits=[(sem_ve, sem_ve.n)], inc=sem_pe)
        v_bmt = s.op("vector", lambda e: e.tensor_copy(BmT[:], hp0[:, 256:384].bitcast(BF16)[:, 0:P]),
                     waits=[(sem_pe, v_bt)])
        dd3 = dd_tm.unsqueeze(-1).broadcast_to((P, NH, HD))
        v_xw = s.op("vector", lambda e, xh3=xh3: e.tensor_tensor(
            xw_sb[:].rearrange("p (h q) -> p h q", q=HD), xh3, dd3, ALU.mult),
            waits=[(sem_ve, v_ddtm)])
        smm = []
        for jj in range(4):
            ps = (dnA if jj < 2 else dnB)
            def mms2(e, jj=jj, ps=ps):
                return e.matmul(ps[:, (jj % 2) * 512:(jj % 2 + 1) * 512], BmT[:],
                                xw_sb[:, jj * 512:(jj + 1) * 512],
                                start=True, stop=True, skip_group_check=True)
            wts = []
            if jj == 0:
                wts = [(sem_ve, max(v_xw, v_bmt, v_y4))]
            smm.append(s.op("tensor", mms2, waits=wts, inc=sem_pe if jj == 3 else None))
        v_smm = smm[-1]
        cd3 = cdbc.unsqueeze(-1).broadcast_to((P, NH, HD))
        v_su1 = s.op("vector", lambda e: e.tensor_tensor(
            state[:].rearrange("p (h q) -> p h q", q=HD),
            state[:].rearrange("p (h q) -> p h q", q=HD), cd3, ALU.mult),
            waits=[(sem_ve, v_cdbc), (sem_pe, v_smm)])
        s.op("vector", lambda e: e.tensor_tensor(state[:, 0:1024], state[:, 0:1024],
                                                 dnA[:, 0:1024], ALU.add))
        v_sup = s.op("vector", lambda e: e.tensor_tensor(state[:, 1024:2048], state[:, 1024:2048],
                                                         dnB[:, 0:1024], ALU.add))
        ch_cons[c] = v_sup
        if c + 2 < NTB:
            c_load(c + 2)
    v_phC_ve = sem_ve.n
    v_phC_pe = sem_pe.n
    ar.release()
    ar.release()

    ar.mark()
    wgT = ar.alloc([P, 8, 2816], BF16)
    wuT = ar.alloc([P, 8, 2816], BF16)
    wdT = ar.alloc([P, 22, 1024], BF16)
    xq8 = ar.alloc([P, 2, 8, 128], BF16)
    sg = ar.alloc([P, 2, 512])
    htm = ar.alloc([P, 2816])
    hq_tm = ar.alloc([P, 2, 2816], BF16)
    hqf = ar.alloc([P, 22, 128], BF16)
    dq_absx = ar.alloc([P, 1408])
    dq_t1 = ar.alloc([P, 1408], BF16)
    dq_q = ar.alloc([P, 1408])
    dq_bm = ar.alloc([P, 88])
    dq_sc = ar.alloc([P, 88])
    dq_rs = ar.alloc([P, 88])
    x1r = ar.alloc([P, 2, DM])
    s1r = ar.alloc([P, 2, DM], BF16)
    gsb = ar.alloc([P, DM])
    xo = ar.alloc([P, DM])

    wld = {}

    def d_wload(tt):
        waits = [(sem_wq, sem_wq.n), (sem_ve, v_phC_ve), (sem_pe, v_phC_pe),
                 (sem_act, sem_act.n), (sem_ss1, v_ss1_prev)]
        s.op("sync", lambda e: e.dma_start(
            wgT[:], d["wgt_d"].rearrange("(c p) n -> p c n", p=P)), waits=waits, inc=sem_wq, inc_by=16)
        s.op("sync", lambda e: e.dma_start(
            wuT[:], d["wut_d"].rearrange("(c p) n -> p c n", p=P)), inc=sem_wq, inc_by=16)
        wld["gu"] = sem_wq.n
        s.op("sync", lambda e: e.dma_start(
            wdT[:], d["wdt_d"].rearrange("(c p) n -> p c n", p=P)), inc=sem_wq, inc_by=16)
        wld["d"] = sem_wq.n

    d_pe_after = {}
    d_wload(0)
    xq_ld = {}
    x1_ld = {}

    xs_ld = {}

    def d_xqload_xq(tbk):
        if tbk <= 1:
            waits = [(sem_sxq, sem_sxq.n), (sem_ss1, sem_ss1.n)]
        else:
            waits = [(sem_pe, fwd_pe[tbk - 2])]
        s.op("sync", lambda e, tbk=tbk: e.dma_start(
            xq8[:, tbk % 2, :, :],
            d["xqfm_d"].rearrange("(c p) t -> p c t", p=P)[:, :, tbk * P:(tbk + 1) * P]),
            waits=waits, inc=sem_xq, inc_by=16)
        xq_ld[tbk] = sem_xq.n

    def d_xqload_xs(tbk):
        waits = [] if tbk <= 1 else [(sem_ve, d_cons[tbk - 2])]
        s.op("sync", lambda e, tbk=tbk: e.dma_start(
            x1r[:, tbk % 2, :], d["x1_l1"][tbk * P:(tbk + 1) * P, :]),
            waits=waits, inc=sem_xq, inc_by=16)
        s.op("sync", lambda e, tbk=tbk: e.dma_start(
            s1r[:, tbk % 2, :], d["s1_d"][tbk * P:(tbk + 1) * P, :]), inc=sem_xq, inc_by=16)
        xs_ld[tbk] = sem_xq.n

    def d_xqload(tbk):
        d_xqload_xq(tbk)
        d_xqload_xs(tbk)

    d_tbk_pe = {}
    d_cons = {}
    d_xqload(0)
    d_xqload(1)
    prev_x2_st = [0]
    CH = [(0, 512), (512, 512), (1024, 512), (1536, 512), (2048, 512), (2304, 512)]
    CH = [(0, 512), (512, 512), (1024, 512), (1536, 512), (2048, 512), (2560, 256)]
    evac_done = {}
    quant_done = {}
    fwd_pe = {}

    def d_fwd(tbk):
        buf = tbk % 2
        for ci, (c0, cw) in enumerate(CH):
            gps = dnA[:, (ci % 2) * 512:(ci % 2) * 512 + cw]
            ups = dnB[:, (ci % 2) * 512:(ci % 2) * 512 + cw]
            for (mat, ps) in ((wgT, gps), (wuT, ups)):
                for k in range(8):
                    def mmg(e, mat=mat, ps=ps, k=k, c0=c0, cw=cw, buf=buf):
                        return e.matmul(ps, xq8[:, buf, k, :], mat[:, k, c0:c0 + cw],
                                        start=(k == 0), stop=(k == 7), skip_group_check=True)
                    wts = []
                    if k == 0:
                        if ci >= 2:
                            wts.append((sem_ve, evac_done[(tbk, ci - 2)]))
                        elif tbk > 0:
                            wts.append((sem_ve, evac_done[(tbk - 1, len(CH) - 2 + ci)]))
                        if ci == 0 and mat is wgT:
                            wts += [(sem_xq, xq_ld[tbk]), (sem_wq, wld["gu"])]
                    if k == 7 and mat is wuT:
                        v_mm = s.op("tensor", mmg, waits=wts, inc=sem_pe)
                    else:
                        s.op("tensor", mmg, waits=wts)
            v_sg = s.op("scalar", lambda e, ci=ci, gps=gps, cw=cw: e.activation(
                sg[:, ci % 2, 0:cw], gps, AF.Silu), waits=[(sem_pe, v_mm)])
            evac_done[(tbk, ci)] = s.op("vector", lambda e, ci=ci, ups=ups, c0=c0, cw=cw, buf=buf:
                e.tensor_tensor(htm[:, c0:c0 + cw], sg[:, ci % 2, 0:cw], ups, ALU.mult),
                waits=[(sem_act, v_sg)])
        fwd_pe[tbk] = sem_pe.n
        d_tbk_pe[tbk] = sem_pe.n
        for piece in range(2):
            p0 = piece * 1408
            dqb = {"absx": dq_absx, "t1": dq_t1,
                   "t2": hq_tm[:, buf, p0:p0 + 1408], "q": dq_q,
                   "bm": dq_bm, "sc": dq_sc, "rs": dq_rs}
            v_hq = emit_quant(s, sems, dqb, htm[:, p0:p0 + 1408],
                              hq_tm[:, buf, p0:p0 + 1408], 1408,
                              src_ready=[(sem_ve, sem_ve.n)])
        quant_done[tbk] = v_hq

    def d_bwd(tbk):
        buf = tbk % 2
        for n in range(22):
            v_tq = s.op("tensor", lambda e, n=n, buf=buf: e.transpose(
                hp0[:, 128:256].bitcast(BF16)[:, 0:P], hq_tm[:, buf, n * P:(n + 1) * P], identb[:]),
                waits=[(sem_ve, quant_done[tbk] if n == 0 else sem_ve.n)], inc=sem_pe)
            s.op("vector", lambda e, n=n: e.tensor_copy(hqf[:, n, :],
                                                        hp0[:, 128:256].bitcast(BF16)[:, 0:P]),
                 waits=[(sem_pe, v_tq)])
        v_hqf = sem_ve.n
        dmm = None
        for n in range(22):
            for j in range(2):
                def mmd2(e, n=n, j=j):
                    return e.matmul(dnC[:, j * 512:(j + 1) * 512], hqf[:, n, :],
                                    wdT[:, n, j * 512:(j + 1) * 512],
                                    start=(n == 0), stop=(n == 21), skip_group_check=True)
                wts = []
                if n == 0 and j == 0:
                    wts = [(sem_ve, v_hqf), (sem_act, sem_act.n), (sem_wq, wld["d"])]
                if n == 21 and j == 1:
                    dmm = s.op("tensor", mmd2, waits=wts, inc=sem_pe)
                else:
                    s.op("tensor", mmd2, waits=wts)
        v_gs = s.op("scalar", lambda e: e.activation(gsb[:], dnC[:, 0:1024], AF.Silu),
                    waits=[(sem_pe, dmm)])
        v_t3 = s.op("vector", lambda e, buf=buf: e.tensor_tensor(gsb[:], s1r[:, buf, :], gsb[:],
                                                                 ALU.mult),
                    waits=[(sem_act, v_gs), (sem_xq, xs_ld[tbk])])
        v_xo = s.op("vector", lambda e, buf=buf: e.tensor_tensor(xo[:], x1r[:, buf, :], gsb[:],
                                                                 ALU.add),
                    waits=[(sem_st, prev_x2_st[0])] if prev_x2_st[0] else [])
        d_cons[tbk] = v_xo
        prev_x2_st[0] = s.op("sync", lambda e, tbk=tbk: e.dma_start(
            d["x2"][tbk * P:(tbk + 1) * P, :], xo[:]),
            waits=[(sem_ve, v_xo)], inc=sem_st, inc_by=16)

    d_fwd(0)
    for tbk in range(1, NTB):
        if tbk + 1 < NTB:
            d_xqload_xq(tbk + 1)
        d_fwd(tbk)
        d_bwd(tbk - 1)
        if tbk + 1 < NTB:
            d_xqload_xs(tbk + 1)
    d_bwd(NTB - 1)
    ar.release()


def build_program(which="l0"):
    nc = bass.Bass("TRN2", target_bir_lowering=False, detect_race_conditions=False)
    d = {}
    d["u"] = nc.dram_tensor("u", [T, DM], F32, kind="ExternalInput").ap()
    l0_specs = [
            ("L0_attn_dt", [512]), ("L0_attn_A", [512]), ("L0_attn_Bmat", [1024, 512]),
            ("L0_attn_C", [512, 1024]), ("L0_attn_D", [1024]), ("L0_attn_norm_w", [1024]),
            ("L0_ffn_gate_w", [1024, 8]), ("L0_ffn_w1", [8, 1024, 4096]), ("L0_ffn_b1", [8, 4096]),
            ("L0_ffn_w2", [8, 4096, 1024]), ("L0_ffn_b2", [8, 1024]),
            ("L0_ffn_gamma", [1024]), ("L0_ffn_beta", [1024])]
    l1_specs = [
            ("L1_attn_W_in", [1024, 4384]), ("L1_attn_conv_w", [2304, 4]),
            ("L1_attn_conv_b", [2304]), ("L1_attn_dt_bias", [32]), ("L1_attn_A_log", [32]),
            ("L1_attn_D", [32]), ("L1_attn_norm_w", [2048]), ("L1_attn_W_out", [2048, 1024]),
            ("L1_ffn_w_gate", [2816, 1024]), ("L1_ffn_w_up", [2816, 1024]),
            ("L1_ffn_w_down", [1024, 2816])]
    specs = {"l0": l0_specs, "l1": l1_specs, "both": l0_specs + l1_specs}[which]
    for name, shape in specs:
        d[name] = nc.dram_tensor(name, shape, F32, kind="ExternalInput").ap()
    d["c_ident"] = nc.dram_tensor("c_ident", [P, P], F32, kind="ExternalInput").ap()
    d["c_onesf"] = nc.dram_tensor("c_onesf", [1, P], F32, kind="ExternalInput").ap()
    d["c_lt"] = nc.dram_tensor("c_lt", [P, P], F32, kind="ExternalInput").ap()
    d["c_eh"] = nc.dram_tensor("c_eh", [32, 32 * P], F32, kind="ExternalInput").ap()
    d["c_identb"] = nc.dram_tensor("c_identb", [P, P], BF16, kind="ExternalInput").ap()
    d["c_onesb"] = nc.dram_tensor("c_onesb", [1, P], BF16, kind="ExternalInput").ap()
    _ik = "ExternalOutput" if os.environ.get("KDBG") else "Internal"
    d["xn2fm_d"] = nc.dram_tensor("xn2fm_d", [DM, T], BF16, kind=_ik).ap()
    d["xn_d"] = nc.dram_tensor("xn_d", [T, DM], BF16, kind=_ik).ap()
    d["xscan_d"] = nc.dram_tensor("xscan_d", [512, T], BF16, kind=_ik).ap()
    if which == "l0":
        d["x1"] = nc.dram_tensor("x1", [T, DM], F32, kind="ExternalOutput").ap()
    elif which == "both":
        d["x1"] = nc.dram_tensor("x1", [T, DM], F32, kind="Internal").ap()
        d["x1_l1"] = d["x1"]
        d["xnfm_d"] = nc.dram_tensor("xnfm_d", [DM, T], BF16, kind="Internal").ap()
        d["xqfm_d"] = nc.dram_tensor("xqfm_d", [DM, T], BF16, kind="Internal").ap()
        d["ztm_d"] = nc.dram_tensor("ztm_d", [T, 2048], BF16, kind="Internal").ap()
        d["xhtm_d"] = nc.dram_tensor("xhtm_d", [T, 2048], BF16, kind="Internal").ap()
        d["s1_d"] = nc.dram_tensor("s1_d", [T, DM], BF16, kind="Internal").ap()
        d["wgt_d"] = nc.dram_tensor("wgt_d", [1024, 2816], BF16, kind="Internal").ap()
        d["wut_d"] = nc.dram_tensor("wut_d", [1024, 2816], BF16, kind="Internal").ap()
        d["wdt_d"] = nc.dram_tensor("wdt_d", [2816, 1024], BF16, kind="Internal").ap()
        d["x2"] = nc.dram_tensor("x2", [T, DM], F32, kind="ExternalOutput").ap()
    else:
        d["x1_l1"] = nc.dram_tensor("x1_l1", [T, DM], F32, kind="ExternalInput").ap()
        d["xnfm_d"] = nc.dram_tensor("xnfm_d", [DM, T], BF16, kind="Internal").ap()
        d["xqfm_d"] = nc.dram_tensor("xqfm_d", [DM, T], BF16, kind="Internal").ap()
        d["ztm_d"] = nc.dram_tensor("ztm_d", [T, 2048], BF16, kind="Internal").ap()
        d["xhtm_d"] = nc.dram_tensor("xhtm_d", [T, 2048], BF16, kind="Internal").ap()
        d["s1_d"] = nc.dram_tensor("s1_d", [T, DM], BF16, kind="Internal").ap()
        d["wgt_d"] = nc.dram_tensor("wgt_d", [1024, 2816], BF16, kind="Internal").ap()
        d["wut_d"] = nc.dram_tensor("wut_d", [1024, 2816], BF16, kind="Internal").ap()
        d["wdt_d"] = nc.dram_tensor("wdt_d", [2816, 1024], BF16, kind="Internal").ap()
        d["x2"] = nc.dram_tensor("x2", [T, DM], F32, kind="ExternalOutput").ap()
        if os.environ.get("KDBG2"):
            d["dbg_d"] = nc.dram_tensor("dbg_d", [P, 256], F32, kind="ExternalOutput").ap()
            d["dbg_bm"] = nc.dram_tensor("dbg_bm", [P, T], BF16, kind="ExternalOutput").ap()
            d["dbg_cm"] = nc.dram_tensor("dbg_cm", [P, T], BF16, kind="ExternalOutput").ap()
            d["dbg_xh"] = nc.dram_tensor("dbg_xh", [P, T], BF16, kind="ExternalOutput").ap()
            d["dbg_bst"] = nc.dram_tensor("dbg_bst", [P, T], BF16, kind="ExternalOutput").ap()
            d["dbg_st"] = nc.dram_tensor("dbg_st", [P, T], BF16, kind="ExternalOutput").ap()

    from contextlib import ExitStack
    with ExitStack() as stack:
        en = stack.enter_context
        ident_t = en(nc.sbuf_tensor("ident_sb", [P, P], F32))
        identb_t = en(nc.sbuf_tensor("identb_sb", [P, P], BF16))
        onesb_t = en(nc.sbuf_tensor("onesb_sb", [1, P], BF16))
        if which == "l0":
            cw_t = en(nc.sbuf_tensor("cw_sb", [P, NTB, 8], F32))
            accb_t = en(nc.sbuf_tensor("accb", [P, NTB, DM], BF16))
            b1c_t = en(nc.sbuf_tensor("b1c", [P, 8, 32], F32))
            w2b_t = en(nc.sbuf_tensor("w2b", [P, 32, 1024], BF16))
            b2r_t = en(nc.sbuf_tensor("b2r", [1, 2, 1024], BF16))
            xnq_t = en(nc.sbuf_tensor("xnq", [P, 8, 512], BF16))
            AR_WORDS = 24800
        else:
            AR_WORDS = 52800
        ar_t = en(nc.sbuf_tensor("arena", [P, AR_WORDS], F32))
        cst_t = en(nc.sbuf_tensor("cst", [P, 4], F32))
        dnA = en(nc.psum_tensor("dnA", [P, 1024], F32))
        dnB = en(nc.psum_tensor("dnB", [P, 1024], F32))
        dnC = en(nc.psum_tensor("dnC", [P, 1024], F32))
        hp0 = en(nc.psum_tensor("hp0", [P, 512], F32))
        hp1 = en(nc.psum_tensor("hp1", [P, 512], F32))
        dma_sems = ("ld", "lxa", "lxb", "sxn", "sx2", "sscan", "xq", "w1s", "w2s",
                    "fc", "fla", "flb", "st", "wqh", "wgh")
        sems = {k: Sem(en(nc.semaphore(k)), dma=True) for k in dma_sems}
        sems.update({k: Sem(en(nc.semaphore(k))) for k in ("pe", "act", "ve", "gp")})
        block = en(nc.Block())

        s = Sched(self_serial={"vector": sems["ve"], "scalar": sems["act"],
                               "gpsimd": sems["gp"]})
        ctx = dict(nc=nc, s=s, dram=d, ar=Arena(ar_t[:], AR_WORDS),
                   ident=ident_t[:], identb=identb_t[:], onesb=onesb_t[:],
                   dnA=dnA[:], dnB=dnB[:], dnC=dnC[:], hp0=hp0[:], hp1=hp1[:],
                   sems=sems, cst=cst_t[:])
        if which == "l0":
            ctx.update(cw_sb=cw_t[:], accb=accb_t[:], b1c=b1c_t[:],
                       w2b=w2b_t[:], b2r=b2r_t[:], xnq=xnq_t[:])
        v_c0 = s.op("vector", lambda e: e.memset(cst_t[:, 0:1], 1e-5), inc=sems["ve"])
        v_c1 = s.op("vector", lambda e: e.memset(cst_t[:, 1:2], 1e-6), inc=sems["ve"])
        s.wait("scalar", sems["ve"], v_c1)
        s.op("sync", lambda e: e.dma_start(ctx["ident"], d["c_ident"]), inc=sems["ld"], inc_by=16)
        s.op("sync", lambda e: e.dma_start(ctx["identb"], d["c_identb"]), inc=sems["ld"], inc_by=16)
        s.op("sync", lambda e: e.dma_start(ctx["onesb"], d["c_onesb"]), inc=sems["ld"], inc_by=16)
        s.wait("tensor", sems["ld"], 48)

        if which == "l0":
            build_l0(ctx)
        elif which == "l1":
            build_l1(ctx)
        else:
            ctx["ar"].mark()
            build_l0(ctx)
            ctx["ar"].release()
            ctx["l1_gate"] = [(sems["ve"], sems["ve"].n), (sems["pe"], sems["pe"].n),
                              (sems["st"], sems["st"].n)]
            if sems["wqh"].n:
                ctx["l1_gate"].append((sems["wqh"], sems["wqh"].n))
            if sems["wgh"].n:
                ctx["l1_gate"].append((sems["wgh"], sems["wgh"].n))
            build_l1(ctx)
        for k in dma_sems:
            if sems[k].n:
                s.wait("sync", sems[k], sems[k].n)
        s.build(block)
    return nc


def kernel(**inputs):
    consts = _consts()
    wdict = {k: np.ascontiguousarray(v) for k, v in inputs.items() if k != "L0_attn_u"}
    nc = build_program("both")
    in_maps = []
    for b in range(8):
        m = {"u": np.ascontiguousarray(inputs["L0_attn_u"][b])}
        m.update(wdict)
        m.update(consts)
        in_maps.append(m)
    res = run_bass_kernel_spmd(nc, in_maps, core_ids=list(range(8)))
    return np.stack([np.asarray(r["x2"], dtype=np.float32) for r in res.results], axis=0)


# version 07
# speedup vs baseline: 1.1567x; 1.0151x over previous
import os
import numpy as np
import ml_dtypes
import concourse.bass as bass
import concourse.mybir as mybir
from concourse.bass_utils import run_bass_kernel_spmd

F32 = mybir.dt.float32
BF16 = mybir.dt.bfloat16
AF = mybir.ActivationFunctionType
ALU = mybir.AluOpType
AX = mybir.AxisListType

P = 128
T = 2048
DM = 1024
NTB = T // P
ENGINES = ("sync", "tensor", "vector", "scalar", "gpsimd")


class Sem:

    def __init__(self, handle, dma=False):
        self.h = handle
        self.n = 0
        self.dma = dma
        self.events = []

    def inc(self, k=1, gated=False):
        self.n += k
        if self.dma:
            self.events.append((self.n, gated))
        return self.n

    def resolve(self, v):
        if v <= 0:
            return 0
        if not self.dma:
            return v
        idx = None
        for i, (val, _) in enumerate(self.events):
            if val >= v:
                idx = i
                break
        assert idx is not None, (v, self.events[-5:])
        j = idx + 1
        while j < len(self.events) and not self.events[j][1]:
            j += 1
        return self.events[j - 1][0]


class Sched:

    def __init__(self, self_serial=None):
        self.q = {e: [] for e in ENGINES}
        self.self_serial = self_serial or {}

    def op(self, eng, fn, waits=(), inc=None, inc_by=1):
        ss = self.self_serial.get(eng)
        if ss is not None and inc is not None and inc is not ss:
            ss = None
        if ss is not None:
            pre = ss.n
            planned = ss.inc(1)
            def run(e, fn=fn, waits=tuple(waits), ss=ss, pre=pre):
                for sm, v in waits:
                    rv = sm.resolve(v)
                    if rv > 0:
                        e.wait_ge(sm.h, rv)
                ins = fn(e)
                if pre > 0:
                    ins._wait_ge(ss.h, pre)
                ins.then_inc(ss.h, 1)
            self.q[eng].append(run)
            return planned
        planned = inc.inc(inc_by, gated=bool(waits)) if inc is not None else None

        def run(e, fn=fn, waits=tuple(waits), inc=inc, inc_by=inc_by):
            first = True
            ws = []
            for sm, v in waits:
                rv = sm.resolve(v)
                if rv > 0:
                    ws.append((sm, rv))
            for sm, rv in ws[1:]:
                e.wait_ge(sm.h, rv)
            ins = fn(e)
            if ws:
                ins._wait_ge(ws[0][0].h, ws[0][1])
            if inc is not None:
                ins.then_inc(inc.h, inc_by)
        self.q[eng].append(run)
        return planned

    def wait(self, eng, sem, val):
        self.q[eng].append(lambda e, sm=sem, v=val: e.wait_ge(sm.h, sm.resolve(v)))

    def build(self, block):
        for eng in ENGINES:
            qe = self.q[eng]
            if not qe:
                continue

            def body(engobj, qe=qe):
                for fn in qe:
                    fn(engobj)
            getattr(block, eng)(body)


class Arena:
    def __init__(self, tensor, words):
        self.t = tensor
        self.words = words
        self.top = 0
        self.marks = []
        self.peak = 0

    def _take(self, w):
        w = (w + 15) & ~15
        assert self.top + w <= self.words, f"arena overflow {self.top}+{w}>{self.words}"
        off = self.top
        self.top += w
        self.peak = max(self.peak, self.top)
        return off, w

    def alloc(self, shape, dtype=F32):
        assert shape[0] == P
        elems = 1
        for x in shape[1:]:
            elems *= x
        off, _ = self._take(elems if dtype == F32 else (elems + 1) // 2)
        ap = self.t[:, off:off + (elems if dtype == F32 else (elems + 1) // 2)]
        if dtype != F32:
            ap = ap.bitcast(dtype)
        if len(shape) == 2:
            return ap[:, 0:shape[1]]
        if len(shape) == 3:
            return ap[:, 0:shape[1] * shape[2]].rearrange("p (a b) -> p a b", b=shape[2])
        assert len(shape) == 4
        n = shape[1] * shape[2] * shape[3]
        return ap[:, 0:n].rearrange("p (a b c) -> p a b c", b=shape[2], c=shape[3])

    def row(self, n, dtype=F32):
        off, _ = self._take(n if dtype == F32 else (n + 1) // 2)
        ap = self.t[0:1, off:off + (n if dtype == F32 else (n + 1) // 2)]
        if dtype != F32:
            ap = ap.bitcast(dtype)
        return ap[:, 0:n]

    def mark(self):
        self.marks.append(self.top)

    def release(self):
        self.top = self.marks.pop()


def _consts():
    lt = np.triu(np.ones((P, P), dtype=np.float32))
    eh = np.zeros((32, 32 * P), dtype=np.float32)
    for h in range(32):
        eh[h, h * P:(h + 1) * P] = 1.0
    return {"c_ident": np.eye(P, dtype=np.float32),
            "c_identb": np.eye(P, dtype=ml_dtypes.bfloat16),
            "c_onesb": np.ones((1, P), dtype=ml_dtypes.bfloat16),
            "c_onesf": np.ones((1, P), dtype=np.float32),
            "c_lt": lt, "c_eh": eh}


def build_l0(ctx):
    s = ctx["s"]
    ar = ctx["ar"]
    d = ctx["dram"]
    sems = ctx["sems"]
    sem_ld = sems["ld"]; sem_pe = sems["pe"]; sem_act = sems["act"]
    sem_ve = sems["ve"]; sem_st = sems["st"]
    sem_lx = (sems["lxa"], sems["lxb"])
    sem_sxn = sems["sxn"]; sem_sx2 = sems["sx2"]; sem_sscan = sems["sscan"]
    sem_xq = sems["xq"]; sem_w1s = sems["w1s"]; sem_w2s = sems["w2s"]
    sem_fc = sems["fc"]; sem_fl = (sems["fla"], sems["flb"])
    ident = ctx["ident"]; identb = ctx["identb"]; onesb = ctx["onesb"]
    dnA = ctx["dnA"]; dnB = ctx["dnB"]; dnC = ctx["dnC"]
    hp0 = ctx["hp0"]; hp1 = ctx["hp1"]
    cst = ctx["cst"]
    if "cw_sb" in ctx:
        cw_sb = ctx["cw_sb"]; accb = ctx["accb"]; b1c = ctx["b1c"]
        w2b = ctx["w2b"]; b2r = ctx["b2r"]; xnq = ctx["xnq"]
    else:
        cw_sb = ar.alloc([P, NTB, 8])
        accb = ar.alloc([P, NTB, DM], BF16)
        b1c = ar.alloc([P, 8, 32])
        w2b = ar.alloc([P, 32, 1024], BF16)
        b2r_t2 = ar.alloc([P, 2048], BF16)
        b2r = b2r_t2[0:1, :].rearrange("o (a b) -> o a b", b=1024)
        xnq = ar.alloc([P, 8, 512], BF16)

    ar.mark()
    gw = ar.alloc([P, 8, 8])
    Bm = ar.alloc([P, 8, 512], BF16)
    dt4 = ar.alloc([P, 128])
    A4 = ar.alloc([P, 128])
    b1r = ar.alloc([P, 128])
    dtc = ar.alloc([P, 4])
    ac = ar.alloc([P, 4])
    upj = ar.alloc([P, 4, T], BF16)
    gamma_b = ar.alloc([P, DM])
    beta_b = ar.alloc([P, DM])
    normw_b = ar.alloc([P, DM])

    s.op("sync", lambda e: e.dma_start(gw[:], d["L0_ffn_gate_w"].rearrange("(c p) e -> p c e", p=P)),
         inc=sem_ld, inc_by=16)
    v_gw = sem_ld.n
    s.op("sync", lambda e: e.dma_start(gamma_b, d["L0_ffn_gamma"].unsqueeze(0).broadcast_to((P, DM))),
         inc=sem_ld, inc_by=16)
    s.op("sync", lambda e: e.dma_start(beta_b, d["L0_ffn_beta"].unsqueeze(0).broadcast_to((P, DM))),
         inc=sem_ld, inc_by=16)
    s.op("sync", lambda e: e.dma_start(normw_b, d["L0_attn_norm_w"].unsqueeze(0).broadcast_to((P, DM))),
         inc=sem_ld, inc_by=16)
    v_rows = sem_ld.n
    s.op("sync", lambda e: e.dma_start(dt4[0:4, :], d["L0_attn_dt"].rearrange("(c p) -> c p", p=P)),
         inc=sem_ld, inc_by=16)
    s.op("sync", lambda e: e.dma_start(A4[0:4, :], d["L0_attn_A"].rearrange("(c p) -> c p", p=P)),
         inc=sem_ld, inc_by=16)
    v_dtA = sem_ld.n
    s.op("gpsimd", lambda e: e.dma_start(Bm[:], d["L0_attn_Bmat"].rearrange("(c p) s -> p c s", p=P)),
         inc=sem_w2s, inc_by=16)
    v_Bm = sem_w2s.n
    w1_done = [None] * 8
    w2_done = [None] * 8
    b2_done = [None] * 8

    def emit_w1(e_, gate_waits):
        for k in range(8):
            wts = list(gate_waits) if k == 0 else []
            s.op("gpsimd", lambda en, e_=e_, k=k: en.dma_start(
                w1b[:, k, :], d["L0_ffn_w1"][e_, k * P:(k + 1) * P, :]),
                waits=wts, inc=sem_w1s, inc_by=16)
        w1_done[e_] = sem_w1s.n

    def emit_w2b2(e_, gate_pe):
        s.op("gpsimd", lambda en, e_=e_: en.dma_start(
            b2r[0:1, e_ % 2, :], d["L0_ffn_b2"][e_:e_ + 1, :]),
            waits=[(sem_pe, gate_pe)] if gate_pe is not None else [], inc=sem_w2s, inc_by=16)
        b2_done[e_] = sem_w2s.n
        for g in range(4):
            s.op("gpsimd", lambda en, e_=e_, g=g: en.dma_start(
                w2b[:, g * 8:(g + 1) * 8, :],
                d["L0_ffn_w2"][e_, g * 1024:(g + 1) * 1024, :].rearrange("(j p) dd -> p j dd", p=P)),
                inc=sem_w2s, inc_by=16)
        w2_done[e_] = sem_w2s.n

    emit_w2b2(0, None)

    v_b1p = 0
    for e_ in range(8):
        v_l = s.op("sync", lambda en, e_=e_: en.dma_start(
            b1r[0:32, :], d["L0_ffn_b1"][e_].rearrange("(n p) -> n p", p=P)),
            waits=([] if e_ == 0 else [(sem_ve, v_b1p)]), inc=sem_ld, inc_by=16)
        v_t = s.op("tensor", lambda en: en.matmul(hp0[:, 0:32], b1r[0:32, :], ident[0:32, 0:32],
                                                  start=True, stop=True),
                   waits=[(sem_ld, v_l)] + ([(sem_ve, v_b1p)] if e_ else []), inc=sem_pe)
        v_b1p = s.op("vector", lambda en, e_=e_: en.tensor_copy(b1c[:, e_, :], hp0[:, 0:32]),
                     waits=[(sem_pe, v_t)], inc=sem_ve)
    v_b1all = v_b1p
    PH = os.environ.get("KPHASES", "all")

    def trivial_final_u():
        ar.mark()
        tmpf = ar.alloc([P, DM])
        vcp = 0
        vst = 0
        for tb in range(NTB):
            vl = s.op("sync", lambda e, tb=tb: e.dma_start(tmpf[:], d["u"][tb * P:(tb + 1) * P, :]),
                      waits=[(sem_st, vst)] if vst else [], inc=sem_fc, inc_by=16)
            vst = s.op("sync", lambda e, tb=tb: e.dma_start(d["x1"][tb * P:(tb + 1) * P, :], tmpf[:]),
                      waits=[(sem_fc, vl)], inc=sem_st, inc_by=16)
        ar.release()

    def trivial_final():
        ar.mark()
        tmpb = ar.alloc([P, DM], BF16)
        tmpf = ar.alloc([P, DM])
        vcp = 0
        vst = 0
        for tb in range(NTB):
            vl = s.op("sync", lambda e, tb=tb: e.dma_start(tmpb[:], d["xn_d"][tb * P:(tb + 1) * P, :]),
                      waits=[(sem_ve, vcp)] if vcp else [], inc=sem_fc, inc_by=16)
            vcp = s.op("vector", lambda e: e.tensor_copy(tmpf[:], tmpb[:]),
                       waits=[(sem_fc, vl)] + ([(sem_st, vst)] if vst else []), inc=sem_ve)
            vst = s.op("sync", lambda e, tb=tb: e.dma_start(d["x1"][tb * P:(tb + 1) * P, :], tmpf[:]),
                       waits=[(sem_ve, vcp)], inc=sem_st, inc_by=16)
        ar.release()

    if PH == "pro":
        ar.release()
        trivial_final_u()
        return
    ar.mark()
    xr = ar.alloc([P, 2, DM])
    xc = ar.alloc([P, DM])
    xsq = ar.alloc([P, DM])
    xn_t = ar.alloc([P, DM])
    xn2_t = ar.alloc([P, DM])
    xg = ar.alloc([P, DM])
    xn2f = ar.alloc([P, 8, 128])
    xn2b = ar.alloc([P, 8, 128], BF16)
    attb = ar.alloc([P, DM], BF16)
    attf = ar.alloc([P, 8, 128], BF16)
    xnb = ar.alloc([P, DM], BF16)
    lg = ar.alloc([P, 8])
    st_ = ar.alloc([P, 16])
    zrt = ar.alloc([P, 8])
    msk = ar.alloc([P, 8])

    x_rd = {}
    x_consumed = {}
    last_hp0_act = 0
    prev_xn2dma = 0
    prev_xndma = 0
    LNTB = int(os.environ.get("KLNTB", NTB))
    SKIP = set(os.environ.get("KLNSKIP", "").split(","))
    DEPTH = int(os.environ.get("KLNDEPTH", 99))

    def emit_xload(tb):
        waits = []
        if tb >= 2:
            waits.append((sem_ve, x_consumed[tb - 2]))
        x_rd[tb] = s.op("sync", lambda e, tb=tb: e.dma_start(
            xr[:, tb % 2, :], d["u"][tb * P:(tb + 1) * P, :]),
            waits=waits, inc=sem_lx[tb % 2], inc_by=16)

    emit_xload(0)
    if LNTB > 1:
        emit_xload(1)
    for tb in range(LNTB):
        xin = xr[:, tb % 2, :]
        s.op("vector", lambda e, xin=xin: e.reduce_sum(st_[:, 0:1], xin, axis=AX.X),
             waits=[(sem_lx[tb % 2], x_rd[tb])])
        s.op("vector", lambda e: e.tensor_scalar(st_[:, 0:1], st_[:, 0:1], 1.0 / DM, None, ALU.mult))
        v_xc = s.op("vector", lambda e, xin=xin: e.tensor_scalar(xc[:], xin, st_[:, 0:1], None, ALU.subtract),
                    inc=sem_ve)
        x_consumed[tb] = v_xc
        if tb + 2 < LNTB:
            emit_xload(tb + 2)
        s.op("scalar", lambda e: e.activation(xsq[:], xc[:], AF.Square, accum_out=st_[:, 1:2]),
             waits=[(sem_ve, v_xc)])
        s.op("scalar", lambda e: e.activation(st_[:, 1:2], st_[:, 1:2], AF.Ln, bias=cst[:, 0:1], scale=1.0 / DM))
        v_rs = s.op("scalar", lambda e: e.activation(st_[:, 1:2], st_[:, 1:2], AF.Exp, scale=-0.5),
                    inc=sem_act)
        v_xn = s.op("vector", lambda e: e.tensor_scalar(xn_t[:], xc[:], st_[:, 1:2], None, ALU.mult),
                    waits=[(sem_act, v_rs)], inc=sem_ve)
        if "xnd" in SKIP:
            v_xnb = None
        if "xnd" not in SKIP:
            v_xnb = s.op("vector", lambda e: e.tensor_copy(xnb[:], xn_t[:]),
                         waits=[(sem_sxn, prev_xndma)] if prev_xndma else [], inc=sem_ve)
            prev_xndma = s.op("sync", lambda e, tb=tb: e.dma_start(d["xn_d"][tb * P:(tb + 1) * P, :], xnb[:]),
                              waits=[(sem_ve, v_xnb)], inc=sem_sxn, inc_by=16)
        if DEPTH < 2:
            continue
        s.op("vector", lambda e: e.reduce_sum(st_[:, 2:3], xn_t[:], axis=AX.X))
        s.op("vector", lambda e: e.tensor_scalar(st_[:, 2:3], st_[:, 2:3], 1.0 / DM, None, ALU.mult))
        v_xc2 = s.op("vector", lambda e: e.tensor_scalar(xc[:], xn_t[:], st_[:, 2:3], None, ALU.subtract),
                     inc=sem_ve)
        s.op("scalar", lambda e: e.activation(xsq[:], xc[:], AF.Square, accum_out=st_[:, 3:4]),
             waits=[(sem_ve, v_xc2)])
        s.op("scalar", lambda e: e.activation(st_[:, 3:4], st_[:, 3:4], AF.Ln, bias=cst[:, 0:1], scale=1.0 / DM))
        v_rs2 = s.op("scalar", lambda e: e.activation(st_[:, 3:4], st_[:, 3:4], AF.Exp, scale=-0.5),
                     inc=sem_act)
        s.op("vector", lambda e: e.scalar_tensor_tensor(xg[:], xc[:], st_[:, 3:4], gamma_b, ALU.mult, ALU.mult),
             waits=[(sem_act, v_rs2), (sem_ld, v_rows)])
        v_xn2 = s.op("vector", lambda e: e.tensor_tensor(xn2_t[:], xg[:], beta_b, ALU.add), inc=sem_ve)
        if DEPTH < 3:
            continue
        s.op("scalar", lambda e: e.activation(xsq[:], xn_t[:], AF.Square, accum_out=st_[:, 4:5]),
             waits=[(sem_ve, v_xn)])
        s.op("scalar", lambda e: e.activation(st_[:, 4:5], st_[:, 4:5], AF.Ln, bias=cst[:, 1:2], scale=1.0 / DM))
        v_rs3 = s.op("scalar", lambda e: e.activation(st_[:, 4:5], st_[:, 4:5], AF.Exp, scale=-0.5),
                     inc=sem_act)
        v_att = s.op("vector", lambda e: e.scalar_tensor_tensor(
            attb[:], xn_t[:], st_[:, 4:5], normw_b, ALU.mult, ALU.mult),
            waits=[(sem_act, v_rs3)], inc=sem_ve)
        if DEPTH < 4:
            continue
        hp1b = hp1[:, 0:P].bitcast(BF16)[:, 0:P]
        for c in range(8):
            w1_ = [(sem_ve, v_xn2 if c == 0 else sem_ve.n)]
            v_t1 = s.op("tensor", lambda e, c=c: e.transpose(hp0[:, 0:P], xn2_t[:, c * P:(c + 1) * P], ident[:]),
                        waits=w1_, inc=sem_pe)
            v_c1 = s.op("vector", lambda e, c=c: e.tensor_copy(xn2f[:, c, :], hp0[:, 0:P]),
                        waits=[(sem_pe, v_t1)], inc=sem_ve)
            wc1b = []
            if c == 0 and prev_xn2dma:
                wc1b.append((sem_sx2, prev_xn2dma))
            v_c1b = s.op("vector", lambda e, c=c: e.tensor_copy(xn2b[:, c, :], xn2f[:, c, :]),
                         waits=wc1b, inc=sem_ve)
            v_t2 = s.op("tensor", lambda e, c=c, hp1b=hp1b: e.transpose(
                hp1b, attb[:, c * P:(c + 1) * P], identb[:]),
                waits=[(sem_ve, v_att if c == 0 else sem_ve.n)], inc=sem_pe)
            s.op("vector", lambda e, c=c, hp1b=hp1b: e.tensor_copy(attf[:, c, :], hp1b),
                 waits=[(sem_pe, v_t2)], inc=sem_ve)
        v_xn2b_done = sem_ve.n
        v_trdone = sem_ve.n
        if DEPTH < 5:
            continue
        prev_xn2dma = s.op("sync", lambda e, tb=tb: e.dma_start(
            d["xn2fm_d"].rearrange("(c p) t -> p c t", p=P)[:, :, tb * P:(tb + 1) * P], xn2b[:]),
            waits=[(sem_ve, v_xn2b_done)], inc=sem_sx2, inc_by=16)
        if DEPTH < 6:
            continue
        for c in range(8):
            def mml(e, c=c):
                return e.matmul(dnA[:, 0:8], xn2f[:, c, :], gw[:, c, :],
                                start=(c == 0), stop=(c == 7))
            if c == 0:
                s.op("tensor", mml, waits=[(sem_ve, v_trdone), (sem_ld, v_gw)])
            elif c == 7:
                v_lgmm = s.op("tensor", mml, inc=sem_pe)
            else:
                s.op("tensor", mml)
        v_lgc = s.op("vector", lambda e: e.tensor_copy(lg[:], dnA[:, 0:8]),
                     waits=[(sem_pe, v_lgmm)], inc=sem_ve)
        s.op("vector", lambda e: e.reduce_max(st_[:, 5:6], lg[:], axis=AX.X))
        s.op("vector", lambda e: e.tensor_scalar(st_[:, 6:7], st_[:, 5:6], -1.0, None, ALU.mult))
        s.op("vector", lambda e: e.tensor_scalar(msk[:], lg[:], st_[:, 5:6], None, ALU.is_ge))
        s.op("vector", lambda e: e.scalar_tensor_tensor(zrt[:], msk[:], -1e30, lg[:], ALU.mult, ALU.add))
        s.op("vector", lambda e: e.reduce_max(st_[:, 7:8], zrt[:], axis=AX.X))
        v_mk = s.op("vector", lambda e: e.tensor_scalar(msk[:], lg[:], st_[:, 7:8], None, ALU.is_ge),
                    inc=sem_ve)
        v_ez = s.op("scalar", lambda e: e.activation(zrt[:], lg[:], AF.Exp, bias=st_[:, 6:7]),
                    waits=[(sem_ve, v_mk)], inc=sem_act)
        s.op("vector", lambda e: e.tensor_tensor(zrt[:], zrt[:], msk[:], ALU.mult),
             waits=[(sem_act, v_ez)])
        s.op("vector", lambda e: e.reduce_sum(st_[:, 8:9], zrt[:], axis=AX.X))
        s.op("vector", lambda e: e.reciprocal(st_[:, 9:10], st_[:, 8:9]))
        s.op("vector", lambda e, tb=tb: e.tensor_scalar(cw_sb[:, tb, :], zrt[:], st_[:, 9:10], None, ALU.mult),
             inc=sem_ve)
        if DEPTH < 7:
            continue
        for st4 in range(4):
            psU = (dnB if st4 % 2 == 0 else dnC)[:, 0:P]
            for k in range(8):
                def mmu(e, st4=st4, k=k, psU=psU):
                    return e.matmul(psU, Bm[:, k, st4 * P:(st4 + 1) * P], attf[:, k, :],
                                    start=(k == 0), stop=(k == 7))
                if k == 0:
                    wts = [(sem_ve, v_trdone)]
                    if tb == 0 and st4 == 0:
                        wts.append((sem_w2s, v_Bm))
                    if st4 >= 2 or tb > 0:
                        wts.append((sem_ve, sem_ve.n))
                    s.op("tensor", mmu, waits=wts)
                elif k == 7:
                    v_um = s.op("tensor", mmu, inc=sem_pe)
                else:
                    s.op("tensor", mmu)
            s.op("vector", lambda e, st4=st4, tb=tb, psU=psU: e.tensor_copy(
                upj[:, st4, tb * P:(tb + 1) * P], psU),
                waits=[(sem_pe, v_um)], inc=sem_ve)
    v_ln_pe = sem_pe.n
    ar.release()
    if PH == "ln":
        ar.release()
        trivial_final()
        return

    ar.mark()
    binc = ar.alloc([P, T])
    xsc_f = ar.alloc([P, T])
    xsc_b = ar.alloc([P, T], BF16)
    s.op("scalar", lambda e: e.activation(dt4[0:4, :], dt4[0:4, :], AF.Sigmoid),
         waits=[(sem_ld, v_dtA)])
    v_ab = s.op("scalar", lambda e: e.activation(A4[0:4, :], A4[0:4, :], AF.Abs), inc=sem_act)
    v_m = s.op("vector", lambda e: e.tensor_tensor(A4[0:4, :], A4[0:4, :], dt4[0:4, :], ALU.mult),
               waits=[(sem_act, v_ab)], inc=sem_ve)
    v_ex = s.op("scalar", lambda e: e.activation(A4[0:4, :], A4[0:4, :], AF.Exp, scale=-1.0),
                waits=[(sem_ve, v_m)], inc=sem_act)
    v_tt1 = s.op("tensor", lambda e: e.matmul(hp0[:, 0:4], dt4[0:4, :], ident[0:4, 0:4],
                                              start=True, stop=True),
                 waits=[(sem_act, v_ex), (sem_ve, sem_ve.n)], inc=sem_pe)
    v_tt2 = s.op("tensor", lambda e: e.matmul(hp1[:, 0:4], A4[0:4, :], ident[0:4, 0:4],
                                              start=True, stop=True), inc=sem_pe)
    v_dc = s.op("vector", lambda e: e.tensor_copy(dtc[:], hp0[:, 0:4]), waits=[(sem_pe, v_tt1)], inc=sem_ve)
    v_ac = s.op("vector", lambda e: e.tensor_copy(ac[:], hp1[:, 0:4]), waits=[(sem_pe, v_tt2)], inc=sem_ve)
    v_sdma = 0
    for c in range(4):
        wts = [(sem_ve, max(v_dc, v_ac))] if c == 0 else [(sem_sscan, v_sdma)]
        s.op("vector", lambda e, c=c: e.tensor_scalar(binc[:], upj[:, c, :], dtc[:, c:c + 1], None, ALU.mult),
             waits=wts)
        s.op("vector", lambda e, c=c: e.tensor_tensor_scan(
            xsc_f[:], ac[:, c:c + 1].broadcast_to((P, T)), binc[:], 0.0, ALU.mult, ALU.add))
        v_cb = s.op("vector", lambda e: e.tensor_copy(xsc_b[:], xsc_f[:]), inc=sem_ve)
        v_sdma = s.op("sync", lambda e, c=c: e.dma_start(d["xscan_d"][c * P:(c + 1) * P, :], xsc_b[:]),
                      waits=[(sem_ve, v_cb)], inc=sem_sscan, inc_by=16)
    v_scan_ve = sem_ve.n
    ar.release()
    ar.release()
    if PH == "scan":
        trivial_final()
        return

    ar.mark()
    w1b = ar.alloc([P, 8, 4096], BF16)
    hb = ar.alloc([P, 32, 512], BF16)
    QT = 512

    v_z = s.op("vector", lambda e: e.memset(accb[:], 0.0), inc=sem_ve)
    emit_w1(0, [(sem_ve, v_scan_ve), (sem_sscan, v_sdma)])

    xnq_done = {}

    def emit_xnq(e_, q, gate_pe):
        xnq_done[(e_, q)] = s.op("sync", lambda en, q=q: en.dma_start(
            xnq[:],
            d["xn2fm_d"].rearrange("(c p) t -> p c t", p=P)[:, :, q * QT:(q + 1) * QT]),
            waits=[(sem_pe, gate_pe)] if gate_pe is not None else [], inc=sem_xq, inc_by=16)

    xnq_done[(0, 0)] = s.op("sync", lambda en: en.dma_start(
        xnq[:], d["xn2fm_d"].rearrange("(c p) t -> p c t", p=P)[:, :, 0:QT]),
        waits=[(sem_sx2, sem_sx2.n)], inc=sem_xq, inc_by=16)

    dn_evac = []
    ngrp = 0
    prev_g30 = None
    prev_g31 = None
    pe_after_h = {}
    pe_after_dn = {}
    for e_ in range(8):
        for q in range(4):
            gelu_vals = []
            for n in range(32):
                ps = hp0 if n % 2 == 0 else hp1
                waits = []
                if n == 0:
                    waits.append((sem_xq, xnq_done[(e_, q)]))
                    waits.append((sem_w1s, w1_done[e_]))
                    if e_ == 0 and q == 0:
                        waits.append((sem_ve, max(v_b1all, v_scan_ve)))
                        waits.append((sem_pe, v_ln_pe))
                    else:
                        waits.append((sem_act, prev_g30))
                if n == 1 and prev_g31 is not None and not (e_ == 0 and q == 0):
                    waits.append((sem_act, prev_g31))
                if n >= 2:
                    waits.append((sem_act, gelu_vals[n - 2]))
                for k in range(8):
                    def mmh(e, k=k, n=n, ps=ps):
                        return e.matmul(ps[:], w1b[:, k, n * P:(n + 1) * P], xnq[:, k, :],
                                        start=(k == 0), stop=(k == 7))
                    if k == 0:
                        s.op("tensor", mmh, waits=waits)
                    elif k == 7:
                        v_hm = s.op("tensor", mmh, inc=sem_pe)
                    else:
                        s.op("tensor", mmh)
                gelu_vals.append(s.op("scalar", lambda e, n=n, ps=ps, e_=e_: e.activation(
                    hb[:, n, :], ps[:], AF.Gelu, bias=b1c[:, e_, n:n + 1]),
                    waits=[(sem_pe, v_hm)], inc=sem_act))
            prev_g30, prev_g31 = gelu_vals[30], gelu_vals[31]
            pe_after_h[(e_, q)] = sem_pe.n
            nxt = (e_, q + 1) if q < 3 else (e_ + 1, 0)
            if nxt[0] < 8:
                emit_xnq(nxt[0], nxt[1], pe_after_h[(e_, q)])
            for tb in range(4):
                tbg = q * 4 + tb
                ps = (dnA, dnB, dnC)[ngrp % 3]
                waits = [(sem_act, gelu_vals[-1])]
                if ngrp == 0:
                    waits.append((sem_w2s, w2_done[0]))
                    waits.append((sem_ve, v_scan_ve))
                elif ngrp >= 3:
                    waits.append((sem_ve, dn_evac[ngrp - 3]))
                first = True
                for n in range(32):
                    for j in range(2):
                        def mmd(e, n=n, j=j, tb=tb, ps=ps):
                            return e.matmul(ps[:, j * 512:(j + 1) * 512],
                                            hb[:, n, tb * P:(tb + 1) * P],
                                            w2b[:, n, j * 512:(j + 1) * 512],
                                            start=(n == 0), stop=False, skip_group_check=True)
                        if first:
                            s.op("tensor", mmd, waits=waits)
                            first = False
                        else:
                            s.op("tensor", mmd)
                for j in range(2):
                    def mmb(e, j=j, ps=ps, e_=e_):
                        return e.matmul(ps[:, j * 512:(j + 1) * 512], onesb[:],
                                        b2r[0:1, e_ % 2, j * 512:(j + 1) * 512],
                                        start=False, stop=True, skip_group_check=True)
                    if j == 1:
                        v_dm = s.op("tensor", mmb, inc=sem_pe)
                    else:
                        s.op("tensor", mmb)
                dn_evac.append(s.op("vector", lambda e, tbg=tbg, ps=ps, e_=e_: e.scalar_tensor_tensor(
                    accb[:, tbg, :], ps[:], cw_sb[:, tbg, e_].unsqueeze(-1), accb[:, tbg, :],
                    ALU.mult, ALU.add),
                    waits=[(sem_pe, v_dm), (sem_ve, v_z)], inc=sem_ve))
                ngrp += 1
            pe_after_dn[(e_, q)] = sem_pe.n
        if e_ + 1 < 8:
            emit_w1(e_ + 1, [(sem_pe, pe_after_h[(e_, 3)])])
            emit_w2b2(e_ + 1, pe_after_dn[(e_, 3)])
    v_moe_ve = sem_ve.n
    ar.release()
    if PH == "moe":
        s.wait("vector", sem_ve, v_moe_ve)
        trivial_final()
        return

    ar.mark()
    C_sb = ar.alloc([P, 4, 1024], BF16)
    xscan = ar.alloc([P, 4, T], BF16)
    D_b = ar.alloc([P, DM])
    xr2 = ar.alloc([P, 2, DM])
    xnr = ar.alloc([P, 2, DM], BF16)
    sD = ar.alloc([P, DM])
    gsum = ar.alloc([P, DM])
    gout = ar.alloc([P, DM])
    t3 = ar.alloc([P, DM])
    xout = ar.alloc([P, DM])
    v_C = s.op("gpsimd", lambda e: e.dma_start(C_sb[:], d["L0_attn_C"].rearrange("(c p) o -> p c o", p=P)),
               waits=[(sem_pe, pe_after_dn[(7, 3)])], inc=sem_w2s, inc_by=16)
    v_xs = s.op("sync", lambda e: e.dma_start(xscan[:], d["xscan_d"].rearrange("(c p) t -> p c t", p=P)),
                waits=[(sem_pe, pe_after_dn[(7, 3)]), (sem_sscan, sem_sscan.n)], inc=sem_fc, inc_by=16)
    v_D = s.op("sync", lambda e: e.dma_start(D_b, d["L0_attn_D"].unsqueeze(0).broadcast_to((P, DM))),
                inc=sem_fc, inc_by=16)

    f_x = {}
    f_xn = {}
    f_consumed = {}
    f_sD = {}

    def emit_fload(tb):
        sm = sem_fl[tb % 2]
        waits = [(sem_pe, pe_after_dn[(7, 3)])] if tb == 0 else []
        if tb >= 2:
            waits.append((sem_ve, f_consumed[tb - 2]))
        s.op("sync", lambda e, tb=tb: e.dma_start(xr2[:, tb % 2, :], d["u"][tb * P:(tb + 1) * P, :]),
             waits=waits, inc=sm, inc_by=16)
        f_x[tb] = (sm, sm.n)
        s.op("sync", lambda e, tb=tb: e.dma_start(xnr[:, tb % 2, :], d["xn_d"][tb * P:(tb + 1) * P, :]),
             waits=[(sem_sxn, sem_sxn.n)] if tb == 0 else [], inc=sm, inc_by=16)
        f_xn[tb] = (sm, sm.n)

    emit_fload(0)
    emit_fload(1)
    for tb in range(NTB):
        ps = (dnA, dnB, dnC)[tb % 3]
        waits = []
        if tb == 0:
            waits = [(sem_fc, v_D), (sem_w2s, v_C), (sem_ve, v_moe_ve)]
        elif tb >= 3:
            waits = [(sem_ve, f_sD[tb - 3])]
        first = True
        for sc in range(4):
            for j in range(2):
                def mms(e, sc=sc, j=j, tb=tb, ps=ps):
                    return e.matmul(ps[:, j * 512:(j + 1) * 512],
                                    xscan[:, sc, tb * P:(tb + 1) * P],
                                    C_sb[:, sc, j * 512:(j + 1) * 512],
                                    start=(sc == 0), stop=(sc == 3), skip_group_check=True)
                if first:
                    s.op("tensor", mms, waits=waits)
                    first = False
                elif sc == 3 and j == 1:
                    v_sm = s.op("tensor", mms, inc=sem_pe)
                else:
                    s.op("tensor", mms)
        f_sD[tb] = s.op("vector", lambda e, ps=ps: e.tensor_tensor(sD[:], ps[:], D_b, ALU.add),
                        waits=[(sem_pe, v_sm), (sem_fc, v_D)], inc=sem_ve)
        v_gs = s.op("vector", lambda e, tb=tb: e.tensor_tensor(
            gsum[:], accb[:, tb, :], xnr[:, tb % 2, :], ALU.add),
            waits=[f_xn[tb]], inc=sem_ve)
        v_g = s.op("scalar", lambda e: e.activation(gout[:], gsum[:], AF.Silu),
                   waits=[(sem_ve, v_gs)], inc=sem_act)
        s.op("vector", lambda e: e.tensor_tensor(t3[:], sD[:], gout[:], ALU.mult),
             waits=[(sem_act, v_g)])
        wxo = [f_x[tb]]
        if tb > 0:
            wxo.append((sem_st, sem_st.n))
        v_xo = s.op("vector", lambda e, tb=tb: e.tensor_tensor(xout[:], xr2[:, tb % 2, :], t3[:], ALU.add),
                    waits=wxo, inc=sem_ve)
        f_consumed[tb] = v_xo
        s.op("sync", lambda e, tb=tb: e.dma_start(d["x1"][tb * P:(tb + 1) * P, :], xout[:]),
             waits=[(sem_ve, v_xo)], inc=sem_st, inc_by=16)
        if tb + 2 < NTB:
            emit_fload(tb + 2)
    ar.release()





def emit_quant(s, sems, bufs, src_ap, out_ap, W, src_ready=(), eng_big="vector"):
    sem_ve = sems["ve"]; sem_gp = sems["gp"]; sem_act = sems["act"]
    nb = W // 16
    absx, t1, t2, q = bufs["absx"], bufs["t1"], bufs["t2"], bufs["q"]
    bm, sc, rs = bufs["bm"], bufs["sc"], bufs["rs"]
    v_sg = s.op("scalar", lambda e: e.sign(q[:, 0:W], src_ap),
                waits=list(src_ready) + [(sem_ve, sems["ve"].n)])
    v_a = s.op("vector", lambda e: e.tensor_tensor(absx[:, 0:W], src_ap, q[:, 0:W], ALU.mult),
               waits=list(src_ready) + [(sem_gp, sem_gp.n), (sem_act, v_sg)])
    s.op("vector", lambda e: e.tensor_reduce(bm[:, 0:nb].unsqueeze(-1),
                                             absx[:, 0:W].rearrange("p (a b) -> p a b", b=16),
                                             axis=AX.X, op=ALU.max))
    s.op("vector", lambda e: e.tensor_scalar(sc[:, 0:nb], bm[:, 0:nb], 1.0 / 6.0, 1e-12,
                                             ALU.mult, ALU.max))
    s.op("vector", lambda e: e.reciprocal(rs[:, 0:nb], sc[:, 0:nb]))
    rs_b = rs[:, 0:nb].unsqueeze(-1).broadcast_to((P, nb, 16))
    sc_b = sc[:, 0:nb].unsqueeze(-1).broadcast_to((P, nb, 16))
    v_v = s.op("vector", lambda e: e.tensor_tensor(
        absx[:, 0:W].rearrange("p (a b) -> p a b", b=16),
        absx[:, 0:W].rearrange("p (a b) -> p a b", b=16), rs_b, ALU.mult))
    vv = absx[:, 0:W]
    v_ss = s.op("vector", lambda e: e.tensor_tensor(
        q[:, 0:W].rearrange("p (a b) -> p a b", b=16),
        q[:, 0:W].rearrange("p (a b) -> p a b", b=16), sc_b, ALU.mult),
        waits=[(sem_act, v_sg)])
    s.op("vector", lambda e: e.tensor_scalar(t1[:, 0:W], vv, 0.25, None, ALU.is_gt))
    s.op("vector", lambda e: e.scalar_tensor_tensor(t1[:, 0:W], vv, 0.75, t1[:, 0:W],
                                                    ALU.is_gt, ALU.add))
    s.op("vector", lambda e: e.scalar_tensor_tensor(t1[:, 0:W], vv, 1.25, t1[:, 0:W],
                                                    ALU.is_gt, ALU.add))
    g4 = s.op("vector", lambda e: e.scalar_tensor_tensor(t1[:, 0:W], vv, 1.75, t1[:, 0:W],
                                                         ALU.is_gt, ALU.add))
    s.op("vector", lambda e: e.tensor_scalar(t2[:, 0:W], vv, 2.5, None, ALU.is_gt))
    s.op("vector", lambda e: e.scalar_tensor_tensor(t2[:, 0:W], vv, 3.5, t2[:, 0:W],
                                                    ALU.is_gt, ALU.add))
    s.op("vector", lambda e: e.scalar_tensor_tensor(t2[:, 0:W], vv, 5.0, t2[:, 0:W],
                                                    ALU.is_gt, ALU.add))
    s.op("vector", lambda e: e.scalar_tensor_tensor(t2[:, 0:W], vv, 5.0, t2[:, 0:W],
                                                    ALU.is_gt, ALU.add))
    s.op("vector", lambda e: e.scalar_tensor_tensor(t2[:, 0:W], t1[:, 0:W], 0.5, t2[:, 0:W],
                                                    ALU.mult, ALU.add))
    v_out = s.op("vector", lambda e: e.tensor_tensor(out_ap, t2[:, 0:W], q[:, 0:W], ALU.mult))
    return v_out


def build_l1(ctx):
    s = ctx["s"]
    ar = ctx["ar"]
    d = ctx["dram"]
    sems = ctx["sems"]
    cst = ctx["cst"]
    sem_ld = sems["ld"]; sem_pe = sems["pe"]; sem_act = sems["act"]
    sem_ve = sems["ve"]; sem_st = sems["st"]; sem_gp = sems["gp"]
    sem_lx = (sems["lxa"], sems["lxb"])
    sem_sxf = sems["sx2"]
    sem_sxq = sems["sscan"]
    sem_sbz = sems["sxn"]
    sem_cc = (sems["fla"], sems["flb"])
    sem_ss1 = sems["fc"]
    sem_wg = sems["w1s"]
    sem_wq = sems["wqh"] if "wqh" in sems else sems["w2s"]
    sem_xq = sems["xq"]
    ident = ctx["ident"]; identb = ctx["identb"]
    dnA = ctx["dnA"]; dnB = ctx["dnB"]; dnC = ctx["dnC"]
    hp0 = ctx["hp0"]; hp1 = ctx["hp1"]
    DI = 2048
    NH = 32
    HD = 64
    DS = 128

    ar.mark()
    dtraw = ar.alloc([P, NTB, NH])
    Wdt = ar.alloc([P, 8, NH])
    w4 = ar.alloc([P, 18, 4])
    cb_c = ar.alloc([P, 18])
    A_bc = ar.alloc([P, NH])
    D_bc = ar.alloc([P, NH])
    dtb_bc = ar.alloc([P, NH])
    normw_bc = ar.alloc([P, DI])
    eh_flat = ar.alloc([P, 4096])
    eh_sb = eh_flat[0:32, :]
    lt_sb = ar.alloc([P, P])
    onesf_t = ar.alloc([P, 128])
    onesf = onesf_t[0:1, :]
    state = ar.alloc([P, DI])
    state_bf = ar.alloc([P, DI], BF16)
    BmF = ar.alloc([P, T], BF16)
    CmF = ar.alloc([P, T], BF16)
    Wout_sb = ar.alloc([P, 16, 1024], BF16)

    gate0 = ctx.get("l1_gate", [])
    s.op("sync", lambda e: e.dma_start(lt_sb[:], d["c_lt"]), waits=gate0, inc=sem_ld, inc_by=16)
    s.op("sync", lambda e: e.dma_start(onesf_t[0:1, :], d["c_onesf"]), inc=sem_ld, inc_by=16)
    s.op("sync", lambda e: e.dma_start(eh_flat[0:32, :], d["c_eh"]), inc=sem_ld, inc_by=16)
    s.op("sync", lambda e: e.dma_start(
        Wdt[:], d["L1_attn_W_in"][:, 4352:4384].rearrange("(c p) e -> p c e", p=P)),
        inc=sem_ld, inc_by=16)
    s.op("sync", lambda e: e.dma_start(
        normw_bc[:], d["L1_attn_norm_w"].unsqueeze(0).broadcast_to((P, DI))),
        inc=sem_ld, inc_by=16)
    s.op("sync", lambda e: e.dma_start(
        dtb_bc[:], d["L1_attn_dt_bias"].unsqueeze(0).broadcast_to((P, NH))),
        inc=sem_ld, inc_by=16)
    ar.mark()
    arow = ar.alloc([P, NH])
    s.op("sync", lambda e: e.dma_start(arow[0:1, :], d["L1_attn_A_log"].unsqueeze(0)),
         inc=sem_ld, inc_by=16)
    drow = ar.alloc([P, NH])
    s.op("sync", lambda e: e.dma_start(drow[0:1, :], d["L1_attn_D"].unsqueeze(0)),
         inc=sem_ld, inc_by=16)
    cwst = ar.alloc([P, 512])
    s.op("sync", lambda e: e.dma_start(
        cwst[0:18, :], d["L1_attn_conv_w"].rearrange("(c p) j -> c (p j)", p=P)),
        inc=sem_ld, inc_by=16)
    s.op("sync", lambda e: e.dma_start(
        cwst[32:50, 0:P], d["L1_attn_conv_b"].rearrange("(c p) -> c p", p=P)),
        inc=sem_ld, inc_by=16)
    v_par = sem_ld.n

    v_ae = s.op("scalar", lambda e: e.activation(arow[0:1, :], arow[0:1, :], AF.Exp),
                waits=[(sem_ld, v_par)])
    v_an = s.op("vector", lambda e: e.tensor_scalar(arow[0:1, :], arow[0:1, :], -1.0, None, ALU.mult),
                waits=[(sem_act, v_ae)])
    v_tA = s.op("tensor", lambda e: e.matmul(hp0[:, 0:NH], onesf, arow[0:1, :], start=True, stop=True),
                waits=[(sem_ve, v_an), (sem_ld, v_par)], inc=sem_pe)
    v_cA = s.op("vector", lambda e: e.tensor_copy(A_bc[:], hp0[:, 0:NH]), waits=[(sem_pe, v_tA)])
    v_tD = s.op("tensor", lambda e: e.matmul(hp1[:, 0:NH], onesf, drow[0:1, :], start=True, stop=True),
                waits=[(sem_ve, v_cA)], inc=sem_pe)
    v_cD = s.op("vector", lambda e: e.tensor_copy(D_bc[:], hp1[:, 0:NH]), waits=[(sem_pe, v_tD)])
    for j in range(4):
        v_tw = s.op("tensor", lambda e, j=j: e.matmul(
            hp0[:, 0:18], cwst[0:18, :].rearrange("c (p j) -> c p j", j=4)[:, :, j],
            ident[0:18, 0:18], start=True, stop=True),
            waits=[(sem_ve, v_cD if j == 0 else sem_ve.n)], inc=sem_pe)
        s.op("vector", lambda e, j=j: e.tensor_copy(w4[:, :, j], hp0[:, 0:18]),
             waits=[(sem_pe, v_tw)])
    v_tb = s.op("tensor", lambda e: e.matmul(hp0[:, 0:18], cwst[32:50, 0:P], ident[32:50, 32:50],
                                             start=True, stop=True),
                waits=[(sem_ve, sem_ve.n)], inc=sem_pe)
    v_cbc = s.op("vector", lambda e: e.tensor_copy(cb_c[:], hp0[:, 0:18]), waits=[(sem_pe, v_tb)])
    if "dbg_d" in d and "p" in os.environ.get("KDBGSEL", "psmbB"):
        s.op("sync", lambda e: e.dma_start(d["dbg_d"][:, 0:72], w4[:].rearrange("p a b -> p (a b)")),
             waits=[(sem_ve, v_cbc)], inc=sems["fc"], inc_by=16)
        s.op("sync", lambda e: e.dma_start(d["dbg_d"][:, 128:146], cb_c[:]), inc=sems["fc"], inc_by=16)
    s.op("gpsimd", lambda e: e.dma_start(
        Wout_sb[:], d["L1_attn_W_out"].rearrange("(c p) o -> p c o", p=P)),
        waits=gate0, inc=sem_wg, inc_by=16)
    v_wout = sem_wg.n
    v_zero = s.op("vector", lambda e: e.memset(state[:], 0.0))
    v_one = s.op("vector", lambda e: e.memset(cst[:, 2:3], 1.0))
    ar.release()

    slb = {}

    sl_loaded = {}
    sl_ve_done = {}
    prev_wq_dma = [0]

    slabs = []
    for i in range(22):
        slabs.append(("L1_ffn_w_gate", i, 0, 1024, "wgt_d"))
    for i in range(22):
        slabs.append(("L1_ffn_w_up", i, 0, 1024, "wut_d"))
    for i in range(8):
        for piece in range(3):
            w0 = piece * 1024
            w1 = min(2816, w0 + 1024)
            slabs.append(("L1_ffn_w_down", i, w0, w1 - w0, "wdt_d"))
    NSLAB = len(slabs)

    def emit_slab_load2(j, extra_gate=None):
        if j >= NSLAB:
            return
        nm, i, w0, w, _dst = slabs[j]
        buf = slb["wsl"][j % 2]
        if extra_gate is not None:
            gate = list(extra_gate) + [(sem_ve, sl_ve_done.get(j - 2, 0))]
        elif j >= 2:
            gate = [(sem_ve, sl_ve_done.get(j - 2, 0))]
        else:
            gate = list(gate0) + [(sem_act, sem_act.n), (sem_gp, sem_gp.n)]
        sl_loaded[j] = s.op("gpsimd", lambda e, nm=nm, i=i, w0=w0, w=w, buf=buf: e.dma_start(
            buf[:, 0:w], d[nm][i * P:(i + 1) * P, w0:w0 + w]),
            waits=gate, inc=sem_wg, inc_by=16)

    def emit_slab_quant(j, psum_region, pe_waits):
        nm, i, w0, w, dst = slabs[j]
        buf = slb["wsl"][j % 2]
        wqo = slb["wqo"]
        wqs = slb["wqs"]
        v_q = emit_quant(s, sems, slb["wqb"], buf[:, 0:w], wqo[:, 0:w], w,
                         src_ready=[(sem_wg, sl_loaded[j])])
        nch = w // 128
        prb = psum_region.bitcast(BF16)[:, 0:P]
        for cc in range(nch):
            v_tt = s.op("tensor", lambda e, cc=cc, prb=prb, wqo=wqo: e.transpose(
                prb, wqo[:, cc * P:(cc + 1) * P], identb[:]),
                waits=[(sem_ve, v_q if cc == 0 else sem_ve.n)] + (pe_waits if cc == 0 else []),
                inc=sem_pe)
            s.op("vector", lambda e, cc=cc, prb=prb, wqs=wqs: e.tensor_copy(wqs[:, cc, :], prb),
                 waits=[(sem_pe, v_tt)] + ([(sem_wq, prev_wq_dma[0])] if cc == 0 and prev_wq_dma[0] else []))
        sl_ve_done[j] = sem_ve.n
        s.op("sync", lambda e, dst=dst, i=i, w0=w0, nch=nch, wqs=wqs: e.dma_start(
            d[dst].rearrange("(b p) n -> p b n", p=P)[:, w0 // P:w0 // P + nch, i * P:(i + 1) * P],
            wqs[:, 0:nch, :]),
            waits=[(sem_ve, sl_ve_done[j])], inc=sem_wq, inc_by=16)
        prev_wq_dma[0] = sem_wq.n

    sj = [0]

    def emit_slabs(k, psum_region, preload=True):
        for _ in range(k):
            if sj[0] < NSLAB:
                emit_slab_quant(sj[0], psum_region, [])
                if preload:
                    emit_slab_load2(sj[0] + 2)
                sj[0] += 1

    ar.mark()
    xr = ar.alloc([P, 2, DM])
    xc = ar.alloc([P, DM])
    xsq = ar.alloc([P, DM])
    xn_t = ar.alloc([P, DM])
    st_ = ar.alloc([P, 16])
    xnb = ar.alloc([P, DM], BF16)
    xqb = ar.alloc([P, DM], BF16)
    xnf32 = ar.alloc([P, 8, 128])
    xnf8 = ar.alloc([P, 8, 128], BF16)
    xqf8 = ar.alloc([P, 8, 128], BF16)
    qb = {"absx": ar.alloc([P, DM]), "t1": ar.alloc([P, DM]), "t2": ar.alloc([P, DM]),
          "q": ar.alloc([P, DM]), "bm": ar.alloc([P, DM // 16]), "sc": ar.alloc([P, DM // 16]),
          "rs": ar.alloc([P, DM // 16])}
    slb["wsl"] = [ar.alloc([P, 1024]) for _ in range(2)]
    slb["wqb"] = qb
    slb["wqo"] = ar.alloc([P, 1024], BF16)
    slb["wqs"] = ar.alloc([P, 8, 128], BF16)
    emit_slab_load2(0)
    emit_slab_load2(1)

    x_rd = {}
    x_cons = {}
    prev_fdma = 0
    prev_qdma = 0

    def a_xload(tb):
        waits = []
        if tb <= 1:
            waits.append((sem_ve, v_cbc))
        if tb >= 2:
            waits.append((sem_ve, x_cons[tb - 2]))
        x_rd[tb] = s.op("sync", lambda e, tb=tb: e.dma_start(
            xr[:, tb % 2, :], d["x1_l1"][tb * P:(tb + 1) * P, :]),
            waits=waits, inc=sem_lx[tb % 2], inc_by=16)

    a_xload(0)
    a_xload(1)
    for tb in range(NTB):
        xin = xr[:, tb % 2, :]
        s.op("vector", lambda e, xin=xin: e.reduce_sum(st_[:, 0:1], xin, axis=AX.X),
             waits=[(sem_lx[tb % 2], x_rd[tb])])
        s.op("vector", lambda e: e.tensor_scalar(st_[:, 0:1], st_[:, 0:1], 1.0 / DM, None, ALU.mult))
        v_xc = s.op("vector", lambda e, xin=xin: e.tensor_scalar(xc[:], xin, st_[:, 0:1], None,
                                                                 ALU.subtract))
        x_cons[tb] = v_xc
        if tb + 2 < NTB:
            a_xload(tb + 2)
        s.op("scalar", lambda e: e.activation(xsq[:], xc[:], AF.Square, accum_out=st_[:, 1:2]),
             waits=[(sem_ve, v_xc)])
        s.op("scalar", lambda e: e.activation(st_[:, 1:2], st_[:, 1:2], AF.Ln, bias=cst[:, 0:1],
                                              scale=1.0 / DM))
        v_rs = s.op("scalar", lambda e: e.activation(st_[:, 1:2], st_[:, 1:2], AF.Exp, scale=-0.5))
        v_xn = s.op("vector", lambda e: e.tensor_scalar(xn_t[:], xc[:], st_[:, 1:2], None, ALU.mult),
                    waits=[(sem_act, v_rs)])
        v_xnb = s.op("vector", lambda e: e.tensor_copy(xnb[:], xn_t[:]))
        v_xq = emit_quant(s, sems, qb, xn_t[:], xqb[:], DM, src_ready=[(sem_ve, v_xn)])
        hp1b = hp1[:, 0:P].bitcast(BF16)[:, 0:P]
        hp0b = hp0[:, 128:256].bitcast(BF16)[:, 0:P]
        for c in range(8):
            v_t1 = s.op("tensor", lambda e, c=c: e.transpose(hp0[:, 0:P], xn_t[:, c * P:(c + 1) * P],
                                                             ident[:]),
                        waits=[(sem_ve, v_xnb if c == 0 else sem_ve.n)], inc=sem_pe)
            v_c1 = s.op("vector", lambda e, c=c: e.tensor_copy(xnf32[:, c, :], hp0[:, 0:P]),
                        waits=[(sem_pe, v_t1)])
            v_c1b = s.op("vector", lambda e, c=c: e.tensor_copy(xnf8[:, c, :], xnf32[:, c, :]),
                         waits=[(sem_sxf, prev_fdma)] if (c == 0 and prev_fdma) else [])
            v_t2 = s.op("tensor", lambda e, c=c, hp1b=hp1b: e.transpose(
                hp1b, xqb[:, c * P:(c + 1) * P], identb[:]),
                waits=[(sem_ve, v_xq if c == 0 else sem_ve.n)], inc=sem_pe)
            v_c2 = s.op("vector", lambda e, c=c, hp1b=hp1b: e.tensor_copy(xqf8[:, c, :], hp1b),
                        waits=[(sem_pe, v_t2)] + ([(sem_sxq, prev_qdma)] if (c == 0 and prev_qdma) else []))
        v_trd = sem_ve.n
        prev_fdma = s.op("sync", lambda e, tb=tb: e.dma_start(
            d["xnfm_d"].rearrange("(c p) t -> p c t", p=P)[:, :, tb * P:(tb + 1) * P], xnf8[:]),
            waits=[(sem_ve, v_trd)], inc=sem_sxf, inc_by=16)
        prev_qdma = s.op("sync", lambda e, tb=tb: e.dma_start(
            d["xqfm_d"].rearrange("(c p) t -> p c t", p=P)[:, :, tb * P:(tb + 1) * P], xqf8[:]),
            waits=[(sem_ve, v_trd)], inc=sem_sxq, inc_by=16)
        for c in range(8):
            def mmd(e, c=c, tb=tb):
                return e.matmul(dnA[:, 0:32], xnf32[:, c, :], Wdt[:, c, :],
                                start=(c == 0), stop=(c == 7))
            if c == 0:
                s.op("tensor", mmd, waits=[(sem_ve, v_trd), (sem_ld, v_par)])
            elif c == 7:
                v_dmm = s.op("tensor", mmd, inc=sem_pe)
            else:
                s.op("tensor", mmd)
        s.op("vector", lambda e, tb=tb: e.tensor_copy(dtraw[:, tb, :], dnA[:, 0:32]),
             waits=[(sem_pe, v_dmm)])
        emit_slabs(2, dnC[:, 0:128], preload=(tb < NTB - 1))
    v_phA_ve = sem_ve.n
    v_phA_pe = sem_pe.n
    ar.release()

    ar.mark()
    win = [ar.alloc([P, 2304], BF16) for _ in range(8)]
    xbc = ar.alloc([P, T])
    cacc = ar.alloc([P, T])
    btile = ar.alloc([P, T], BF16)
    bst = ar.alloc([P, 16, 128], BF16)
    slb["wsl"] = [ar.alloc([P, 1024]) for _ in range(2)]
    slb["wqb"] = {"absx": cacc[:, 0:1024], "t1": cacc[:, 1024:2048],
                  "t2": xbc[:, 0:1024], "q": xbc[:, 1024:2048],
                  "bm": ar.alloc([P, 64]), "sc": ar.alloc([P, 64]), "rs": ar.alloc([P, 64])}
    slb["wqo"] = ar.alloc([P, 1024], BF16)
    slb["wqs"] = ar.alloc([P, 8, 128], BF16)
    for k in range(8):
        s.op("gpsimd", lambda e, k=k: e.dma_start(
            win[k][:], d["L1_attn_W_in"][k * P:(k + 1) * P, 0:2304]),
             waits=[(sem_ve, v_phA_ve), (sem_act, sem_act.n), (sem_gp, sem_gp.n),
                    (sem_lx[0], sem_lx[0].n), (sem_lx[1], sem_lx[1].n)] if k == 0 else [],
             inc=sem_wg, inc_by=16)
    v_win = sem_wg.n
    v_win2 = [0]
    WIN_SPLIT = 18

    def phaseB_tile(n, kind):
        halves = []
        for half in range(2):
            ps = (dnB if half == 0 else dnC)
            for k in range(8):
                for j in range(2):
                    woff = 0 if n < WIN_SPLIT else WIN_SPLIT * P
                    def mmw(e, n=n, k=k, half=half, j=j, ps=ps, woff=woff):
                        return e.matmul(ps[:, j * 512:(j + 1) * 512],
                                        win[k][:, n * P - woff:(n + 1) * P - woff],
                                        d_xnfm[:, k, half * 1024 + j * 512:half * 1024 + (j + 1) * 512],
                                        start=(k == 0), stop=(k == 7), skip_group_check=True)
                    wts = []
                    if k == 0 and j == 0:
                        wts = [(sem_wg, v_win if n < WIN_SPLIT else v_win2[0]),
                               (sem_ld, v_xnfm_ld)]
                        wts.append((sem_ve, sem_ve.n))
                    if k == 0 and j == 0:
                        s.op("tensor", mmw, waits=wts)
                    elif k == 7 and j == 1:
                        halves.append(s.op("tensor", mmw, inc=sem_pe))
                    else:
                        s.op("tensor", mmw)
        v_mm = halves[-1]
        if kind == "z":
            v_e1 = s.op("vector", lambda e: e.tensor_copy(btile[:, 0:1024], dnB[:, 0:1024]),
                        waits=[(sem_pe, halves[0])])
            v_e2 = s.op("vector", lambda e: e.tensor_copy(btile[:, 1024:2048], dnC[:, 0:1024]),
                        waits=[(sem_pe, v_mm)])
            src = btile
            v_rdy = v_e2
        else:
            v_e1 = s.op("vector", lambda e: e.tensor_copy(xbc[:, 0:1024], dnB[:, 0:1024]),
                        waits=[(sem_pe, halves[0])])
            v_e2 = s.op("vector", lambda e: e.tensor_copy(xbc[:, 1024:2048], dnC[:, 0:1024]),
                        waits=[(sem_pe, v_mm)])
            ci = n - 16
            v_cv = s.op("vector", lambda e, ci=ci: e.tensor_scalar(
                cacc[:], xbc[:], w4[:, ci, 3:4], None, ALU.mult),
                waits=[(sem_ld, v_par), (sem_act, sem_act.n)])
            v_cva = None
            for jj in range(3):
                sh = 3 - jj
                v_cva = s.op("vector", lambda e, ci=ci, jj=jj, sh=sh: e.scalar_tensor_tensor(
                    cacc[:, sh:T], xbc[:, 0:T - sh], w4[:, ci, jj:jj + 1], cacc[:, sh:T],
                    ALU.mult, ALU.add))
            v_si = s.op("scalar", lambda e, ci=ci: e.activation(
                xbc[:], cacc[:], AF.Silu, bias=cb_c[:, ci:ci + 1]),
                waits=[(sem_ve, v_cva)])
            if kind == "bm":
                v_rdy = s.op("vector", lambda e: e.tensor_copy(BmF[:], xbc[:]),
                             waits=[(sem_act, v_si)])
                src = None
            elif kind == "cm":
                v_rdy = s.op("vector", lambda e: e.tensor_copy(CmF[:], xbc[:]),
                             waits=[(sem_act, v_si)])
                src = None
            else:
                v_rdy = s.op("vector", lambda e: e.tensor_copy(btile[:], xbc[:]),
                             waits=[(sem_act, v_si)])
                src = btile
                if n == 16 and "dbg_xh" in d and "b" in os.environ.get("KDBGSEL", "psmbB"):
                    s.op("sync", lambda e: e.dma_start(d["dbg_xh"], btile[:]),
                         waits=[(sem_ve, v_rdy)], inc=sems["fc"], inc_by=16)
        if src is not None:
            for tbk in range(16):
                v_tt = s.op("tensor", lambda e, tbk=tbk: e.transpose(
                    hp0[:, 0:P].bitcast(BF16)[:, 0:P], src[:, tbk * P:(tbk + 1) * P], identb[:]),
                    waits=[(sem_ve, v_rdy if tbk == 0 else sem_ve.n)], inc=sem_pe)
                s.op("vector", lambda e, tbk=tbk: e.tensor_copy(
                    bst[:, tbk, :], hp0[:, 0:P].bitcast(BF16)[:, 0:P]),
                    waits=[(sem_pe, v_tt)] + ([(sem_sbz, sem_sbz.n)] if tbk == 0 else []))
            dst = d_ztm if kind == "z" else d_xhtm
            col = n if kind == "z" else (n - 16)
            s.op("sync", lambda e, dst=dst, col=col: e.dma_start(
                dst.rearrange("(b p) ch -> p b ch", p=P)[:, :, col * P:(col + 1) * P], bst[:]),
                waits=[(sem_ve, sem_ve.n)], inc=sem_sbz, inc_by=16)
            if n == 16 and "dbg_bst" in d and "b" in os.environ.get("KDBGSEL", "psmbB"):
                s.op("sync", lambda e: e.dma_start(d["dbg_bst"],
                     bst[:].rearrange("p a b -> p (a b)")), inc=sems["fc"], inc_by=16)

    d_xnfm = None

    xnfm_sb = ar.alloc([P, 8, T], BF16)
    v_xnfm_ld = s.op("sync", lambda e: e.dma_start(
        xnfm_sb[:], d["xnfm_d"].rearrange("(c p) t -> p c t", p=P)),
        waits=[(sem_sxf, sem_sxf.n)], inc=sem_ld, inc_by=16)
    d_xnfm = xnfm_sb
    d_ztm = d["ztm_d"]
    d_xhtm = d["xhtm_d"]
    emit_slab_load2(sj[0], extra_gate=[(sem_ve, v_phA_ve)])
    emit_slab_load2(sj[0] + 1, extra_gate=[(sem_ve, v_phA_ve)])

    for n in range(16):
        phaseB_tile(n, "z")
        emit_slabs(2 if n < 2 else 1, dnA[:, 0:128])
    for n in range(16, 32):
        phaseB_tile(n, "xh")
        emit_slabs(1, dnA[:, 0:128])
        if n == 17:
            for k in range(8):
                s.op("gpsimd", lambda e, k=k: e.dma_start(
                    win[k][:, 0:2080], d["L1_attn_W_in"][k * P:(k + 1) * P, 2304:4384]),
                     waits=[(sem_pe, sem_pe.n)] if k == 0 else [],
                     inc=sem_wg, inc_by=16)
            v_win2[0] = sem_wg.n
    phaseB_tile(32, "bm")
    emit_slabs(1, dnA[:, 0:128])
    phaseB_tile(33, "cm")
    emit_slabs(NSLAB, dnA[:, 0:128])
    if "dbg_bm" in d and "B" in os.environ.get("KDBGSEL", "psmbB"):
        s.op("sync", lambda e: e.dma_start(d["dbg_bm"], BmF[:]), waits=[(sem_ve, sem_ve.n)],
             inc=sems["fc"], inc_by=16)
        s.op("sync", lambda e: e.dma_start(d["dbg_cm"], CmF[:]), inc=sems["fc"], inc_by=16)
    v_phB_ve = sem_ve.n
    v_phB_pe = sem_pe.n
    ar.release()

    ar.mark()
    xh_r = ar.alloc([P, 2, DI], BF16)
    z_r = ar.alloc([P, 2, DI], BF16)
    dacs = ar.alloc([P, NH])
    dacsT = ar.alloc([P, 128])
    dtc_sb = ar.alloc([P, NH])
    dtx = ar.alloc([P, NH])
    sp1 = ar.alloc([P, NH])
    sp2 = ar.alloc([P, NH])
    dA_sb = ar.alloc([P, NH])
    bevac = ar.alloc([P, NH])
    expdacs = ar.alloc([P, NH])
    dT1 = ar.alloc([P, 128])
    dtT = ar.alloc([P, 128])
    dd_tm = ar.alloc([P, NH])
    cdbc = ar.alloc([P, NH])
    cdrow = ar.alloc([P, NH])
    Gmask = ar.alloc([P, P])
    mex = [ar.alloc([P, P]) for _ in range(2)]
    mexp = [ar.alloc([P, P]) for _ in range(2)]
    mh = [ar.alloc([P, P], BF16) for _ in range(2)]
    yoff = ar.alloc([P, DI])
    y_sb = ar.alloc([P, DI])
    zsil = ar.alloc([P, DI])
    ysq = ar.alloc([P, DI])
    ynb = ar.alloc([P, DI], BF16)
    ynfm = ar.alloc([P, 16, 128], BF16)
    BmT = ar.alloc([P, P], BF16)
    xw_sb = ar.alloc([P, DI], BF16)
    s1b = ar.alloc([P, DM], BF16)
    st2 = ar.alloc([P, 4])
    mex_done = {}

    ch_ld = {}
    ch_cons = {}

    def c_load(c):
        waits = [(sem_sbz, sem_sbz.n), (sem_ve, v_phB_ve)] if c <= 1 else []
        if c >= 2:
            waits = [(sem_ve, ch_cons[c - 2])]
        sm = sem_cc[c % 2]
        s.op("sync", lambda e, c=c: e.dma_start(
            xh_r[:, c % 2, :], d_xhtm[c * P:(c + 1) * P, :]), waits=waits, inc=sm, inc_by=16)
        s.op("sync", lambda e, c=c: e.dma_start(
            z_r[:, c % 2, :], d_ztm[c * P:(c + 1) * P, :]), inc=sm, inc_by=16)
        ch_ld[c] = (sm, sm.n)

    c_load(0)
    c_load(1)
    v_ss1_prev = 0
    for c in range(NTB):
        xh_c = xh_r[:, c % 2, :]
        z_c = z_r[:, c % 2, :]
        xh3 = xh_c.rearrange("p (h q) -> p h q", q=HD)
        v_1 = s.op("vector", lambda e, c=c: e.tensor_tensor(dtx[:], dtraw[:, c, :], dtb_bc, ALU.add),
                   waits=[(sem_ld, v_par)] + ([(sem_ve, v_phB_ve), (sem_pe, v_phB_pe)] if c == 0 else []))
        s.op("scalar", lambda e: e.activation(sp1[:], dtx[:], AF.Abs), waits=[(sem_ve, v_1)])
        s.op("scalar", lambda e: e.activation(sp1[:], sp1[:], AF.Exp, scale=-1.0))
        v_sp = s.op("scalar", lambda e: e.activation(sp1[:], sp1[:], AF.Ln, bias=cst[:, 2:3]),
                    waits=[(sem_ve, v_one)])
        s.op("vector", lambda e: e.tensor_scalar(sp2[:], dtx[:], 0.0, None, ALU.max))
        v_dt = s.op("vector", lambda e: e.tensor_tensor(dtc_sb[:], sp1[:], sp2[:], ALU.add),
                    waits=[(sem_act, v_sp)])
        v_dA = s.op("vector", lambda e: e.tensor_tensor(dA_sb[:], dtc_sb[:], A_bc, ALU.mult))
        v_cs = s.op("tensor", lambda e: e.matmul(hp0[:, 0:NH], lt_sb[:], dA_sb[:],
                                                 start=True, stop=True),
                    waits=[(sem_ve, v_dA)], inc=sem_pe)
        v_dacs = s.op("vector", lambda e: e.tensor_copy(dacs[:], hp0[:, 0:NH]),
                      waits=[(sem_pe, v_cs)])
        v_tT = s.op("tensor", lambda e: e.transpose(hp1[0:32, 0:P], dacs[:], ident[:]),
                    waits=[(sem_ve, v_dacs)], inc=sem_pe)
        v_dacsT = s.op("vector", lambda e: e.tensor_copy(dacsT[0:32, :], hp1[0:32, 0:P]),
                       waits=[(sem_pe, v_tT)])
        v_ed = s.op("scalar", lambda e: e.activation(expdacs[:], dacs[:], AF.Exp),
                    waits=[(sem_ve, v_dacs)])
        v_ln = s.op("scalar", lambda e: e.activation(bevac[:], dtc_sb[:], AF.Ln),
                    waits=[(sem_ve, v_dt)])
        v_be = s.op("vector", lambda e: e.tensor_tensor(bevac[:], bevac[:], dacs[:], ALU.subtract),
                    waits=[(sem_act, v_ln)])
        v_d1 = s.op("vector", lambda e: e.tensor_scalar(dT1[0:32, :], dacsT[0:32, :],
                                                        dacsT[0:32, 127:128], None, ALU.subtract),
                    waits=[(sem_ve, v_dacsT)])
        v_d2 = s.op("scalar", lambda e: e.activation(dT1[0:32, :], dT1[0:32, :], AF.Exp, scale=-1.0),
                    waits=[(sem_ve, v_d1)])
        v_t3 = s.op("tensor", lambda e: e.transpose(hp1[0:32, 64:192], dtc_sb[:], ident[:]),
                    waits=[(sem_ve, v_be)], inc=sem_pe)
        v_dtT = s.op("vector", lambda e: e.tensor_copy(dtT[0:32, :], hp1[0:32, 64:192]),
                     waits=[(sem_pe, v_t3)])
        v_dd = s.op("vector", lambda e: e.tensor_tensor(dT1[0:32, :], dT1[0:32, :], dtT[0:32, :],
                                                        ALU.mult),
                    waits=[(sem_act, v_d2)])
        v_t4 = s.op("tensor", lambda e: e.transpose(hp1[:, 96:128], dT1[0:32, :], ident[0:32, 0:32]),
                    waits=[(sem_ve, v_dd)], inc=sem_pe)
        v_ddtm = s.op("vector", lambda e: e.tensor_copy(dd_tm[:], hp1[0:P, 96:128]),
                      waits=[(sem_pe, v_t4)])
        v_t5 = s.op("tensor", lambda e: e.matmul(hp1[0:1, 128:160], ident[:, 127:128], dacs[:],
                                                 start=True, stop=True), inc=sem_pe)
        v_cd1 = s.op("scalar", lambda e: e.activation(cdrow[0:1, :], hp1[0:1, 128:160], AF.Exp),
                     waits=[(sem_pe, v_t5)])
        v_t6 = s.op("tensor", lambda e: e.matmul(hp1[:, 160:192], onesf, cdrow[0:1, :],
                                                 start=True, stop=True),
                    waits=[(sem_act, v_cd1)], inc=sem_pe)
        v_cdbc = s.op("vector", lambda e: e.tensor_copy(cdbc[:], hp1[:, 160:192]),
                      waits=[(sem_pe, v_t6)])
        if c == 0 and "dbg_d" in d and "m" in os.environ.get("KDBGSEL", "psmbB"):
            s.op("sync", lambda e: e.dma_start(d["dbg_d"][:, 160:192], dacs[:]),
                 waits=[(sem_ve, v_dacs)], inc=sems["fc"], inc_by=16)
            s.op("sync", lambda e: e.dma_start(d["dbg_d"][:, 192:224], dd_tm[:]),
                 waits=[(sem_ve, v_ddtm)], inc=sems["fc"], inc_by=16)
            s.op("sync", lambda e: e.dma_start(d["dbg_d"][:, 224:256], cdbc[:]),
                 waits=[(sem_ve, v_cdbc)], inc=sems["fc"], inc_by=16)
            s.op("sync", lambda e: e.dma_start(d["dbg_d"][:, 96:128], dtc_sb[:]),
                 waits=[(sem_ve, v_dt)], inc=sems["fc"], inc_by=16)
        v_g = s.op("tensor", lambda e, c=c: e.matmul(dnC[:, 0:P], BmF[:, c * P:(c + 1) * P],
                                                     CmF[:, c * P:(c + 1) * P],
                                                     start=True, stop=True),
                   waits=[(sem_ve, sem_ve.n)], inc=sem_pe)
        v_gm = s.op("vector", lambda e: e.tensor_tensor(Gmask[:], dnC[:, 0:P], lt_sb[:], ALU.mult),
                    waits=[(sem_pe, v_g)])
        v_sbf = s.op("vector", lambda e: e.tensor_copy(state_bf[:], state[:]),
                     waits=[(sem_ve, v_zero)] if c == 0 else [])
        if c == 1 and "dbg_st" in d and "s" in os.environ.get("KDBGSEL", "psmbB"):
            s.op("sync", lambda e: e.dma_start(d["dbg_st"], state_bf[:]),
                 waits=[(sem_ve, v_sbf)], inc=sems["fc"], inc_by=16)
        offmm = []
        for jj in range(4):
            ps = (dnA if jj < 2 else dnB)
            def mmo(e, jj=jj, ps=ps, c=c):
                return e.matmul(ps[:, (jj % 2) * 512:(jj % 2 + 1) * 512],
                                CmF[:, c * P:(c + 1) * P],
                                state_bf[:, (jj) * 512:(jj + 1) * 512],
                                start=True, stop=True, skip_group_check=True)
            wts = [(sem_ve, v_sbf)]
            if c > 0:
                wts.append((sem_ve, sem_ve.n))
            offmm.append(s.op("tensor", mmo, waits=wts if jj == 0 else [],
                              inc=sem_pe if jj == 3 else None))
        v_off = offmm[-1]
        ed3a = expdacs[:, 0:16].unsqueeze(-1).broadcast_to((P, 16, HD))
        ed3b = expdacs[:, 16:32].unsqueeze(-1).broadcast_to((P, 16, HD))
        v_ev1 = s.op("vector", lambda e: e.tensor_tensor(
            yoff[:, 0:1024].rearrange("p (h q) -> p h q", q=HD), dnA[:, 0:1024].rearrange("p (h q) -> p h q", q=HD),
            ed3a, ALU.mult), waits=[(sem_pe, v_off), (sem_act, sem_act.n)])
        v_ev2 = s.op("vector", lambda e: e.tensor_tensor(
            yoff[:, 1024:2048].rearrange("p (h q) -> p h q", q=HD), dnB[:, 0:1024].rearrange("p (h q) -> p h q", q=HD),
            ed3b, ALU.mult))
        ydmm = []
        for h in range(NH):
            pr = (hp0[:, 256:384] if h % 2 == 0 else hp1[:, 256:384])
            v_bc = s.op("tensor", lambda e, h=h, pr=pr: e.matmul(
                pr[:, 0:P], eh_sb[:, h * P:(h + 1) * P], dacsT[0:32, :],
                start=True, stop=True),
                waits=[(sem_ve, sem_ve.n)] if h < 2 else [(sem_ve, mex_done[h - 2])],
                inc=sem_pe)
            v_mx = s.op("vector", lambda e, h=h, pr=pr: e.tensor_scalar(
                mex[h % 2][:], pr[:, 0:P], bevac[:, h:h + 1], 80.0, ALU.add, ALU.min),
                waits=[(sem_pe, v_bc)])
            mex_done[h] = v_mx
            v_me = s.op("scalar", lambda e, h=h: e.activation(mexp[h % 2][:], mex[h % 2][:], AF.Exp),
                        waits=[(sem_ve, v_mx)])
            v_mh = s.op("vector", lambda e, h=h: e.tensor_tensor(mh[h % 2][:], mexp[h % 2][:],
                                                                 Gmask[:], ALU.mult),
                        waits=[(sem_act, v_me), (sem_ve, v_gm)])
            ps = (dnA if h < 16 else dnB)
            def mmy(e, h=h, ps=ps, xh3=xh3):
                return e.matmul(ps[:, (h % 16) // 8 * 512 + (h % 8) * 64:
                                   (h % 16) // 8 * 512 + (h % 8) * 64 + 64],
                                mh[h % 2][:], xh3[:, h, :],
                                start=(h % 8 == 0), stop=(h % 8 == 7), skip_group_check=True)
            wts = [(sem_ve, v_mh)]
            if h % 8 == 0:
                wts.append((sem_ve, v_ev2))
                wts.append(ch_ld[c])
            ydmm.append(s.op("tensor", mmy, waits=wts, inc=sem_pe if h == NH - 1 else None))
        v_yd = ydmm[-1]
        v_y1 = s.op("vector", lambda e: e.tensor_tensor(y_sb[:, 0:1024], dnA[:, 0:1024],
                                                        yoff[:, 0:1024], ALU.add),
                    waits=[(sem_pe, v_yd)])
        v_y2 = s.op("vector", lambda e: e.tensor_tensor(y_sb[:, 1024:2048], dnB[:, 0:1024],
                                                        yoff[:, 1024:2048], ALU.add))
        d3 = D_bc.unsqueeze(-1).broadcast_to((P, NH, HD))
        v_y3 = s.op("vector", lambda e, xh3=xh3: e.tensor_tensor(
            yoff[:].rearrange("p (h q) -> p h q", q=HD), xh3, d3, ALU.mult))
        v_y4 = s.op("vector", lambda e: e.tensor_tensor(y_sb[:], y_sb[:], yoff[:], ALU.add))
        v_zs = s.op("scalar", lambda e, z_c=z_c: e.activation(zsil[:], z_c, AF.Silu),
                    waits=[ch_ld[c]])
        v_y5 = s.op("vector", lambda e: e.tensor_tensor(y_sb[:], y_sb[:], zsil[:], ALU.mult),
                    waits=[(sem_act, v_zs)])
        s.op("scalar", lambda e: e.activation(ysq[:], y_sb[:], AF.Square, accum_out=st2[:, 0:1]),
             waits=[(sem_ve, v_y5)])
        s.op("scalar", lambda e: e.activation(st2[:, 0:1], st2[:, 0:1], AF.Ln, bias=cst[:, 0:1],
                                              scale=1.0 / DI))
        v_rst = s.op("scalar", lambda e: e.activation(st2[:, 0:1], st2[:, 0:1], AF.Exp, scale=-0.5))
        v_yn = s.op("vector", lambda e: e.scalar_tensor_tensor(ynb[:], y_sb[:], st2[:, 0:1],
                                                               normw_bc, ALU.mult, ALU.mult),
                    waits=[(sem_act, v_rst)])
        for k in range(16):
            v_tt = s.op("tensor", lambda e, k=k: e.transpose(
                hp0[:, 384:512].bitcast(BF16)[:, 0:P], ynb[:, k * P:(k + 1) * P], identb[:]),
                waits=[(sem_ve, v_yn if k == 0 else sem_ve.n)], inc=sem_pe)
            s.op("vector", lambda e, k=k: e.tensor_copy(ynfm[:, k, :],
                                                        hp0[:, 384:512].bitcast(BF16)[:, 0:P]),
                 waits=[(sem_pe, v_tt)])
        v_tr_done = sem_ve.n
        womm = None
        for k in range(16):
            for j in range(2):
                def mmw2(e, k=k, j=j):
                    return e.matmul(dnC[:, j * 512:(j + 1) * 512], ynfm[:, k, :],
                                    Wout_sb[:, k, j * 512:(j + 1) * 512],
                                    start=(k == 0), stop=(k == 15), skip_group_check=True)
                wts = []
                if k == 0 and j == 0:
                    wts = [(sem_ve, v_tr_done), (sem_wg, v_wout)]
                if k == 15 and j == 1:
                    womm = s.op("tensor", mmw2, waits=wts, inc=sem_pe)
                else:
                    s.op("tensor", mmw2, waits=wts)
        v_s1 = s.op("vector", lambda e: e.tensor_copy(s1b[:], dnC[:, 0:1024]),
                    waits=[(sem_pe, womm), (sem_ss1, v_ss1_prev)])
        v_ss1_prev = s.op("sync", lambda e, c=c: e.dma_start(
            d["s1_d"][c * P:(c + 1) * P, :], s1b[:]),
            waits=[(sem_ve, v_s1)], inc=sem_ss1, inc_by=16)
        v_bt = s.op("tensor", lambda e, c=c: e.transpose(
            hp0[:, 256:384].bitcast(BF16)[:, 0:P], BmF[:, c * P:(c + 1) * P], identb[:]),
            waits=[(sem_ve, sem_ve.n)], inc=sem_pe)
        v_bmt = s.op("vector", lambda e: e.tensor_copy(BmT[:], hp0[:, 256:384].bitcast(BF16)[:, 0:P]),
                     waits=[(sem_pe, v_bt)])
        dd3 = dd_tm.unsqueeze(-1).broadcast_to((P, NH, HD))
        v_xw = s.op("vector", lambda e, xh3=xh3: e.tensor_tensor(
            xw_sb[:].rearrange("p (h q) -> p h q", q=HD), xh3, dd3, ALU.mult),
            waits=[(sem_ve, v_ddtm)])
        smm = []
        for jj in range(4):
            ps = (dnA if jj < 2 else dnB)
            def mms2(e, jj=jj, ps=ps):
                return e.matmul(ps[:, (jj % 2) * 512:(jj % 2 + 1) * 512], BmT[:],
                                xw_sb[:, jj * 512:(jj + 1) * 512],
                                start=True, stop=True, skip_group_check=True)
            wts = []
            if jj == 0:
                wts = [(sem_ve, max(v_xw, v_bmt, v_y4))]
            smm.append(s.op("tensor", mms2, waits=wts, inc=sem_pe if jj == 3 else None))
        v_smm = smm[-1]
        cd3 = cdbc.unsqueeze(-1).broadcast_to((P, NH, HD))
        v_su1 = s.op("vector", lambda e: e.tensor_tensor(
            state[:].rearrange("p (h q) -> p h q", q=HD),
            state[:].rearrange("p (h q) -> p h q", q=HD), cd3, ALU.mult),
            waits=[(sem_ve, v_cdbc), (sem_pe, v_smm)])
        s.op("vector", lambda e: e.tensor_tensor(state[:, 0:1024], state[:, 0:1024],
                                                 dnA[:, 0:1024], ALU.add))
        v_sup = s.op("vector", lambda e: e.tensor_tensor(state[:, 1024:2048], state[:, 1024:2048],
                                                         dnB[:, 0:1024], ALU.add))
        ch_cons[c] = v_sup
        if c + 2 < NTB:
            c_load(c + 2)
    v_phC_ve = sem_ve.n
    v_phC_pe = sem_pe.n
    ar.release()
    ar.release()

    ar.mark()
    wgT = ar.alloc([P, 8, 2816], BF16)
    wuT = ar.alloc([P, 8, 2816], BF16)
    wdT = ar.alloc([P, 22, 1024], BF16)
    xq8 = ar.alloc([P, 2, 8, 128], BF16)
    sg = ar.alloc([P, 2, 512])
    htm = ar.alloc([P, 2816])
    hq_tm = ar.alloc([P, 2, 2816], BF16)
    hqf = ar.alloc([P, 22, 128], BF16)
    dq_absx = ar.alloc([P, 1408])
    dq_t1 = ar.alloc([P, 1408], BF16)
    dq_q = ar.alloc([P, 1408])
    dq_bm = ar.alloc([P, 88])
    dq_sc = ar.alloc([P, 88])
    dq_rs = ar.alloc([P, 88])
    x1r = ar.alloc([P, 2, DM])
    s1r = ar.alloc([P, 2, DM], BF16)
    gsb = ar.alloc([P, DM])
    xo = ar.alloc([P, DM])

    wld = {}

    def d_wload(tt):
        waits = [(sem_wq, sem_wq.n), (sem_ve, v_phC_ve), (sem_pe, v_phC_pe),
                 (sem_act, sem_act.n), (sem_ss1, v_ss1_prev)]
        s.op("sync", lambda e: e.dma_start(
            wgT[:], d["wgt_d"].rearrange("(c p) n -> p c n", p=P)), waits=waits, inc=sem_wq, inc_by=16)
        s.op("sync", lambda e: e.dma_start(
            wuT[:], d["wut_d"].rearrange("(c p) n -> p c n", p=P)), inc=sem_wq, inc_by=16)
        wld["gu"] = sem_wq.n
        s.op("sync", lambda e: e.dma_start(
            wdT[:], d["wdt_d"].rearrange("(c p) n -> p c n", p=P)), inc=sem_wq, inc_by=16)
        wld["d"] = sem_wq.n

    d_pe_after = {}
    d_wload(0)
    xq_ld = {}
    x1_ld = {}

    xs_ld = {}

    def d_xqload_xq(tbk):
        if tbk <= 1:
            waits = [(sem_sxq, sem_sxq.n), (sem_ss1, sem_ss1.n)]
        else:
            waits = [(sem_pe, fwd_pe[tbk - 2])]
        s.op("sync", lambda e, tbk=tbk: e.dma_start(
            xq8[:, tbk % 2, :, :],
            d["xqfm_d"].rearrange("(c p) t -> p c t", p=P)[:, :, tbk * P:(tbk + 1) * P]),
            waits=waits, inc=sem_xq, inc_by=16)
        xq_ld[tbk] = sem_xq.n

    def d_xqload_xs(tbk):
        waits = [] if tbk <= 1 else [(sem_ve, d_cons[tbk - 2])]
        s.op("sync", lambda e, tbk=tbk: e.dma_start(
            x1r[:, tbk % 2, :], d["x1_l1"][tbk * P:(tbk + 1) * P, :]),
            waits=waits, inc=sem_xq, inc_by=16)
        s.op("sync", lambda e, tbk=tbk: e.dma_start(
            s1r[:, tbk % 2, :], d["s1_d"][tbk * P:(tbk + 1) * P, :]), inc=sem_xq, inc_by=16)
        xs_ld[tbk] = sem_xq.n

    def d_xqload(tbk):
        d_xqload_xq(tbk)
        d_xqload_xs(tbk)

    d_tbk_pe = {}
    d_cons = {}
    d_xqload(0)
    d_xqload(1)
    prev_x2_st = [0]
    CH = [(0, 512), (512, 512), (1024, 512), (1536, 512), (2048, 512), (2304, 512)]
    CH = [(0, 512), (512, 512), (1024, 512), (1536, 512), (2048, 512), (2560, 256)]
    evac_done = {}
    quant_done = {}
    fwd_pe = {}

    def d_fwd(tbk):
        buf = tbk % 2
        for ci, (c0, cw) in enumerate(CH):
            gps = dnA[:, (ci % 2) * 512:(ci % 2) * 512 + cw]
            ups = dnB[:, (ci % 2) * 512:(ci % 2) * 512 + cw]
            for (mat, ps) in ((wgT, gps), (wuT, ups)):
                for k in range(8):
                    def mmg(e, mat=mat, ps=ps, k=k, c0=c0, cw=cw, buf=buf):
                        return e.matmul(ps, xq8[:, buf, k, :], mat[:, k, c0:c0 + cw],
                                        start=(k == 0), stop=(k == 7), skip_group_check=True)
                    wts = []
                    if k == 0:
                        if ci >= 2:
                            wts.append((sem_ve, evac_done[(tbk, ci - 2)]))
                        elif tbk > 0:
                            wts.append((sem_ve, evac_done[(tbk - 1, len(CH) - 2 + ci)]))
                        if ci == 0 and mat is wgT:
                            wts += [(sem_xq, xq_ld[tbk]), (sem_wq, wld["gu"])]
                    if k == 7 and mat is wuT:
                        v_mm = s.op("tensor", mmg, waits=wts, inc=sem_pe)
                    else:
                        s.op("tensor", mmg, waits=wts)
            v_sg = s.op("scalar", lambda e, ci=ci, gps=gps, cw=cw: e.activation(
                sg[:, ci % 2, 0:cw], gps, AF.Silu), waits=[(sem_pe, v_mm)])
            evac_done[(tbk, ci)] = s.op("vector", lambda e, ci=ci, ups=ups, c0=c0, cw=cw, buf=buf:
                e.tensor_tensor(htm[:, c0:c0 + cw], sg[:, ci % 2, 0:cw], ups, ALU.mult),
                waits=[(sem_act, v_sg)])
        fwd_pe[tbk] = sem_pe.n
        d_tbk_pe[tbk] = sem_pe.n
        for piece in range(2):
            p0 = piece * 1408
            dqb = {"absx": dq_absx, "t1": dq_t1,
                   "t2": hq_tm[:, buf, p0:p0 + 1408], "q": dq_q,
                   "bm": dq_bm, "sc": dq_sc, "rs": dq_rs}
            v_hq = emit_quant(s, sems, dqb, htm[:, p0:p0 + 1408],
                              hq_tm[:, buf, p0:p0 + 1408], 1408,
                              src_ready=[(sem_ve, sem_ve.n)])
        quant_done[tbk] = v_hq

    def d_bwd(tbk):
        buf = tbk % 2
        for n in range(22):
            v_tq = s.op("tensor", lambda e, n=n, buf=buf: e.transpose(
                hp0[:, 128:256].bitcast(BF16)[:, 0:P], hq_tm[:, buf, n * P:(n + 1) * P], identb[:]),
                waits=[(sem_ve, quant_done[tbk] if n == 0 else sem_ve.n)], inc=sem_pe)
            s.op("vector", lambda e, n=n: e.tensor_copy(hqf[:, n, :],
                                                        hp0[:, 128:256].bitcast(BF16)[:, 0:P]),
                 waits=[(sem_pe, v_tq)])
        v_hqf = sem_ve.n
        dmm = None
        for n in range(22):
            for j in range(2):
                def mmd2(e, n=n, j=j):
                    return e.matmul(dnC[:, j * 512:(j + 1) * 512], hqf[:, n, :],
                                    wdT[:, n, j * 512:(j + 1) * 512],
                                    start=(n == 0), stop=(n == 21), skip_group_check=True)
                wts = []
                if n == 0 and j == 0:
                    wts = [(sem_ve, v_hqf), (sem_act, sem_act.n), (sem_wq, wld["d"])]
                if n == 21 and j == 1:
                    dmm = s.op("tensor", mmd2, waits=wts, inc=sem_pe)
                else:
                    s.op("tensor", mmd2, waits=wts)
        v_gs = s.op("scalar", lambda e: e.activation(gsb[:], dnC[:, 0:1024], AF.Silu),
                    waits=[(sem_pe, dmm)])
        v_t3 = s.op("vector", lambda e, buf=buf: e.tensor_tensor(gsb[:], s1r[:, buf, :], gsb[:],
                                                                 ALU.mult),
                    waits=[(sem_act, v_gs), (sem_xq, xs_ld[tbk])])
        v_xo = s.op("vector", lambda e, buf=buf: e.tensor_tensor(xo[:], x1r[:, buf, :], gsb[:],
                                                                 ALU.add),
                    waits=[(sem_st, prev_x2_st[0])] if prev_x2_st[0] else [])
        d_cons[tbk] = v_xo
        prev_x2_st[0] = s.op("sync", lambda e, tbk=tbk: e.dma_start(
            d["x2"][tbk * P:(tbk + 1) * P, :], xo[:]),
            waits=[(sem_ve, v_xo)], inc=sem_st, inc_by=16)

    d_fwd(0)
    for tbk in range(1, NTB):
        if tbk + 1 < NTB:
            d_xqload_xq(tbk + 1)
        d_fwd(tbk)
        d_bwd(tbk - 1)
        if tbk + 1 < NTB:
            d_xqload_xs(tbk + 1)
    d_bwd(NTB - 1)
    ar.release()


def build_program(which="l0"):
    nc = bass.Bass("TRN2", target_bir_lowering=False, detect_race_conditions=False)
    d = {}
    d["u"] = nc.dram_tensor("u", [T, DM], F32, kind="ExternalInput").ap()
    l0_specs = [
            ("L0_attn_dt", [512]), ("L0_attn_A", [512]), ("L0_attn_Bmat", [1024, 512]),
            ("L0_attn_C", [512, 1024]), ("L0_attn_D", [1024]), ("L0_attn_norm_w", [1024]),
            ("L0_ffn_gate_w", [1024, 8]), ("L0_ffn_w1", [8, 1024, 4096]), ("L0_ffn_b1", [8, 4096]),
            ("L0_ffn_w2", [8, 4096, 1024]), ("L0_ffn_b2", [8, 1024]),
            ("L0_ffn_gamma", [1024]), ("L0_ffn_beta", [1024])]
    l1_specs = [
            ("L1_attn_W_in", [1024, 4384]), ("L1_attn_conv_w", [2304, 4]),
            ("L1_attn_conv_b", [2304]), ("L1_attn_dt_bias", [32]), ("L1_attn_A_log", [32]),
            ("L1_attn_D", [32]), ("L1_attn_norm_w", [2048]), ("L1_attn_W_out", [2048, 1024]),
            ("L1_ffn_w_gate", [2816, 1024]), ("L1_ffn_w_up", [2816, 1024]),
            ("L1_ffn_w_down", [1024, 2816])]
    specs = {"l0": l0_specs, "l1": l1_specs, "both": l0_specs + l1_specs}[which]
    for name, shape in specs:
        d[name] = nc.dram_tensor(name, shape, F32, kind="ExternalInput").ap()
    d["c_ident"] = nc.dram_tensor("c_ident", [P, P], F32, kind="ExternalInput").ap()
    d["c_onesf"] = nc.dram_tensor("c_onesf", [1, P], F32, kind="ExternalInput").ap()
    d["c_lt"] = nc.dram_tensor("c_lt", [P, P], F32, kind="ExternalInput").ap()
    d["c_eh"] = nc.dram_tensor("c_eh", [32, 32 * P], F32, kind="ExternalInput").ap()
    d["c_identb"] = nc.dram_tensor("c_identb", [P, P], BF16, kind="ExternalInput").ap()
    d["c_onesb"] = nc.dram_tensor("c_onesb", [1, P], BF16, kind="ExternalInput").ap()
    _ik = "ExternalOutput" if os.environ.get("KDBG") else "Internal"
    d["xn2fm_d"] = nc.dram_tensor("xn2fm_d", [DM, T], BF16, kind=_ik).ap()
    d["xn_d"] = nc.dram_tensor("xn_d", [T, DM], BF16, kind=_ik).ap()
    d["xscan_d"] = nc.dram_tensor("xscan_d", [512, T], BF16, kind=_ik).ap()
    if which == "l0":
        d["x1"] = nc.dram_tensor("x1", [T, DM], F32, kind="ExternalOutput").ap()
    elif which == "both":
        d["x1"] = nc.dram_tensor("x1", [T, DM], F32, kind="Internal").ap()
        d["x1_l1"] = d["x1"]
        d["xnfm_d"] = nc.dram_tensor("xnfm_d", [DM, T], BF16, kind="Internal").ap()
        d["xqfm_d"] = nc.dram_tensor("xqfm_d", [DM, T], BF16, kind="Internal").ap()
        d["ztm_d"] = nc.dram_tensor("ztm_d", [T, 2048], BF16, kind="Internal").ap()
        d["xhtm_d"] = nc.dram_tensor("xhtm_d", [T, 2048], BF16, kind="Internal").ap()
        d["s1_d"] = nc.dram_tensor("s1_d", [T, DM], BF16, kind="Internal").ap()
        d["wgt_d"] = nc.dram_tensor("wgt_d", [1024, 2816], BF16, kind="Internal").ap()
        d["wut_d"] = nc.dram_tensor("wut_d", [1024, 2816], BF16, kind="Internal").ap()
        d["wdt_d"] = nc.dram_tensor("wdt_d", [2816, 1024], BF16, kind="Internal").ap()
        d["x2"] = nc.dram_tensor("x2", [T, DM], F32, kind="ExternalOutput").ap()
    else:
        d["x1_l1"] = nc.dram_tensor("x1_l1", [T, DM], F32, kind="ExternalInput").ap()
        d["xnfm_d"] = nc.dram_tensor("xnfm_d", [DM, T], BF16, kind="Internal").ap()
        d["xqfm_d"] = nc.dram_tensor("xqfm_d", [DM, T], BF16, kind="Internal").ap()
        d["ztm_d"] = nc.dram_tensor("ztm_d", [T, 2048], BF16, kind="Internal").ap()
        d["xhtm_d"] = nc.dram_tensor("xhtm_d", [T, 2048], BF16, kind="Internal").ap()
        d["s1_d"] = nc.dram_tensor("s1_d", [T, DM], BF16, kind="Internal").ap()
        d["wgt_d"] = nc.dram_tensor("wgt_d", [1024, 2816], BF16, kind="Internal").ap()
        d["wut_d"] = nc.dram_tensor("wut_d", [1024, 2816], BF16, kind="Internal").ap()
        d["wdt_d"] = nc.dram_tensor("wdt_d", [2816, 1024], BF16, kind="Internal").ap()
        d["x2"] = nc.dram_tensor("x2", [T, DM], F32, kind="ExternalOutput").ap()
        if os.environ.get("KDBG2"):
            d["dbg_d"] = nc.dram_tensor("dbg_d", [P, 256], F32, kind="ExternalOutput").ap()
            d["dbg_bm"] = nc.dram_tensor("dbg_bm", [P, T], BF16, kind="ExternalOutput").ap()
            d["dbg_cm"] = nc.dram_tensor("dbg_cm", [P, T], BF16, kind="ExternalOutput").ap()
            d["dbg_xh"] = nc.dram_tensor("dbg_xh", [P, T], BF16, kind="ExternalOutput").ap()
            d["dbg_bst"] = nc.dram_tensor("dbg_bst", [P, T], BF16, kind="ExternalOutput").ap()
            d["dbg_st"] = nc.dram_tensor("dbg_st", [P, T], BF16, kind="ExternalOutput").ap()

    from contextlib import ExitStack
    with ExitStack() as stack:
        en = stack.enter_context
        ident_t = en(nc.sbuf_tensor("ident_sb", [P, P], F32))
        identb_t = en(nc.sbuf_tensor("identb_sb", [P, P], BF16))
        onesb_t = en(nc.sbuf_tensor("onesb_sb", [1, P], BF16))
        if which == "l0":
            cw_t = en(nc.sbuf_tensor("cw_sb", [P, NTB, 8], F32))
            accb_t = en(nc.sbuf_tensor("accb", [P, NTB, DM], BF16))
            b1c_t = en(nc.sbuf_tensor("b1c", [P, 8, 32], F32))
            w2b_t = en(nc.sbuf_tensor("w2b", [P, 32, 1024], BF16))
            b2r_t = en(nc.sbuf_tensor("b2r", [1, 2, 1024], BF16))
            xnq_t = en(nc.sbuf_tensor("xnq", [P, 8, 512], BF16))
            AR_WORDS = 24800
        else:
            AR_WORDS = 52800
        ar_t = en(nc.sbuf_tensor("arena", [P, AR_WORDS], F32))
        cst_t = en(nc.sbuf_tensor("cst", [P, 4], F32))
        dnA = en(nc.psum_tensor("dnA", [P, 1024], F32))
        dnB = en(nc.psum_tensor("dnB", [P, 1024], F32))
        dnC = en(nc.psum_tensor("dnC", [P, 1024], F32))
        hp0 = en(nc.psum_tensor("hp0", [P, 512], F32))
        hp1 = en(nc.psum_tensor("hp1", [P, 512], F32))
        dma_sems = ("ld", "lxa", "lxb", "sxn", "sx2", "sscan", "xq", "w1s", "w2s",
                    "fc", "fla", "flb", "st", "wqh")
        sems = {k: Sem(en(nc.semaphore(k)), dma=True) for k in dma_sems}
        sems.update({k: Sem(en(nc.semaphore(k))) for k in ("pe", "act", "ve", "gp")})
        block = en(nc.Block())

        s = Sched(self_serial={"vector": sems["ve"], "scalar": sems["act"],
                               "gpsimd": sems["gp"]})
        ctx = dict(nc=nc, s=s, dram=d, ar=Arena(ar_t[:], AR_WORDS),
                   ident=ident_t[:], identb=identb_t[:], onesb=onesb_t[:],
                   dnA=dnA[:], dnB=dnB[:], dnC=dnC[:], hp0=hp0[:], hp1=hp1[:],
                   sems=sems, cst=cst_t[:])
        if which == "l0":
            ctx.update(cw_sb=cw_t[:], accb=accb_t[:], b1c=b1c_t[:],
                       w2b=w2b_t[:], b2r=b2r_t[:], xnq=xnq_t[:])
        v_c0 = s.op("vector", lambda e: e.memset(cst_t[:, 0:1], 1e-5), inc=sems["ve"])
        v_c1 = s.op("vector", lambda e: e.memset(cst_t[:, 1:2], 1e-6), inc=sems["ve"])
        s.wait("scalar", sems["ve"], v_c1)
        s.op("sync", lambda e: e.dma_start(ctx["ident"], d["c_ident"]), inc=sems["ld"], inc_by=16)
        s.op("sync", lambda e: e.dma_start(ctx["identb"], d["c_identb"]), inc=sems["ld"], inc_by=16)
        s.op("sync", lambda e: e.dma_start(ctx["onesb"], d["c_onesb"]), inc=sems["ld"], inc_by=16)
        s.wait("tensor", sems["ld"], 48)

        if which == "l0":
            build_l0(ctx)
        elif which == "l1":
            build_l1(ctx)
        else:
            ctx["ar"].mark()
            build_l0(ctx)
            ctx["ar"].release()
            ctx["l1_gate"] = [(sems["ve"], sems["ve"].n), (sems["pe"], sems["pe"].n),
                              (sems["st"], sems["st"].n)]
            build_l1(ctx)
        for k in dma_sems:
            if sems[k].n:
                s.wait("sync", sems[k], sems[k].n)
        s.build(block)
    return nc


def kernel(**inputs):
    consts = _consts()
    wdict = {k: np.ascontiguousarray(v) for k, v in inputs.items() if k != "L0_attn_u"}
    nc = build_program("both")
    in_maps = []
    for b in range(8):
        m = {"u": np.ascontiguousarray(inputs["L0_attn_u"][b])}
        m.update(wdict)
        m.update(consts)
        in_maps.append(m)
    res = run_bass_kernel_spmd(nc, in_maps, core_ids=list(range(8)))
    return np.stack([np.asarray(r["x2"], dtype=np.float32) for r in res.results], axis=0)


# version 08
# speedup vs baseline: 1.1582x; 1.0013x over previous
import os
import numpy as np
import ml_dtypes
import concourse.bass as bass
import concourse.mybir as mybir
from concourse.bass_utils import run_bass_kernel_spmd

F32 = mybir.dt.float32
BF16 = mybir.dt.bfloat16
AF = mybir.ActivationFunctionType
ALU = mybir.AluOpType
AX = mybir.AxisListType

P = 128
T = 2048
DM = 1024
NTB = T // P
ENGINES = ("sync", "tensor", "vector", "scalar", "gpsimd")


class Sem:

    def __init__(self, handle, dma=False):
        self.h = handle
        self.n = 0
        self.dma = dma
        self.events = []

    def inc(self, k=1, gated=False):
        self.n += k
        if self.dma:
            self.events.append((self.n, gated))
        return self.n

    def resolve(self, v):
        if v <= 0:
            return 0
        if not self.dma:
            return v
        idx = None
        for i, (val, _) in enumerate(self.events):
            if val >= v:
                idx = i
                break
        assert idx is not None, (v, self.events[-5:])
        j = idx + 1
        while j < len(self.events) and not self.events[j][1]:
            j += 1
        return self.events[j - 1][0]


class Sched:

    def __init__(self, self_serial=None):
        self.q = {e: [] for e in ENGINES}
        self.self_serial = self_serial or {}

    def op(self, eng, fn, waits=(), inc=None, inc_by=1):
        ss = self.self_serial.get(eng)
        if ss is not None and inc is not None and inc is not ss:
            ss = None
        if ss is not None:
            pre = ss.n
            planned = ss.inc(1)
            def run(e, fn=fn, waits=tuple(waits), ss=ss, pre=pre):
                for sm, v in waits:
                    rv = sm.resolve(v)
                    if rv > 0:
                        e.wait_ge(sm.h, rv)
                ins = fn(e)
                if pre > 0:
                    ins._wait_ge(ss.h, pre)
                ins.then_inc(ss.h, 1)
            self.q[eng].append(run)
            return planned
        planned = inc.inc(inc_by, gated=bool(waits)) if inc is not None else None

        def run(e, fn=fn, waits=tuple(waits), inc=inc, inc_by=inc_by):
            first = True
            ws = []
            for sm, v in waits:
                rv = sm.resolve(v)
                if rv > 0:
                    ws.append((sm, rv))
            for sm, rv in ws[1:]:
                e.wait_ge(sm.h, rv)
            ins = fn(e)
            if ws:
                ins._wait_ge(ws[0][0].h, ws[0][1])
            if inc is not None:
                ins.then_inc(inc.h, inc_by)
        self.q[eng].append(run)
        return planned

    def wait(self, eng, sem, val):
        self.q[eng].append(lambda e, sm=sem, v=val: e.wait_ge(sm.h, sm.resolve(v)))

    def build(self, block):
        for eng in ENGINES:
            qe = self.q[eng]
            if not qe:
                continue

            def body(engobj, qe=qe):
                for fn in qe:
                    fn(engobj)
            getattr(block, eng)(body)


class Arena:
    def __init__(self, tensor, words):
        self.t = tensor
        self.words = words
        self.top = 0
        self.marks = []
        self.peak = 0

    def _take(self, w):
        w = (w + 15) & ~15
        assert self.top + w <= self.words, f"arena overflow {self.top}+{w}>{self.words}"
        off = self.top
        self.top += w
        self.peak = max(self.peak, self.top)
        return off, w

    def alloc(self, shape, dtype=F32):
        assert shape[0] == P
        elems = 1
        for x in shape[1:]:
            elems *= x
        off, _ = self._take(elems if dtype == F32 else (elems + 1) // 2)
        ap = self.t[:, off:off + (elems if dtype == F32 else (elems + 1) // 2)]
        if dtype != F32:
            ap = ap.bitcast(dtype)
        if len(shape) == 2:
            return ap[:, 0:shape[1]]
        if len(shape) == 3:
            return ap[:, 0:shape[1] * shape[2]].rearrange("p (a b) -> p a b", b=shape[2])
        assert len(shape) == 4
        n = shape[1] * shape[2] * shape[3]
        return ap[:, 0:n].rearrange("p (a b c) -> p a b c", b=shape[2], c=shape[3])

    def row(self, n, dtype=F32):
        off, _ = self._take(n if dtype == F32 else (n + 1) // 2)
        ap = self.t[0:1, off:off + (n if dtype == F32 else (n + 1) // 2)]
        if dtype != F32:
            ap = ap.bitcast(dtype)
        return ap[:, 0:n]

    def mark(self):
        self.marks.append(self.top)

    def release(self):
        self.top = self.marks.pop()


def _consts():
    lt = np.triu(np.ones((P, P), dtype=np.float32))
    eh = np.zeros((32, 32 * P), dtype=np.float32)
    for h in range(32):
        eh[h, h * P:(h + 1) * P] = 1.0
    return {"c_ident": np.eye(P, dtype=np.float32),
            "c_identb": np.eye(P, dtype=ml_dtypes.bfloat16),
            "c_onesb": np.ones((1, P), dtype=ml_dtypes.bfloat16),
            "c_onesf": np.ones((1, P), dtype=np.float32),
            "c_lt": lt, "c_eh": eh}


def build_l0(ctx):
    s = ctx["s"]
    ar = ctx["ar"]
    d = ctx["dram"]
    sems = ctx["sems"]
    sem_ld = sems["ld"]; sem_pe = sems["pe"]; sem_act = sems["act"]
    sem_ve = sems["ve"]; sem_st = sems["st"]
    sem_lx = (sems["lxa"], sems["lxb"])
    sem_sxn = sems["sxn"]; sem_sx2 = sems["sx2"]; sem_sscan = sems["sscan"]
    sem_xq = sems["xq"]; sem_w1s = sems["w1s"]; sem_w2s = sems["w2s"]
    sem_fc = sems["fc"]; sem_fl = (sems["fla"], sems["flb"])
    ident = ctx["ident"]; identb = ctx["identb"]; onesb = ctx["onesb"]
    dnA = ctx["dnA"]; dnB = ctx["dnB"]; dnC = ctx["dnC"]
    hp0 = ctx["hp0"]; hp1 = ctx["hp1"]
    cst = ctx["cst"]
    if "cw_sb" in ctx:
        cw_sb = ctx["cw_sb"]; accb = ctx["accb"]; b1c = ctx["b1c"]
        w2b = ctx["w2b"]; b2r = ctx["b2r"]; xnq = ctx["xnq"]
    else:
        cw_sb = ar.alloc([P, NTB, 8])
        accb = ar.alloc([P, NTB, DM], BF16)
        b1c = ar.alloc([P, 8, 32])
        w2b = ar.alloc([P, 32, 1024], BF16)
        b2r_t2 = ar.alloc([P, 2048], BF16)
        b2r = b2r_t2[0:1, :].rearrange("o (a b) -> o a b", b=1024)
        xnq = ar.alloc([P, 8, 512], BF16)

    ar.mark()
    gw = ar.alloc([P, 8, 8])
    Bm = ar.alloc([P, 8, 512], BF16)
    dt4 = ar.alloc([P, 128])
    A4 = ar.alloc([P, 128])
    b1r = ar.alloc([P, 128])
    dtc = ar.alloc([P, 4])
    ac = ar.alloc([P, 4])
    upj = ar.alloc([P, 4, T], BF16)
    gamma_b = ar.alloc([P, DM])
    beta_b = ar.alloc([P, DM])
    normw_b = ar.alloc([P, DM])

    s.op("sync", lambda e: e.dma_start(gw[:], d["L0_ffn_gate_w"].rearrange("(c p) e -> p c e", p=P)),
         inc=sem_ld, inc_by=16)
    v_gw = sem_ld.n
    s.op("sync", lambda e: e.dma_start(gamma_b, d["L0_ffn_gamma"].unsqueeze(0).broadcast_to((P, DM))),
         inc=sem_ld, inc_by=16)
    s.op("sync", lambda e: e.dma_start(beta_b, d["L0_ffn_beta"].unsqueeze(0).broadcast_to((P, DM))),
         inc=sem_ld, inc_by=16)
    s.op("sync", lambda e: e.dma_start(normw_b, d["L0_attn_norm_w"].unsqueeze(0).broadcast_to((P, DM))),
         inc=sem_ld, inc_by=16)
    v_rows = sem_ld.n
    s.op("sync", lambda e: e.dma_start(dt4[0:4, :], d["L0_attn_dt"].rearrange("(c p) -> c p", p=P)),
         inc=sem_ld, inc_by=16)
    s.op("sync", lambda e: e.dma_start(A4[0:4, :], d["L0_attn_A"].rearrange("(c p) -> c p", p=P)),
         inc=sem_ld, inc_by=16)
    v_dtA = sem_ld.n
    s.op("gpsimd", lambda e: e.dma_start(Bm[:], d["L0_attn_Bmat"].rearrange("(c p) s -> p c s", p=P)),
         inc=sem_w2s, inc_by=16)
    v_Bm = sem_w2s.n
    w1_done = [None] * 8
    w2_done = [None] * 8
    b2_done = [None] * 8

    def emit_w1(e_, gate_waits):
        for k in range(8):
            wts = list(gate_waits) if k == 0 else []
            s.op("gpsimd", lambda en, e_=e_, k=k: en.dma_start(
                w1b[:, k, :], d["L0_ffn_w1"][e_, k * P:(k + 1) * P, :]),
                waits=wts, inc=sem_w1s, inc_by=16)
        w1_done[e_] = sem_w1s.n

    def emit_w2b2(e_, gate_pe):
        s.op("gpsimd", lambda en, e_=e_: en.dma_start(
            b2r[0:1, e_ % 2, :], d["L0_ffn_b2"][e_:e_ + 1, :]),
            waits=[(sem_pe, gate_pe)] if gate_pe is not None else [], inc=sem_w2s, inc_by=16)
        b2_done[e_] = sem_w2s.n
        for g in range(4):
            s.op("gpsimd", lambda en, e_=e_, g=g: en.dma_start(
                w2b[:, g * 8:(g + 1) * 8, :],
                d["L0_ffn_w2"][e_, g * 1024:(g + 1) * 1024, :].rearrange("(j p) dd -> p j dd", p=P)),
                inc=sem_w2s, inc_by=16)
        w2_done[e_] = sem_w2s.n

    emit_w2b2(0, None)

    v_b1p = 0
    for e_ in range(8):
        v_l = s.op("sync", lambda en, e_=e_: en.dma_start(
            b1r[0:32, :], d["L0_ffn_b1"][e_].rearrange("(n p) -> n p", p=P)),
            waits=([] if e_ == 0 else [(sem_ve, v_b1p)]), inc=sem_ld, inc_by=16)
        v_t = s.op("tensor", lambda en: en.matmul(hp0[:, 0:32], b1r[0:32, :], ident[0:32, 0:32],
                                                  start=True, stop=True),
                   waits=[(sem_ld, v_l)] + ([(sem_ve, v_b1p)] if e_ else []), inc=sem_pe)
        v_b1p = s.op("vector", lambda en, e_=e_: en.tensor_copy(b1c[:, e_, :], hp0[:, 0:32]),
                     waits=[(sem_pe, v_t)], inc=sem_ve)
    v_b1all = v_b1p
    PH = os.environ.get("KPHASES", "all")

    def trivial_final_u():
        ar.mark()
        tmpf = ar.alloc([P, DM])
        vcp = 0
        vst = 0
        for tb in range(NTB):
            vl = s.op("sync", lambda e, tb=tb: e.dma_start(tmpf[:], d["u"][tb * P:(tb + 1) * P, :]),
                      waits=[(sem_st, vst)] if vst else [], inc=sem_fc, inc_by=16)
            vst = s.op("sync", lambda e, tb=tb: e.dma_start(d["x1"][tb * P:(tb + 1) * P, :], tmpf[:]),
                      waits=[(sem_fc, vl)], inc=sem_st, inc_by=16)
        ar.release()

    def trivial_final():
        ar.mark()
        tmpb = ar.alloc([P, DM], BF16)
        tmpf = ar.alloc([P, DM])
        vcp = 0
        vst = 0
        for tb in range(NTB):
            vl = s.op("sync", lambda e, tb=tb: e.dma_start(tmpb[:], d["xn_d"][tb * P:(tb + 1) * P, :]),
                      waits=[(sem_ve, vcp)] if vcp else [], inc=sem_fc, inc_by=16)
            vcp = s.op("vector", lambda e: e.tensor_copy(tmpf[:], tmpb[:]),
                       waits=[(sem_fc, vl)] + ([(sem_st, vst)] if vst else []), inc=sem_ve)
            vst = s.op("sync", lambda e, tb=tb: e.dma_start(d["x1"][tb * P:(tb + 1) * P, :], tmpf[:]),
                       waits=[(sem_ve, vcp)], inc=sem_st, inc_by=16)
        ar.release()

    if PH == "pro":
        ar.release()
        trivial_final_u()
        return
    ar.mark()
    xr = ar.alloc([P, 2, DM])
    xc = ar.alloc([P, DM])
    xsq = ar.alloc([P, DM])
    xn_t = ar.alloc([P, DM])
    xn2_t = ar.alloc([P, DM])
    xg = ar.alloc([P, DM])
    xn2f = ar.alloc([P, 8, 128])
    xn2b = ar.alloc([P, 8, 128], BF16)
    attb = ar.alloc([P, DM], BF16)
    attf = ar.alloc([P, 8, 128], BF16)
    xnb = ar.alloc([P, DM], BF16)
    lg = ar.alloc([P, 8])
    st_ = ar.alloc([P, 16])
    zrt = ar.alloc([P, 8])
    msk = ar.alloc([P, 8])

    x_rd = {}
    x_consumed = {}
    last_hp0_act = 0
    prev_xn2dma = 0
    prev_xndma = 0
    LNTB = int(os.environ.get("KLNTB", NTB))
    SKIP = set(os.environ.get("KLNSKIP", "").split(","))
    DEPTH = int(os.environ.get("KLNDEPTH", 99))

    def emit_xload(tb):
        waits = []
        if tb >= 2:
            waits.append((sem_ve, x_consumed[tb - 2]))
        x_rd[tb] = s.op("sync", lambda e, tb=tb: e.dma_start(
            xr[:, tb % 2, :], d["u"][tb * P:(tb + 1) * P, :]),
            waits=waits, inc=sem_lx[tb % 2], inc_by=16)

    emit_xload(0)
    if LNTB > 1:
        emit_xload(1)
    for tb in range(LNTB):
        xin = xr[:, tb % 2, :]
        s.op("vector", lambda e, xin=xin: e.reduce_sum(st_[:, 0:1], xin, axis=AX.X),
             waits=[(sem_lx[tb % 2], x_rd[tb])])
        s.op("vector", lambda e: e.tensor_scalar(st_[:, 0:1], st_[:, 0:1], 1.0 / DM, None, ALU.mult))
        v_xc = s.op("vector", lambda e, xin=xin: e.tensor_scalar(xc[:], xin, st_[:, 0:1], None, ALU.subtract),
                    inc=sem_ve)
        x_consumed[tb] = v_xc
        if tb + 2 < LNTB:
            emit_xload(tb + 2)
        s.op("scalar", lambda e: e.activation(xsq[:], xc[:], AF.Square, accum_out=st_[:, 1:2]),
             waits=[(sem_ve, v_xc)])
        s.op("scalar", lambda e: e.activation(st_[:, 1:2], st_[:, 1:2], AF.Ln, bias=cst[:, 0:1], scale=1.0 / DM))
        v_rs = s.op("scalar", lambda e: e.activation(st_[:, 1:2], st_[:, 1:2], AF.Exp, scale=-0.5),
                    inc=sem_act)
        v_xn = s.op("vector", lambda e: e.tensor_scalar(xn_t[:], xc[:], st_[:, 1:2], None, ALU.mult),
                    waits=[(sem_act, v_rs)], inc=sem_ve)
        if "xnd" in SKIP:
            v_xnb = None
        if "xnd" not in SKIP:
            v_xnb = s.op("vector", lambda e: e.tensor_copy(xnb[:], xn_t[:]),
                         waits=[(sem_sxn, prev_xndma)] if prev_xndma else [], inc=sem_ve)
            prev_xndma = s.op("sync", lambda e, tb=tb: e.dma_start(d["xn_d"][tb * P:(tb + 1) * P, :], xnb[:]),
                              waits=[(sem_ve, v_xnb)], inc=sem_sxn, inc_by=16)
        if DEPTH < 2:
            continue
        s.op("vector", lambda e: e.reduce_sum(st_[:, 2:3], xn_t[:], axis=AX.X))
        s.op("vector", lambda e: e.tensor_scalar(st_[:, 2:3], st_[:, 2:3], 1.0 / DM, None, ALU.mult))
        v_xc2 = s.op("vector", lambda e: e.tensor_scalar(xc[:], xn_t[:], st_[:, 2:3], None, ALU.subtract),
                     inc=sem_ve)
        s.op("scalar", lambda e: e.activation(xsq[:], xc[:], AF.Square, accum_out=st_[:, 3:4]),
             waits=[(sem_ve, v_xc2)])
        s.op("scalar", lambda e: e.activation(st_[:, 3:4], st_[:, 3:4], AF.Ln, bias=cst[:, 0:1], scale=1.0 / DM))
        v_rs2 = s.op("scalar", lambda e: e.activation(st_[:, 3:4], st_[:, 3:4], AF.Exp, scale=-0.5),
                     inc=sem_act)
        s.op("vector", lambda e: e.scalar_tensor_tensor(xg[:], xc[:], st_[:, 3:4], gamma_b, ALU.mult, ALU.mult),
             waits=[(sem_act, v_rs2), (sem_ld, v_rows)])
        v_xn2 = s.op("vector", lambda e: e.tensor_tensor(xn2_t[:], xg[:], beta_b, ALU.add), inc=sem_ve)
        if DEPTH < 3:
            continue
        s.op("scalar", lambda e: e.activation(xsq[:], xn_t[:], AF.Square, accum_out=st_[:, 4:5]),
             waits=[(sem_ve, v_xn)])
        s.op("scalar", lambda e: e.activation(st_[:, 4:5], st_[:, 4:5], AF.Ln, bias=cst[:, 1:2], scale=1.0 / DM))
        v_rs3 = s.op("scalar", lambda e: e.activation(st_[:, 4:5], st_[:, 4:5], AF.Exp, scale=-0.5),
                     inc=sem_act)
        v_att = s.op("vector", lambda e: e.scalar_tensor_tensor(
            attb[:], xn_t[:], st_[:, 4:5], normw_b, ALU.mult, ALU.mult),
            waits=[(sem_act, v_rs3)], inc=sem_ve)
        if DEPTH < 4:
            continue
        hp1b = hp1[:, 0:P].bitcast(BF16)[:, 0:P]
        for c in range(8):
            w1_ = [(sem_ve, v_xn2 if c == 0 else sem_ve.n)]
            v_t1 = s.op("tensor", lambda e, c=c: e.transpose(hp0[:, 0:P], xn2_t[:, c * P:(c + 1) * P], ident[:]),
                        waits=w1_, inc=sem_pe)
            v_c1 = s.op("vector", lambda e, c=c: e.tensor_copy(xn2f[:, c, :], hp0[:, 0:P]),
                        waits=[(sem_pe, v_t1)], inc=sem_ve)
            wc1b = []
            if c == 0 and prev_xn2dma:
                wc1b.append((sem_sx2, prev_xn2dma))
            v_c1b = s.op("vector", lambda e, c=c: e.tensor_copy(xn2b[:, c, :], xn2f[:, c, :]),
                         waits=wc1b, inc=sem_ve)
            v_t2 = s.op("tensor", lambda e, c=c, hp1b=hp1b: e.transpose(
                hp1b, attb[:, c * P:(c + 1) * P], identb[:]),
                waits=[(sem_ve, v_att if c == 0 else sem_ve.n)], inc=sem_pe)
            s.op("vector", lambda e, c=c, hp1b=hp1b: e.tensor_copy(attf[:, c, :], hp1b),
                 waits=[(sem_pe, v_t2)], inc=sem_ve)
        v_xn2b_done = sem_ve.n
        v_trdone = sem_ve.n
        if DEPTH < 5:
            continue
        prev_xn2dma = s.op("sync", lambda e, tb=tb: e.dma_start(
            d["xn2fm_d"].rearrange("(c p) t -> p c t", p=P)[:, :, tb * P:(tb + 1) * P], xn2b[:]),
            waits=[(sem_ve, v_xn2b_done)], inc=sem_sx2, inc_by=16)
        if DEPTH < 6:
            continue
        for c in range(8):
            def mml(e, c=c):
                return e.matmul(dnA[:, 0:8], xn2f[:, c, :], gw[:, c, :],
                                start=(c == 0), stop=(c == 7))
            if c == 0:
                s.op("tensor", mml, waits=[(sem_ve, v_trdone), (sem_ld, v_gw)])
            elif c == 7:
                v_lgmm = s.op("tensor", mml, inc=sem_pe)
            else:
                s.op("tensor", mml)
        v_lgc = s.op("vector", lambda e: e.tensor_copy(lg[:], dnA[:, 0:8]),
                     waits=[(sem_pe, v_lgmm)], inc=sem_ve)
        s.op("vector", lambda e: e.reduce_max(st_[:, 5:6], lg[:], axis=AX.X))
        s.op("vector", lambda e: e.tensor_scalar(st_[:, 6:7], st_[:, 5:6], -1.0, None, ALU.mult))
        s.op("vector", lambda e: e.tensor_scalar(msk[:], lg[:], st_[:, 5:6], None, ALU.is_ge))
        s.op("vector", lambda e: e.scalar_tensor_tensor(zrt[:], msk[:], -1e30, lg[:], ALU.mult, ALU.add))
        s.op("vector", lambda e: e.reduce_max(st_[:, 7:8], zrt[:], axis=AX.X))
        v_mk = s.op("vector", lambda e: e.tensor_scalar(msk[:], lg[:], st_[:, 7:8], None, ALU.is_ge),
                    inc=sem_ve)
        v_ez = s.op("scalar", lambda e: e.activation(zrt[:], lg[:], AF.Exp, bias=st_[:, 6:7]),
                    waits=[(sem_ve, v_mk)], inc=sem_act)
        s.op("vector", lambda e: e.tensor_tensor(zrt[:], zrt[:], msk[:], ALU.mult),
             waits=[(sem_act, v_ez)])
        s.op("vector", lambda e: e.reduce_sum(st_[:, 8:9], zrt[:], axis=AX.X))
        s.op("vector", lambda e: e.reciprocal(st_[:, 9:10], st_[:, 8:9]))
        s.op("vector", lambda e, tb=tb: e.tensor_scalar(cw_sb[:, tb, :], zrt[:], st_[:, 9:10], None, ALU.mult),
             inc=sem_ve)
        if DEPTH < 7:
            continue
        for st4 in range(4):
            psU = (dnB if st4 % 2 == 0 else dnC)[:, 0:P]
            for k in range(8):
                def mmu(e, st4=st4, k=k, psU=psU):
                    return e.matmul(psU, Bm[:, k, st4 * P:(st4 + 1) * P], attf[:, k, :],
                                    start=(k == 0), stop=(k == 7))
                if k == 0:
                    wts = [(sem_ve, v_trdone)]
                    if tb == 0 and st4 == 0:
                        wts.append((sem_w2s, v_Bm))
                    if st4 >= 2 or tb > 0:
                        wts.append((sem_ve, sem_ve.n))
                    s.op("tensor", mmu, waits=wts)
                elif k == 7:
                    v_um = s.op("tensor", mmu, inc=sem_pe)
                else:
                    s.op("tensor", mmu)
            s.op("vector", lambda e, st4=st4, tb=tb, psU=psU: e.tensor_copy(
                upj[:, st4, tb * P:(tb + 1) * P], psU),
                waits=[(sem_pe, v_um)], inc=sem_ve)
    v_ln_pe = sem_pe.n
    ar.release()
    if PH == "ln":
        ar.release()
        trivial_final()
        return

    ar.mark()
    binc = ar.alloc([P, T])
    xsc_f = ar.alloc([P, T])
    xsc_b = ar.alloc([P, T], BF16)
    s.op("scalar", lambda e: e.activation(dt4[0:4, :], dt4[0:4, :], AF.Sigmoid),
         waits=[(sem_ld, v_dtA)])
    v_ab = s.op("scalar", lambda e: e.activation(A4[0:4, :], A4[0:4, :], AF.Abs), inc=sem_act)
    v_m = s.op("vector", lambda e: e.tensor_tensor(A4[0:4, :], A4[0:4, :], dt4[0:4, :], ALU.mult),
               waits=[(sem_act, v_ab)], inc=sem_ve)
    v_ex = s.op("scalar", lambda e: e.activation(A4[0:4, :], A4[0:4, :], AF.Exp, scale=-1.0),
                waits=[(sem_ve, v_m)], inc=sem_act)
    v_tt1 = s.op("tensor", lambda e: e.matmul(hp0[:, 0:4], dt4[0:4, :], ident[0:4, 0:4],
                                              start=True, stop=True),
                 waits=[(sem_act, v_ex), (sem_ve, sem_ve.n)], inc=sem_pe)
    v_tt2 = s.op("tensor", lambda e: e.matmul(hp1[:, 0:4], A4[0:4, :], ident[0:4, 0:4],
                                              start=True, stop=True), inc=sem_pe)
    v_dc = s.op("vector", lambda e: e.tensor_copy(dtc[:], hp0[:, 0:4]), waits=[(sem_pe, v_tt1)], inc=sem_ve)
    v_ac = s.op("vector", lambda e: e.tensor_copy(ac[:], hp1[:, 0:4]), waits=[(sem_pe, v_tt2)], inc=sem_ve)
    v_sdma = 0
    for c in range(4):
        wts = [(sem_ve, max(v_dc, v_ac))] if c == 0 else [(sem_sscan, v_sdma)]
        s.op("vector", lambda e, c=c: e.tensor_scalar(binc[:], upj[:, c, :], dtc[:, c:c + 1], None, ALU.mult),
             waits=wts)
        s.op("vector", lambda e, c=c: e.tensor_tensor_scan(
            xsc_f[:], ac[:, c:c + 1].broadcast_to((P, T)), binc[:], 0.0, ALU.mult, ALU.add))
        v_cb = s.op("vector", lambda e: e.tensor_copy(xsc_b[:], xsc_f[:]), inc=sem_ve)
        v_sdma = s.op("sync", lambda e, c=c: e.dma_start(d["xscan_d"][c * P:(c + 1) * P, :], xsc_b[:]),
                      waits=[(sem_ve, v_cb)], inc=sem_sscan, inc_by=16)
    v_scan_ve = sem_ve.n
    ar.release()
    ar.release()
    if PH == "scan":
        trivial_final()
        return

    ar.mark()
    w1b = ar.alloc([P, 8, 4096], BF16)
    hb = ar.alloc([P, 32, 512], BF16)
    QT = 512

    v_z = s.op("vector", lambda e: e.memset(accb[:], 0.0), inc=sem_ve)
    emit_w1(0, [(sem_ve, v_scan_ve), (sem_sscan, v_sdma)])

    xnq_done = {}

    def emit_xnq(e_, q, gate_pe):
        xnq_done[(e_, q)] = s.op("sync", lambda en, q=q: en.dma_start(
            xnq[:],
            d["xn2fm_d"].rearrange("(c p) t -> p c t", p=P)[:, :, q * QT:(q + 1) * QT]),
            waits=[(sem_pe, gate_pe)] if gate_pe is not None else [], inc=sem_xq, inc_by=16)

    xnq_done[(0, 0)] = s.op("sync", lambda en: en.dma_start(
        xnq[:], d["xn2fm_d"].rearrange("(c p) t -> p c t", p=P)[:, :, 0:QT]),
        waits=[(sem_sx2, sem_sx2.n)], inc=sem_xq, inc_by=16)

    dn_evac = []
    ngrp = 0
    prev_g30 = None
    prev_g31 = None
    pe_after_h = {}
    pe_after_dn = {}
    for e_ in range(8):
        for q in range(4):
            gelu_vals = []
            for n in range(32):
                ps = hp0 if n % 2 == 0 else hp1
                waits = []
                if n == 0:
                    waits.append((sem_xq, xnq_done[(e_, q)]))
                    waits.append((sem_w1s, w1_done[e_]))
                    if e_ == 0 and q == 0:
                        waits.append((sem_ve, max(v_b1all, v_scan_ve)))
                        waits.append((sem_pe, v_ln_pe))
                    else:
                        waits.append((sem_act, prev_g30))
                if n == 1 and prev_g31 is not None and not (e_ == 0 and q == 0):
                    waits.append((sem_act, prev_g31))
                if n >= 2:
                    waits.append((sem_act, gelu_vals[n - 2]))
                for k in range(8):
                    def mmh(e, k=k, n=n, ps=ps):
                        return e.matmul(ps[:], w1b[:, k, n * P:(n + 1) * P], xnq[:, k, :],
                                        start=(k == 0), stop=(k == 7))
                    if k == 0:
                        s.op("tensor", mmh, waits=waits)
                    elif k == 7:
                        v_hm = s.op("tensor", mmh, inc=sem_pe)
                    else:
                        s.op("tensor", mmh)
                gelu_vals.append(s.op("scalar", lambda e, n=n, ps=ps, e_=e_: e.activation(
                    hb[:, n, :], ps[:], AF.Gelu, bias=b1c[:, e_, n:n + 1]),
                    waits=[(sem_pe, v_hm)], inc=sem_act))
            prev_g30, prev_g31 = gelu_vals[30], gelu_vals[31]
            pe_after_h[(e_, q)] = sem_pe.n
            nxt = (e_, q + 1) if q < 3 else (e_ + 1, 0)
            if nxt[0] < 8:
                emit_xnq(nxt[0], nxt[1], pe_after_h[(e_, q)])
            for tb in range(4):
                tbg = q * 4 + tb
                ps = (dnA, dnB, dnC)[ngrp % 3]
                waits = [(sem_act, gelu_vals[-1])]
                if ngrp == 0:
                    waits.append((sem_w2s, w2_done[0]))
                    waits.append((sem_ve, v_scan_ve))
                elif ngrp >= 3:
                    waits.append((sem_ve, dn_evac[ngrp - 3]))
                first = True
                for n in range(32):
                    for j in range(2):
                        def mmd(e, n=n, j=j, tb=tb, ps=ps):
                            return e.matmul(ps[:, j * 512:(j + 1) * 512],
                                            hb[:, n, tb * P:(tb + 1) * P],
                                            w2b[:, n, j * 512:(j + 1) * 512],
                                            start=(n == 0), stop=False, skip_group_check=True)
                        if first:
                            s.op("tensor", mmd, waits=waits)
                            first = False
                        else:
                            s.op("tensor", mmd)
                for j in range(2):
                    def mmb(e, j=j, ps=ps, e_=e_):
                        return e.matmul(ps[:, j * 512:(j + 1) * 512], onesb[:],
                                        b2r[0:1, e_ % 2, j * 512:(j + 1) * 512],
                                        start=False, stop=True, skip_group_check=True)
                    if j == 1:
                        v_dm = s.op("tensor", mmb, inc=sem_pe)
                    else:
                        s.op("tensor", mmb)
                dn_evac.append(s.op("vector", lambda e, tbg=tbg, ps=ps, e_=e_: e.scalar_tensor_tensor(
                    accb[:, tbg, :], ps[:], cw_sb[:, tbg, e_].unsqueeze(-1), accb[:, tbg, :],
                    ALU.mult, ALU.add),
                    waits=[(sem_pe, v_dm), (sem_ve, v_z)], inc=sem_ve))
                ngrp += 1
            pe_after_dn[(e_, q)] = sem_pe.n
        if e_ + 1 < 8:
            emit_w1(e_ + 1, [(sem_pe, pe_after_h[(e_, 3)])])
            emit_w2b2(e_ + 1, pe_after_dn[(e_, 3)])
    v_moe_ve = sem_ve.n
    ar.release()
    if PH == "moe":
        s.wait("vector", sem_ve, v_moe_ve)
        trivial_final()
        return

    ar.mark()
    C_sb = ar.alloc([P, 4, 1024], BF16)
    xscan = ar.alloc([P, 4, T], BF16)
    D_b = ar.alloc([P, DM])
    xr2 = ar.alloc([P, 2, DM])
    xnr = ar.alloc([P, 2, DM], BF16)
    sD = ar.alloc([P, DM])
    gsum = ar.alloc([P, DM])
    gout = ar.alloc([P, DM])
    t3 = ar.alloc([P, DM])
    xout = ar.alloc([P, DM])
    v_C = s.op("gpsimd", lambda e: e.dma_start(C_sb[:], d["L0_attn_C"].rearrange("(c p) o -> p c o", p=P)),
               waits=[(sem_pe, pe_after_dn[(7, 3)])], inc=sem_w2s, inc_by=16)
    v_xs = s.op("sync", lambda e: e.dma_start(xscan[:], d["xscan_d"].rearrange("(c p) t -> p c t", p=P)),
                waits=[(sem_pe, pe_after_dn[(7, 3)]), (sem_sscan, sem_sscan.n)], inc=sem_fc, inc_by=16)
    v_D = s.op("sync", lambda e: e.dma_start(D_b, d["L0_attn_D"].unsqueeze(0).broadcast_to((P, DM))),
                inc=sem_fc, inc_by=16)

    f_x = {}
    f_xn = {}
    f_consumed = {}
    f_sD = {}

    def emit_fload(tb):
        sm = sem_fl[tb % 2]
        waits = [(sem_pe, pe_after_dn[(7, 3)])] if tb == 0 else []
        if tb >= 2:
            waits.append((sem_ve, f_consumed[tb - 2]))
        s.op("sync", lambda e, tb=tb: e.dma_start(xr2[:, tb % 2, :], d["u"][tb * P:(tb + 1) * P, :]),
             waits=waits, inc=sm, inc_by=16)
        f_x[tb] = (sm, sm.n)
        s.op("sync", lambda e, tb=tb: e.dma_start(xnr[:, tb % 2, :], d["xn_d"][tb * P:(tb + 1) * P, :]),
             waits=[(sem_sxn, sem_sxn.n)] if tb == 0 else [], inc=sm, inc_by=16)
        f_xn[tb] = (sm, sm.n)

    emit_fload(0)
    emit_fload(1)
    for tb in range(NTB):
        ps = (dnA, dnB, dnC)[tb % 3]
        waits = []
        if tb == 0:
            waits = [(sem_fc, v_D), (sem_w2s, v_C), (sem_ve, v_moe_ve)]
        elif tb >= 3:
            waits = [(sem_ve, f_sD[tb - 3])]
        first = True
        for sc in range(4):
            for j in range(2):
                def mms(e, sc=sc, j=j, tb=tb, ps=ps):
                    return e.matmul(ps[:, j * 512:(j + 1) * 512],
                                    xscan[:, sc, tb * P:(tb + 1) * P],
                                    C_sb[:, sc, j * 512:(j + 1) * 512],
                                    start=(sc == 0), stop=(sc == 3), skip_group_check=True)
                if first:
                    s.op("tensor", mms, waits=waits)
                    first = False
                elif sc == 3 and j == 1:
                    v_sm = s.op("tensor", mms, inc=sem_pe)
                else:
                    s.op("tensor", mms)
        f_sD[tb] = s.op("vector", lambda e, ps=ps: e.tensor_tensor(sD[:], ps[:], D_b, ALU.add),
                        waits=[(sem_pe, v_sm), (sem_fc, v_D)], inc=sem_ve)
        v_gs = s.op("vector", lambda e, tb=tb: e.tensor_tensor(
            gsum[:], accb[:, tb, :], xnr[:, tb % 2, :], ALU.add),
            waits=[f_xn[tb]], inc=sem_ve)
        v_g = s.op("scalar", lambda e: e.activation(gout[:], gsum[:], AF.Silu),
                   waits=[(sem_ve, v_gs)], inc=sem_act)
        s.op("vector", lambda e: e.tensor_tensor(t3[:], sD[:], gout[:], ALU.mult),
             waits=[(sem_act, v_g)])
        wxo = [f_x[tb]]
        if tb > 0:
            wxo.append((sem_st, sem_st.n))
        v_xo = s.op("vector", lambda e, tb=tb: e.tensor_tensor(xout[:], xr2[:, tb % 2, :], t3[:], ALU.add),
                    waits=wxo, inc=sem_ve)
        f_consumed[tb] = v_xo
        s.op("sync", lambda e, tb=tb: e.dma_start(d["x1"][tb * P:(tb + 1) * P, :], xout[:]),
             waits=[(sem_ve, v_xo)], inc=sem_st, inc_by=16)
        if tb + 2 < NTB:
            emit_fload(tb + 2)
    ar.release()





def emit_quant(s, sems, bufs, src_ap, out_ap, W, src_ready=(), eng_big="vector"):
    sem_ve = sems["ve"]; sem_gp = sems["gp"]; sem_act = sems["act"]
    nb = W // 16
    absx, t1, t2, q = bufs["absx"], bufs["t1"], bufs["t2"], bufs["q"]
    bm, sc, rs = bufs["bm"], bufs["sc"], bufs["rs"]
    v_sg = s.op("scalar", lambda e: e.sign(q[:, 0:W], src_ap),
                waits=list(src_ready) + [(sem_ve, sems["ve"].n)])
    v_a = s.op("vector", lambda e: e.tensor_tensor(absx[:, 0:W], src_ap, q[:, 0:W], ALU.mult),
               waits=list(src_ready) + [(sem_gp, sem_gp.n), (sem_act, v_sg)])
    s.op("vector", lambda e: e.tensor_reduce(bm[:, 0:nb].unsqueeze(-1),
                                             absx[:, 0:W].rearrange("p (a b) -> p a b", b=16),
                                             axis=AX.X, op=ALU.max))
    s.op("vector", lambda e: e.tensor_scalar(sc[:, 0:nb], bm[:, 0:nb], 1.0 / 6.0, 1e-12,
                                             ALU.mult, ALU.max))
    s.op("vector", lambda e: e.reciprocal(rs[:, 0:nb], sc[:, 0:nb]))
    rs_b = rs[:, 0:nb].unsqueeze(-1).broadcast_to((P, nb, 16))
    sc_b = sc[:, 0:nb].unsqueeze(-1).broadcast_to((P, nb, 16))
    v_v = s.op("vector", lambda e: e.tensor_tensor(
        absx[:, 0:W].rearrange("p (a b) -> p a b", b=16),
        absx[:, 0:W].rearrange("p (a b) -> p a b", b=16), rs_b, ALU.mult))
    vv = absx[:, 0:W]
    v_ss = s.op("vector", lambda e: e.tensor_tensor(
        q[:, 0:W].rearrange("p (a b) -> p a b", b=16),
        q[:, 0:W].rearrange("p (a b) -> p a b", b=16), sc_b, ALU.mult),
        waits=[(sem_act, v_sg)])
    s.op("vector", lambda e: e.tensor_scalar(t1[:, 0:W], vv, 0.25, None, ALU.is_gt))
    s.op("vector", lambda e: e.scalar_tensor_tensor(t1[:, 0:W], vv, 0.75, t1[:, 0:W],
                                                    ALU.is_gt, ALU.add))
    s.op("vector", lambda e: e.scalar_tensor_tensor(t1[:, 0:W], vv, 1.25, t1[:, 0:W],
                                                    ALU.is_gt, ALU.add))
    g4 = s.op("vector", lambda e: e.scalar_tensor_tensor(t1[:, 0:W], vv, 1.75, t1[:, 0:W],
                                                         ALU.is_gt, ALU.add))
    s.op("vector", lambda e: e.tensor_scalar(t2[:, 0:W], vv, 2.5, None, ALU.is_gt))
    s.op("vector", lambda e: e.scalar_tensor_tensor(t2[:, 0:W], vv, 3.5, t2[:, 0:W],
                                                    ALU.is_gt, ALU.add))
    s.op("vector", lambda e: e.scalar_tensor_tensor(t2[:, 0:W], vv, 5.0, t2[:, 0:W],
                                                    ALU.is_gt, ALU.add))
    s.op("vector", lambda e: e.scalar_tensor_tensor(t2[:, 0:W], vv, 5.0, t2[:, 0:W],
                                                    ALU.is_gt, ALU.add))
    s.op("vector", lambda e: e.scalar_tensor_tensor(t2[:, 0:W], t1[:, 0:W], 0.5, t2[:, 0:W],
                                                    ALU.mult, ALU.add))
    v_out = s.op("vector", lambda e: e.tensor_tensor(out_ap, t2[:, 0:W], q[:, 0:W], ALU.mult))
    return v_out


def build_l1(ctx):
    s = ctx["s"]
    ar = ctx["ar"]
    d = ctx["dram"]
    sems = ctx["sems"]
    cst = ctx["cst"]
    sem_ld = sems["ld"]; sem_pe = sems["pe"]; sem_act = sems["act"]
    sem_ve = sems["ve"]; sem_st = sems["st"]; sem_gp = sems["gp"]
    sem_lx = (sems["lxa"], sems["lxb"])
    sem_sxf = sems["sx2"]
    sem_sxq = sems["sscan"]
    sem_sbz = sems["sxn"]
    sem_cc = (sems["fla"], sems["flb"])
    sem_ss1 = sems["fc"]
    sem_wg = sems["w1s"]
    sem_wq = sems["wqh"] if "wqh" in sems else sems["w2s"]
    sem_xq = sems["xq"]
    ident = ctx["ident"]; identb = ctx["identb"]
    dnA = ctx["dnA"]; dnB = ctx["dnB"]; dnC = ctx["dnC"]
    hp0 = ctx["hp0"]; hp1 = ctx["hp1"]
    DI = 2048
    NH = 32
    HD = 64
    DS = 128

    ar.mark()
    dtraw = ar.alloc([P, NTB, NH])
    Wdt = ar.alloc([P, 8, NH])
    w4 = ar.alloc([P, 18, 4])
    cb_c = ar.alloc([P, 18])
    A_bc = ar.alloc([P, NH])
    D_bc = ar.alloc([P, NH])
    dtb_bc = ar.alloc([P, NH])
    normw_bc = ar.alloc([P, DI])
    eh_flat = ar.alloc([P, 4096])
    eh_sb = eh_flat[0:32, :]
    lt_sb = ar.alloc([P, P])
    onesf_t = ar.alloc([P, 128])
    onesf = onesf_t[0:1, :]
    state = ar.alloc([P, DI])
    state_bf = ar.alloc([P, DI], BF16)
    BmF = ar.alloc([P, T], BF16)
    CmF = ar.alloc([P, T], BF16)
    Wout_sb = ar.alloc([P, 16, 1024], BF16)

    gate0 = ctx.get("l1_gate", [])
    s.op("sync", lambda e: e.dma_start(lt_sb[:], d["c_lt"]), waits=gate0, inc=sem_ld, inc_by=16)
    s.op("sync", lambda e: e.dma_start(onesf_t[0:1, :], d["c_onesf"]), inc=sem_ld, inc_by=16)
    s.op("sync", lambda e: e.dma_start(eh_flat[0:32, :], d["c_eh"]), inc=sem_ld, inc_by=16)
    s.op("sync", lambda e: e.dma_start(
        Wdt[:], d["L1_attn_W_in"][:, 4352:4384].rearrange("(c p) e -> p c e", p=P)),
        inc=sem_ld, inc_by=16)
    s.op("sync", lambda e: e.dma_start(
        normw_bc[:], d["L1_attn_norm_w"].unsqueeze(0).broadcast_to((P, DI))),
        inc=sem_ld, inc_by=16)
    s.op("sync", lambda e: e.dma_start(
        dtb_bc[:], d["L1_attn_dt_bias"].unsqueeze(0).broadcast_to((P, NH))),
        inc=sem_ld, inc_by=16)
    ar.mark()
    arow = ar.alloc([P, NH])
    s.op("sync", lambda e: e.dma_start(arow[0:1, :], d["L1_attn_A_log"].unsqueeze(0)),
         inc=sem_ld, inc_by=16)
    drow = ar.alloc([P, NH])
    s.op("sync", lambda e: e.dma_start(drow[0:1, :], d["L1_attn_D"].unsqueeze(0)),
         inc=sem_ld, inc_by=16)
    cwst = ar.alloc([P, 512])
    s.op("sync", lambda e: e.dma_start(
        cwst[0:18, :], d["L1_attn_conv_w"].rearrange("(c p) j -> c (p j)", p=P)),
        inc=sem_ld, inc_by=16)
    s.op("sync", lambda e: e.dma_start(
        cwst[32:50, 0:P], d["L1_attn_conv_b"].rearrange("(c p) -> c p", p=P)),
        inc=sem_ld, inc_by=16)
    v_par = sem_ld.n

    v_ae = s.op("scalar", lambda e: e.activation(arow[0:1, :], arow[0:1, :], AF.Exp),
                waits=[(sem_ld, v_par)])
    v_an = s.op("vector", lambda e: e.tensor_scalar(arow[0:1, :], arow[0:1, :], -1.0, None, ALU.mult),
                waits=[(sem_act, v_ae)])
    v_tA = s.op("tensor", lambda e: e.matmul(hp0[:, 0:NH], onesf, arow[0:1, :], start=True, stop=True),
                waits=[(sem_ve, v_an), (sem_ld, v_par)], inc=sem_pe)
    v_cA = s.op("vector", lambda e: e.tensor_copy(A_bc[:], hp0[:, 0:NH]), waits=[(sem_pe, v_tA)])
    v_tD = s.op("tensor", lambda e: e.matmul(hp1[:, 0:NH], onesf, drow[0:1, :], start=True, stop=True),
                waits=[(sem_ve, v_cA)], inc=sem_pe)
    v_cD = s.op("vector", lambda e: e.tensor_copy(D_bc[:], hp1[:, 0:NH]), waits=[(sem_pe, v_tD)])
    for j in range(4):
        v_tw = s.op("tensor", lambda e, j=j: e.matmul(
            hp0[:, 0:18], cwst[0:18, :].rearrange("c (p j) -> c p j", j=4)[:, :, j],
            ident[0:18, 0:18], start=True, stop=True),
            waits=[(sem_ve, v_cD if j == 0 else sem_ve.n)], inc=sem_pe)
        s.op("vector", lambda e, j=j: e.tensor_copy(w4[:, :, j], hp0[:, 0:18]),
             waits=[(sem_pe, v_tw)])
    v_tb = s.op("tensor", lambda e: e.matmul(hp0[:, 0:18], cwst[32:50, 0:P], ident[32:50, 32:50],
                                             start=True, stop=True),
                waits=[(sem_ve, sem_ve.n)], inc=sem_pe)
    v_cbc = s.op("vector", lambda e: e.tensor_copy(cb_c[:], hp0[:, 0:18]), waits=[(sem_pe, v_tb)])
    if "dbg_d" in d and "p" in os.environ.get("KDBGSEL", "psmbB"):
        s.op("sync", lambda e: e.dma_start(d["dbg_d"][:, 0:72], w4[:].rearrange("p a b -> p (a b)")),
             waits=[(sem_ve, v_cbc)], inc=sems["fc"], inc_by=16)
        s.op("sync", lambda e: e.dma_start(d["dbg_d"][:, 128:146], cb_c[:]), inc=sems["fc"], inc_by=16)
    s.op("gpsimd", lambda e: e.dma_start(
        Wout_sb[:], d["L1_attn_W_out"].rearrange("(c p) o -> p c o", p=P)),
        waits=gate0, inc=sem_wg, inc_by=16)
    v_wout = sem_wg.n
    v_zero = s.op("vector", lambda e: e.memset(state[:], 0.0))
    v_one = s.op("vector", lambda e: e.memset(cst[:, 2:3], 1.0))
    ar.release()

    slb = {}

    sl_loaded = {}
    sl_ve_done = {}
    prev_wq_dma = [0]

    slabs = []
    for i in range(22):
        slabs.append(("L1_ffn_w_gate", i, 0, 1024, "wgt_d"))
    for i in range(22):
        slabs.append(("L1_ffn_w_up", i, 0, 1024, "wut_d"))
    for i in range(8):
        for piece in range(3):
            w0 = piece * 1024
            w1 = min(2816, w0 + 1024)
            slabs.append(("L1_ffn_w_down", i, w0, w1 - w0, "wdt_d"))
    NSLAB = len(slabs)

    def emit_slab_load2(j, extra_gate=None):
        if j >= NSLAB:
            return
        nm, i, w0, w, _dst = slabs[j]
        buf = slb["wsl"][j % 2]
        if extra_gate is not None:
            gate = list(extra_gate) + [(sem_ve, sl_ve_done.get(j - 2, 0))]
        elif j >= 2:
            gate = [(sem_ve, sl_ve_done.get(j - 2, 0))]
        else:
            gate = list(gate0) + [(sem_act, sem_act.n), (sem_gp, sem_gp.n)]
        sl_loaded[j] = s.op("gpsimd", lambda e, nm=nm, i=i, w0=w0, w=w, buf=buf: e.dma_start(
            buf[:, 0:w], d[nm][i * P:(i + 1) * P, w0:w0 + w]),
            waits=gate, inc=sem_wg, inc_by=16)

    def emit_slab_quant(j, psum_region, pe_waits):
        nm, i, w0, w, dst = slabs[j]
        buf = slb["wsl"][j % 2]
        wqo = slb["wqo"]
        wqs = slb["wqs"]
        v_q = emit_quant(s, sems, slb["wqb"], buf[:, 0:w], wqo[:, 0:w], w,
                         src_ready=[(sem_wg, sl_loaded[j])])
        nch = w // 128
        prb = psum_region.bitcast(BF16)[:, 0:P]
        for cc in range(nch):
            v_tt = s.op("tensor", lambda e, cc=cc, prb=prb, wqo=wqo: e.transpose(
                prb, wqo[:, cc * P:(cc + 1) * P], identb[:]),
                waits=[(sem_ve, v_q if cc == 0 else sem_ve.n)] + (pe_waits if cc == 0 else []),
                inc=sem_pe)
            s.op("vector", lambda e, cc=cc, prb=prb, wqs=wqs: e.tensor_copy(wqs[:, cc, :], prb),
                 waits=[(sem_pe, v_tt)] + ([(sem_wq, prev_wq_dma[0])] if cc == 0 and prev_wq_dma[0] else []))
        sl_ve_done[j] = sem_ve.n
        s.op("sync", lambda e, dst=dst, i=i, w0=w0, nch=nch, wqs=wqs: e.dma_start(
            d[dst].rearrange("(b p) n -> p b n", p=P)[:, w0 // P:w0 // P + nch, i * P:(i + 1) * P],
            wqs[:, 0:nch, :]),
            waits=[(sem_ve, sl_ve_done[j])], inc=sem_wq, inc_by=16)
        prev_wq_dma[0] = sem_wq.n

    sj = [0]

    def emit_slabs(k, psum_region, preload=True):
        for _ in range(k):
            if sj[0] < NSLAB:
                emit_slab_quant(sj[0], psum_region, [])
                if preload:
                    emit_slab_load2(sj[0] + 2)
                sj[0] += 1

    ar.mark()
    xr = ar.alloc([P, 2, DM])
    xc = ar.alloc([P, DM])
    xsq = ar.alloc([P, DM])
    xn_t = ar.alloc([P, DM])
    st_ = ar.alloc([P, 16])
    xnb = ar.alloc([P, DM], BF16)
    xqb = ar.alloc([P, DM], BF16)
    xnf32 = ar.alloc([P, 8, 128])
    xnf8 = ar.alloc([P, 8, 128], BF16)
    xqf8 = ar.alloc([P, 8, 128], BF16)
    qb = {"absx": ar.alloc([P, DM]), "t1": ar.alloc([P, DM]), "t2": ar.alloc([P, DM]),
          "q": ar.alloc([P, DM]), "bm": ar.alloc([P, DM // 16]), "sc": ar.alloc([P, DM // 16]),
          "rs": ar.alloc([P, DM // 16])}
    slb["wsl"] = [ar.alloc([P, 1024]) for _ in range(2)]
    slb["wqb"] = qb
    slb["wqo"] = ar.alloc([P, 1024], BF16)
    slb["wqs"] = ar.alloc([P, 8, 128], BF16)
    emit_slab_load2(0)
    emit_slab_load2(1)

    x_rd = {}
    x_cons = {}
    prev_fdma = 0
    prev_qdma = 0

    def a_xload(tb):
        waits = []
        if tb <= 1:
            waits.append((sem_ve, v_cbc))
        if tb >= 2:
            waits.append((sem_ve, x_cons[tb - 2]))
        x_rd[tb] = s.op("sync", lambda e, tb=tb: e.dma_start(
            xr[:, tb % 2, :], d["x1_l1"][tb * P:(tb + 1) * P, :]),
            waits=waits, inc=sem_lx[tb % 2], inc_by=16)

    a_xload(0)
    a_xload(1)
    for tb in range(NTB):
        xin = xr[:, tb % 2, :]
        s.op("vector", lambda e, xin=xin: e.reduce_sum(st_[:, 0:1], xin, axis=AX.X),
             waits=[(sem_lx[tb % 2], x_rd[tb])])
        s.op("vector", lambda e: e.tensor_scalar(st_[:, 0:1], st_[:, 0:1], 1.0 / DM, None, ALU.mult))
        v_xc = s.op("vector", lambda e, xin=xin: e.tensor_scalar(xc[:], xin, st_[:, 0:1], None,
                                                                 ALU.subtract))
        x_cons[tb] = v_xc
        if tb + 2 < NTB:
            a_xload(tb + 2)
        s.op("scalar", lambda e: e.activation(xsq[:], xc[:], AF.Square, accum_out=st_[:, 1:2]),
             waits=[(sem_ve, v_xc)])
        s.op("scalar", lambda e: e.activation(st_[:, 1:2], st_[:, 1:2], AF.Ln, bias=cst[:, 0:1],
                                              scale=1.0 / DM))
        v_rs = s.op("scalar", lambda e: e.activation(st_[:, 1:2], st_[:, 1:2], AF.Exp, scale=-0.5))
        v_xn = s.op("vector", lambda e: e.tensor_scalar(xn_t[:], xc[:], st_[:, 1:2], None, ALU.mult),
                    waits=[(sem_act, v_rs)])
        v_xnb = s.op("vector", lambda e: e.tensor_copy(xnb[:], xn_t[:]))
        v_xq = emit_quant(s, sems, qb, xn_t[:], xqb[:], DM, src_ready=[(sem_ve, v_xn)])
        hp1b = hp1[:, 0:P].bitcast(BF16)[:, 0:P]
        hp0b = hp0[:, 128:256].bitcast(BF16)[:, 0:P]
        for c in range(8):
            v_t1 = s.op("tensor", lambda e, c=c: e.transpose(hp0[:, 0:P], xn_t[:, c * P:(c + 1) * P],
                                                             ident[:]),
                        waits=[(sem_ve, v_xnb if c == 0 else sem_ve.n)], inc=sem_pe)
            v_c1 = s.op("vector", lambda e, c=c: e.tensor_copy(xnf32[:, c, :], hp0[:, 0:P]),
                        waits=[(sem_pe, v_t1)])
            v_c1b = s.op("vector", lambda e, c=c: e.tensor_copy(xnf8[:, c, :], xnf32[:, c, :]),
                         waits=[(sem_sxf, prev_fdma)] if (c == 0 and prev_fdma) else [])
            v_t2 = s.op("tensor", lambda e, c=c, hp1b=hp1b: e.transpose(
                hp1b, xqb[:, c * P:(c + 1) * P], identb[:]),
                waits=[(sem_ve, v_xq if c == 0 else sem_ve.n)], inc=sem_pe)
            v_c2 = s.op("vector", lambda e, c=c, hp1b=hp1b: e.tensor_copy(xqf8[:, c, :], hp1b),
                        waits=[(sem_pe, v_t2)] + ([(sem_sxq, prev_qdma)] if (c == 0 and prev_qdma) else []))
        v_trd = sem_ve.n
        prev_fdma = s.op("sync", lambda e, tb=tb: e.dma_start(
            d["xnfm_d"].rearrange("(c p) t -> p c t", p=P)[:, :, tb * P:(tb + 1) * P], xnf8[:]),
            waits=[(sem_ve, v_trd)], inc=sem_sxf, inc_by=16)
        prev_qdma = s.op("sync", lambda e, tb=tb: e.dma_start(
            d["xqfm_d"].rearrange("(c p) t -> p c t", p=P)[:, :, tb * P:(tb + 1) * P], xqf8[:]),
            waits=[(sem_ve, v_trd)], inc=sem_sxq, inc_by=16)
        for c in range(8):
            def mmd(e, c=c, tb=tb):
                return e.matmul(dnA[:, 0:32], xnf32[:, c, :], Wdt[:, c, :],
                                start=(c == 0), stop=(c == 7))
            if c == 0:
                s.op("tensor", mmd, waits=[(sem_ve, v_trd), (sem_ld, v_par)])
            elif c == 7:
                v_dmm = s.op("tensor", mmd, inc=sem_pe)
            else:
                s.op("tensor", mmd)
        s.op("vector", lambda e, tb=tb: e.tensor_copy(dtraw[:, tb, :], dnA[:, 0:32]),
             waits=[(sem_pe, v_dmm)])
        emit_slabs(2, dnC[:, 0:128], preload=(tb < NTB - 1))
    v_phA_ve = sem_ve.n
    v_phA_pe = sem_pe.n
    ar.release()

    ar.mark()
    win = [ar.alloc([P, 2304], BF16) for _ in range(8)]
    xbc = ar.alloc([P, T])
    cacc = ar.alloc([P, T])
    btile = ar.alloc([P, T], BF16)
    bst = ar.alloc([P, 16, 128], BF16)
    slb["wsl"] = [ar.alloc([P, 1024]) for _ in range(2)]
    slb["wqb"] = {"absx": cacc[:, 0:1024], "t1": cacc[:, 1024:2048],
                  "t2": xbc[:, 0:1024], "q": xbc[:, 1024:2048],
                  "bm": ar.alloc([P, 64]), "sc": ar.alloc([P, 64]), "rs": ar.alloc([P, 64])}
    slb["wqo"] = ar.alloc([P, 1024], BF16)
    slb["wqs"] = ar.alloc([P, 8, 128], BF16)
    for k in range(8):
        s.op("gpsimd", lambda e, k=k: e.dma_start(
            win[k][:], d["L1_attn_W_in"][k * P:(k + 1) * P, 0:2304]),
             waits=[(sem_ve, v_phA_ve), (sem_act, sem_act.n), (sem_gp, sem_gp.n),
                    (sem_lx[0], sem_lx[0].n), (sem_lx[1], sem_lx[1].n)] if k == 0 else [],
             inc=sem_wg, inc_by=16)
    v_win = sem_wg.n
    v_win2 = [0]
    WIN_SPLIT = 18

    def phaseB_tile(n, kind):
        nonlocal prev_evac
        halves = []
        for half in range(2):
            ps = (dnB if half == 0 else dnC)
            for k in range(8):
                for j in range(2):
                    woff = 0 if n < WIN_SPLIT else WIN_SPLIT * P
                    def mmw(e, n=n, k=k, half=half, j=j, ps=ps, woff=woff):
                        return e.matmul(ps[:, j * 512:(j + 1) * 512],
                                        win[k][:, n * P - woff:(n + 1) * P - woff],
                                        d_xnfm[:, k, half * 1024 + j * 512:half * 1024 + (j + 1) * 512],
                                        start=(k == 0), stop=(k == 7), skip_group_check=True)
                    wts = []
                    if k == 0 and j == 0:
                        wts = [(sem_wg, v_win if n < WIN_SPLIT else v_win2[0]),
                               (sem_ld, v_xnfm_ld)]
                        if prev_evac:
                            wts.append((sem_ve, prev_evac))
                    if k == 0 and j == 0:
                        s.op("tensor", mmw, waits=wts)
                    elif k == 7 and j == 1:
                        halves.append(s.op("tensor", mmw, inc=sem_pe))
                    else:
                        s.op("tensor", mmw)
        v_mm = halves[-1]
        if kind == "z":
            v_e1 = s.op("vector", lambda e: e.tensor_copy(btile[:, 0:1024], dnB[:, 0:1024]),
                        waits=[(sem_pe, halves[0])])
            v_e2 = s.op("vector", lambda e: e.tensor_copy(btile[:, 1024:2048], dnC[:, 0:1024]),
                        waits=[(sem_pe, v_mm)])
            prev_evac = v_e2
            src = btile
            v_rdy = v_e2
        else:
            v_e1 = s.op("vector", lambda e: e.tensor_copy(xbc[:, 0:1024], dnB[:, 0:1024]),
                        waits=[(sem_pe, halves[0])])
            v_e2 = s.op("vector", lambda e: e.tensor_copy(xbc[:, 1024:2048], dnC[:, 0:1024]),
                        waits=[(sem_pe, v_mm)])
            prev_evac = v_e2
            ci = n - 16
            v_cv = s.op("vector", lambda e, ci=ci: e.tensor_scalar(
                cacc[:], xbc[:], w4[:, ci, 3:4], None, ALU.mult),
                waits=[(sem_ld, v_par), (sem_act, sem_act.n)])
            v_cva = None
            for jj in range(3):
                sh = 3 - jj
                v_cva = s.op("vector", lambda e, ci=ci, jj=jj, sh=sh: e.scalar_tensor_tensor(
                    cacc[:, sh:T], xbc[:, 0:T - sh], w4[:, ci, jj:jj + 1], cacc[:, sh:T],
                    ALU.mult, ALU.add))
            v_si = s.op("scalar", lambda e, ci=ci: e.activation(
                xbc[:], cacc[:], AF.Silu, bias=cb_c[:, ci:ci + 1]),
                waits=[(sem_ve, v_cva)])
            if kind == "bm":
                v_rdy = s.op("vector", lambda e: e.tensor_copy(BmF[:], xbc[:]),
                             waits=[(sem_act, v_si)])
                src = None
            elif kind == "cm":
                v_rdy = s.op("vector", lambda e: e.tensor_copy(CmF[:], xbc[:]),
                             waits=[(sem_act, v_si)])
                src = None
            else:
                v_rdy = s.op("vector", lambda e: e.tensor_copy(btile[:], xbc[:]),
                             waits=[(sem_act, v_si)])
                src = btile
                if n == 16 and "dbg_xh" in d and "b" in os.environ.get("KDBGSEL", "psmbB"):
                    s.op("sync", lambda e: e.dma_start(d["dbg_xh"], btile[:]),
                         waits=[(sem_ve, v_rdy)], inc=sems["fc"], inc_by=16)
        if src is not None:
            for tbk in range(16):
                v_tt = s.op("tensor", lambda e, tbk=tbk: e.transpose(
                    hp0[:, 0:P].bitcast(BF16)[:, 0:P], src[:, tbk * P:(tbk + 1) * P], identb[:]),
                    waits=[(sem_ve, v_rdy if tbk == 0 else sem_ve.n)], inc=sem_pe)
                s.op("vector", lambda e, tbk=tbk: e.tensor_copy(
                    bst[:, tbk, :], hp0[:, 0:P].bitcast(BF16)[:, 0:P]),
                    waits=[(sem_pe, v_tt)] + ([(sem_sbz, sem_sbz.n)] if tbk == 0 else []))
            dst = d_ztm if kind == "z" else d_xhtm
            col = n if kind == "z" else (n - 16)
            s.op("sync", lambda e, dst=dst, col=col: e.dma_start(
                dst.rearrange("(b p) ch -> p b ch", p=P)[:, :, col * P:(col + 1) * P], bst[:]),
                waits=[(sem_ve, sem_ve.n)], inc=sem_sbz, inc_by=16)
            if n == 16 and "dbg_bst" in d and "b" in os.environ.get("KDBGSEL", "psmbB"):
                s.op("sync", lambda e: e.dma_start(d["dbg_bst"],
                     bst[:].rearrange("p a b -> p (a b)")), inc=sems["fc"], inc_by=16)

    d_xnfm = None
    prev_evac = v_phA_ve

    xnfm_sb = ar.alloc([P, 8, T], BF16)
    v_xnfm_ld = s.op("sync", lambda e: e.dma_start(
        xnfm_sb[:], d["xnfm_d"].rearrange("(c p) t -> p c t", p=P)),
        waits=[(sem_sxf, sem_sxf.n)], inc=sem_ld, inc_by=16)
    d_xnfm = xnfm_sb
    d_ztm = d["ztm_d"]
    d_xhtm = d["xhtm_d"]
    emit_slab_load2(sj[0], extra_gate=[(sem_ve, v_phA_ve)])
    emit_slab_load2(sj[0] + 1, extra_gate=[(sem_ve, v_phA_ve)])

    for n in range(16):
        phaseB_tile(n, "z")
        emit_slabs(2 if n < 2 else 1, dnA[:, 0:128])
    for n in range(16, 32):
        phaseB_tile(n, "xh")
        emit_slabs(1, dnA[:, 0:128])
        if n == 17:
            for k in range(8):
                s.op("gpsimd", lambda e, k=k: e.dma_start(
                    win[k][:, 0:2080], d["L1_attn_W_in"][k * P:(k + 1) * P, 2304:4384]),
                     waits=[(sem_pe, sem_pe.n)] if k == 0 else [],
                     inc=sem_wg, inc_by=16)
            v_win2[0] = sem_wg.n
    phaseB_tile(32, "bm")
    emit_slabs(1, dnA[:, 0:128])
    phaseB_tile(33, "cm")
    emit_slabs(NSLAB, dnA[:, 0:128])
    if "dbg_bm" in d and "B" in os.environ.get("KDBGSEL", "psmbB"):
        s.op("sync", lambda e: e.dma_start(d["dbg_bm"], BmF[:]), waits=[(sem_ve, sem_ve.n)],
             inc=sems["fc"], inc_by=16)
        s.op("sync", lambda e: e.dma_start(d["dbg_cm"], CmF[:]), inc=sems["fc"], inc_by=16)
    v_phB_ve = sem_ve.n
    v_phB_pe = sem_pe.n
    ar.release()

    ar.mark()
    xh_r = ar.alloc([P, 2, DI], BF16)
    z_r = ar.alloc([P, 2, DI], BF16)
    dacs = ar.alloc([P, NH])
    dacsT = ar.alloc([P, 128])
    dtc_sb = ar.alloc([P, NH])
    dtx = ar.alloc([P, NH])
    sp1 = ar.alloc([P, NH])
    sp2 = ar.alloc([P, NH])
    dA_sb = ar.alloc([P, NH])
    bevac = ar.alloc([P, NH])
    expdacs = ar.alloc([P, NH])
    dT1 = ar.alloc([P, 128])
    dtT = ar.alloc([P, 128])
    dd_tm = ar.alloc([P, NH])
    cdbc = ar.alloc([P, NH])
    cdrow = ar.alloc([P, NH])
    Gmask = ar.alloc([P, P])
    mex = [ar.alloc([P, P]) for _ in range(2)]
    mexp = [ar.alloc([P, P]) for _ in range(2)]
    mh = [ar.alloc([P, P], BF16) for _ in range(2)]
    yoff = ar.alloc([P, DI])
    y_sb = ar.alloc([P, DI])
    zsil = ar.alloc([P, DI])
    ysq = ar.alloc([P, DI])
    ynb = ar.alloc([P, DI], BF16)
    ynfm = ar.alloc([P, 16, 128], BF16)
    BmT = ar.alloc([P, P], BF16)
    xw_sb = ar.alloc([P, DI], BF16)
    s1b = ar.alloc([P, DM], BF16)
    st2 = ar.alloc([P, 4])
    mex_done = {}

    ch_ld = {}
    ch_cons = {}

    def c_load(c):
        waits = [(sem_sbz, sem_sbz.n), (sem_ve, v_phB_ve)] if c <= 1 else []
        if c >= 2:
            waits = [(sem_ve, ch_cons[c - 2])]
        sm = sem_cc[c % 2]
        s.op("sync", lambda e, c=c: e.dma_start(
            xh_r[:, c % 2, :], d_xhtm[c * P:(c + 1) * P, :]), waits=waits, inc=sm, inc_by=16)
        s.op("sync", lambda e, c=c: e.dma_start(
            z_r[:, c % 2, :], d_ztm[c * P:(c + 1) * P, :]), inc=sm, inc_by=16)
        ch_ld[c] = (sm, sm.n)

    c_load(0)
    c_load(1)
    v_ss1_prev = 0
    for c in range(NTB):
        xh_c = xh_r[:, c % 2, :]
        z_c = z_r[:, c % 2, :]
        xh3 = xh_c.rearrange("p (h q) -> p h q", q=HD)
        v_1 = s.op("vector", lambda e, c=c: e.tensor_tensor(dtx[:], dtraw[:, c, :], dtb_bc, ALU.add),
                   waits=[(sem_ld, v_par)] + ([(sem_ve, v_phB_ve), (sem_pe, v_phB_pe)] if c == 0 else []))
        s.op("scalar", lambda e: e.activation(sp1[:], dtx[:], AF.Abs), waits=[(sem_ve, v_1)])
        s.op("scalar", lambda e: e.activation(sp1[:], sp1[:], AF.Exp, scale=-1.0))
        v_sp = s.op("scalar", lambda e: e.activation(sp1[:], sp1[:], AF.Ln, bias=cst[:, 2:3]),
                    waits=[(sem_ve, v_one)])
        s.op("vector", lambda e: e.tensor_scalar(sp2[:], dtx[:], 0.0, None, ALU.max))
        v_dt = s.op("vector", lambda e: e.tensor_tensor(dtc_sb[:], sp1[:], sp2[:], ALU.add),
                    waits=[(sem_act, v_sp)])
        v_dA = s.op("vector", lambda e: e.tensor_tensor(dA_sb[:], dtc_sb[:], A_bc, ALU.mult))
        v_cs = s.op("tensor", lambda e: e.matmul(hp0[:, 0:NH], lt_sb[:], dA_sb[:],
                                                 start=True, stop=True),
                    waits=[(sem_ve, v_dA)], inc=sem_pe)
        v_dacs = s.op("vector", lambda e: e.tensor_copy(dacs[:], hp0[:, 0:NH]),
                      waits=[(sem_pe, v_cs)])
        v_tT = s.op("tensor", lambda e: e.transpose(hp1[0:32, 0:P], dacs[:], ident[:]),
                    waits=[(sem_ve, v_dacs)], inc=sem_pe)
        v_dacsT = s.op("vector", lambda e: e.tensor_copy(dacsT[0:32, :], hp1[0:32, 0:P]),
                       waits=[(sem_pe, v_tT)])
        v_ed = s.op("scalar", lambda e: e.activation(expdacs[:], dacs[:], AF.Exp),
                    waits=[(sem_ve, v_dacs)])
        v_ln = s.op("scalar", lambda e: e.activation(bevac[:], dtc_sb[:], AF.Ln),
                    waits=[(sem_ve, v_dt)])
        v_be = s.op("vector", lambda e: e.tensor_tensor(bevac[:], bevac[:], dacs[:], ALU.subtract),
                    waits=[(sem_act, v_ln)])
        v_d1 = s.op("vector", lambda e: e.tensor_scalar(dT1[0:32, :], dacsT[0:32, :],
                                                        dacsT[0:32, 127:128], None, ALU.subtract),
                    waits=[(sem_ve, v_dacsT)])
        v_d2 = s.op("scalar", lambda e: e.activation(dT1[0:32, :], dT1[0:32, :], AF.Exp, scale=-1.0),
                    waits=[(sem_ve, v_d1)])
        v_t3 = s.op("tensor", lambda e: e.transpose(hp1[0:32, 64:192], dtc_sb[:], ident[:]),
                    waits=[(sem_ve, v_be)], inc=sem_pe)
        v_dtT = s.op("vector", lambda e: e.tensor_copy(dtT[0:32, :], hp1[0:32, 64:192]),
                     waits=[(sem_pe, v_t3)])
        v_dd = s.op("vector", lambda e: e.tensor_tensor(dT1[0:32, :], dT1[0:32, :], dtT[0:32, :],
                                                        ALU.mult),
                    waits=[(sem_act, v_d2)])
        v_t4 = s.op("tensor", lambda e: e.transpose(hp1[:, 96:128], dT1[0:32, :], ident[0:32, 0:32]),
                    waits=[(sem_ve, v_dd)], inc=sem_pe)
        v_ddtm = s.op("vector", lambda e: e.tensor_copy(dd_tm[:], hp1[0:P, 96:128]),
                      waits=[(sem_pe, v_t4)])
        v_t5 = s.op("tensor", lambda e: e.matmul(hp1[0:1, 128:160], ident[:, 127:128], dacs[:],
                                                 start=True, stop=True), inc=sem_pe)
        v_cd1 = s.op("scalar", lambda e: e.activation(cdrow[0:1, :], hp1[0:1, 128:160], AF.Exp),
                     waits=[(sem_pe, v_t5)])
        v_t6 = s.op("tensor", lambda e: e.matmul(hp1[:, 160:192], onesf, cdrow[0:1, :],
                                                 start=True, stop=True),
                    waits=[(sem_act, v_cd1)], inc=sem_pe)
        v_cdbc = s.op("vector", lambda e: e.tensor_copy(cdbc[:], hp1[:, 160:192]),
                      waits=[(sem_pe, v_t6)])
        if c == 0 and "dbg_d" in d and "m" in os.environ.get("KDBGSEL", "psmbB"):
            s.op("sync", lambda e: e.dma_start(d["dbg_d"][:, 160:192], dacs[:]),
                 waits=[(sem_ve, v_dacs)], inc=sems["fc"], inc_by=16)
            s.op("sync", lambda e: e.dma_start(d["dbg_d"][:, 192:224], dd_tm[:]),
                 waits=[(sem_ve, v_ddtm)], inc=sems["fc"], inc_by=16)
            s.op("sync", lambda e: e.dma_start(d["dbg_d"][:, 224:256], cdbc[:]),
                 waits=[(sem_ve, v_cdbc)], inc=sems["fc"], inc_by=16)
            s.op("sync", lambda e: e.dma_start(d["dbg_d"][:, 96:128], dtc_sb[:]),
                 waits=[(sem_ve, v_dt)], inc=sems["fc"], inc_by=16)
        v_g = s.op("tensor", lambda e, c=c: e.matmul(dnC[:, 0:P], BmF[:, c * P:(c + 1) * P],
                                                     CmF[:, c * P:(c + 1) * P],
                                                     start=True, stop=True),
                   waits=[(sem_ve, sem_ve.n)], inc=sem_pe)
        v_gm = s.op("vector", lambda e: e.tensor_tensor(Gmask[:], dnC[:, 0:P], lt_sb[:], ALU.mult),
                    waits=[(sem_pe, v_g)])
        v_sbf = s.op("vector", lambda e: e.tensor_copy(state_bf[:], state[:]),
                     waits=[(sem_ve, v_zero)] if c == 0 else [])
        if c == 1 and "dbg_st" in d and "s" in os.environ.get("KDBGSEL", "psmbB"):
            s.op("sync", lambda e: e.dma_start(d["dbg_st"], state_bf[:]),
                 waits=[(sem_ve, v_sbf)], inc=sems["fc"], inc_by=16)
        offmm = []
        for jj in range(4):
            ps = (dnA if jj < 2 else dnB)
            def mmo(e, jj=jj, ps=ps, c=c):
                return e.matmul(ps[:, (jj % 2) * 512:(jj % 2 + 1) * 512],
                                CmF[:, c * P:(c + 1) * P],
                                state_bf[:, (jj) * 512:(jj + 1) * 512],
                                start=True, stop=True, skip_group_check=True)
            wts = [(sem_ve, v_sbf)]
            if c > 0:
                wts.append((sem_ve, sem_ve.n))
            offmm.append(s.op("tensor", mmo, waits=wts if jj == 0 else [],
                              inc=sem_pe if jj == 3 else None))
        v_off = offmm[-1]
        ed3a = expdacs[:, 0:16].unsqueeze(-1).broadcast_to((P, 16, HD))
        ed3b = expdacs[:, 16:32].unsqueeze(-1).broadcast_to((P, 16, HD))
        v_ev1 = s.op("vector", lambda e: e.tensor_tensor(
            yoff[:, 0:1024].rearrange("p (h q) -> p h q", q=HD), dnA[:, 0:1024].rearrange("p (h q) -> p h q", q=HD),
            ed3a, ALU.mult), waits=[(sem_pe, v_off), (sem_act, sem_act.n)])
        v_ev2 = s.op("vector", lambda e: e.tensor_tensor(
            yoff[:, 1024:2048].rearrange("p (h q) -> p h q", q=HD), dnB[:, 0:1024].rearrange("p (h q) -> p h q", q=HD),
            ed3b, ALU.mult))
        ydmm = []
        for h in range(NH):
            pr = (hp0[:, 256:384] if h % 2 == 0 else hp1[:, 256:384])
            v_bc = s.op("tensor", lambda e, h=h, pr=pr: e.matmul(
                pr[:, 0:P], eh_sb[:, h * P:(h + 1) * P], dacsT[0:32, :],
                start=True, stop=True),
                waits=[(sem_ve, sem_ve.n)] if h < 2 else [(sem_ve, mex_done[h - 2])],
                inc=sem_pe)
            v_mx = s.op("vector", lambda e, h=h, pr=pr: e.tensor_scalar(
                mex[h % 2][:], pr[:, 0:P], bevac[:, h:h + 1], 80.0, ALU.add, ALU.min),
                waits=[(sem_pe, v_bc)])
            mex_done[h] = v_mx
            v_me = s.op("scalar", lambda e, h=h: e.activation(mexp[h % 2][:], mex[h % 2][:], AF.Exp),
                        waits=[(sem_ve, v_mx)])
            v_mh = s.op("vector", lambda e, h=h: e.tensor_tensor(mh[h % 2][:], mexp[h % 2][:],
                                                                 Gmask[:], ALU.mult),
                        waits=[(sem_act, v_me), (sem_ve, v_gm)])
            ps = (dnA if h < 16 else dnB)
            def mmy(e, h=h, ps=ps, xh3=xh3):
                return e.matmul(ps[:, (h % 16) // 8 * 512 + (h % 8) * 64:
                                   (h % 16) // 8 * 512 + (h % 8) * 64 + 64],
                                mh[h % 2][:], xh3[:, h, :],
                                start=(h % 8 == 0), stop=(h % 8 == 7), skip_group_check=True)
            wts = [(sem_ve, v_mh)]
            if h % 8 == 0:
                wts.append((sem_ve, v_ev2))
                wts.append(ch_ld[c])
            ydmm.append(s.op("tensor", mmy, waits=wts, inc=sem_pe if h == NH - 1 else None))
        v_yd = ydmm[-1]
        v_y1 = s.op("vector", lambda e: e.tensor_tensor(y_sb[:, 0:1024], dnA[:, 0:1024],
                                                        yoff[:, 0:1024], ALU.add),
                    waits=[(sem_pe, v_yd)])
        v_y2 = s.op("vector", lambda e: e.tensor_tensor(y_sb[:, 1024:2048], dnB[:, 0:1024],
                                                        yoff[:, 1024:2048], ALU.add))
        d3 = D_bc.unsqueeze(-1).broadcast_to((P, NH, HD))
        v_y3 = s.op("vector", lambda e, xh3=xh3: e.tensor_tensor(
            yoff[:].rearrange("p (h q) -> p h q", q=HD), xh3, d3, ALU.mult))
        v_y4 = s.op("vector", lambda e: e.tensor_tensor(y_sb[:], y_sb[:], yoff[:], ALU.add))
        v_zs = s.op("scalar", lambda e, z_c=z_c: e.activation(zsil[:], z_c, AF.Silu),
                    waits=[ch_ld[c]])
        v_y5 = s.op("vector", lambda e: e.tensor_tensor(y_sb[:], y_sb[:], zsil[:], ALU.mult),
                    waits=[(sem_act, v_zs)])
        s.op("scalar", lambda e: e.activation(ysq[:], y_sb[:], AF.Square, accum_out=st2[:, 0:1]),
             waits=[(sem_ve, v_y5)])
        s.op("scalar", lambda e: e.activation(st2[:, 0:1], st2[:, 0:1], AF.Ln, bias=cst[:, 0:1],
                                              scale=1.0 / DI))
        v_rst = s.op("scalar", lambda e: e.activation(st2[:, 0:1], st2[:, 0:1], AF.Exp, scale=-0.5))
        v_yn = s.op("vector", lambda e: e.scalar_tensor_tensor(ynb[:], y_sb[:], st2[:, 0:1],
                                                               normw_bc, ALU.mult, ALU.mult),
                    waits=[(sem_act, v_rst)])
        for k in range(16):
            v_tt = s.op("tensor", lambda e, k=k: e.transpose(
                hp0[:, 384:512].bitcast(BF16)[:, 0:P], ynb[:, k * P:(k + 1) * P], identb[:]),
                waits=[(sem_ve, v_yn if k == 0 else sem_ve.n)], inc=sem_pe)
            s.op("vector", lambda e, k=k: e.tensor_copy(ynfm[:, k, :],
                                                        hp0[:, 384:512].bitcast(BF16)[:, 0:P]),
                 waits=[(sem_pe, v_tt)])
        v_tr_done = sem_ve.n
        womm = None
        for k in range(16):
            for j in range(2):
                def mmw2(e, k=k, j=j):
                    return e.matmul(dnC[:, j * 512:(j + 1) * 512], ynfm[:, k, :],
                                    Wout_sb[:, k, j * 512:(j + 1) * 512],
                                    start=(k == 0), stop=(k == 15), skip_group_check=True)
                wts = []
                if k == 0 and j == 0:
                    wts = [(sem_ve, v_tr_done), (sem_wg, v_wout)]
                if k == 15 and j == 1:
                    womm = s.op("tensor", mmw2, waits=wts, inc=sem_pe)
                else:
                    s.op("tensor", mmw2, waits=wts)
        v_s1 = s.op("vector", lambda e: e.tensor_copy(s1b[:], dnC[:, 0:1024]),
                    waits=[(sem_pe, womm), (sem_ss1, v_ss1_prev)])
        v_ss1_prev = s.op("sync", lambda e, c=c: e.dma_start(
            d["s1_d"][c * P:(c + 1) * P, :], s1b[:]),
            waits=[(sem_ve, v_s1)], inc=sem_ss1, inc_by=16)
        v_bt = s.op("tensor", lambda e, c=c: e.transpose(
            hp0[:, 256:384].bitcast(BF16)[:, 0:P], BmF[:, c * P:(c + 1) * P], identb[:]),
            waits=[(sem_ve, sem_ve.n)], inc=sem_pe)
        v_bmt = s.op("vector", lambda e: e.tensor_copy(BmT[:], hp0[:, 256:384].bitcast(BF16)[:, 0:P]),
                     waits=[(sem_pe, v_bt)])
        dd3 = dd_tm.unsqueeze(-1).broadcast_to((P, NH, HD))
        v_xw = s.op("vector", lambda e, xh3=xh3: e.tensor_tensor(
            xw_sb[:].rearrange("p (h q) -> p h q", q=HD), xh3, dd3, ALU.mult),
            waits=[(sem_ve, v_ddtm)])
        smm = []
        for jj in range(4):
            ps = (dnA if jj < 2 else dnB)
            def mms2(e, jj=jj, ps=ps):
                return e.matmul(ps[:, (jj % 2) * 512:(jj % 2 + 1) * 512], BmT[:],
                                xw_sb[:, jj * 512:(jj + 1) * 512],
                                start=True, stop=True, skip_group_check=True)
            wts = []
            if jj == 0:
                wts = [(sem_ve, max(v_xw, v_bmt, v_y4))]
            smm.append(s.op("tensor", mms2, waits=wts, inc=sem_pe if jj == 3 else None))
        v_smm = smm[-1]
        cd3 = cdbc.unsqueeze(-1).broadcast_to((P, NH, HD))
        v_su1 = s.op("vector", lambda e: e.tensor_tensor(
            state[:].rearrange("p (h q) -> p h q", q=HD),
            state[:].rearrange("p (h q) -> p h q", q=HD), cd3, ALU.mult),
            waits=[(sem_ve, v_cdbc), (sem_pe, v_smm)])
        s.op("vector", lambda e: e.tensor_tensor(state[:, 0:1024], state[:, 0:1024],
                                                 dnA[:, 0:1024], ALU.add))
        v_sup = s.op("vector", lambda e: e.tensor_tensor(state[:, 1024:2048], state[:, 1024:2048],
                                                         dnB[:, 0:1024], ALU.add))
        ch_cons[c] = v_sup
        if c + 2 < NTB:
            c_load(c + 2)
    v_phC_ve = sem_ve.n
    v_phC_pe = sem_pe.n
    ar.release()
    ar.release()

    ar.mark()
    wgT = ar.alloc([P, 8, 2816], BF16)
    wuT = ar.alloc([P, 8, 2816], BF16)
    wdT = ar.alloc([P, 22, 1024], BF16)
    xq8 = ar.alloc([P, 2, 8, 128], BF16)
    sg = ar.alloc([P, 2, 512])
    htm = ar.alloc([P, 2816])
    hq_tm = ar.alloc([P, 2, 2816], BF16)
    hqf = ar.alloc([P, 22, 128], BF16)
    dq_absx = ar.alloc([P, 1408])
    dq_t1 = ar.alloc([P, 1408], BF16)
    dq_q = ar.alloc([P, 1408])
    dq_bm = ar.alloc([P, 88])
    dq_sc = ar.alloc([P, 88])
    dq_rs = ar.alloc([P, 88])
    x1r = ar.alloc([P, 2, DM])
    s1r = ar.alloc([P, 2, DM], BF16)
    gsb = ar.alloc([P, DM])
    xo = ar.alloc([P, DM])

    wld = {}

    def d_wload(tt):
        waits = [(sem_wq, sem_wq.n), (sem_ve, v_phC_ve), (sem_pe, v_phC_pe),
                 (sem_act, sem_act.n), (sem_ss1, v_ss1_prev)]
        s.op("sync", lambda e: e.dma_start(
            wgT[:], d["wgt_d"].rearrange("(c p) n -> p c n", p=P)), waits=waits, inc=sem_wq, inc_by=16)
        s.op("sync", lambda e: e.dma_start(
            wuT[:], d["wut_d"].rearrange("(c p) n -> p c n", p=P)), inc=sem_wq, inc_by=16)
        wld["gu"] = sem_wq.n
        s.op("sync", lambda e: e.dma_start(
            wdT[:], d["wdt_d"].rearrange("(c p) n -> p c n", p=P)), inc=sem_wq, inc_by=16)
        wld["d"] = sem_wq.n

    d_pe_after = {}
    d_wload(0)
    xq_ld = {}
    x1_ld = {}

    xs_ld = {}

    def d_xqload_xq(tbk):
        if tbk <= 1:
            waits = [(sem_sxq, sem_sxq.n), (sem_ss1, sem_ss1.n)]
        else:
            waits = [(sem_pe, fwd_pe[tbk - 2])]
        s.op("sync", lambda e, tbk=tbk: e.dma_start(
            xq8[:, tbk % 2, :, :],
            d["xqfm_d"].rearrange("(c p) t -> p c t", p=P)[:, :, tbk * P:(tbk + 1) * P]),
            waits=waits, inc=sem_xq, inc_by=16)
        xq_ld[tbk] = sem_xq.n

    def d_xqload_xs(tbk):
        waits = [] if tbk <= 1 else [(sem_ve, d_cons[tbk - 2])]
        s.op("sync", lambda e, tbk=tbk: e.dma_start(
            x1r[:, tbk % 2, :], d["x1_l1"][tbk * P:(tbk + 1) * P, :]),
            waits=waits, inc=sem_xq, inc_by=16)
        s.op("sync", lambda e, tbk=tbk: e.dma_start(
            s1r[:, tbk % 2, :], d["s1_d"][tbk * P:(tbk + 1) * P, :]), inc=sem_xq, inc_by=16)
        xs_ld[tbk] = sem_xq.n

    def d_xqload(tbk):
        d_xqload_xq(tbk)
        d_xqload_xs(tbk)

    d_tbk_pe = {}
    d_cons = {}
    d_xqload(0)
    d_xqload(1)
    prev_x2_st = [0]
    CH = [(0, 512), (512, 512), (1024, 512), (1536, 512), (2048, 512), (2304, 512)]
    CH = [(0, 512), (512, 512), (1024, 512), (1536, 512), (2048, 512), (2560, 256)]
    evac_done = {}
    quant_done = {}
    fwd_pe = {}

    def d_fwd(tbk):
        buf = tbk % 2
        for ci, (c0, cw) in enumerate(CH):
            gps = dnA[:, (ci % 2) * 512:(ci % 2) * 512 + cw]
            ups = dnB[:, (ci % 2) * 512:(ci % 2) * 512 + cw]
            for (mat, ps) in ((wgT, gps), (wuT, ups)):
                for k in range(8):
                    def mmg(e, mat=mat, ps=ps, k=k, c0=c0, cw=cw, buf=buf):
                        return e.matmul(ps, xq8[:, buf, k, :], mat[:, k, c0:c0 + cw],
                                        start=(k == 0), stop=(k == 7), skip_group_check=True)
                    wts = []
                    if k == 0:
                        if ci >= 2:
                            wts.append((sem_ve, evac_done[(tbk, ci - 2)]))
                        elif tbk > 0:
                            wts.append((sem_ve, evac_done[(tbk - 1, len(CH) - 2 + ci)]))
                        if ci == 0 and mat is wgT:
                            wts += [(sem_xq, xq_ld[tbk]), (sem_wq, wld["gu"])]
                    if k == 7 and mat is wuT:
                        v_mm = s.op("tensor", mmg, waits=wts, inc=sem_pe)
                    else:
                        s.op("tensor", mmg, waits=wts)
            v_sg = s.op("scalar", lambda e, ci=ci, gps=gps, cw=cw: e.activation(
                sg[:, ci % 2, 0:cw], gps, AF.Silu), waits=[(sem_pe, v_mm)])
            evac_done[(tbk, ci)] = s.op("vector", lambda e, ci=ci, ups=ups, c0=c0, cw=cw, buf=buf:
                e.tensor_tensor(htm[:, c0:c0 + cw], sg[:, ci % 2, 0:cw], ups, ALU.mult),
                waits=[(sem_act, v_sg)])
        fwd_pe[tbk] = sem_pe.n
        d_tbk_pe[tbk] = sem_pe.n
        for piece in range(2):
            p0 = piece * 1408
            dqb = {"absx": dq_absx, "t1": dq_t1,
                   "t2": hq_tm[:, buf, p0:p0 + 1408], "q": dq_q,
                   "bm": dq_bm, "sc": dq_sc, "rs": dq_rs}
            v_hq = emit_quant(s, sems, dqb, htm[:, p0:p0 + 1408],
                              hq_tm[:, buf, p0:p0 + 1408], 1408,
                              src_ready=[(sem_ve, sem_ve.n)])
        quant_done[tbk] = v_hq

    def d_bwd(tbk):
        buf = tbk % 2
        for n in range(22):
            v_tq = s.op("tensor", lambda e, n=n, buf=buf: e.transpose(
                hp0[:, 128:256].bitcast(BF16)[:, 0:P], hq_tm[:, buf, n * P:(n + 1) * P], identb[:]),
                waits=[(sem_ve, quant_done[tbk] if n == 0 else sem_ve.n)], inc=sem_pe)
            s.op("vector", lambda e, n=n: e.tensor_copy(hqf[:, n, :],
                                                        hp0[:, 128:256].bitcast(BF16)[:, 0:P]),
                 waits=[(sem_pe, v_tq)])
        v_hqf = sem_ve.n
        dmm = None
        for n in range(22):
            for j in range(2):
                def mmd2(e, n=n, j=j):
                    return e.matmul(dnC[:, j * 512:(j + 1) * 512], hqf[:, n, :],
                                    wdT[:, n, j * 512:(j + 1) * 512],
                                    start=(n == 0), stop=(n == 21), skip_group_check=True)
                wts = []
                if n == 0 and j == 0:
                    wts = [(sem_ve, v_hqf), (sem_act, sem_act.n), (sem_wq, wld["d"])]
                if n == 21 and j == 1:
                    dmm = s.op("tensor", mmd2, waits=wts, inc=sem_pe)
                else:
                    s.op("tensor", mmd2, waits=wts)
        v_gs = s.op("scalar", lambda e: e.activation(gsb[:], dnC[:, 0:1024], AF.Silu),
                    waits=[(sem_pe, dmm)])
        v_t3 = s.op("vector", lambda e, buf=buf: e.tensor_tensor(gsb[:], s1r[:, buf, :], gsb[:],
                                                                 ALU.mult),
                    waits=[(sem_act, v_gs), (sem_xq, xs_ld[tbk])])
        v_xo = s.op("vector", lambda e, buf=buf: e.tensor_tensor(xo[:], x1r[:, buf, :], gsb[:],
                                                                 ALU.add),
                    waits=[(sem_st, prev_x2_st[0])] if prev_x2_st[0] else [])
        d_cons[tbk] = v_xo
        prev_x2_st[0] = s.op("sync", lambda e, tbk=tbk: e.dma_start(
            d["x2"][tbk * P:(tbk + 1) * P, :], xo[:]),
            waits=[(sem_ve, v_xo)], inc=sem_st, inc_by=16)

    d_fwd(0)
    for tbk in range(1, NTB):
        if tbk + 1 < NTB:
            d_xqload_xq(tbk + 1)
        d_fwd(tbk)
        d_bwd(tbk - 1)
        if tbk + 1 < NTB:
            d_xqload_xs(tbk + 1)
    d_bwd(NTB - 1)
    ar.release()


def build_program(which="l0"):
    nc = bass.Bass("TRN2", target_bir_lowering=False, detect_race_conditions=False)
    d = {}
    d["u"] = nc.dram_tensor("u", [T, DM], F32, kind="ExternalInput").ap()
    l0_specs = [
            ("L0_attn_dt", [512]), ("L0_attn_A", [512]), ("L0_attn_Bmat", [1024, 512]),
            ("L0_attn_C", [512, 1024]), ("L0_attn_D", [1024]), ("L0_attn_norm_w", [1024]),
            ("L0_ffn_gate_w", [1024, 8]), ("L0_ffn_w1", [8, 1024, 4096]), ("L0_ffn_b1", [8, 4096]),
            ("L0_ffn_w2", [8, 4096, 1024]), ("L0_ffn_b2", [8, 1024]),
            ("L0_ffn_gamma", [1024]), ("L0_ffn_beta", [1024])]
    l1_specs = [
            ("L1_attn_W_in", [1024, 4384]), ("L1_attn_conv_w", [2304, 4]),
            ("L1_attn_conv_b", [2304]), ("L1_attn_dt_bias", [32]), ("L1_attn_A_log", [32]),
            ("L1_attn_D", [32]), ("L1_attn_norm_w", [2048]), ("L1_attn_W_out", [2048, 1024]),
            ("L1_ffn_w_gate", [2816, 1024]), ("L1_ffn_w_up", [2816, 1024]),
            ("L1_ffn_w_down", [1024, 2816])]
    specs = {"l0": l0_specs, "l1": l1_specs, "both": l0_specs + l1_specs}[which]
    for name, shape in specs:
        d[name] = nc.dram_tensor(name, shape, F32, kind="ExternalInput").ap()
    d["c_ident"] = nc.dram_tensor("c_ident", [P, P], F32, kind="ExternalInput").ap()
    d["c_onesf"] = nc.dram_tensor("c_onesf", [1, P], F32, kind="ExternalInput").ap()
    d["c_lt"] = nc.dram_tensor("c_lt", [P, P], F32, kind="ExternalInput").ap()
    d["c_eh"] = nc.dram_tensor("c_eh", [32, 32 * P], F32, kind="ExternalInput").ap()
    d["c_identb"] = nc.dram_tensor("c_identb", [P, P], BF16, kind="ExternalInput").ap()
    d["c_onesb"] = nc.dram_tensor("c_onesb", [1, P], BF16, kind="ExternalInput").ap()
    _ik = "ExternalOutput" if os.environ.get("KDBG") else "Internal"
    d["xn2fm_d"] = nc.dram_tensor("xn2fm_d", [DM, T], BF16, kind=_ik).ap()
    d["xn_d"] = nc.dram_tensor("xn_d", [T, DM], BF16, kind=_ik).ap()
    d["xscan_d"] = nc.dram_tensor("xscan_d", [512, T], BF16, kind=_ik).ap()
    if which == "l0":
        d["x1"] = nc.dram_tensor("x1", [T, DM], F32, kind="ExternalOutput").ap()
    elif which == "both":
        d["x1"] = nc.dram_tensor("x1", [T, DM], F32, kind="Internal").ap()
        d["x1_l1"] = d["x1"]
        d["xnfm_d"] = nc.dram_tensor("xnfm_d", [DM, T], BF16, kind="Internal").ap()
        d["xqfm_d"] = nc.dram_tensor("xqfm_d", [DM, T], BF16, kind="Internal").ap()
        d["ztm_d"] = nc.dram_tensor("ztm_d", [T, 2048], BF16, kind="Internal").ap()
        d["xhtm_d"] = nc.dram_tensor("xhtm_d", [T, 2048], BF16, kind="Internal").ap()
        d["s1_d"] = nc.dram_tensor("s1_d", [T, DM], BF16, kind="Internal").ap()
        d["wgt_d"] = nc.dram_tensor("wgt_d", [1024, 2816], BF16, kind="Internal").ap()
        d["wut_d"] = nc.dram_tensor("wut_d", [1024, 2816], BF16, kind="Internal").ap()
        d["wdt_d"] = nc.dram_tensor("wdt_d", [2816, 1024], BF16, kind="Internal").ap()
        d["x2"] = nc.dram_tensor("x2", [T, DM], F32, kind="ExternalOutput").ap()
    else:
        d["x1_l1"] = nc.dram_tensor("x1_l1", [T, DM], F32, kind="ExternalInput").ap()
        d["xnfm_d"] = nc.dram_tensor("xnfm_d", [DM, T], BF16, kind="Internal").ap()
        d["xqfm_d"] = nc.dram_tensor("xqfm_d", [DM, T], BF16, kind="Internal").ap()
        d["ztm_d"] = nc.dram_tensor("ztm_d", [T, 2048], BF16, kind="Internal").ap()
        d["xhtm_d"] = nc.dram_tensor("xhtm_d", [T, 2048], BF16, kind="Internal").ap()
        d["s1_d"] = nc.dram_tensor("s1_d", [T, DM], BF16, kind="Internal").ap()
        d["wgt_d"] = nc.dram_tensor("wgt_d", [1024, 2816], BF16, kind="Internal").ap()
        d["wut_d"] = nc.dram_tensor("wut_d", [1024, 2816], BF16, kind="Internal").ap()
        d["wdt_d"] = nc.dram_tensor("wdt_d", [2816, 1024], BF16, kind="Internal").ap()
        d["x2"] = nc.dram_tensor("x2", [T, DM], F32, kind="ExternalOutput").ap()
        if os.environ.get("KDBG2"):
            d["dbg_d"] = nc.dram_tensor("dbg_d", [P, 256], F32, kind="ExternalOutput").ap()
            d["dbg_bm"] = nc.dram_tensor("dbg_bm", [P, T], BF16, kind="ExternalOutput").ap()
            d["dbg_cm"] = nc.dram_tensor("dbg_cm", [P, T], BF16, kind="ExternalOutput").ap()
            d["dbg_xh"] = nc.dram_tensor("dbg_xh", [P, T], BF16, kind="ExternalOutput").ap()
            d["dbg_bst"] = nc.dram_tensor("dbg_bst", [P, T], BF16, kind="ExternalOutput").ap()
            d["dbg_st"] = nc.dram_tensor("dbg_st", [P, T], BF16, kind="ExternalOutput").ap()

    from contextlib import ExitStack
    with ExitStack() as stack:
        en = stack.enter_context
        ident_t = en(nc.sbuf_tensor("ident_sb", [P, P], F32))
        identb_t = en(nc.sbuf_tensor("identb_sb", [P, P], BF16))
        onesb_t = en(nc.sbuf_tensor("onesb_sb", [1, P], BF16))
        if which == "l0":
            cw_t = en(nc.sbuf_tensor("cw_sb", [P, NTB, 8], F32))
            accb_t = en(nc.sbuf_tensor("accb", [P, NTB, DM], BF16))
            b1c_t = en(nc.sbuf_tensor("b1c", [P, 8, 32], F32))
            w2b_t = en(nc.sbuf_tensor("w2b", [P, 32, 1024], BF16))
            b2r_t = en(nc.sbuf_tensor("b2r", [1, 2, 1024], BF16))
            xnq_t = en(nc.sbuf_tensor("xnq", [P, 8, 512], BF16))
            AR_WORDS = 24800
        else:
            AR_WORDS = 52800
        ar_t = en(nc.sbuf_tensor("arena", [P, AR_WORDS], F32))
        cst_t = en(nc.sbuf_tensor("cst", [P, 4], F32))
        dnA = en(nc.psum_tensor("dnA", [P, 1024], F32))
        dnB = en(nc.psum_tensor("dnB", [P, 1024], F32))
        dnC = en(nc.psum_tensor("dnC", [P, 1024], F32))
        hp0 = en(nc.psum_tensor("hp0", [P, 512], F32))
        hp1 = en(nc.psum_tensor("hp1", [P, 512], F32))
        dma_sems = ("ld", "lxa", "lxb", "sxn", "sx2", "sscan", "xq", "w1s", "w2s",
                    "fc", "fla", "flb", "st", "wqh")
        sems = {k: Sem(en(nc.semaphore(k)), dma=True) for k in dma_sems}
        sems.update({k: Sem(en(nc.semaphore(k))) for k in ("pe", "act", "ve", "gp")})
        block = en(nc.Block())

        s = Sched(self_serial={"vector": sems["ve"], "scalar": sems["act"],
                               "gpsimd": sems["gp"]})
        ctx = dict(nc=nc, s=s, dram=d, ar=Arena(ar_t[:], AR_WORDS),
                   ident=ident_t[:], identb=identb_t[:], onesb=onesb_t[:],
                   dnA=dnA[:], dnB=dnB[:], dnC=dnC[:], hp0=hp0[:], hp1=hp1[:],
                   sems=sems, cst=cst_t[:])
        if which == "l0":
            ctx.update(cw_sb=cw_t[:], accb=accb_t[:], b1c=b1c_t[:],
                       w2b=w2b_t[:], b2r=b2r_t[:], xnq=xnq_t[:])
        v_c0 = s.op("vector", lambda e: e.memset(cst_t[:, 0:1], 1e-5), inc=sems["ve"])
        v_c1 = s.op("vector", lambda e: e.memset(cst_t[:, 1:2], 1e-6), inc=sems["ve"])
        s.wait("scalar", sems["ve"], v_c1)
        s.op("sync", lambda e: e.dma_start(ctx["ident"], d["c_ident"]), inc=sems["ld"], inc_by=16)
        s.op("sync", lambda e: e.dma_start(ctx["identb"], d["c_identb"]), inc=sems["ld"], inc_by=16)
        s.op("sync", lambda e: e.dma_start(ctx["onesb"], d["c_onesb"]), inc=sems["ld"], inc_by=16)
        s.wait("tensor", sems["ld"], 48)

        if which == "l0":
            build_l0(ctx)
        elif which == "l1":
            build_l1(ctx)
        else:
            ctx["ar"].mark()
            build_l0(ctx)
            ctx["ar"].release()
            ctx["l1_gate"] = [(sems["ve"], sems["ve"].n), (sems["pe"], sems["pe"].n),
                              (sems["st"], sems["st"].n)]
            build_l1(ctx)
        for k in dma_sems:
            if sems[k].n:
                s.wait("sync", sems[k], sems[k].n)
        s.build(block)
    return nc


def kernel(**inputs):
    consts = _consts()
    wdict = {k: np.ascontiguousarray(v) for k, v in inputs.items() if k != "L0_attn_u"}
    nc = build_program("both")
    in_maps = []
    for b in range(8):
        m = {"u": np.ascontiguousarray(inputs["L0_attn_u"][b])}
        m.update(wdict)
        m.update(consts)
        in_maps.append(m)
    res = run_bass_kernel_spmd(nc, in_maps, core_ids=list(range(8)))
    return np.stack([np.asarray(r["x2"], dtype=np.float32) for r in res.results], axis=0)
